# version 03
# baseline (speedup 1.0000x reference)
import contextlib
import numpy as np
import ml_dtypes
import concourse.bass as bass
import concourse.mybir as mybir
from concourse.bass_utils import run_bass_kernel_spmd

F32 = mybir.dt.float32
BF16 = mybir.dt.bfloat16
I32 = mybir.dt.int32
AF = mybir.ActivationFunctionType
ALU = mybir.AluOpType
AX = mybir.AxisListType
BF = ml_dtypes.bfloat16

NCORES = 8
T = 16384
TQ = T // 4
D = 1024
DC = D // 128
ENGINES = ("tensor", "vector", "scalar", "gpsimd", "sync")

GELU_K = 1.5957691216057308
GELU_C = 0.044715
LN_EPS = 1e-5
DEQ = 1.0 / 127.0
C0 = 64


class Sem:
    __slots__ = ("h", "count", "dma")

    def __init__(self, h, dma=False):
        self.h = h
        self.count = 0
        self.dma = dma


class Ctx:
    def __init__(self):
        self.nc = bass.Bass("TRN2", target_bir_lowering=False)
        self.es = contextlib.ExitStack()
        self.q = {e: [] for e in ENGINES}
        self.hwm = {e: {} for e in ENGINES}
        self._semi = 0
        self.sems = []
        self.mm = self.sem()
        self.vS = self.sem()
        self.sS = self.sem()
        self.gS = self.sem()
        self.ldS = self.sem(dma=True)
        self.lsS = self.sem(dma=True)
        self.stS = self.sem(dma=True)
        self.pool = None
        self._scope = None
        self.serial = False
        self.eng_sem = {"vector": self.vS, "scalar": self.sS, "gpsimd": self.gS}
        self.eng_last = {"vector": 0, "scalar": 0, "gpsimd": 0}
        import os as _os
        self.auto_serial = not _os.environ.get("KNOSER")

    def sem(self, dma=False):
        self._semi += 1
        s = Sem(self.es.enter_context(self.nc.semaphore(f"s{self._semi}")), dma)
        self.sems.append(s)
        return s

    def sbuf(self, name, shape, dt=F32):
        tgt = self._scope if self._scope is not None else self.es
        return tgt.enter_context(self.nc.sbuf_tensor(name, shape, dt))

    def scope(self):
        import contextlib as _cl

        @_cl.contextmanager
        def _s():
            old = self._scope
            self._scope = contextlib.ExitStack()
            try:
                yield
            finally:
                self._scope.close()
                self._scope = old

        return _s()

    def wait(self, eng, sem, val):
        if sem.dma:
            val = sem.count
        if val <= 0:
            return
        hw = self.hwm[eng]
        if hw.get(id(sem), 0) >= val:
            return
        hw[id(sem)] = val
        self.q[eng].append(lambda e, s=sem, v=val: e.wait_ge(s.h, v))

    def op(self, eng, fn, inc=None, waits=()):
        if self.serial:
            waits = list(waits) + self.barrier_waits()
        es = self.eng_sem.get(eng)
        if es is not None:
            if self.auto_serial:
                self.wait(eng, es, self.eng_last[eng])
            if inc is None:
                inc = es
        for (s, v) in waits:
            self.wait(eng, s, v)
        if inc is None:
            self.q[eng].append(fn)
            return None
        inc.count += 1

        def run(e, fn=fn, s=inc):
            fn(e).then_inc(s.h, 1)

        self.q[eng].append(run)
        if inc is es and es is not None:
            self.eng_last[eng] = inc.count
        return inc.count

    def dma(self, eng, out, in_, sem, waits=(), **kw):
        if self.serial:
            waits = list(waits) + self.barrier_waits()
        for (s, v) in waits:
            self.wait(eng, s, v)
        sem.count += 16

        def run(e, out=out, in_=in_, s=sem, kw=kw):
            e.dma_start(out=out, in_=in_, **kw).then_inc(s.h, 16)

        self.q[eng].append(run)
        return sem.count

    def barrier_waits(self):
        return [(s, s.count) for s in (self.mm, self.vS, self.sS, self.gS, self.ldS, self.lsS, self.stS)]

    def run(self, in_maps):
        nc = self.nc
        with nc.Block() as block:
            for eng in ENGINES:
                ops = self.q[eng]

                def section(obj, ops=ops):
                    for fn in ops:
                        fn(obj)

                getattr(block, eng)(section)
        res = run_bass_kernel_spmd(nc, in_maps, core_ids=list(range(len(in_maps))))
        self.es.close()
        return res.results


class PsumPool:
    def __init__(self, ctx, n=8, cols=512):
        self.ctx = ctx
        self.tiles = [ctx.es.enter_context(ctx.nc.psum_tensor(f"ps{i}", [128, cols], F32)) for i in range(n)]
        self.free = [ctx.sem() for _ in range(n)]
        self.uses = [0] * n
        self.rr = 0
        self.n = n

    def acquire(self):
        i = self.rr % self.n
        self.rr += 1
        need = self.uses[i]
        self.uses[i] += 1
        return i, self.tiles[i], (self.free[i], need)

    def release(self, ctx, eng, i, waits):
        ctx.op(eng, lambda e: e.nop(), inc=self.free[i], waits=waits)


def mm_group(ctx, pairs, out_slice=None, waits=(), psum=None):
    pool = ctx.pool
    i, tile, (fsem, fneed) = pool.acquire()
    ctx.wait("tensor", fsem, fneed)
    if ctx.serial:
        waits = list(waits) + ctx.barrier_waits()
    for (s, v) in waits:
        ctx.wait("tensor", s, v)
    n = len(pairs)
    ctx.mm.count += 1
    tok = ctx.mm.count

    def run(t, pairs=pairs, tile=tile, sem=ctx.mm, out_slice=out_slice, n=n):
        dst = tile[:, 0:pairs[0][1].shape[-1]] if out_slice is None else tile[out_slice[0], out_slice[1]]
        inst = None
        for j, (lh, rh) in enumerate(pairs):
            inst = t.matmul(dst, lh, rh, start=(j == 0), stop=(j == n - 1))
        inst.then_inc(sem.h, 1)

    ctx.q["tensor"].append(run)
    return i, tile, tok


def emit_ln(ctx, consts, xt, ncols, xn_write_fn, ld_waits, stride=None):
    onecol, onerow, epsc, rows, sq = (consts["onecol"], consts["onerow"], consts["epsc"],
                                      consts["rows"], consts["sq"])
    NT = ncols
    ST = stride if stride is not None else ncols
    bi_m, pm, tok_m = mm_group(
        ctx, [(onecol[:, 0:1], xt[:, c * ST: c * ST + NT]) for c in range(DC)],
        out_slice=(slice(0, 1), slice(0, NT)), waits=ld_waits)
    sqt = []
    for c in range(DC):
        sqt.append(ctx.op("scalar",
                          lambda s, c=c: s.activation(out=sq[:, c * ST: c * ST + NT],
                                                      in_=xt[:, c * ST: c * ST + NT], func=AF.Square),
                          inc=ctx.sS, waits=ld_waits))
    bi_q, pq, tok_q = mm_group(
        ctx, [(onecol[:, 0:1], sq[:, c * ST: c * ST + NT]) for c in range(DC)],
        out_slice=(slice(0, 1), slice(0, NT)), waits=[(ctx.sS, sqt[-1])])
    R_m = rows[0:1, 0 * NT:1 * NT]
    R_t = rows[0:1, 1 * NT:2 * NT]
    R_v = rows[0:1, 2 * NT:3 * NT]
    R_r = rows[0:1, 3 * NT:4 * NT]
    ctx.op("vector", lambda v: v.tensor_scalar_mul(R_m, pm[0:1, 0:NT], 1.0 / D), inc=ctx.vS,
           waits=[(ctx.mm, tok_m)])
    ctx.op("vector", lambda v: v.tensor_scalar_mul(R_v, pq[0:1, 0:NT], 1.0 / D), inc=ctx.vS,
           waits=[(ctx.mm, tok_q)])
    ctx.op("vector", lambda v: v.tensor_mul(R_t, R_m, R_m), inc=ctx.vS)
    ctx.op("vector", lambda v: v.tensor_sub(R_v, R_v, R_t), inc=ctx.vS)
    ctx.op("scalar", lambda s: s.activation(out=R_t, in_=R_v, func=AF.Sqrt, bias=epsc[0:1, 0:1]),
           inc=ctx.sS, waits=[(ctx.vS, ctx.vS.count)])
    ctx.op("vector", lambda v: v.reciprocal(R_r, R_t), inc=ctx.vS, waits=[(ctx.sS, ctx.sS.count)])
    row_tok = ctx.vS.count
    ctx.pool.release(ctx, "vector", bi_m, [(ctx.vS, row_tok)])
    ctx.pool.release(ctx, "vector", bi_q, [(ctx.vS, row_tok)])
    bi_M, pM, tok_M = mm_group(ctx, [(onerow[0:1, :], R_m)], waits=[(ctx.vS, row_tok)])
    bi_R, pR, tok_R = mm_group(ctx, [(onerow[0:1, :], R_r)], waits=[(ctx.vS, row_tok)])
    for c in range(DC):
        dst = xn_write_fn(c)
        ctx.op("vector", lambda v, c=c: v.tensor_sub(xt[:, c * ST: c * ST + NT],
                                                     xt[:, c * ST: c * ST + NT], pM[:, 0:NT]),
               inc=ctx.vS, waits=[(ctx.mm, tok_M), (ctx.sS, sqt[-1])])
        ctx.op("vector", lambda v, c=c, dst=dst: v.tensor_mul(dst, xt[:, c * ST: c * ST + NT],
                                                              pR[:, 0:NT]),
               inc=ctx.vS, waits=[(ctx.mm, tok_R)])
    xn_tok = ctx.vS.count
    ctx.pool.release(ctx, "vector", bi_M, [(ctx.vS, xn_tok)])
    ctx.pool.release(ctx, "vector", bi_R, [(ctx.vS, xn_tok)])
    return xn_tok


def ln_consts(ctx, NT):
    consts = {
        "onecol": ctx.sbuf("onecol", [128, 1]),
        "onerow": ctx.sbuf("onerow", [1, 128]),
        "epsc": ctx.sbuf("epsc", [1, 1]),
        "rows": ctx.sbuf("rows", [1, 4 * NT]),
        "sq": ctx.sbuf("sq", [128, DC * NT]),
    }
    ctx.op("gpsimd", lambda g: g.memset(consts["onecol"][:, :], 1.0), inc=ctx.gS)
    ctx.op("gpsimd", lambda g: g.memset(consts["onerow"][:, :], 1.0), inc=ctx.gS)
    ctx.op("gpsimd", lambda g: g.memset(consts["epsc"][:, :], LN_EPS), inc=ctx.gS)
    return consts, ctx.gS.count


def _fm(x):
    return np.ascontiguousarray(np.asarray(x).T)


def _bcols(b, nchunks):
    return np.ascontiguousarray(np.asarray(b).reshape(nchunks, 128).T).astype(np.float32)


def build_l0main():
    ctx = Ctx()
    import os as _os
    ctx.serial = bool(_os.environ.get("KSERIAL"))
    _parts = _os.environ.get("KPARTS", "lhusgy")
    _ntile_env = int(_os.environ.get("KNT", "0"))
    nc = ctx.nc
    NT = 512
    ntiles = _ntile_env or (TQ // NT)

    x_d = nc.dram_tensor("x", [D, TQ], F32, kind="ExternalInput").ap()
    sw1_d = nc.dram_tensor("sw1", [D, 512], F32, kind="ExternalInput").ap()
    sw2_d = nc.dram_tensor("sw2", [512, 128], F32, kind="ExternalInput").ap()
    bw_d = nc.dram_tensor("bw", [D, 128], F32, kind="ExternalInput").ap()
    dw_d = nc.dram_tensor("dw", [D, D], F32, kind="ExternalInput").ap()
    gw_d = nc.dram_tensor("gw", [D, D], F32, kind="ExternalInput").ap()
    sb1_d = nc.dram_tensor("sb1", [128, 4], F32, kind="ExternalInput").ap()
    sb2_d = nc.dram_tensor("sb2", [128, 1], F32, kind="ExternalInput").ap()
    bb_d = nc.dram_tensor("bb", [128, 1], F32, kind="ExternalInput").ap()
    cdb_d = nc.dram_tensor("cdb", [128, 16], F32, kind="ExternalInput").ap()
    gb_d = nc.dram_tensor("gb", [128, 8], F32, kind="ExternalInput").ap()

    xn_o = nc.dram_tensor("xn", [D, TQ], BF16, kind="ExternalOutput").ap()
    u_o = nc.dram_tensor("u", [128, TQ], BF16, kind="ExternalOutput").ap()
    gate_o = nc.dram_tensor("gate", [D, TQ], F32, kind="ExternalOutput").ap()
    y0d_o = nc.dram_tensor("y0d", [D, TQ], F32, kind="ExternalOutput").ap()

    xn = ctx.sbuf("xns", [128, DC * TQ], BF16)
    xt = ctx.sbuf("xt", [128, DC * NT])
    sw1 = ctx.sbuf("sw1s", [128, 4 * 512], BF16)
    sw2 = ctx.sbuf("sw2s", [128, 4 * 128], BF16)
    bw = ctx.sbuf("bws", [128, DC * 128], BF16)
    dw = ctx.sbuf("dws", [128, DC * D], BF16)
    gw = ctx.sbuf("gws", [128, DC * D], BF16)
    sw1f = ctx.sbuf("sw1f", [128, DC * 512], BF16)
    sb1 = ctx.sbuf("sb1s", [128, 4])
    sb2 = ctx.sbuf("sb2s", [128, 1])
    bb = ctx.sbuf("bbs", [128, 1])
    cdb = ctx.sbuf("cdbs", [128, 16])
    ydb = ctx.sbuf("ydbs", [128, 8])
    gb = ctx.sbuf("gbs", [128, 8])
    h1 = ctx.sbuf("h1", [128, 4 * NT], BF16)
    sg = ctx.sbuf("sg", [128, NT])
    ev1 = ctx.sbuf("ev1", [128, NT])
    ev2 = ctx.sbuf("ev2", [128, NT])

    ctx.pool = PsumPool(ctx)
    consts, memset_tok = ln_consts(ctx, NT)

    wt = {}
    for c in range(DC):
        wt[("sw1", c)] = ctx.dma("gpsimd", sw1f[:, c * 512:(c + 1) * 512], sw1_d[c * 128:(c + 1) * 128, :], ctx.ldS)
        wt[("bw", c)] = ctx.dma("gpsimd", bw[:, c * 128:(c + 1) * 128], bw_d[c * 128:(c + 1) * 128, :], ctx.ldS)
        wt[("dw", c)] = ctx.dma("gpsimd", dw[:, c * D:(c + 1) * D], dw_d[c * 128:(c + 1) * 128, :], ctx.ldS)
        wt[("gw", c)] = ctx.dma("gpsimd", gw[:, c * D:(c + 1) * D], gw_d[c * 128:(c + 1) * 128, :], ctx.ldS)
    for c in range(4):
        wt[("sw2", c)] = ctx.dma("gpsimd", sw2[:, c * 128:(c + 1) * 128], sw2_d[c * 128:(c + 1) * 128, :], ctx.ldS)
    wld = ctx.ldS.count
    btoks = []
    for dst, src in ((sb1, sb1_d), (sb2, sb2_d), (bb, bb_d), (cdb, cdb_d), (gb, gb_d)):
        btoks.append(ctx.dma("sync", dst[:, :], src[:, :], ctx.lsS))
    bld = ctx.lsS.count
    ctx.op("vector", lambda v: v.tensor_add(ydb[:, :], cdb[:, 0:8], cdb[:, 8:16]),
           inc=ctx.vS, waits=[(ctx.lsS, bld)])
    ydb_tok = ctx.vS.count

    for ti in range(ntiles):
        tsl = slice(ti * NT, (ti + 1) * NT)
        bw_waits = ctx.barrier_waits()
        xw = 0
        for c in range(DC):
            xw = ctx.dma("sync", xt[:, c * NT:(c + 1) * NT], x_d[c * 128:(c + 1) * 128, tsl],
                         ctx.lsS, waits=bw_waits)
        ld_waits = [(ctx.lsS, xw), (ctx.gS, memset_tok)]
        xn_tok = emit_ln(ctx, consts, xt, NT,
                         lambda c: xn[:, c * TQ + ti * NT: c * TQ + (ti + 1) * NT], ld_waits)
        for c in range(DC):
            ctx.dma("sync", xn_o[c * 128:(c + 1) * 128, tsl],
                    xn[:, c * TQ + ti * NT: c * TQ + (ti + 1) * NT], ctx.stS, waits=[(ctx.vS, xn_tok)])

        def xnsl(c):
            return xn[:, c * TQ + ti * NT: c * TQ + (ti + 1) * NT]

        for o in range(4 if "h" in _parts else 0):
            bi, p, tk = mm_group(ctx, [(sw1f[:, c * 512 + o * 128: c * 512 + (o + 1) * 128], xnsl(c))
                                       for c in range(DC)],
                                 waits=[(ctx.ldS, wld), (ctx.vS, xn_tok)])
            etk = ctx.op("scalar", lambda s, o=o, p=p: s.activation(
                out=h1[:, o * NT:(o + 1) * NT], in_=p[:, 0:NT], func=AF.Silu, bias=sb1[:, o:o + 1]),
                inc=ctx.sS, waits=[(ctx.mm, tk), (ctx.lsS, bld)])
            ctx.pool.release(ctx, "scalar", bi, [(ctx.sS, etk)])
        h1_tok = ctx.sS.count
        if "s" not in _parts:
            continue
        bi, p, tk = mm_group(ctx, [(sw2[:, c * 128:(c + 1) * 128], h1[:, c * NT:(c + 1) * NT])
                                   for c in range(4)], waits=[(ctx.sS, h1_tok)])
        etk = ctx.op("scalar", lambda s, p=p: s.activation(out=sg[:, :], in_=p[:, 0:NT],
                                                           func=AF.Sigmoid, bias=sb2[:, 0:1]),
                     inc=ctx.sS, waits=[(ctx.mm, tk)])
        ctx.pool.release(ctx, "scalar", bi, [(ctx.sS, etk)])
        sg_tok = etk
        if "u" not in _parts:
            continue
        bi, p, tk = mm_group(ctx, [(bw[:, c * 128:(c + 1) * 128], xnsl(c)) for c in range(DC)],
                             waits=[(ctx.vS, xn_tok)])
        etk = ctx.op("scalar", lambda s, p=p: s.activation(out=ev1[:, :], in_=p[:, 0:NT],
                                                           func=AF.Identity, bias=bb[:, 0:1]),
                     inc=ctx.sS, waits=[(ctx.mm, tk)])
        ctx.pool.release(ctx, "scalar", bi, [(ctx.sS, etk)])
        utk = ctx.op("vector", lambda v: v.tensor_mul(ev2[:, :], ev1[:, :], sg[:, :]),
                     inc=ctx.vS, waits=[(ctx.sS, max(etk, sg_tok))])
        ctx.dma("gpsimd", u_o[:, tsl], ev2[:, :], ctx.ldS, waits=[(ctx.vS, utk)])
        _sel = [(gw, gb, gate_o, AF.Sigmoid)] if "g" in _parts else []
        _sel += [(dw, ydb, y0d_o, AF.Identity)] if "y" in _parts else []
        for (wsb, bias_sb, out_d, func) in _sel:
            for o in range(DC):
                bi, p, tk = mm_group(ctx, [(wsb[:, c * D + o * 128: c * D + (o + 1) * 128], xnsl(c))
                                           for c in range(DC)], waits=[(ctx.vS, xn_tok)])
                etk = ctx.op("scalar", lambda s, p=p, o=o, b=bias_sb, f=func: s.activation(
                    out=ev1[:, :], in_=p[:, 0:NT], func=f, bias=b[:, o:o + 1]),
                    inc=ctx.sS, waits=[(ctx.mm, tk), (ctx.vS, ctx.vS.count), (ctx.stS, ctx.stS.count),
                                       (ctx.ldS, ctx.ldS.count)])
                ctx.pool.release(ctx, "scalar", bi, [(ctx.sS, etk)])
                ctx.dma("sync", out_d[o * 128:(o + 1) * 128, tsl], ev1[:, :], ctx.stS,
                        waits=[(ctx.sS, etk)])
    ctx.wait("sync", ctx.stS, ctx.stS.count)
    ctx.wait("gpsimd", ctx.ldS, ctx.ldS.count)
    return ctx


def host_l0main(inputs, core):
    b, q = core // 4, core % 4
    x = np.asarray(inputs["L0_attn_x"])[b, q * TQ:(q + 1) * TQ, :]
    return {
        "x": _fm(x).astype(np.float32),
        "sw1": _fm(inputs["L0_attn_s_w1"]).astype(np.float32),
        "sw2": _fm(inputs["L0_attn_s_w2"]).astype(np.float32),
        "bw": _fm(inputs["L0_attn_B_w"]).astype(np.float32),
        "dw": _fm(inputs["L0_attn_D_w"]).astype(np.float32),
        "gw": _fm(inputs["L0_attn_g_w"]).astype(np.float32),
        "sb1": _bcols(inputs["L0_attn_s_b1"], 4),
        "sb2": _bcols(inputs["L0_attn_s_b2"], 1),
        "bb": _bcols(inputs["L0_attn_B_b"], 1),
        "cdb": np.concatenate([_bcols(inputs["L0_attn_C_b"], 8), _bcols(inputs["L0_attn_D_b"], 8)], 1),
        "gb": _bcols(inputs["L0_attn_g_b"], 8),
    }


def build_l0ffn():
    ctx = Ctx()
    nc = ctx.nc
    NT = 256
    ntiles = TQ // NT
    H = 3072
    HC = H // 128

    xn_d = nc.dram_tensor("xn", [D, TQ], BF16, kind="ExternalInput").ap()
    wg_d = nc.dram_tensor("wg", [D, H], I32, kind="ExternalInput").ap()
    wu_d = nc.dram_tensor("wu", [D, H], I32, kind="ExternalInput").ap()
    wd_d = nc.dram_tensor("wd", [H, D], I32, kind="ExternalInput").ap()
    g0_o = nc.dram_tensor("g0", [D, TQ], F32, kind="ExternalOutput").ap()

    wg = ctx.sbuf("wgs", [128, DC * H], BF16)
    wu = ctx.sbuf("wus", [128, DC * H], BF16)
    wd = ctx.sbuf("wds", [128, HC * D], BF16)
    xnt = ctx.sbuf("xnt", [128, DC * NT], BF16)
    gsp = ctx.sbuf("gsp", [128, HC * NT])
    hbf = ctx.sbuf("hbf", [128, HC * NT], BF16)
    ev1 = ctx.sbuf("ev1", [128, NT])

    ctx.pool = PsumPool(ctx)
    for c in range(DC):
        ctx.dma("gpsimd", wg[:, c * H:(c + 1) * H], wg_d[c * 128:(c + 1) * 128, :], ctx.ldS)
        ctx.dma("gpsimd", wu[:, c * H:(c + 1) * H], wu_d[c * 128:(c + 1) * 128, :], ctx.ldS)
    for c in range(HC):
        ctx.dma("gpsimd", wd[:, c * D:(c + 1) * D], wd_d[c * 128:(c + 1) * 128, :], ctx.ldS)
    wld = ctx.ldS.count

    for ti in range(ntiles):
        tsl = slice(ti * NT, (ti + 1) * NT)
        bw_waits = ctx.barrier_waits()
        xw = 0
        for c in range(DC):
            xw = ctx.dma("sync", xnt[:, c * NT:(c + 1) * NT], xn_d[c * 128:(c + 1) * 128, tsl],
                         ctx.lsS, waits=bw_waits)

        def xnsl(c):
            return xnt[:, c * NT:(c + 1) * NT]

        for o in range(HC):
            bi, p, tk = mm_group(ctx, [(wg[:, c * H + o * 128: c * H + (o + 1) * 128], xnsl(c))
                                       for c in range(DC)],
                                 waits=[(ctx.ldS, wld), (ctx.lsS, xw)])
            etk = ctx.op("scalar", lambda s, o=o, p=p: s.activation(
                out=gsp[:, o * NT:(o + 1) * NT], in_=p[:, 0:NT], func=AF.Silu, scale=DEQ),
                inc=ctx.sS, waits=[(ctx.mm, tk)])
            ctx.pool.release(ctx, "scalar", bi, [(ctx.sS, etk)])
        for o in range(HC):
            bi, p, tk = mm_group(ctx, [(wu[:, c * H + o * 128: c * H + (o + 1) * 128], xnsl(c))
                                       for c in range(DC)], waits=[(ctx.lsS, xw)])
            etk = ctx.op("vector", lambda v, o=o, p=p: v.scalar_tensor_tensor(
                hbf[:, o * NT:(o + 1) * NT], p[:, 0:NT], DEQ, gsp[:, o * NT:(o + 1) * NT],
                ALU.mult, ALU.mult),
                inc=ctx.vS, waits=[(ctx.mm, tk), (ctx.sS, ctx.sS.count)])
            ctx.pool.release(ctx, "vector", bi, [(ctx.vS, etk)])
        h_tok = ctx.vS.count
        for o in range(DC):
            bi, p, tk = mm_group(ctx, [(wd[:, c * D + o * 128: c * D + (o + 1) * 128],
                                        hbf[:, c * NT:(c + 1) * NT]) for c in range(HC)],
                                 waits=[(ctx.vS, h_tok)])
            etk = ctx.op("scalar", lambda s, p=p: s.activation(
                out=ev1[:, :], in_=p[:, 0:NT], func=AF.Silu, scale=DEQ),
                inc=ctx.sS, waits=[(ctx.mm, tk), (ctx.stS, ctx.stS.count)])
            ctx.pool.release(ctx, "scalar", bi, [(ctx.sS, etk)])
            ctx.dma("sync", g0_o[o * 128:(o + 1) * 128, tsl], ev1[:, :], ctx.stS,
                    waits=[(ctx.sS, etk)])
    ctx.wait("sync", ctx.stS, ctx.stS.count)
    return ctx


def host_l0ffn(inputs, core, st):
    return {
        "xn": st["xn"][core],
        "wg": _fm(inputs["L0_ffn_w_gate"]).astype(np.int32),
        "wu": _fm(inputs["L0_ffn_w_up"]).astype(np.int32),
        "wd": _fm(inputs["L0_ffn_w_down"]).astype(np.int32),
    }


def build_l0scan():
    ctx = Ctx()
    nc = ctx.nc
    c0 = C0
    TE = T + c0
    u_d = nc.dram_tensor("u", [128, T], BF16, kind="ExternalInput").ap()
    A_d = nc.dram_tensor("A", [128, 128], F32, kind="ExternalInput").ap()
    I_d = nc.dram_tensor("ident", [128, 128], F32, kind="ExternalInput").ap()
    hs_o = nc.dram_tensor("hs", [128, T], BF16, kind="ExternalOutput").ap()

    ue = ctx.sbuf("ue", [128, TE], BF16)
    P = ctx.sbuf("P", [128, (c0 + 1) * 128], BF16)
    Fb = ctx.sbuf("F", [128, TE])
    h = ctx.sbuf("h", [128, TE], BF16)
    A_sb = ctx.sbuf("A_sb", [128, 128], BF16)

    ctx.pool = PsumPool(ctx)
    ctx.op("gpsimd", lambda g: g.memset(ue[:, 0:c0], 0.0), inc=ctx.gS)
    ms_tok = ctx.gS.count
    ctx.dma("gpsimd", ue[:, c0:], u_d[:, :], ctx.ldS)
    ctx.dma("gpsimd", A_sb[:, :], A_d[:, :], ctx.ldS)
    ctx.dma("gpsimd", P[:, 0:128], I_d[:, :], ctx.ldS)
    ld_tok = ctx.ldS.count

    for k in range(1, c0 + 1):
        bi, p, tk = mm_group(ctx, [(A_sb[:, :], P[:, (k - 1) * 128: k * 128])],
                             waits=[(ctx.ldS, ld_tok), (ctx.vS, ctx.vS.count)])
        etk = ctx.op("vector", lambda v, k=k, p=p: v.tensor_copy(P[:, k * 128:(k + 1) * 128],
                                                                 p[:, 0:128]),
                     inc=ctx.vS, waits=[(ctx.mm, tk)])
        ctx.pool.release(ctx, "vector", bi, [(ctx.vS, etk)])
    pow_tok = ctx.vS.count

    tiles = [(i * 512, 512) for i in range(TE // 512)]
    if TE % 512:
        tiles.append((TE - TE % 512, TE % 512))
    for (ts, n) in tiles:
        i, tile, (fsem, fneed) = ctx.pool.acquire()
        ctx.wait("tensor", fsem, fneed)
        ctx.wait("tensor", ctx.vS, pow_tok)
        ctx.wait("tensor", ctx.gS, ms_tok)
        ctx.mm.count += 1
        tok = ctx.mm.count

        def run(t, ts=ts, n=n, tile=tile, sem=ctx.mm):
            inst = None
            first = True
            for k in range(c0):
                if ts - k + n <= 0:
                    continue
                lo = max(0, k - ts)
                src0 = ts - k + lo
                inst = t.matmul(tile[:, lo:n], P[:, k * 128:(k + 1) * 128],
                                ue[:, src0: src0 + (n - lo)], start=first, stop=(k == c0 - 1))
                first = False
            inst.then_inc(sem.h, 1)

        ctx.q["tensor"].append(run)
        etk = ctx.op("vector", lambda v, ts=ts, n=n, tile=tile: v.tensor_copy(
            Fb[:, ts:ts + n], tile[:, 0:n]), inc=ctx.vS, waits=[(ctx.mm, tok)])
        ctx.pool.release(ctx, "vector", i, [(ctx.vS, etk)])
    f_tok = ctx.vS.count

    ctx.op("vector", lambda v: v.tensor_copy(h[:, 0:c0], Fb[:, 0:c0]), inc=ctx.vS,
           waits=[(ctx.vS, f_tok)])
    for r in range(1, TE // c0):
        bi, p, tk = mm_group(ctx, [(P[:, c0 * 128:(c0 + 1) * 128], h[:, (r - 1) * c0: r * c0])],
                             waits=[(ctx.vS, ctx.vS.count)])
        etk = ctx.op("vector", lambda v, r=r, p=p: v.tensor_add(
            h[:, r * c0:(r + 1) * c0], Fb[:, r * c0:(r + 1) * c0], p[:, 0:c0]),
            inc=ctx.vS, waits=[(ctx.mm, tk)])
        ctx.pool.release(ctx, "vector", bi, [(ctx.vS, etk)])
    ctx.dma("sync", hs_o[:, :], h[:, c0:], ctx.stS, waits=[(ctx.vS, ctx.vS.count)])
    ctx.wait("sync", ctx.stS, ctx.stS.count)
    return ctx


def build_combine1():
    ctx = Ctx()
    nc = ctx.nc
    NT = 512
    ntiles = TQ // NT

    x_d = nc.dram_tensor("x", [D, TQ], F32, kind="ExternalInput").ap()
    hs_d = nc.dram_tensor("hs", [128, TQ], BF16, kind="ExternalInput").ap()
    y0d_d = nc.dram_tensor("y0d", [D, TQ], F32, kind="ExternalInput").ap()
    gate_d = nc.dram_tensor("gate", [D, TQ], F32, kind="ExternalInput").ap()
    g0_d = nc.dram_tensor("g0", [D, TQ], F32, kind="ExternalInput").ap()
    cw_d = nc.dram_tensor("cw", [128, D], F32, kind="ExternalInput").ap()
    x1_o = nc.dram_tensor("x1", [D, TQ], F32, kind="ExternalOutput").ap()
    xn1_o = nc.dram_tensor("xn1", [D, TQ], BF16, kind="ExternalOutput").ap()

    cw = ctx.sbuf("cws", [128, D], BF16)
    hs = ctx.sbuf("hss", [128, TQ], BF16)
    xt = ctx.sbuf("xt", [128, DC * NT])
    aux = ctx.sbuf("aux", [128, 3 * DC * NT])
    xnt = ctx.sbuf("xnt", [128, DC * NT], BF16)

    ctx.pool = PsumPool(ctx)
    consts, memset_tok = ln_consts(ctx, NT)
    ctx.dma("gpsimd", cw[:, :], cw_d[:, :], ctx.ldS)
    ctx.dma("sync", hs[:, :], hs_d[:, :], ctx.lsS)
    wld, hld = ctx.ldS.count, ctx.lsS.count

    for ti in range(ntiles):
        tsl = slice(ti * NT, (ti + 1) * NT)
        bw_waits = ctx.barrier_waits()
        xw = 0
        for c in range(DC):
            xw = ctx.dma("sync", xt[:, c * NT:(c + 1) * NT], x_d[c * 128:(c + 1) * 128, tsl],
                         ctx.lsS, waits=bw_waits)
            xw = ctx.dma("sync", aux[:, (0 * DC + c) * NT:(0 * DC + c + 1) * NT],
                         y0d_d[c * 128:(c + 1) * 128, tsl], ctx.lsS)
            xw = ctx.dma("sync", aux[:, (1 * DC + c) * NT:(1 * DC + c + 1) * NT],
                         gate_d[c * 128:(c + 1) * 128, tsl], ctx.lsS)
            xw = ctx.dma("sync", aux[:, (2 * DC + c) * NT:(2 * DC + c + 1) * NT],
                         g0_d[c * 128:(c + 1) * 128, tsl], ctx.lsS)
        for o in range(DC):
            bi, p, tk = mm_group(ctx, [(cw[:, o * 128:(o + 1) * 128], hs[:, tsl])],
                                 waits=[(ctx.ldS, wld), (ctx.lsS, max(xw, hld))])
            y0sl = aux[:, (0 * DC + o) * NT:(0 * DC + o + 1) * NT]
            gsl = aux[:, (1 * DC + o) * NT:(1 * DC + o + 1) * NT]
            g0sl = aux[:, (2 * DC + o) * NT:(2 * DC + o + 1) * NT]
            xsl = xt[:, o * NT:(o + 1) * NT]
            ctx.op("vector", lambda v, p=p, y0sl=y0sl: v.tensor_add(y0sl, y0sl, p[:, 0:NT]),
                   inc=ctx.vS, waits=[(ctx.mm, tk)])
            ctx.pool.release(ctx, "vector", bi, [(ctx.vS, ctx.vS.count)])
            ctx.op("vector", lambda v, y0sl=y0sl, gsl=gsl: v.tensor_mul(y0sl, y0sl, gsl), inc=ctx.vS)
            ctx.op("vector", lambda v, y0sl=y0sl, g0sl=g0sl: v.tensor_mul(y0sl, y0sl, g0sl), inc=ctx.vS)
            ctx.op("vector", lambda v, xsl=xsl, y0sl=y0sl: v.tensor_add(xsl, xsl, y0sl), inc=ctx.vS)
        x1_tok = ctx.vS.count
        for c in range(DC):
            ctx.dma("sync", x1_o[c * 128:(c + 1) * 128, tsl], xt[:, c * NT:(c + 1) * NT],
                    ctx.stS, waits=[(ctx.vS, x1_tok)])
        xn_tok = emit_ln(ctx, consts, xt, NT, lambda c: xnt[:, c * NT:(c + 1) * NT],
                         [(ctx.vS, x1_tok), (ctx.stS, ctx.stS.count), (ctx.gS, memset_tok)])
        for c in range(DC):
            ctx.dma("sync", xn1_o[c * 128:(c + 1) * 128, tsl], xnt[:, c * NT:(c + 1) * NT],
                    ctx.stS, waits=[(ctx.vS, xn_tok)])
    ctx.wait("sync", ctx.stS, ctx.stS.count)
    return ctx


_CTX_CACHE = {}


import os
def _run(name, builder, in_maps):
    ctx = builder()
    return ctx.run(in_maps)


def kernel(**inputs):
    import os as _os
    if not _os.environ.get("KUNFUSED"):
        return kernel_fused(**inputs)
    st = {}
    maps = [host_l0main(inputs, c) for c in range(NCORES)]
    res = _run("l0main", build_l0main, maps)
    st["xn"] = [np.asarray(r["xn"]) for r in res]
    st["u"] = [np.asarray(r["u"]) for r in res]
    st["gate"] = [np.asarray(r["gate"]) for r in res]
    st["y0d"] = [np.asarray(r["y0d"]) for r in res]
    maps = [host_l0ffn(inputs, c, st) for c in range(NCORES)]
    res = _run("l0ffn", build_l0ffn, maps)
    st["g0"] = [np.asarray(r["g0"]) for r in res]
    A = np.asarray(inputs["L0_attn_A"]).astype(np.float32)
    ident = np.eye(128, dtype=np.float32)
    maps = []
    for c in range(NCORES):
        b = c // 4
        ufull = np.concatenate([st["u"][4 * b + q] for q in range(4)], axis=1)
        maps.append({"u": ufull, "A": A, "ident": ident})
    res = _run("l0scan", build_l0scan, maps)
    st["hs"] = [np.asarray(res[4 * b]["hs"]) for b in range(2)]
    maps = []
    for c in range(NCORES):
        b, q = c // 4, c % 4
        maps.append({
            "x": _fm(np.asarray(inputs["L0_attn_x"])[b, q * TQ:(q + 1) * TQ, :]).astype(np.float32),
            "hs": np.ascontiguousarray(st["hs"][b][:, q * TQ:(q + 1) * TQ]),
            "y0d": st["y0d"][c], "gate": st["gate"][c], "g0": st["g0"][c],
            "cw": np.ascontiguousarray(np.asarray(inputs["L0_attn_C_w"]).T).astype(np.float32),
        })
    res = _run("combine1", build_combine1, maps)
    st["x1"] = [np.asarray(r["x1"]) for r in res]
    st["xn1"] = [np.asarray(r["xn1"]) for r in res]

    maps = [host_moe(inputs, c, st) for c in range(NCORES)]
    res = _run("moe", build_moe, maps)
    st["g1"] = [np.asarray(r["g1"]) for r in res]
    maps = [host_mamba_a(inputs, c, st) for c in range(NCORES)]
    res = _run("mamba_a", build_mamba_a, maps)
    for k in ("uc", "ucb", "dt", "sz", "bct"):
        st[k] = [np.asarray(r[k]) for r in res]
    maps = [host_mamba_scan(inputs, c, st) for c in range(NCORES)]
    res = _run("mamba_scan", build_mamba_scan, maps)
    st["y3s"] = [np.asarray(r["y3"]) for r in res]
    maps = [host_final(inputs, c, st) for c in range(NCORES)]
    res = _run("final", build_final, maps)
    out = np.empty((2, T, D), np.float32)
    for c in range(NCORES):
        b, q = c // 4, c % 4
        out[b, q * TQ:(q + 1) * TQ, :] = np.asarray(res[c]["out"]).T
    return out


def build_moe():
    ctx = Ctx()
    nc = ctx.nc
    NE, HP = 8, 4096
    QW = 1024
    xn_d = nc.dram_tensor("xn1", [D, TQ], BF16, kind="ExternalInput").ap()
    w1_d = nc.dram_tensor("w1", [NE * D, HP], F32, kind="ExternalInput").ap()
    w2_d = nc.dram_tensor("w2", [NE * HP, D], F32, kind="ExternalInput").ap()
    b1_d = nc.dram_tensor("b1", [128, NE * 32], F32, kind="ExternalInput").ap()
    b2_d = nc.dram_tensor("b2", [128, NE * 8], F32, kind="ExternalInput").ap()
    wr_d = nc.dram_tensor("wr", [D, NE], F32, kind="ExternalInput").ap()
    br_d = nc.dram_tensor("br", [1, NE], F32, kind="ExternalInput").ap()
    id_d = nc.dram_tensor("ident", [128, 128], F32, kind="ExternalInput").ap()
    g1_o = nc.dram_tensor("g1", [D, TQ], F32, kind="ExternalOutput").ap()
    import os as _os
    _dbg = bool(_os.environ.get("KMOE_DBG"))
    if _dbg:
        wf_d = nc.dram_tensor("wfd", [NE, TQ], F32, kind="ExternalOutput").ap()
        sc_o = nc.dram_tensor("sc_o", [128, 32 * NE], F32, kind="ExternalOutput").ap()
    else:
        wf_d = nc.dram_tensor("wfd", [NE, TQ], F32).ap()

    xnt = ctx.sbuf("xnt", [128, DC * 512], BF16)
    xnq = ctx.sbuf("xnq", [128, DC * QW], BF16)
    wrb = ctx.sbuf("wrbs", [128, DC * NE], BF16)
    brr = ctx.sbuf("brrs", [1, NE])
    brbc = ctx.sbuf("brbc", [128, NE])
    idf = ctx.sbuf("idfs", [128, 128])
    sc = ctx.sbuf("scs", [128, 32 * NE])
    m1 = ctx.sbuf("m1", [128, 32])
    m2 = ctx.sbuf("m2", [128, 32])
    lt = ctx.sbuf("lt", [128, 32 * NE])
    msk = ctx.sbuf("msk", [128, 32 * NE])
    dd = ctx.sbuf("dd", [128, 32])
    w1v = ctx.sbuf("w1v", [128, 32])
    w2v = ctx.sbuf("w2v", [128, 32])
    eq = ctx.sbuf("eq", [128, 32 * NE])
    wsb = ctx.sbuf("wsb", [128, 32 * NE])
    wrows = ctx.sbuf("wrows", [8, 128])
    wfrow = ctx.sbuf("wfrow", [1, 512])
    onerow = ctx.sbuf("onerow", [1, 128])
    b1c = ctx.sbuf("b1c", [128, NE * 32])
    b2c = ctx.sbuf("b2c", [128, NE * 8])
    w1b = ctx.sbuf("w1b", [128, DC * 128], BF16)
    w2b = ctx.sbuf("w2b", [128, 32 * 128], BF16)
    gsb = ctx.sbuf("gsb", [128, 32 * QW], BF16)
    acc = ctx.sbuf("acc", [128, DC * QW])
    wbc = ctx.sbuf("wbc", [128, 2 * 512])
    xbf = ctx.sbuf("xbf", [128, 512])
    sqf = ctx.sbuf("sqf", [128, 512])
    inn = ctx.sbuf("inn", [128, 512])
    ev1 = ctx.sbuf("ev1", [128, 512])
    yw = ctx.sbuf("yw", [128, 512])

    ctx.pool = PsumPool(ctx)
    ctx.op("gpsimd", lambda g: g.memset(onerow[:, :], 1.0))
    ms_tok = ctx.gS.count
    ctx.dma("gpsimd", wrb[:, :], wr_d.rearrange("(c p) e -> p c e", p=128), ctx.ldS)
    ctx.dma("sync", brr[0:1, :], br_d[0:1, :], ctx.lsS)
    ctx.dma("sync", idf[:, :], id_d[:, :], ctx.lsS)
    ctx.dma("sync", b1c[:, :], b1_d[:, :], ctx.lsS)
    ctx.dma("sync", b2c[:, :], b2_d[:, :], ctx.lsS)
    cst_tok = ctx.lsS.count

    bi, p, tk = mm_group(ctx, [(onerow[0:1, :], brr[0:1, :])],
                         waits=[(ctx.lsS, cst_tok), (ctx.gS, ms_tok)])
    etk = ctx.op("vector", lambda v, p=p: v.tensor_copy(brbc[:, :], p[:, 0:NE]),
                 inc=ctx.vS, waits=[(ctx.mm, tk)])
    ctx.pool.release(ctx, "vector", bi, [(ctx.vS, etk)])

    wr_ld = ctx.ldS.count
    for tt in range(8):
        xw = 0
        for c in range(DC):
            xw = ctx.dma("sync", xnt[:, c * 512:(c + 1) * 512],
                         xn_d[c * 128:(c + 1) * 128, tt * 512:(tt + 1) * 512], ctx.lsS,
                         waits=[(ctx.mm, ctx.mm.count)])
        for j in range(4):
            tc = tt * 4 + j
            bi, p, tk = mm_group(ctx, [(xnt[:, c * 512 + j * 128: c * 512 + (j + 1) * 128],
                                        wrb[:, c * NE:(c + 1) * NE]) for c in range(DC)],
                                 waits=[(ctx.ldS, wr_ld), (ctx.lsS, xw)])
            etk = ctx.op("vector", lambda v, tc=tc, p=p: v.tensor_add(
                sc[:, tc * NE:(tc + 1) * NE], p[:, 0:NE], brbc[:, :]),
                inc=ctx.vS, waits=[(ctx.mm, tk)])
            ctx.pool.release(ctx, "vector", bi, [(ctx.vS, etk)])
    sc3 = sc.ap().rearrange("p (t e) -> p t e", e=NE)
    ctx.op("vector", lambda v: v.tensor_reduce(m1[:, :], sc3, AX.X, ALU.max), inc=ctx.vS)
    m1bc = m1.ap().unsqueeze(2).broadcast_to([128, 32, NE])
    lt3 = lt.ap().rearrange("p (t e) -> p t e", e=NE)
    ctx.op("vector", lambda v: v.tensor_tensor(lt3, sc3, m1bc, ALU.is_ge), inc=ctx.vS)
    ctx.op("vector", lambda v: v.scalar_tensor_tensor(msk[:, :], lt[:, :], -1.0e4, sc[:, :],
                                                      ALU.mult, ALU.add), inc=ctx.vS)
    msk3 = msk.ap().rearrange("p (t e) -> p t e", e=NE)
    ctx.op("vector", lambda v: v.tensor_reduce(m2[:, :], msk3, AX.X, ALU.max), inc=ctx.vS)
    ctx.op("vector", lambda v: v.tensor_sub(dd[:, :], m2[:, :], m1[:, :]), inc=ctx.vS)
    ctx.op("scalar", lambda sx: sx.activation(out=dd[:, :], in_=dd[:, :], func=AF.Exp),
           inc=ctx.sS, waits=[(ctx.vS, ctx.vS.count)])
    ctx.op("vector", lambda v: v.tensor_scalar_add(w1v[:, :], dd[:, :], 1.0), inc=ctx.vS,
           waits=[(ctx.sS, ctx.sS.count)])
    ctx.op("vector", lambda v: v.reciprocal(w1v[:, :], w1v[:, :]), inc=ctx.vS)
    ctx.op("vector", lambda v: v.tensor_mul(w2v[:, :], dd[:, :], w1v[:, :]), inc=ctx.vS)
    eq3 = eq.ap().rearrange("p (t e) -> p t e", e=NE)
    wsb3 = wsb.ap().rearrange("p (t e) -> p t e", e=NE)
    w1bc = w1v.ap().unsqueeze(2).broadcast_to([128, 32, NE])
    w2bc = w2v.ap().unsqueeze(2).broadcast_to([128, 32, NE])
    m2bc = m2.ap().unsqueeze(2).broadcast_to([128, 32, NE])
    ctx.op("vector", lambda v: v.tensor_tensor(eq3, sc3, m1bc, ALU.is_equal), inc=ctx.vS)
    ctx.op("vector", lambda v: v.tensor_tensor(wsb3, eq3, w1bc, ALU.mult), inc=ctx.vS)
    ctx.op("vector", lambda v: v.tensor_tensor(eq3, sc3, m2bc, ALU.is_equal), inc=ctx.vS)
    ctx.op("vector", lambda v: v.tensor_tensor(eq3, eq3, w2bc, ALU.mult), inc=ctx.vS)
    ctx.op("vector", lambda v: v.tensor_add(wsb[:, :], wsb[:, :], eq[:, :]), inc=ctx.vS)
    wsb_tok = ctx.vS.count
    for j in range(32):
        i, tile, (fsem, fneed) = ctx.pool.acquire()
        ctx.wait("tensor", fsem, fneed)
        ctx.wait("tensor", ctx.vS, wsb_tok)
        ctx.wait("tensor", ctx.lsS, cst_tok)
        ctx.mm.count += 1
        tk = ctx.mm.count

        def runt(t, j=j, tile=tile, sem=ctx.mm):
            t.transpose(tile[0:NE, 0:128], wsb[:, j * NE:(j + 1) * NE], idf[:, :]).then_inc(sem.h, 1)

        ctx.q["tensor"].append(runt)
        etk = ctx.op("vector", lambda v, tile=tile: v.tensor_copy(
            wrows[0:NE, :], tile[0:NE, 0:128]),
            inc=ctx.vS, waits=[(ctx.mm, tk), (ctx.ldS, ctx.ldS.count)])
        ctx.pool.release(ctx, "vector", i, [(ctx.vS, etk)])
        ctx.dma("gpsimd", wf_d[0:NE, j * 128:(j + 1) * 128], wrows[0:NE, :], ctx.ldS,
                waits=[(ctx.vS, etk)])
    wflat_tok = ctx.ldS.count
    if _dbg:
        ctx.dma("sync", sc_o[:, :], wsb[:, :], ctx.stS, waits=[(ctx.vS, wsb_tok)])

    GC = GELU_C
    GK = GELU_K
    for q in range(4):
        qsl = slice(q * QW, (q + 1) * QW)
        bwq = ctx.barrier_waits()
        xq = 0
        for c in range(DC):
            xq = ctx.dma("sync", xnq[:, c * QW:(c + 1) * QW],
                         xn_d[c * 128:(c + 1) * 128, qsl], ctx.lsS, waits=bwq)
        ctx.op("gpsimd", lambda g: g.memset(acc[:, :], 0.0), waits=bwq)
        acc0_tok = ctx.gS.count
        for e in range(NE):
            for o in range(32):
                w1t = ctx.dma("gpsimd", w1b[:, :],
                              w1_d[e * D:(e + 1) * D, o * 128:(o + 1) * 128]
                              .rearrange("(c p) n -> p c n", p=128), ctx.ldS,
                              waits=[(ctx.mm, ctx.mm.count)])
                for t2 in range(2):
                    bi, p, tk = mm_group(ctx, [(w1b[:, c * 128:(c + 1) * 128],
                                                xnq[:, c * QW + t2 * 512: c * QW + t2 * 512 + 512])
                                               for c in range(DC)],
                                         waits=[(ctx.ldS, w1t), (ctx.lsS, xq)])
                    bcol = b1c[:, e * 32 + o: e * 32 + o + 1]
                    ctx.op("scalar", lambda sx, p=p, bcol=bcol: sx.activation(
                        out=xbf[:, :], in_=p[:, 0:512], func=AF.Identity, bias=bcol),
                        inc=ctx.sS, waits=[(ctx.mm, tk), (ctx.lsS, cst_tok), (ctx.vS, ctx.vS.count)])
                    etk = ctx.op("scalar", lambda sx: sx.activation(
                        out=sqf[:, :], in_=xbf[:, :], func=AF.Square), inc=ctx.sS)
                    ctx.pool.release(ctx, "scalar", bi, [(ctx.sS, etk)])
                    ctx.op("vector", lambda v: v.tensor_scalar(inn[:, :], sqf[:, :], GC, 1.0,
                                                               ALU.mult, ALU.add),
                           inc=ctx.vS, waits=[(ctx.sS, etk)])
                    ctx.op("vector", lambda v: v.tensor_mul(inn[:, :], inn[:, :], xbf[:, :]),
                           inc=ctx.vS)
                    stk = ctx.op("scalar", lambda sx: sx.activation(
                        out=inn[:, :], in_=inn[:, :], func=AF.Sigmoid, scale=GK),
                        inc=ctx.sS, waits=[(ctx.vS, ctx.vS.count)])
                    gdst = gsb[:, o * QW + t2 * 512: o * QW + t2 * 512 + 512]
                    ctx.op("vector", lambda v, gdst=gdst: v.tensor_mul(gdst, inn[:, :], xbf[:, :]),
                           inc=ctx.vS, waits=[(ctx.sS, stk)])
            g_tok = ctx.vS.count
            for t2 in range(2):
                wr0 = q * QW + t2 * 512
                wl = ctx.dma("sync", wfrow[0:1, :], wf_d[e:e + 1, wr0:wr0 + 512], ctx.lsS,
                             waits=[(ctx.ldS, wflat_tok), (ctx.mm, ctx.mm.count)])
                bi, p, tk = mm_group(ctx, [(onerow[0:1, :], wfrow[0:1, :])],
                                     waits=[(ctx.lsS, wl), (ctx.gS, ms_tok)])
                etk = ctx.op("vector", lambda v, p=p, t2=t2: v.tensor_copy(
                    wbc[:, t2 * 512:(t2 + 1) * 512], p[:, 0:512]),
                    inc=ctx.vS, waits=[(ctx.mm, tk)])
                ctx.pool.release(ctx, "vector", bi, [(ctx.vS, etk)])
                for oc in range(DC):
                    w2t = ctx.dma("gpsimd", w2b[:, :],
                                  w2_d[e * HP: (e + 1) * HP, oc * 128:(oc + 1) * 128]
                                  .rearrange("(k p) n -> p k n", p=128), ctx.ldS,
                                  waits=[(ctx.mm, ctx.mm.count)])
                    bi, p, tk = mm_group(ctx, [(w2b[:, k * 128:(k + 1) * 128],
                                                gsb[:, k * QW + t2 * 512: k * QW + t2 * 512 + 512])
                                               for k in range(32)],
                                         waits=[(ctx.ldS, w2t), (ctx.vS, g_tok)])
                    b2col = b2c[:, e * 8 + oc: e * 8 + oc + 1]
                    ctx.op("vector", lambda v, p=p, b2col=b2col, t2=t2: v.scalar_tensor_tensor(
                        yw[:, :], p[:, 0:512], b2col, wbc[:, t2 * 512:(t2 + 1) * 512],
                        ALU.add, ALU.mult),
                        inc=ctx.vS, waits=[(ctx.mm, tk), (ctx.lsS, cst_tok), (ctx.gS, acc0_tok)])
                    adst = acc[:, oc * QW + t2 * 512: oc * QW + t2 * 512 + 512]
                    etk = ctx.op("vector", lambda v, adst=adst: v.tensor_add(adst, adst, yw[:, :]),
                                 inc=ctx.vS)
                    ctx.pool.release(ctx, "vector", bi, [(ctx.vS, etk)])
        for oc in range(DC):
            etk = ctx.op("scalar", lambda sx, oc=oc: sx.activation(
                out=ev1[:, 0:512], in_=acc[:, oc * QW: oc * QW + 512], func=AF.Silu),
                inc=ctx.sS, waits=[(ctx.vS, ctx.vS.count), (ctx.stS, ctx.stS.count)])
            ctx.dma("sync", g1_o[oc * 128:(oc + 1) * 128, q * QW: q * QW + 512], ev1[:, 0:512],
                    ctx.stS, waits=[(ctx.sS, etk)])
            etk = ctx.op("scalar", lambda sx, oc=oc: sx.activation(
                out=ev1[:, 0:512], in_=acc[:, oc * QW + 512: (oc + 1) * QW], func=AF.Silu),
                inc=ctx.sS, waits=[(ctx.stS, ctx.stS.count)])
            ctx.dma("sync", g1_o[oc * 128:(oc + 1) * 128, q * QW + 512: (q + 1) * QW], ev1[:, 0:512],
                    ctx.stS, waits=[(ctx.sS, etk)])
    ctx.wait("sync", ctx.stS, ctx.stS.count)
    return ctx


def host_moe(inputs, core, st):
    W1 = np.asarray(inputs["L1_ffn_W1"]).astype(np.float32)
    W2 = np.asarray(inputs["L1_ffn_W2"]).astype(np.float32)
    b1 = np.asarray(inputs["L1_ffn_b1"]).astype(np.float32)
    b2 = np.asarray(inputs["L1_ffn_b2"]).astype(np.float32)
    return {
        "xn1": st["xn1"][core],
        "w1": np.ascontiguousarray(W1.reshape(8 * 1024, 4096)),
        "w2": np.ascontiguousarray(W2.reshape(8 * 4096, 1024)),
        "b1": np.ascontiguousarray(np.concatenate([_bcols(b1[e], 32) for e in range(8)], 1)),
        "b2": np.ascontiguousarray(np.concatenate([_bcols(b2[e], 8) for e in range(8)], 1)),
        "wr": np.ascontiguousarray(np.asarray(inputs["L1_ffn_Wr"]).astype(np.float32)),
        "br": np.ascontiguousarray(np.asarray(inputs["L1_ffn_br"]).astype(np.float32).reshape(1, 8)),
        "ident": np.eye(128, dtype=np.float32),
    }


def build_mamba_a():
    ctx = Ctx()
    import os as _os
    _mp = _os.environ.get("KMA_PARTS", "uzxdb")
    nc = ctx.nc
    TH = TQ + 3
    DI = 2048

    xnh_d = nc.dram_tensor("xnh", [D, TH], BF16, kind="ExternalInput").ap()
    win_d = nc.dram_tensor("win", [D, 2 * DI], F32, kind="ExternalInput").ap()
    convc_d = nc.dram_tensor("convc", [128, 16 * 4], F32, kind="ExternalInput").ap()
    convb_d = nc.dram_tensor("convb", [128, 16], F32, kind="ExternalInput").ap()
    wx_d = nc.dram_tensor("wx", [DI, 96], F32, kind="ExternalInput").ap()
    wdt_d = nc.dram_tensor("wdt", [64, DI], F32, kind="ExternalInput").ap()
    bdt_d = nc.dram_tensor("bdt", [128, 16], F32, kind="ExternalInput").ap()
    id_d = nc.dram_tensor("ident", [128, 128], F32, kind="ExternalInput").ap()
    uc_o = nc.dram_tensor("uc", [DI, TQ], F32, kind="ExternalOutput").ap()
    ucb_o = nc.dram_tensor("ucb", [DI, TQ], BF16, kind="ExternalOutput").ap()
    dt_o = nc.dram_tensor("dt", [DI, TQ], F32, kind="ExternalOutput").ap()
    sz_o = nc.dram_tensor("sz", [DI, TQ], F32, kind="ExternalOutput").ap()
    bct_o = nc.dram_tensor("bct", [TQ, 32], F32, kind="ExternalOutput").ap()
    _madbg = bool(_os.environ.get("KMA_DBG"))
    if _madbg:
        xd_dbg = nc.dram_tensor("xd_dbg", [128, TQ], F32, kind="ExternalOutput").ap()
        xdb_dbg = nc.dram_tensor("xdb_dbg", [128, TQ], BF16, kind="ExternalOutput").ap()

    xnh = ctx.sbuf("xnhs", [128, DC * TH], BF16)
    wb = ctx.sbuf("wbs", [128, DC * 128], BF16)
    convc = ctx.sbuf("convcs", [128, 64])
    convb = ctx.sbuf("convbs", [128, 16])
    uo = ctx.sbuf("uos", [128, TH])
    ca = ctx.sbuf("ca", [128, TQ])
    cb = ctx.sbuf("cb", [128, TQ])
    ucf = ctx.sbuf("ucf", [128, TQ])
    ucc = ctx.sbuf("ucc", [128, TQ], BF16)
    wxb = ctx.sbuf("wxb", [128, 16 * 96], BF16)
    wdtb = ctx.sbuf("wdtb", [64, DI], BF16)
    bdt = ctx.sbuf("bdts", [128, 16])
    idf = ctx.sbuf("idfs", [128, 128])
    onec = ctx.sbuf("onec", [128, 1])
    xdbl = ctx.sbuf("xdbls", [128, TQ])
    xdblb = ctx.sbuf("xdblb", [128, TQ], BF16)
    eva = ctx.sbuf("eva", [128, 1024])
    evb = ctx.sbuf("evb", [128, 1024])
    evc = ctx.sbuf("evc", [128, 1024])
    bcsb = ctx.sbuf("bcsb", [128, 32 * 32])

    ctx.pool = PsumPool(ctx)
    ctx.op("gpsimd", lambda g: g.memset(onec[:, :], 1.0))
    ms_tok = ctx.gS.count
    xw = 0
    for c in range(DC):
        xw = ctx.dma("sync", xnh[:, c * TH:(c + 1) * TH], xnh_d[c * 128:(c + 1) * 128, :], ctx.lsS)
    ctx.dma("sync", convc[:, :], convc_d[:, :], ctx.lsS)
    ctx.dma("sync", convb[:, :], convb_d[:, :], ctx.lsS)
    ctx.dma("sync", bdt[:, :], bdt_d[:, :], ctx.lsS)
    ctx.dma("sync", idf[:, :], id_d[:, :], ctx.lsS)
    cst_tok = ctx.lsS.count
    ctx.dma("gpsimd", wxb[:, :], wx_d.rearrange("(c p) n -> p c n", p=128), ctx.ldS)
    ctx.dma("gpsimd", wdtb[0:64, :], wdt_d[:, :], ctx.ldS)
    wx_tok = ctx.ldS.count

    utiles = [(t * 512, 512) for t in range(8)] + [(4096, 3)]

    for o in range(16 if "u" in _mp else 0):
        wt = ctx.dma("gpsimd", wb[:, :],
                     win_d[:, o * 128:(o + 1) * 128].rearrange("(c p) n -> p c n", p=128),
                     ctx.ldS, waits=[(ctx.mm, ctx.mm.count)])
        for (ts, n) in utiles:
            bi, p, tk = mm_group(ctx, [(wb[:, c * 128:(c + 1) * 128],
                                        xnh[:, c * TH + ts: c * TH + ts + n]) for c in range(DC)],
                                 waits=[(ctx.ldS, wt), (ctx.lsS, xw)])
            etk = ctx.op("vector", lambda v, ts=ts, n=n, p=p: v.tensor_copy(
                uo[:, ts:ts + n], p[:, 0:n]), inc=ctx.vS, waits=[(ctx.mm, tk)])
            ctx.pool.release(ctx, "vector", bi, [(ctx.vS, etk)])
        ctx.op("vector", lambda v, o=o: v.tensor_scalar(
            ca[:, :], uo[:, 0:TQ], convc[:, o * 4: o * 4 + 1], None, ALU.mult),
            inc=ctx.vS, waits=[(ctx.lsS, cst_tok), (ctx.sS, ctx.sS.count)])
        ctx.op("vector", lambda v, o=o: v.scalar_tensor_tensor(
            cb[:, :], uo[:, 1:TQ + 1], convc[:, o * 4 + 1: o * 4 + 2], ca[:, :], ALU.mult, ALU.add),
            inc=ctx.vS)
        ctx.op("vector", lambda v, o=o: v.scalar_tensor_tensor(
            ca[:, :], uo[:, 2:TQ + 2], convc[:, o * 4 + 2: o * 4 + 3], cb[:, :], ALU.mult, ALU.add),
            inc=ctx.vS)
        ctx.op("vector", lambda v, o=o: v.scalar_tensor_tensor(
            cb[:, :], uo[:, 3:TQ + 3], convc[:, o * 4 + 3: o * 4 + 4], ca[:, :], ALU.mult, ALU.add),
            inc=ctx.vS)
        stk = ctx.op("scalar", lambda sx, o=o: sx.activation(
            out=ucf[:, :], in_=cb[:, :], func=AF.Silu, bias=convb[:, o:o + 1]),
            inc=ctx.sS, waits=[(ctx.vS, ctx.vS.count), (ctx.stS, ctx.stS.count),
                               (ctx.ldS, ctx.ldS.count)])
        ctx.dma("sync", uc_o[o * 128:(o + 1) * 128, :], ucf[:, :], ctx.stS, waits=[(ctx.sS, stk)])
        ctx.dma("gpsimd", ucb_o[o * 128:(o + 1) * 128, :], ucf[:, :], ctx.ldS, waits=[(ctx.sS, stk)])
    for o in range(16 if "z" in _mp else 0):
        wt = ctx.dma("gpsimd", wb[:, :],
                     win_d[:, DI + o * 128: DI + (o + 1) * 128].rearrange("(c p) n -> p c n", p=128),
                     ctx.ldS, waits=[(ctx.mm, ctx.mm.count)])
        for t in range(8):
            ts = 3 + t * 512
            bi, p, tk = mm_group(ctx, [(wb[:, c * 128:(c + 1) * 128],
                                        xnh[:, c * TH + ts: c * TH + ts + 512]) for c in range(DC)],
                                 waits=[(ctx.ldS, wt)])
            etk = ctx.op("scalar", lambda sx, p=p: sx.activation(
                out=eva[:, 0:512], in_=p[:, 0:512], func=AF.Silu),
                inc=ctx.sS, waits=[(ctx.mm, tk), (ctx.stS, ctx.stS.count)])
            ctx.pool.release(ctx, "scalar", bi, [(ctx.sS, etk)])
            ctx.dma("sync", sz_o[o * 128:(o + 1) * 128, t * 512:(t + 1) * 512], eva[:, 0:512],
                    ctx.stS, waits=[(ctx.sS, etk)])
    if "x" not in _mp:
        ctx.wait("sync", ctx.stS, ctx.stS.count)
        ctx.wait("gpsimd", ctx.ldS, ctx.ldS.count)
        return ctx
    ctx.wait("tensor", ctx.ldS, ctx.ldS.count)
    for i in range(8):
        ctx.wait("tensor", ctx.pool.free[i], ctx.pool.uses[i])
        ctx.pool.uses[i] += 1
    xd_tok = []
    for c in range(16):
        lt = ctx.dma("sync", ucc[:, :], ucb_o[c * 128:(c + 1) * 128, :], ctx.lsS,
                     waits=[(ctx.mm, ctx.mm.count)] + ([(ctx.ldS, ctx.ldS.count)] if c == 0 else []))
        for t in range(8):
            ctx.wait("tensor", ctx.lsS, lt)
            ctx.mm.count += 1

            def runx(tt, c=c, t=t, sem=ctx.mm):
                tt.matmul(ctx.pool.tiles[t][0:96, 0:512], wxb[:, c * 96:(c + 1) * 96],
                          ucc[:, t * 512:(t + 1) * 512], start=(c == 0),
                          stop=(c == 15)).then_inc(sem.h, 1)

            ctx.q["tensor"].append(runx)
            if c == 15:
                xd_tok.append(ctx.mm.count)
    for t in range(8):
        etk = ctx.op("vector", lambda v, t=t: v.tensor_copy(
            xdbl[0:96, t * 512:(t + 1) * 512], ctx.pool.tiles[t][0:96, 0:512]),
            inc=ctx.vS, waits=[(ctx.mm, xd_tok[t])])
        ctx.pool.release(ctx, "vector", t, [(ctx.vS, etk)])
    ctx.op("vector", lambda v: v.tensor_copy(xdblb[0:64, :], xdbl[0:64, :]), inc=ctx.vS)
    xdbl_tok = ctx.vS.count
    if _madbg:
        ctx.dma("sync", xd_dbg[:, :], xdbl[:, :], ctx.stS, waits=[(ctx.vS, xdbl_tok)])
        ctx.dma("sync", xdb_dbg[:, :], xdblb[:, :], ctx.stS, waits=[(ctx.vS, xdbl_tok)])
    for o in range(16 if "d" in _mp else 0):
        for t in range(8):
            bi, p, tk = mm_group(ctx, [(wdtb[0:64, o * 128:(o + 1) * 128],
                                        xdblb[0:64, t * 512:(t + 1) * 512])],
                                 waits=[(ctx.vS, xdbl_tok), (ctx.ldS, wx_tok)])
            bcol = bdt[:, o:o + 1]
            par = ((o * 8 + t) % 2) * 512
            A_ = eva[:, par:par + 512]
            B_ = evb[:, par:par + 512]
            C_ = evc[:, par:par + 512]
            ctx.op("vector", lambda v, p=p, bcol=bcol, C_=C_: v.tensor_scalar_add(
                C_, p[:, 0:512], bcol),
                inc=ctx.vS, waits=[(ctx.mm, tk), (ctx.lsS, cst_tok), (ctx.sS, ctx.sS.count)])
            ctx.op("vector", lambda v, A_=A_, C_=C_: v.tensor_scalar_max(A_, C_, 0.0), inc=ctx.vS,
                   waits=[(ctx.stS, 1)])
            ctx.op("vector", lambda v, B_=B_, C_=C_: v.tensor_scalar_mul(B_, C_, -1.0), inc=ctx.vS)
            ctx.op("vector", lambda v, B_=B_, C_=C_: v.tensor_max(B_, B_, C_), inc=ctx.vS)
            vt = ctx.vS.count
            ctx.op("scalar", lambda sx, B_=B_, C_=C_: sx.activation(
                out=C_, in_=B_, func=AF.Exp, scale=-1.0), inc=ctx.sS,
                waits=[(ctx.vS, vt)])
            etk2 = ctx.op("scalar", lambda sx, B_=B_, C_=C_: sx.activation(
                out=B_, in_=C_, func=AF.Ln, bias=onec[:, 0:1]),
                inc=ctx.sS, waits=[(ctx.gS, ms_tok)])
            ctx.pool.release(ctx, "scalar", bi, [(ctx.sS, etk2)])
            etk = ctx.op("vector", lambda v, A_=A_, B_=B_: v.tensor_add(A_, A_, B_),
                         inc=ctx.vS, waits=[(ctx.sS, etk2), (ctx.stS, ctx.stS.count)])
            ctx.dma("sync", dt_o[o * 128:(o + 1) * 128, t * 512:(t + 1) * 512], A_,
                    ctx.stS, waits=[(ctx.vS, etk)])
    for j in range(32 if "b" in _mp else 0):
        i, tile, (fsem, fneed) = ctx.pool.acquire()
        ctx.wait("tensor", fsem, fneed)
        ctx.wait("tensor", ctx.vS, xdbl_tok)
        ctx.wait("tensor", ctx.lsS, cst_tok)
        ctx.mm.count += 1
        tk = ctx.mm.count

        def runt(t, j=j, tile=tile, sem=ctx.mm):
            t.transpose(tile[:, 0:32], xdbl[64:96, j * 128:(j + 1) * 128],
                        idf[64:96, 64:96]).then_inc(sem.h, 1)

        ctx.q["tensor"].append(runt)
        etk = ctx.op("vector", lambda v, j=j, tile=tile: v.tensor_copy(
            bcsb[:, j * 32:(j + 1) * 32], tile[:, 0:32]), inc=ctx.vS, waits=[(ctx.mm, tk)])
        ctx.pool.release(ctx, "vector", i, [(ctx.vS, etk)])
    bc_tok = ctx.vS.count
    if "b" in _mp:
        ctx.dma("sync", bct_o.rearrange("(j p) s -> p j s", p=128), bcsb.ap().rearrange("p (j s) -> p j s", s=32),
                ctx.stS, waits=[(ctx.vS, bc_tok)])
    ctx.wait("sync", ctx.stS, ctx.stS.count)
    ctx.wait("gpsimd", ctx.ldS, ctx.ldS.count)
    return ctx


def host_mamba_a(inputs, core, st):
    b, q = core // 4, core % 4
    xn1 = st["xn1"][core]
    if q == 0:
        halo = np.zeros((D, 3), dtype=xn1.dtype)
    else:
        halo = st["xn1"][core - 1][:, -3:]
    xnh = np.concatenate([halo, xn1], axis=1)
    W_in = np.asarray(inputs["L1_attn_W_in"]).astype(np.float32)
    conv_w = np.asarray(inputs["L1_attn_conv_w"]).astype(np.float32)
    return {
        "xnh": np.ascontiguousarray(xnh),
        "win": np.ascontiguousarray(W_in.T),
        "convc": np.ascontiguousarray(conv_w.reshape(16, 128, 4).transpose(1, 0, 2).reshape(128, 64)),
        "convb": _bcols(inputs["L1_attn_conv_b"], 16),
        "wx": np.ascontiguousarray(np.asarray(inputs["L1_attn_W_x"]).astype(np.float32).T),
        "wdt": np.ascontiguousarray(np.asarray(inputs["L1_attn_W_dt"]).astype(np.float32).T),
        "bdt": _bcols(inputs["L1_attn_b_dt"], 16),
        "ident": np.eye(128, dtype=np.float32),
    }


def build_mamba_scan():
    ctx = Ctx()
    nc = ctx.nc
    NT = 512
    NS = 16
    ntiles = T // NT

    dt_d = nc.dram_tensor("dt", [512, T], F32, kind="ExternalInput").ap()
    uc_d = nc.dram_tensor("uc", [512, T], F32, kind="ExternalInput").ap()
    sz_d = nc.dram_tensor("sz", [512, T], F32, kind="ExternalInput").ap()
    bct_d = nc.dram_tensor("bct", [T, 32], F32, kind="ExternalInput").ap()
    alog_d = nc.dram_tensor("alog", [512, NS], F32, kind="ExternalInput").ap()
    dv_d = nc.dram_tensor("dv", [128, 4], F32, kind="ExternalInput").ap()
    y3_o = nc.dram_tensor("y3", [512, T], BF16, kind="ExternalOutput").ap()

    bufA = ctx.sbuf("bufA", [128, NT * NS])
    bufB = ctx.sbuf("bufB", [128, NT * NS])
    bufC = ctx.sbuf("bufC", [128, NT * NS])
    Bbc = ctx.sbuf("Bbc", [128, NT * NS])
    Cbc = ctx.sbuf("Cbc", [128, NT * NS])
    dtg = ctx.sbuf("dtgs", [128, NT])
    ucg = ctx.sbuf("ucg", [128, NT])
    szg = ctx.sbuf("szg", [128, NT])
    gg = ctx.sbuf("gg", [128, NT])
    yb = ctx.sbuf("yb", [128, NT])
    yb2 = ctx.sbuf("yb2", [128, NT])
    y3b = ctx.sbuf("y3b", [128, NT], BF16)
    rowb = ctx.sbuf("rowb", [1, 2 * 512])
    asb = ctx.sbuf("asbs", [128, 4 * NS])
    dvc = ctx.sbuf("dvcs", [128, 4])
    state = ctx.sbuf("state", [128, 4 * NS])
    onerow = ctx.sbuf("onerow", [1, 128])

    ctx.pool = PsumPool(ctx)
    ctx.op("gpsimd", lambda g: g.memset(onerow[:, :], 1.0))
    ms_tok = ctx.gS.count
    ctx.dma("gpsimd", asb[:, :], alog_d.rearrange("(g p) s -> p g s", p=128), ctx.ldS)
    ctx.dma("gpsimd", dvc[:, :], dv_d[:, :], ctx.ldS)
    al_tok = ctx.ldS.count
    ctx.op("scalar", lambda sx: sx.activation(out=asb[:, :], in_=asb[:, :], func=AF.Exp),
           inc=ctx.sS, waits=[(ctx.ldS, al_tok)])
    atk = ctx.op("vector", lambda v: v.tensor_scalar_mul(asb[:, :], asb[:, :], -1.0),
                 inc=ctx.vS, waits=[(ctx.sS, ctx.sS.count)])

    for ti in range(ntiles):
        tw = ctx.barrier_waits()
        for (coff, dst) in ((0, Bbc), (16, Cbc)):
            for k in range(NS):
                par = (k % 2) * 512
                lw = ctx.dma("sync", rowb[0:1, par:par + 512].rearrange("p (t s) -> p t s", s=16),
                             bct_d[ti * NT + k * 32: ti * NT + (k + 1) * 32, coff:coff + 16]
                             .unsqueeze(0),
                             ctx.lsS, waits=tw + [(ctx.mm, ctx.mm.count)])
                bi, p, tk = mm_group(ctx, [(onerow[0:1, :], rowb[0:1, par:par + 512])],
                                     waits=[(ctx.lsS, lw), (ctx.gS, ms_tok)])
                etk = ctx.op("vector", lambda v, p=p, dst=dst, k=k: v.tensor_copy(
                    dst[:, k * 512:(k + 1) * 512], p[:, 0:512]), inc=ctx.vS, waits=[(ctx.mm, tk)])
                ctx.pool.release(ctx, "vector", bi, [(ctx.vS, etk)])
        bc_tok = ctx.vS.count
        for g in range(4):
            ld = ctx.dma("sync", dtg[:, :], dt_d[g * 128:(g + 1) * 128, ti * NT:(ti + 1) * NT],
                         ctx.lsS, waits=[(ctx.vS, ctx.vS.count), (ctx.sS, ctx.sS.count)])
            ld = ctx.dma("sync", ucg[:, :], uc_d[g * 128:(g + 1) * 128, ti * NT:(ti + 1) * NT], ctx.lsS)
            ld = ctx.dma("sync", szg[:, :], sz_d[g * 128:(g + 1) * 128, ti * NT:(ti + 1) * NT], ctx.lsS)
            ctx.op("vector", lambda v: v.tensor_mul(gg[:, :], dtg[:, :], ucg[:, :]),
                   inc=ctx.vS, waits=[(ctx.lsS, ld), (ctx.vS, atk)])
            a3 = bufA.ap().rearrange("p (t s) -> p t s", s=NS)
            dtbc = dtg.ap().unsqueeze(2).broadcast_to([128, NT, NS])
            abc = asb.ap()[:, g * NS:(g + 1) * NS].unsqueeze(1).broadcast_to([128, NT, NS])
            ctx.op("vector", lambda v, a3=a3, dtbc=dtbc, abc=abc: v.tensor_tensor(
                a3, dtbc, abc, ALU.mult), inc=ctx.vS)
            eE = ctx.vS.count
            sE = ctx.op("scalar", lambda sx: sx.activation(out=bufB[:, :], in_=bufA[:, :],
                                                           func=AF.Exp),
                        inc=ctx.sS, waits=[(ctx.vS, eE)])
            gbc = gg.ap().unsqueeze(2).broadcast_to([128, NT, NS])
            b3 = Bbc.ap().rearrange("p (t s) -> p t s", s=NS)
            ctx.op("vector", lambda v, a3=a3, gbc=gbc, b3=b3: v.tensor_tensor(
                a3, gbc, b3, ALU.mult), inc=ctx.vS, waits=[(ctx.sS, sE), (ctx.vS, bc_tok)])
            aR = bufA.ap().rearrange("p (t s) -> p s t", s=NS)
            bR = bufB.ap().rearrange("p (t s) -> p s t", s=NS)
            cR = bufC.ap().rearrange("p (t s) -> p s t", s=NS)
            for sidx in range(NS):
                if ti == 0:
                    init = 0.0
                else:
                    init = state[:, g * NS + sidx: g * NS + sidx + 1]
                ctx.op("vector", lambda v, sidx=sidx, init=init, aR=aR, bR=bR, cR=cR:
                       v.tensor_tensor_scan(cR[:, sidx, :], bR[:, sidx, :], aR[:, sidx, :],
                                            init, ALU.mult, ALU.add), inc=ctx.vS)
            ctx.op("vector", lambda v, g=g: v.tensor_copy(state[:, g * NS:(g + 1) * NS],
                                                          bufC[:, (NT - 1) * NS: NT * NS]),
                   inc=ctx.vS)
            c3 = Cbc.ap().rearrange("p (t s) -> p t s", s=NS)
            h3 = bufC.ap().rearrange("p (t s) -> p t s", s=NS)
            ctx.op("vector", lambda v, a3=a3, h3=h3, c3=c3: v.tensor_tensor(a3, h3, c3, ALU.mult),
                   inc=ctx.vS)
            ctx.op("vector", lambda v, a3=a3: v.tensor_reduce(yb[:, :], a3, AX.X, ALU.add),
                   inc=ctx.vS)
            ctx.op("vector", lambda v, g=g: v.scalar_tensor_tensor(
                yb2[:, :], ucg[:, :], dvc[:, g:g + 1], yb[:, :], ALU.mult, ALU.add), inc=ctx.vS)
            etk = ctx.op("vector", lambda v: v.tensor_mul(y3b[:, :], yb2[:, :], szg[:, :]),
                         inc=ctx.vS, waits=[(ctx.stS, ctx.stS.count)])
            ctx.dma("sync", y3_o[g * 128:(g + 1) * 128, ti * NT:(ti + 1) * NT], y3b[:, :],
                    ctx.stS, waits=[(ctx.vS, etk)])
    ctx.wait("sync", ctx.stS, ctx.stS.count)
    return ctx


def host_mamba_scan(inputs, core, st):
    b, j = core // 4, core % 4
    rs = slice(512 * j, 512 * (j + 1))
    dt = np.concatenate([st["dt"][4 * b + q][rs, :] for q in range(4)], axis=1)
    uc = np.concatenate([st["uc"][4 * b + q][rs, :] for q in range(4)], axis=1)
    sz = np.concatenate([st["sz"][4 * b + q][rs, :] for q in range(4)], axis=1)
    bct = np.concatenate([st["bct"][4 * b + q] for q in range(4)], axis=0)
    alog = np.asarray(inputs["L1_attn_A_log"]).astype(np.float32)[rs, :]
    dvec = np.asarray(inputs["L1_attn_D"]).astype(np.float32)[rs]
    return {
        "dt": np.ascontiguousarray(dt), "uc": np.ascontiguousarray(uc),
        "sz": np.ascontiguousarray(sz), "bct": np.ascontiguousarray(bct),
        "alog": np.ascontiguousarray(alog),
        "dv": np.ascontiguousarray(dvec.reshape(4, 128).T),
    }


def build_final():
    ctx = Ctx()
    nc = ctx.nc
    NT = 512
    DI = 2048

    y3_d = nc.dram_tensor("y3", [DI, TQ], BF16, kind="ExternalInput").ap()
    wo_d = nc.dram_tensor("wo", [DI, D], F32, kind="ExternalInput").ap()
    x1_d = nc.dram_tensor("x1", [D, TQ], F32, kind="ExternalInput").ap()
    g1_d = nc.dram_tensor("g1", [D, TQ], F32, kind="ExternalInput").ap()
    out_o = nc.dram_tensor("out", [D, TQ], F32, kind="ExternalOutput").ap()

    wo = ctx.sbuf("wos", [128, 16 * D], BF16)
    y3t = ctx.sbuf("y3t", [128, 16 * NT], BF16)
    x1t = ctx.sbuf("x1t", [128, NT])
    g1t = ctx.sbuf("g1t", [128, NT])
    ev = ctx.sbuf("ev", [128, NT])

    ctx.pool = PsumPool(ctx)
    wt = 0
    for k in range(16):
        wt = ctx.dma("gpsimd", wo[:, k * D:(k + 1) * D], wo_d[k * 128:(k + 1) * 128, :], ctx.ldS)
    for ti in range(TQ // NT):
        tsl = slice(ti * NT, (ti + 1) * NT)
        tw = ctx.barrier_waits()
        yw = 0
        for k in range(16):
            yw = ctx.dma("sync", y3t[:, k * NT:(k + 1) * NT], y3_d[k * 128:(k + 1) * 128, tsl],
                         ctx.lsS, waits=tw)
        for o in range(DC):
            bi, p, tk = mm_group(ctx, [(wo[:, k * D + o * 128: k * D + (o + 1) * 128],
                                        y3t[:, k * NT:(k + 1) * NT]) for k in range(16)],
                                 waits=[(ctx.ldS, wt), (ctx.lsS, yw)])
            lx = ctx.dma("sync", x1t[:, :], x1_d[o * 128:(o + 1) * 128, tsl], ctx.lsS,
                         waits=[(ctx.vS, ctx.vS.count)])
            lx = ctx.dma("sync", g1t[:, :], g1_d[o * 128:(o + 1) * 128, tsl], ctx.lsS)
            ctx.op("vector", lambda v, p=p: v.tensor_mul(ev[:, :], p[:, 0:NT], g1t[:, :]),
                   inc=ctx.vS, waits=[(ctx.mm, tk), (ctx.lsS, lx), (ctx.stS, ctx.stS.count)])
            etk = ctx.op("vector", lambda v: v.tensor_add(ev[:, :], ev[:, :], x1t[:, :]),
                         inc=ctx.vS)
            ctx.pool.release(ctx, "vector", bi, [(ctx.vS, etk)])
            ctx.dma("sync", out_o[o * 128:(o + 1) * 128, tsl], ev[:, :], ctx.stS,
                    waits=[(ctx.vS, etk)])
    ctx.wait("sync", ctx.stS, ctx.stS.count)
    return ctx


def host_final(inputs, core, st):
    b, q = core // 4, core % 4
    y3 = np.concatenate([st["y3s"][4 * b + j][:, q * TQ:(q + 1) * TQ] for j in range(4)], axis=0)
    return {
        "y3": np.ascontiguousarray(y3),
        "wo": np.ascontiguousarray(np.asarray(inputs["L1_attn_W_out"]).astype(np.float32).T),
        "x1": st["x1"][core],
        "g1": st["g1"][core],
    }


def _dyn_cols(t, col_expr, nrows, ncols, rowstride):
    return bass.AP(t, col_expr, [[rowstride, nrows], [1, ncols]])


def emit_barrier(ctx):
    sems = [ctx.mm, ctx.vS, ctx.sS, ctx.gS, ctx.ldS, ctx.lsS, ctx.stS]
    for eng in ENGINES:
        for s in sems:
            ctx.wait(eng, s, s.count)


def emit_spinwait(ctx, flg_t, flgsb, idx_val, magic_val):
    psem = ctx.sem(dma=True)

    def run(g, flg_t=flg_t, flgsb=flgsb, idx_val=idx_val, magic_val=magic_val, psem=psem):
        with g.register("nd") as nd, g.register("tgt") as tgt, g.register("fv") as fv, \
             g.register("mgr") as mgr:
            g.reg_alu(mgr, magic_val, 0, ALU.add)
            g.reg_mov(nd, 1)
            g.reg_mov(tgt, 0)
            with g.While(nd):
                g.reg_add(tgt, tgt, 16)
                g.dma_start(out=flgsb[0:1, 0:1],
                            in_=bass.AP(flg_t, idx_val, [[1, 1], [1, 1]])).then_inc(psem.h, 16)
                g.wait_ge(psem.h, tgt)
                g.load(fv, flgsb[0:1, 0:1])
                g.reg_alu(nd, fv, mgr, ALU.not_equal)

    ctx.q["gpsimd"].append(run)
    ctx.op("gpsimd", lambda g: g.nop(), inc=ctx.gS)
    return ctx.gS.count


def emit_l0_upath(ctx, W, x_src, ntok, u_dst):
    NT = 512
    ntiles = ntok // NT
    with ctx.scope():
        xt = ctx.sbuf("p1_xt", [128, DC * NT])
        xn = ctx.sbuf("p1_xn", [128, DC * NT], BF16)
        h1 = ctx.sbuf("p1_h1", [128, 4 * NT], BF16)
        sg = ctx.sbuf("p1_sg", [128, NT])
        ev1 = ctx.sbuf("p1_ev1", [128, NT])
        ev2 = ctx.sbuf("p1_ev2", [128, NT])
        sw1 = ctx.sbuf("p1_sw1", [128, DC * 512], BF16)
        sw2 = ctx.sbuf("p1_sw2", [128, 4 * 128], BF16)
        bwt = ctx.sbuf("p1_bw", [128, DC * 128], BF16)
        for c in range(DC):
            ctx.dma("gpsimd", sw1[:, c * 512:(c + 1) * 512], W["sw1_d"][c * 128:(c + 1) * 128, :], ctx.ldS)
            ctx.dma("gpsimd", bwt[:, c * 128:(c + 1) * 128], W["bw_d"][c * 128:(c + 1) * 128, :], ctx.ldS)
        for c in range(4):
            ctx.dma("gpsimd", sw2[:, c * 128:(c + 1) * 128], W["sw2_d"][c * 128:(c + 1) * 128, :], ctx.ldS)
        W = dict(W)
        W["sw1"], W["sw2"], W["bw"] = sw1, sw2, bwt
        consts, ms_tok = ln_consts2(ctx, NT, "p1")
        for ti in range(ntiles):
            tsl = slice(ti * NT, (ti + 1) * NT)
            bw = ctx.barrier_waits()
            xw = 0
            for c in range(DC):
                xw = ctx.dma("sync", xt[:, c * NT:(c + 1) * NT], x_src[c * 128:(c + 1) * 128, tsl],
                             ctx.lsS, waits=bw)
            xn_tok = emit_ln(ctx, consts, xt, NT, lambda c: xn[:, c * NT:(c + 1) * NT],
                             [(ctx.lsS, xw), (ctx.gS, ms_tok)])

            def xnsl(c):
                return xn[:, c * NT:(c + 1) * NT]

            for o in range(4):
                bi, p, tk = mm_group(ctx, [(W["sw1"][:, c * 512 + o * 128: c * 512 + (o + 1) * 128],
                                            xnsl(c)) for c in range(DC)],
                                     waits=[(ctx.ldS, 1), (ctx.vS, xn_tok)])
                etk = ctx.op("scalar", lambda s, o=o, p=p: s.activation(
                    out=h1[:, o * NT:(o + 1) * NT], in_=p[:, 0:NT], func=AF.Silu,
                    bias=W["sb1"][:, o:o + 1]), inc=ctx.sS, waits=[(ctx.mm, tk), (ctx.lsS, 1)])
                ctx.pool.release(ctx, "scalar", bi, [(ctx.sS, etk)])
            h1_tok = ctx.sS.count
            bi, p, tk = mm_group(ctx, [(W["sw2"][:, c * 128:(c + 1) * 128], h1[:, c * NT:(c + 1) * NT])
                                       for c in range(4)], waits=[(ctx.sS, h1_tok)])
            sg_tok = ctx.op("scalar", lambda s, p=p: s.activation(
                out=sg[:, :], in_=p[:, 0:NT], func=AF.Sigmoid, bias=W["sb2"][:, 0:1]),
                inc=ctx.sS, waits=[(ctx.mm, tk)])
            ctx.pool.release(ctx, "scalar", bi, [(ctx.sS, sg_tok)])
            bi, p, tk = mm_group(ctx, [(W["bw"][:, c * 128:(c + 1) * 128], xnsl(c)) for c in range(DC)],
                                 waits=[(ctx.vS, xn_tok)])
            etk = ctx.op("scalar", lambda s, p=p: s.activation(
                out=ev1[:, :], in_=p[:, 0:NT], func=AF.Identity, bias=W["bb"][:, 0:1]),
                inc=ctx.sS, waits=[(ctx.mm, tk)])
            ctx.pool.release(ctx, "scalar", bi, [(ctx.sS, etk)])
            utk = ctx.op("vector", lambda v: v.tensor_mul(ev2[:, :], ev1[:, :], sg[:, :]),
                         inc=ctx.vS, waits=[(ctx.sS, ctx.sS.count), (ctx.ldS, 1)])
            ctx.dma("gpsimd", u_dst[:, tsl], ev2[:, :], ctx.ldS, waits=[(ctx.vS, utk)])
    emit_barrier(ctx)


def ln_consts2(ctx, NT, tag):
    consts = {
        "onecol": ctx.sbuf(f"{tag}_onecol", [128, 1]),
        "onerow": ctx.sbuf(f"{tag}_onerow", [1, 128]),
        "epsc": ctx.sbuf(f"{tag}_epsc", [1, 1]),
        "rows": ctx.sbuf(f"{tag}_rows", [1, 4 * NT]),
        "sq": ctx.sbuf(f"{tag}_sq", [128, DC * NT]),
    }
    ctx.op("gpsimd", lambda g: g.memset(consts["onecol"][:, :], 1.0), inc=ctx.gS)
    ctx.op("gpsimd", lambda g: g.memset(consts["onerow"][:, :], 1.0), inc=ctx.gS)
    ctx.op("gpsimd", lambda g: g.memset(consts["epsc"][:, :], LN_EPS), inc=ctx.gS)
    return consts, ctx.gS.count


def emit_l0_scan(ctx, u_src, A_src, I_src, hs_dst):
    c0 = C0
    TE = T + c0
    with ctx.scope():
        ue = ctx.sbuf("p2_ue", [128, TE], BF16)
        P = ctx.sbuf("p2_P", [128, (c0 + 1) * 128], BF16)
        Fb = ctx.sbuf("p2_F", [128, TE])
        h = ctx.sbuf("p2_h", [128, TE], BF16)
        A_sb = ctx.sbuf("p2_A", [128, 128], BF16)
        ctx.op("gpsimd", lambda g: g.memset(ue[:, 0:c0], 0.0), inc=ctx.gS)
        ms_tok = ctx.gS.count
        ctx.dma("gpsimd", ue[:, c0:], u_src[:, :], ctx.ldS)
        ctx.dma("gpsimd", A_sb[:, :], A_src[:, :], ctx.ldS)
        ctx.dma("gpsimd", P[:, 0:128], I_src[:, :], ctx.ldS)
        ld_tok = ctx.ldS.count
        for k in range(1, c0 + 1):
            bi, p, tk = mm_group(ctx, [(A_sb[:, :], P[:, (k - 1) * 128: k * 128])],
                                 waits=[(ctx.ldS, ld_tok), (ctx.vS, ctx.vS.count)])
            etk = ctx.op("vector", lambda v, k=k, p=p: v.tensor_copy(P[:, k * 128:(k + 1) * 128],
                                                                     p[:, 0:128]),
                         inc=ctx.vS, waits=[(ctx.mm, tk)])
            ctx.pool.release(ctx, "vector", bi, [(ctx.vS, etk)])
        pow_tok = ctx.vS.count
        tiles = [(i * 512, 512) for i in range(TE // 512)]
        if TE % 512:
            tiles.append((TE - TE % 512, TE % 512))
        for (ts, n) in tiles:
            i, tile, (fsem, fneed) = ctx.pool.acquire()
            ctx.wait("tensor", fsem, fneed)
            ctx.wait("tensor", ctx.vS, pow_tok)
            ctx.wait("tensor", ctx.gS, ms_tok)
            ctx.wait("tensor", ctx.ldS, ctx.ldS.count)
            ctx.mm.count += 1
            tok = ctx.mm.count

            def run(t, ts=ts, n=n, tile=tile, sem=ctx.mm):
                inst = None
                first = True
                for k in range(c0):
                    if ts - k + n <= 0:
                        continue
                    lo = max(0, k - ts)
                    src0 = ts - k + lo
                    inst = t.matmul(tile[:, lo:n], P[:, k * 128:(k + 1) * 128],
                                    ue[:, src0: src0 + (n - lo)], start=first, stop=(k == c0 - 1))
                    first = False
                inst.then_inc(sem.h, 1)

            ctx.q["tensor"].append(run)
            etk = ctx.op("vector", lambda v, ts=ts, n=n, tile=tile: v.tensor_copy(
                Fb[:, ts:ts + n], tile[:, 0:n]), inc=ctx.vS, waits=[(ctx.mm, tok)])
            ctx.pool.release(ctx, "vector", i, [(ctx.vS, etk)])
        f_tok = ctx.vS.count
        ctx.op("vector", lambda v: v.tensor_copy(h[:, 0:c0], Fb[:, 0:c0]), inc=ctx.vS,
               waits=[(ctx.vS, f_tok)])
        for r in range(1, TE // c0):
            bi, p, tk = mm_group(ctx, [(P[:, c0 * 128:(c0 + 1) * 128], h[:, (r - 1) * c0: r * c0])],
                                 waits=[(ctx.vS, ctx.vS.count)])
            etk = ctx.op("vector", lambda v, r=r, p=p: v.tensor_add(
                h[:, r * c0:(r + 1) * c0], Fb[:, r * c0:(r + 1) * c0], p[:, 0:c0]),
                inc=ctx.vS, waits=[(ctx.mm, tk)])
            ctx.pool.release(ctx, "vector", bi, [(ctx.vS, etk)])
        ctx.dma("sync", hs_dst[:, :], h[:, c0:], ctx.stS, waits=[(ctx.vS, ctx.vS.count)])
    emit_barrier(ctx)


def emit_quarter_pipeline(ctx, W, tag, x_src, hs_win, xn_scr, g0_scr, x1_dst, xn1_dst,
                          io, with_z):
    TH = TQ + 3
    tiles9 = [(i * 512, 512) for i in range(8)] + [(4096, 3)]
    NT = 512
    with ctx.scope():
        xt = ctx.sbuf(f"{tag}a_xt", [128, DC * NT])
        xn = ctx.sbuf(f"{tag}a_xn", [128, DC * NT], BF16)
        consts, ms_tok = ln_consts2(ctx, NT, f"{tag}a")
        for (ts, n) in tiles9:
            bw = ctx.barrier_waits()
            xw = 0
            for c in range(DC):
                xw = ctx.dma("sync", xt[:, c * NT: c * NT + n],
                             x_src[c * 128:(c + 1) * 128, ts:ts + n], ctx.lsS, waits=bw)
            xn_tok = emit_ln(ctx, consts, xt, n, lambda c: xn[:, c * NT: c * NT + n],
                             [(ctx.lsS, xw), (ctx.gS, ms_tok)], stride=NT)
            for c in range(DC):
                ctx.dma("sync", xn_scr[c * 128:(c + 1) * 128, ts:ts + n],
                        xn[:, c * NT: c * NT + n], ctx.stS, waits=[(ctx.vS, xn_tok)])
    emit_barrier(ctx)
    H = 3072
    HC = 24
    NT2 = 256
    with ctx.scope():
        wg = ctx.sbuf(f"{tag}b_wg", [128, DC * H], BF16)
        wu = ctx.sbuf(f"{tag}b_wu", [128, DC * H], BF16)
        wd = ctx.sbuf(f"{tag}b_wd", [128, HC * D], BF16)
        xnt = ctx.sbuf(f"{tag}b_xnt", [128, DC * NT2], BF16)
        gsp = ctx.sbuf(f"{tag}b_gsp", [128, HC * NT2])
        hbf = ctx.sbuf(f"{tag}b_hbf", [128, HC * NT2], BF16)
        fev1 = ctx.sbuf(f"{tag}b_ev1", [128, NT2])
        for c in range(DC):
            ctx.dma("gpsimd", wg[:, c * H:(c + 1) * H], W["wg_d"][c * 128:(c + 1) * 128, :], ctx.ldS)
            ctx.dma("gpsimd", wu[:, c * H:(c + 1) * H], W["wu_d"][c * 128:(c + 1) * 128, :], ctx.ldS)
        for c in range(HC):
            ctx.dma("gpsimd", wd[:, c * D:(c + 1) * D], W["wd_d"][c * 128:(c + 1) * 128, :], ctx.ldS)
        wld = ctx.ldS.count
        if W.get("_dbgw") is not None and tag == "o":
            ctx.dma("sync", W["_dbgw"][:, :], wg[:, 0:3072], ctx.stS, waits=[(ctx.ldS, wld)])
        ftiles = [(i * NT2, NT2) for i in range(16)] + [(4096, 3)]
        for (ts, n) in ftiles:
            bw = ctx.barrier_waits()
            xw = 0
            for c in range(DC):
                xw = ctx.dma("sync", xnt[:, c * NT2: c * NT2 + n],
                             xn_scr[c * 128:(c + 1) * 128, ts:ts + n], ctx.lsS, waits=bw)

            def xnsl(c, n=n):
                return xnt[:, c * NT2: c * NT2 + n]

            for o in range(HC):
                bi, p, tk = mm_group(ctx, [(wg[:, c * H + o * 128: c * H + (o + 1) * 128], xnsl(c))
                                           for c in range(DC)],
                                     waits=[(ctx.ldS, wld), (ctx.lsS, xw)])
                etk = ctx.op("scalar", lambda s, o=o, p=p, n=n: s.activation(
                    out=gsp[:, o * NT2: o * NT2 + n], in_=p[:, 0:n], func=AF.Silu, scale=DEQ),
                    inc=ctx.sS, waits=[(ctx.mm, tk)])
                ctx.pool.release(ctx, "scalar", bi, [(ctx.sS, etk)])
            for o in range(HC):
                bi, p, tk = mm_group(ctx, [(wu[:, c * H + o * 128: c * H + (o + 1) * 128], xnsl(c))
                                           for c in range(DC)], waits=[(ctx.lsS, xw)])
                etk = ctx.op("vector", lambda v, o=o, p=p, n=n: v.scalar_tensor_tensor(
                    hbf[:, o * NT2: o * NT2 + n], p[:, 0:n], DEQ, gsp[:, o * NT2: o * NT2 + n],
                    ALU.mult, ALU.mult),
                    inc=ctx.vS, waits=[(ctx.mm, tk), (ctx.sS, ctx.sS.count)])
                ctx.pool.release(ctx, "vector", bi, [(ctx.vS, etk)])
            h_tok = ctx.vS.count
            for o in range(DC):
                bi, p, tk = mm_group(ctx, [(wd[:, c * D + o * 128: c * D + (o + 1) * 128],
                                            hbf[:, c * NT2: c * NT2 + n]) for c in range(HC)],
                                     waits=[(ctx.vS, h_tok)])
                etk = ctx.op("scalar", lambda s, p=p, n=n, fev1=fev1: s.activation(
                    out=fev1[:, 0:n], in_=p[:, 0:n], func=AF.Silu, scale=DEQ),
                    inc=ctx.sS, waits=[(ctx.mm, tk), (ctx.stS, 1)])
                ctx.pool.release(ctx, "scalar", bi, [(ctx.sS, etk)])
                ctx.dma("sync", g0_scr[o * 128:(o + 1) * 128, ts:ts + n], fev1[:, 0:n],
                        ctx.stS, waits=[(ctx.sS, etk)])
    emit_barrier(ctx)
    with ctx.scope():
        dw = ctx.sbuf(f"{tag}d_dw", [128, DC * D], BF16)
        gw = ctx.sbuf(f"{tag}d_gw", [128, DC * D], BF16)
        xt = ctx.sbuf(f"{tag}d_xt", [128, DC * NT])
        xn = ctx.sbuf(f"{tag}d_xn", [128, DC * NT], BF16)
        hst = ctx.sbuf(f"{tag}d_hs", [128, NT], BF16)
        s0t = ctx.sbuf(f"{tag}d_s0", [128, DC * NT])
        ga = ctx.sbuf(f"{tag}d_ga", [128, NT])
        ev1 = ctx.sbuf(f"{tag}d_ev1", [128, NT])
        consts, ms_tok = ln_consts2(ctx, NT, f"{tag}d")
        for c in range(DC):
            ctx.dma("gpsimd", dw[:, c * D:(c + 1) * D], W["dw_d"][c * 128:(c + 1) * 128, :], ctx.ldS)
            ctx.dma("gpsimd", gw[:, c * D:(c + 1) * D], W["gw_d"][c * 128:(c + 1) * 128, :], ctx.ldS)
        wld = ctx.ldS.count
        for (ts, n) in tiles9:
            bw = ctx.barrier_waits()
            xw = 0
            for c in range(DC):
                xw = ctx.dma("sync", xt[:, c * NT: c * NT + n],
                             x_src[c * 128:(c + 1) * 128, ts:ts + n], ctx.lsS, waits=bw)
                xw = ctx.dma("sync", xn[:, c * NT: c * NT + n],
                             xn_scr[c * 128:(c + 1) * 128, ts:ts + n], ctx.lsS)

            hw = ctx.dma("sync", hst[:, 0:n], hs_win[:, ts:ts + n], ctx.lsS)

            def xnsl(c, n=n):
                return xn[:, c * NT: c * NT + n]

            for o in range(DC):
                bi, py, tky = mm_group(ctx, [(W["cw"][:, o * 128:(o + 1) * 128], hst[:, 0:n])] +
                                       [(dw[:, c * D + o * 128: c * D + (o + 1) * 128], xnsl(c))
                                        for c in range(DC)],
                                       waits=[(ctx.lsS, hw), (ctx.ldS, wld)])
                y0tk = ctx.op("vector", lambda v, py=py, o=o, n=n: v.tensor_scalar(
                    s0t[:, o * NT: o * NT + n], py[:, 0:n], W["ydb"][:, o:o + 1], None, ALU.add),
                    inc=ctx.vS, waits=[(ctx.mm, tky), (ctx.sS, ctx.sS.count)])
                ctx.pool.release(ctx, "vector", bi, [(ctx.vS, y0tk)])
                bi, pg, tkg = mm_group(ctx, [(gw[:, c * D + o * 128: c * D + (o + 1) * 128],
                                              xnsl(c)) for c in range(DC)], waits=())
                gtk = ctx.op("scalar", lambda s, pg=pg, o=o, n=n: s.activation(
                    out=ga[:, 0:n], in_=pg[:, 0:n], func=AF.Sigmoid, bias=W["gb"][:, o:o + 1]),
                    inc=ctx.sS, waits=[(ctx.mm, tkg), (ctx.vS, y0tk)])
                ctx.pool.release(ctx, "scalar", bi, [(ctx.sS, gtk)])
                g0w = ctx.dma("sync", ev1[:, 0:n], g0_scr[o * 128:(o + 1) * 128, ts:ts + n],
                              ctx.lsS, waits=[(ctx.vS, ctx.vS.count), (ctx.sS, ctx.sS.count)])
                ctx.op("vector", lambda v, o=o, n=n: v.tensor_mul(
                    s0t[:, o * NT: o * NT + n], s0t[:, o * NT: o * NT + n], ga[:, 0:n]),
                    inc=ctx.vS, waits=[(ctx.sS, gtk), (ctx.lsS, g0w)])
                ctx.op("vector", lambda v, o=o, n=n: v.tensor_mul(
                    s0t[:, o * NT: o * NT + n], s0t[:, o * NT: o * NT + n], ev1[:, 0:n]), inc=ctx.vS)
                ctx.op("vector", lambda v, o=o, n=n: v.tensor_add(
                    xt[:, o * NT: o * NT + n], xt[:, o * NT: o * NT + n],
                    s0t[:, o * NT: o * NT + n]), inc=ctx.vS)
            x1_tok = ctx.vS.count
            for c in range(DC):
                lo = max(0, 3 - ts)
                if n - lo > 0:
                    ctx.dma("sync", x1_dst[c * 128:(c + 1) * 128, ts - 3 + lo: ts - 3 + n],
                            xt[:, c * NT + lo: c * NT + n], ctx.stS, waits=[(ctx.vS, x1_tok)])
            xn1_tok = emit_ln(ctx, consts, xt, n, lambda c: xn[:, c * NT: c * NT + n],
                              [(ctx.vS, x1_tok), (ctx.stS, 1)], stride=NT)
            for c in range(DC):
                ctx.dma("sync", xn1_dst[c * 128:(c + 1) * 128, ts:ts + n],
                        xn[:, c * NT: c * NT + n], ctx.stS, waits=[(ctx.vS, xn1_tok)])
    emit_barrier(ctx)
    emit_mamba_a2(ctx, W, tag, xn1_dst, io, with_z)


def emit_mamba_a2(ctx, W, tag, xn1_src, io, with_z):
    TH = TQ + 3
    DI = 2048
    with ctx.scope():
        xnh = ctx.sbuf(f"{tag}m_xnh", [128, DC * TH], BF16)
        wb = ctx.sbuf(f"{tag}m_wb", [128, DC * 128], BF16)
        uo = ctx.sbuf(f"{tag}m_uo", [128, TH])
        ca = ctx.sbuf(f"{tag}m_ca", [128, TQ])
        cb = ctx.sbuf(f"{tag}m_cb", [128, TQ])
        ucf = ctx.sbuf(f"{tag}m_ucf", [128, TQ])
        ucc = ctx.sbuf(f"{tag}m_ucc", [128, TQ], BF16)
        xdbl = ctx.sbuf(f"{tag}m_xdbl", [128, TQ])
        xdblb = ctx.sbuf(f"{tag}m_xdblb", [128, TQ], BF16)
        eva = ctx.sbuf(f"{tag}m_eva", [128, 1024])
        evb = ctx.sbuf(f"{tag}m_evb", [128, 1024])
        evc = ctx.sbuf(f"{tag}m_evc", [128, 1024])
        bcsb = ctx.sbuf(f"{tag}m_bcsb", [128, 32 * 32])
        onec = ctx.sbuf(f"{tag}m_onec", [128, 1])
        ctx.op("gpsimd", lambda g: g.memset(onec[:, :], 1.0), inc=ctx.gS)
        ms_tok = ctx.gS.count
        xw = 0
        for c in range(DC):
            xw = ctx.dma("sync", xnh[:, c * TH:(c + 1) * TH], xn1_src[c * 128:(c + 1) * 128, :], ctx.lsS)
        utiles = [(t * 512, 512) for t in range(8)] + [(4096, 3)]
        for o in range(16):
            wt = ctx.dma("gpsimd", wb[:, :],
                         W["win_d"][:, o * 128:(o + 1) * 128].rearrange("(c p) n -> p c n", p=128),
                         ctx.ldS, waits=[(ctx.mm, ctx.mm.count)])
            for (ts, n) in utiles:
                bi, p, tk = mm_group(ctx, [(wb[:, c * 128:(c + 1) * 128],
                                            xnh[:, c * TH + ts: c * TH + ts + n]) for c in range(DC)],
                                     waits=[(ctx.ldS, wt), (ctx.lsS, xw)])
                etk = ctx.op("vector", lambda v, ts=ts, n=n, p=p: v.tensor_copy(
                    uo[:, ts:ts + n], p[:, 0:n]), inc=ctx.vS, waits=[(ctx.mm, tk)])
                ctx.pool.release(ctx, "vector", bi, [(ctx.vS, etk)])
            ctx.op("vector", lambda v, o=o: v.tensor_scalar(
                ca[:, :], uo[:, 0:TQ], W["convc"][:, o * 4: o * 4 + 1], None, ALU.mult),
                inc=ctx.vS, waits=[(ctx.lsS, 1), (ctx.sS, ctx.sS.count)])
            ctx.op("vector", lambda v, o=o: v.scalar_tensor_tensor(
                cb[:, :], uo[:, 1:TQ + 1], W["convc"][:, o * 4 + 1: o * 4 + 2], ca[:, :],
                ALU.mult, ALU.add), inc=ctx.vS)
            ctx.op("vector", lambda v, o=o: v.scalar_tensor_tensor(
                ca[:, :], uo[:, 2:TQ + 2], W["convc"][:, o * 4 + 2: o * 4 + 3], cb[:, :],
                ALU.mult, ALU.add), inc=ctx.vS)
            ctx.op("vector", lambda v, o=o: v.scalar_tensor_tensor(
                cb[:, :], uo[:, 3:TQ + 3], W["convc"][:, o * 4 + 3: o * 4 + 4], ca[:, :],
                ALU.mult, ALU.add), inc=ctx.vS)
            stk = ctx.op("scalar", lambda sx, o=o: sx.activation(
                out=ucf[:, :], in_=cb[:, :], func=AF.Silu, bias=W["convb"][:, o:o + 1]),
                inc=ctx.sS, waits=[(ctx.vS, ctx.vS.count), (ctx.stS, 1), (ctx.ldS, 1)])
            ctx.dma("sync", io["uc"][o * 128:(o + 1) * 128, :], ucf[:, :], ctx.stS,
                    waits=[(ctx.sS, stk)])
            ctx.dma("gpsimd", io["ucb"][o * 128:(o + 1) * 128, :], ucf[:, :], ctx.ldS,
                    waits=[(ctx.sS, stk)])
        if with_z:
            for o in range(16):
                wt = ctx.dma("gpsimd", wb[:, :],
                             W["win_d"][:, DI + o * 128: DI + (o + 1) * 128]
                             .rearrange("(c p) n -> p c n", p=128),
                             ctx.ldS, waits=[(ctx.mm, ctx.mm.count)])
                for t in range(8):
                    ts = 3 + t * 512
                    bi, p, tk = mm_group(ctx, [(wb[:, c * 128:(c + 1) * 128],
                                                xnh[:, c * TH + ts: c * TH + ts + 512])
                                               for c in range(DC)],
                                         waits=[(ctx.ldS, wt)])
                    etk = ctx.op("scalar", lambda sx, p=p: sx.activation(
                        out=eva[:, 0:512], in_=p[:, 0:512], func=AF.Silu),
                        inc=ctx.sS, waits=[(ctx.mm, tk), (ctx.stS, 1), (ctx.vS, ctx.vS.count)])
                    ctx.pool.release(ctx, "scalar", bi, [(ctx.sS, etk)])
                    ctx.dma("sync", io["sz"][o * 128:(o + 1) * 128, t * 512:(t + 1) * 512],
                            eva[:, 0:512], ctx.stS, waits=[(ctx.sS, etk)])
        ctx.wait("tensor", ctx.ldS, ctx.ldS.count)
        for i in range(8):
            ctx.wait("tensor", ctx.pool.free[i], ctx.pool.uses[i])
            ctx.pool.uses[i] += 1
        xd_tok = []
        for c in range(16):
            lt = ctx.dma("sync", ucc[:, :], io["ucb"][c * 128:(c + 1) * 128, :], ctx.lsS,
                         waits=[(ctx.mm, ctx.mm.count), (ctx.ldS, 1)])
            for t in range(8):
                ctx.wait("tensor", ctx.lsS, lt)
                ctx.mm.count += 1

                def runx(tt, c=c, t=t, sem=ctx.mm):
                    tt.matmul(ctx.pool.tiles[t][0:96, 0:512], W["wxb"][:, c * 96:(c + 1) * 96],
                              ucc[:, t * 512:(t + 1) * 512], start=(c == 0),
                              stop=(c == 15)).then_inc(sem.h, 1)

                ctx.q["tensor"].append(runx)
                if c == 15:
                    xd_tok.append(ctx.mm.count)
        for t in range(8):
            etk = ctx.op("vector", lambda v, t=t: v.tensor_copy(
                xdbl[0:96, t * 512:(t + 1) * 512], ctx.pool.tiles[t][0:96, 0:512]),
                inc=ctx.vS, waits=[(ctx.mm, xd_tok[t])])
            ctx.pool.release(ctx, "vector", t, [(ctx.vS, etk)])
        ctx.op("vector", lambda v: v.tensor_copy(xdblb[0:64, :], xdbl[0:64, :]), inc=ctx.vS)
        xdbl_tok = ctx.vS.count
        for o in range(16):
            for t in range(8):
                bi, p, tk = mm_group(ctx, [(W["wdtb"][0:64, o * 128:(o + 1) * 128],
                                            xdblb[0:64, t * 512:(t + 1) * 512])],
                                     waits=[(ctx.vS, xdbl_tok), (ctx.ldS, 1)])
                bcol = W["bdt"][:, o:o + 1]
                par = ((o * 8 + t) % 2) * 512
                A_ = eva[:, par:par + 512]
                B_ = evb[:, par:par + 512]
                C_ = evc[:, par:par + 512]
                ctx.op("vector", lambda v, p=p, bcol=bcol, C_=C_: v.tensor_scalar_add(
                    C_, p[:, 0:512], bcol),
                    inc=ctx.vS, waits=[(ctx.mm, tk), (ctx.lsS, 1), (ctx.sS, ctx.sS.count)])
                ctx.op("vector", lambda v, A_=A_, C_=C_: v.tensor_scalar_max(A_, C_, 0.0),
                       inc=ctx.vS, waits=[(ctx.stS, 1)])
                ctx.op("vector", lambda v, B_=B_, C_=C_: v.tensor_scalar_mul(B_, C_, -1.0), inc=ctx.vS)
                ctx.op("vector", lambda v, B_=B_, C_=C_: v.tensor_max(B_, B_, C_), inc=ctx.vS)
                vt = ctx.vS.count
                ctx.op("scalar", lambda sx, B_=B_, C_=C_: sx.activation(
                    out=C_, in_=B_, func=AF.Exp, scale=-1.0), inc=ctx.sS, waits=[(ctx.vS, vt)])
                etk2 = ctx.op("scalar", lambda sx, B_=B_, C_=C_: sx.activation(
                    out=B_, in_=C_, func=AF.Ln, bias=onec[:, 0:1]),
                    inc=ctx.sS, waits=[(ctx.gS, ms_tok)])
                ctx.pool.release(ctx, "scalar", bi, [(ctx.sS, etk2)])
                etk = ctx.op("vector", lambda v, A_=A_, B_=B_: v.tensor_add(A_, A_, B_),
                             inc=ctx.vS, waits=[(ctx.sS, etk2), (ctx.stS, 1)])
                ctx.dma("sync", io["dt"][o * 128:(o + 1) * 128, t * 512:(t + 1) * 512], A_,
                        ctx.stS, waits=[(ctx.vS, etk)])
        for j in range(32):
            i, tile, (fsem, fneed) = ctx.pool.acquire()
            ctx.wait("tensor", fsem, fneed)
            ctx.wait("tensor", ctx.vS, xdbl_tok)
            ctx.wait("tensor", ctx.lsS, ctx.lsS.count)
            ctx.mm.count += 1
            tk = ctx.mm.count

            def runt(t, j=j, tile=tile, sem=ctx.mm):
                t.transpose(tile[:, 0:32], xdbl[64:96, j * 128:(j + 1) * 128],
                            W["idf"][64:96, 64:96]).then_inc(sem.h, 1)

            ctx.q["tensor"].append(runt)
            etk = ctx.op("vector", lambda v, j=j, tile=tile: v.tensor_copy(
                bcsb[:, j * 32:(j + 1) * 32], tile[:, 0:32]), inc=ctx.vS, waits=[(ctx.mm, tk)])
            ctx.pool.release(ctx, "vector", i, [(ctx.vS, etk)])
        bc_tok = ctx.vS.count
        ctx.dma("sync", io["bct"].rearrange("(j p) s -> p j s", p=128),
                bcsb.ap().rearrange("p (j s) -> p j s", s=32), ctx.stS, waits=[(ctx.vS, bc_tok)])
    emit_barrier(ctx)


def emit_scan2(ctx, W, tag, io, with_y, init_tile, state_out):
    NT = 512
    NS = 16
    ntiles = TQ // NT
    NG = 16
    with ctx.scope():
        bufA = ctx.sbuf(f"{tag}s_A", [128, NT * NS])
        bufB = ctx.sbuf(f"{tag}s_B", [128, NT * NS])
        bufC = ctx.sbuf(f"{tag}s_C", [128, NT * NS])
        Bbc = ctx.sbuf(f"{tag}s_Bb", [128, NT * NS])
        Cbc = ctx.sbuf(f"{tag}s_Cb", [128, NT * NS]) if with_y else None
        dtg = ctx.sbuf(f"{tag}s_dt", [128, NT])
        ucg = ctx.sbuf(f"{tag}s_uc", [128, NT])
        szg = ctx.sbuf(f"{tag}s_sz", [128, NT]) if with_y else None
        gg = ctx.sbuf(f"{tag}s_g", [128, NT])
        yb = ctx.sbuf(f"{tag}s_y", [128, NT]) if with_y else None
        yb2 = ctx.sbuf(f"{tag}s_y2", [128, NT]) if with_y else None
        y3b = ctx.sbuf(f"{tag}s_y3", [128, NT], BF16) if with_y else None
        rowb = ctx.sbuf(f"{tag}s_row", [1, 2 * 512])
        onerow = ctx.sbuf(f"{tag}s_or", [1, 128])
        ctx.op("gpsimd", lambda g: g.memset(onerow[:, :], 1.0), inc=ctx.gS)
        ms_tok = ctx.gS.count
        for ti in range(ntiles):
            tw = ctx.barrier_waits()
            pairs = [(0, Bbc)] + ([(16, Cbc)] if with_y else [])
            for (coff, dst) in pairs:
                for k in range(NS):
                    par = (k % 2) * 512
                    lw = ctx.dma("sync", rowb[0:1, par:par + 512].rearrange("p (t s) -> p t s", s=16),
                                 io["bct"][ti * NT + k * 32: ti * NT + (k + 1) * 32, coff:coff + 16]
                                 .unsqueeze(0), ctx.lsS, waits=tw + [(ctx.mm, ctx.mm.count)])
                    bi, p, tk = mm_group(ctx, [(onerow[0:1, :], rowb[0:1, par:par + 512])],
                                         waits=[(ctx.lsS, lw), (ctx.gS, ms_tok)])
                    etk = ctx.op("vector", lambda v, p=p, dst=dst, k=k: v.tensor_copy(
                        dst[:, k * 512:(k + 1) * 512], p[:, 0:512]), inc=ctx.vS,
                        waits=[(ctx.mm, tk)])
                    ctx.pool.release(ctx, "vector", bi, [(ctx.vS, etk)])
            for g in range(NG):
                ld = ctx.dma("sync", dtg[:, :], io["dt"][g * 128:(g + 1) * 128, ti * NT:(ti + 1) * NT],
                             ctx.lsS, waits=[(ctx.vS, ctx.vS.count), (ctx.sS, ctx.sS.count)])
                ld = ctx.dma("sync", ucg[:, :], io["uc"][g * 128:(g + 1) * 128, ti * NT:(ti + 1) * NT],
                             ctx.lsS)
                if with_y:
                    ld = ctx.dma("sync", szg[:, :], io["sz"][g * 128:(g + 1) * 128, ti * NT:(ti + 1) * NT],
                                 ctx.lsS)
                ctx.op("vector", lambda v: v.tensor_mul(gg[:, :], dtg[:, :], ucg[:, :]),
                       inc=ctx.vS, waits=[(ctx.lsS, ld)])
                a3 = bufA.ap().rearrange("p (t s) -> p t s", s=NS)
                dtbc = dtg.ap().unsqueeze(2).broadcast_to([128, NT, NS])
                abc = W["asb"].ap()[:, g * NS:(g + 1) * NS].unsqueeze(1).broadcast_to([128, NT, NS])
                ctx.op("vector", lambda v, a3=a3, dtbc=dtbc, abc=abc: v.tensor_tensor(
                    a3, dtbc, abc, ALU.mult), inc=ctx.vS)
                eE = ctx.vS.count
                sE = ctx.op("scalar", lambda sx: sx.activation(out=bufB[:, :], in_=bufA[:, :],
                                                               func=AF.Exp),
                            inc=ctx.sS, waits=[(ctx.vS, eE)])
                gbc = gg.ap().unsqueeze(2).broadcast_to([128, NT, NS])
                b3 = Bbc.ap().rearrange("p (t s) -> p t s", s=NS)
                ctx.op("vector", lambda v, a3=a3, gbc=gbc, b3=b3: v.tensor_tensor(
                    a3, gbc, b3, ALU.mult), inc=ctx.vS, waits=[(ctx.sS, sE)])
                aR = bufA.ap().rearrange("p (t s) -> p s t", s=NS)
                bR = bufB.ap().rearrange("p (t s) -> p s t", s=NS)
                cR = bufC.ap().rearrange("p (t s) -> p s t", s=NS)
                for sidx in range(NS):
                    if ti == 0:
                        init = 0.0 if init_tile is None else init_tile[:, g * NS + sidx: g * NS + sidx + 1]
                    else:
                        init = state_out[:, g * NS + sidx: g * NS + sidx + 1]
                    ctx.op("vector", lambda v, sidx=sidx, init=init, aR=aR, bR=bR, cR=cR:
                           v.tensor_tensor_scan(cR[:, sidx, :], bR[:, sidx, :], aR[:, sidx, :],
                                                init, ALU.mult, ALU.add), inc=ctx.vS)
                ctx.op("vector", lambda v, g=g: v.tensor_copy(state_out[:, g * NS:(g + 1) * NS],
                                                              bufC[:, (NT - 1) * NS: NT * NS]),
                       inc=ctx.vS)
                if with_y:
                    c3 = Cbc.ap().rearrange("p (t s) -> p t s", s=NS)
                    h3 = bufC.ap().rearrange("p (t s) -> p t s", s=NS)
                    ctx.op("vector", lambda v, a3=a3, h3=h3, c3=c3: v.tensor_tensor(
                        a3, h3, c3, ALU.mult), inc=ctx.vS)
                    ctx.op("vector", lambda v, a3=a3: v.tensor_reduce(yb[:, :], a3, AX.X, ALU.add),
                           inc=ctx.vS)
                    ctx.op("vector", lambda v, g=g: v.scalar_tensor_tensor(
                        yb2[:, :], ucg[:, :], W["dvc"][:, g:g + 1], yb[:, :], ALU.mult, ALU.add),
                        inc=ctx.vS)
                    etk = ctx.op("vector", lambda v: v.tensor_mul(y3b[:, :], yb2[:, :], szg[:, :]),
                                 inc=ctx.vS, waits=[(ctx.stS, 1)])
                    ctx.dma("sync", io["y3"][g * 128:(g + 1) * 128, ti * NT:(ti + 1) * NT], y3b[:, :],
                            ctx.stS, waits=[(ctx.vS, etk)])
    emit_barrier(ctx)


def emit_moe2(ctx, W, tag, xn1_scr, g1_dst):
    NE, HP = 8, 4096
    QW = 1024
    nc = ctx.nc
    wf_d = nc.dram_tensor(f"{tag}_wfd", [NE, TQ], F32).ap()
    with ctx.scope():
        xnt = ctx.sbuf(f"{tag}e_xnt", [128, DC * 512], BF16)
        xnq = ctx.sbuf(f"{tag}e_xnq", [128, DC * QW], BF16)
        sc = ctx.sbuf(f"{tag}e_sc", [128, 32 * NE])
        m1 = ctx.sbuf(f"{tag}e_m1", [128, 32])
        m2 = ctx.sbuf(f"{tag}e_m2", [128, 32])
        lt = ctx.sbuf(f"{tag}e_lt", [128, 32 * NE])
        msk = ctx.sbuf(f"{tag}e_msk", [128, 32 * NE])
        dd = ctx.sbuf(f"{tag}e_dd", [128, 32])
        w1v = ctx.sbuf(f"{tag}e_w1v", [128, 32])
        w2v = ctx.sbuf(f"{tag}e_w2v", [128, 32])
        eq = ctx.sbuf(f"{tag}e_eq", [128, 32 * NE])
        wsb = ctx.sbuf(f"{tag}e_wsb", [128, 32 * NE])
        wrows = ctx.sbuf(f"{tag}e_wr", [8, 128])
        wfrow = ctx.sbuf(f"{tag}e_wf", [1, 512])
        brbc = ctx.sbuf(f"{tag}e_brbc", [128, NE])
        w1b = ctx.sbuf(f"{tag}e_w1b", [128, 2 * DC * 128], BF16)
        w2b = ctx.sbuf(f"{tag}e_w2b", [128, 2 * 32 * 128], BF16)
        gsb = ctx.sbuf(f"{tag}e_gsb", [128, 32 * QW], BF16)
        acc = ctx.sbuf(f"{tag}e_acc", [128, DC * QW])
        wbc = ctx.sbuf(f"{tag}e_wbc", [128, 2 * 512])
        xbf = ctx.sbuf(f"{tag}e_xbf", [128, 1024])
        sqf = ctx.sbuf(f"{tag}e_sqf", [128, 1024])
        inn = ctx.sbuf(f"{tag}e_inn", [128, 1024])
        ev1 = ctx.sbuf(f"{tag}e_ev1", [128, 512])
        yw = ctx.sbuf(f"{tag}e_yw", [128, 512])
        onerow = ctx.sbuf(f"{tag}e_or", [1, 128])
        ctx.op("gpsimd", lambda g: g.memset(onerow[:, :], 1.0), inc=ctx.gS)
        ms_tok = ctx.gS.count

        bi, p, tk = mm_group(ctx, [(onerow[0:1, :], W["brr"][0:1, :])],
                             waits=[(ctx.lsS, 1), (ctx.gS, ms_tok), (ctx.ldS, 1)])
        etk = ctx.op("vector", lambda v, p=p: v.tensor_copy(brbc[:, :], p[:, 0:NE]),
                     inc=ctx.vS, waits=[(ctx.mm, tk)])
        ctx.pool.release(ctx, "vector", bi, [(ctx.vS, etk)])
        for tt in range(8):
            xw = 0
            for c in range(DC):
                xw = ctx.dma("sync", xnt[:, c * 512:(c + 1) * 512],
                             xn1_scr[c * 128:(c + 1) * 128, 3 + tt * 512: 3 + (tt + 1) * 512],
                             ctx.lsS, waits=[(ctx.mm, ctx.mm.count)])
            for j in range(4):
                tc = tt * 4 + j
                bi, p, tk = mm_group(ctx, [(xnt[:, c * 512 + j * 128: c * 512 + (j + 1) * 128],
                                            W["wrb"][:, c * NE:(c + 1) * NE]) for c in range(DC)],
                                     waits=[(ctx.ldS, 1), (ctx.lsS, xw)])
                etk = ctx.op("vector", lambda v, tc=tc, p=p: v.tensor_add(
                    sc[:, tc * NE:(tc + 1) * NE], p[:, 0:NE], brbc[:, :]),
                    inc=ctx.vS, waits=[(ctx.mm, tk)])
                ctx.pool.release(ctx, "vector", bi, [(ctx.vS, etk)])
        sc3 = sc.ap().rearrange("p (t e) -> p t e", e=NE)
        ctx.op("vector", lambda v: v.tensor_reduce(m1[:, :], sc3, AX.X, ALU.max), inc=ctx.vS)
        m1bc = m1.ap().unsqueeze(2).broadcast_to([128, 32, NE])
        lt3 = lt.ap().rearrange("p (t e) -> p t e", e=NE)
        ctx.op("vector", lambda v: v.tensor_tensor(lt3, sc3, m1bc, ALU.is_ge), inc=ctx.vS)
        ctx.op("vector", lambda v: v.scalar_tensor_tensor(msk[:, :], lt[:, :], -1.0e4, sc[:, :],
                                                          ALU.mult, ALU.add), inc=ctx.vS)
        msk3 = msk.ap().rearrange("p (t e) -> p t e", e=NE)
        ctx.op("vector", lambda v: v.tensor_reduce(m2[:, :], msk3, AX.X, ALU.max), inc=ctx.vS)
        ctx.op("vector", lambda v: v.tensor_sub(dd[:, :], m2[:, :], m1[:, :]), inc=ctx.vS)
        ctx.op("scalar", lambda sx: sx.activation(out=dd[:, :], in_=dd[:, :], func=AF.Exp),
               inc=ctx.sS, waits=[(ctx.vS, ctx.vS.count)])
        ctx.op("vector", lambda v: v.tensor_scalar_add(w1v[:, :], dd[:, :], 1.0), inc=ctx.vS,
               waits=[(ctx.sS, ctx.sS.count)])
        ctx.op("vector", lambda v: v.reciprocal(w1v[:, :], w1v[:, :]), inc=ctx.vS)
        ctx.op("vector", lambda v: v.tensor_mul(w2v[:, :], dd[:, :], w1v[:, :]), inc=ctx.vS)
        eq3 = eq.ap().rearrange("p (t e) -> p t e", e=NE)
        wsb3 = wsb.ap().rearrange("p (t e) -> p t e", e=NE)
        w1bc = w1v.ap().unsqueeze(2).broadcast_to([128, 32, NE])
        w2bc = w2v.ap().unsqueeze(2).broadcast_to([128, 32, NE])
        m2bc = m2.ap().unsqueeze(2).broadcast_to([128, 32, NE])
        ctx.op("vector", lambda v: v.tensor_tensor(eq3, sc3, m1bc, ALU.is_equal), inc=ctx.vS)
        ctx.op("vector", lambda v: v.tensor_tensor(wsb3, eq3, w1bc, ALU.mult), inc=ctx.vS)
        ctx.op("vector", lambda v: v.tensor_tensor(eq3, sc3, m2bc, ALU.is_equal), inc=ctx.vS)
        ctx.op("vector", lambda v: v.tensor_tensor(eq3, eq3, w2bc, ALU.mult), inc=ctx.vS)
        ctx.op("vector", lambda v: v.tensor_add(wsb[:, :], wsb[:, :], eq[:, :]), inc=ctx.vS)
        wsb_tok = ctx.vS.count
        for j in range(32):
            i, tile, (fsem, fneed) = ctx.pool.acquire()
            ctx.wait("tensor", fsem, fneed)
            ctx.wait("tensor", ctx.vS, wsb_tok)
            ctx.wait("tensor", ctx.lsS, ctx.lsS.count)
            ctx.mm.count += 1
            tk = ctx.mm.count

            def runt(t, j=j, tile=tile, sem=ctx.mm):
                t.transpose(tile[0:NE, 0:128], wsb[:, j * NE:(j + 1) * NE],
                            W["idf"][:, :]).then_inc(sem.h, 1)

            ctx.q["tensor"].append(runt)
            etk = ctx.op("vector", lambda v, tile=tile: v.tensor_copy(
                wrows[0:NE, :], tile[0:NE, 0:128]),
                inc=ctx.vS, waits=[(ctx.mm, tk), (ctx.ldS, 1)])
            ctx.pool.release(ctx, "vector", i, [(ctx.vS, etk)])
            ctx.dma("gpsimd", wf_d[0:NE, j * 128:(j + 1) * 128], wrows[0:NE, :], ctx.ldS,
                    waits=[(ctx.vS, etk)])
        wflat_tok = ctx.ldS.count
        GC = GELU_C
        GK = GELU_K
        mm_hist = {}
        w2_hist = {}
        for q in range(4):
            qsl = slice(3 + q * QW, 3 + (q + 1) * QW)
            bwq = ctx.barrier_waits()
            xq = 0
            for c in range(DC):
                xq = ctx.dma("sync", xnq[:, c * QW:(c + 1) * QW],
                             xn1_scr[c * 128:(c + 1) * 128, qsl], ctx.lsS, waits=bwq)
            ctx.op("gpsimd", lambda g: g.memset(acc[:, :], 0.0), waits=bwq)
            acc0_tok = ctx.gS.count
            for e in range(NE):
                for o in range(32):
                    par = (o % 2) * DC * 128
                    w1t = ctx.dma("gpsimd", w1b[:, par:par + DC * 128],
                                  W["w1_d"][e * D:(e + 1) * D, o * 128:(o + 1) * 128]
                                  .rearrange("(c p) n -> p c n", p=128), ctx.ldS,
                                  waits=[(ctx.mm, mm_hist.get(o % 2, 0))])
                    banks = []
                    for t2 in range(2):
                        bi, p, tk = mm_group(ctx, [(w1b[:, par + c * 128: par + (c + 1) * 128],
                                                    xnq[:, c * QW + t2 * 512: c * QW + t2 * 512 + 512])
                                                   for c in range(DC)],
                                             waits=[(ctx.ldS, w1t), (ctx.lsS, xq)])
                        banks.append((bi, p, tk))
                    mm_hist[o % 2] = ctx.mm.count
                    bcol = W["b1c"][:, e * 32 + o: e * 32 + o + 1]
                    (b0, p0, tk0), (b1, p1, tk1) = banks
                    ctx.op("scalar", lambda sx, p0=p0, bcol=bcol: sx.activation(
                        out=xbf[:, 0:512], in_=p0[:, 0:512], func=AF.Identity, bias=bcol),
                        inc=ctx.sS, waits=[(ctx.mm, tk0), (ctx.lsS, 1), (ctx.vS, ctx.vS.count)])
                    etk = ctx.op("scalar", lambda sx, p1=p1, bcol=bcol: sx.activation(
                        out=xbf[:, 512:1024], in_=p1[:, 0:512], func=AF.Identity, bias=bcol),
                        inc=ctx.sS, waits=[(ctx.mm, tk1)])
                    ctx.pool.release(ctx, "scalar", b0, [(ctx.sS, etk)])
                    ctx.pool.release(ctx, "scalar", b1, [(ctx.sS, etk)])
                    etk = ctx.op("scalar", lambda sx: sx.activation(
                        out=sqf[:, :], in_=xbf[:, :], func=AF.Square), inc=ctx.sS)
                    ctx.op("vector", lambda v: v.tensor_scalar(inn[:, :], sqf[:, :], GC, 1.0,
                                                               ALU.mult, ALU.add),
                           inc=ctx.vS, waits=[(ctx.sS, etk)])
                    ctx.op("vector", lambda v: v.tensor_mul(inn[:, :], inn[:, :], xbf[:, :]),
                           inc=ctx.vS)
                    stk = ctx.op("scalar", lambda sx: sx.activation(
                        out=inn[:, :], in_=inn[:, :], func=AF.Sigmoid, scale=GK),
                        inc=ctx.sS, waits=[(ctx.vS, ctx.vS.count)])
                    gdst = gsb[:, o * QW: o * QW + QW]
                    ctx.op("vector", lambda v, gdst=gdst: v.tensor_mul(gdst, inn[:, :], xbf[:, :]),
                           inc=ctx.vS, waits=[(ctx.sS, stk)])
                g_tok = ctx.vS.count
                for t2 in range(2):
                    wr0 = q * QW + t2 * 512
                    wl = ctx.dma("sync", wfrow[0:1, :], wf_d[e:e + 1, wr0:wr0 + 512], ctx.lsS,
                                 waits=[(ctx.ldS, wflat_tok), (ctx.mm, ctx.mm.count)])
                    bi, p, tk = mm_group(ctx, [(onerow[0:1, :], wfrow[0:1, :])],
                                         waits=[(ctx.lsS, wl), (ctx.gS, ms_tok)])
                    etk = ctx.op("vector", lambda v, p=p, t2=t2: v.tensor_copy(
                        wbc[:, t2 * 512:(t2 + 1) * 512], p[:, 0:512]),
                        inc=ctx.vS, waits=[(ctx.mm, tk)])
                    ctx.pool.release(ctx, "vector", bi, [(ctx.vS, etk)])
                    for oc in range(DC):
                        par2 = (oc % 2) * 32 * 128
                        w2t = ctx.dma("gpsimd", w2b[:, par2:par2 + 32 * 128],
                                      W["w2_d"][e * HP: (e + 1) * HP, oc * 128:(oc + 1) * 128]
                                      .rearrange("(k p) n -> p k n", p=128), ctx.ldS,
                                      waits=[(ctx.mm, w2_hist.get(oc % 2, 0))])
                        bi, p, tk = mm_group(ctx, [(w2b[:, par2 + k * 128: par2 + (k + 1) * 128],
                                                    gsb[:, k * QW + t2 * 512: k * QW + t2 * 512 + 512])
                                                   for k in range(32)],
                                             waits=[(ctx.ldS, w2t), (ctx.vS, g_tok)])
                        w2_hist[oc % 2] = ctx.mm.count
                        b2col = W["b2c"][:, e * 8 + oc: e * 8 + oc + 1]
                        ctx.op("vector", lambda v, p=p, b2col=b2col, t2=t2: v.scalar_tensor_tensor(
                            yw[:, :], p[:, 0:512], b2col, wbc[:, t2 * 512:(t2 + 1) * 512],
                            ALU.add, ALU.mult),
                            inc=ctx.vS, waits=[(ctx.mm, tk), (ctx.lsS, 1), (ctx.gS, acc0_tok)])
                        adst = acc[:, oc * QW + t2 * 512: oc * QW + t2 * 512 + 512]
                        etk = ctx.op("vector", lambda v, adst=adst: v.tensor_add(adst, adst, yw[:, :]),
                                     inc=ctx.vS)
                        ctx.pool.release(ctx, "vector", bi, [(ctx.vS, etk)])
            for oc in range(DC):
                for t2 in range(2):
                    etk = ctx.op("scalar", lambda sx, oc=oc, t2=t2: sx.activation(
                        out=ev1[:, :], in_=acc[:, oc * QW + t2 * 512: oc * QW + t2 * 512 + 512],
                        func=AF.Silu),
                        inc=ctx.sS, waits=[(ctx.vS, ctx.vS.count), (ctx.stS, 1)])
                    ctx.dma("sync", g1_dst[oc * 128:(oc + 1) * 128,
                                           q * QW + t2 * 512: q * QW + t2 * 512 + 512],
                            ev1[:, :], ctx.stS, waits=[(ctx.sS, etk)])
    emit_barrier(ctx)


def emit_final2(ctx, W, tag, y3_scr, x1_scr, g1_scr, out_dst):
    NT = 512
    with ctx.scope():
        wo = ctx.sbuf(f"{tag}f_wo", [128, 16 * D], BF16)
        y3t = ctx.sbuf(f"{tag}f_y3", [128, 16 * NT], BF16)
        x1t = ctx.sbuf(f"{tag}f_x1", [128, NT])
        g1t = ctx.sbuf(f"{tag}f_g1", [128, NT])
        ev = ctx.sbuf(f"{tag}f_ev", [128, NT])
        wt = 0
        for k in range(16):
            wt = ctx.dma("gpsimd", wo[:, k * D:(k + 1) * D], W["wo_d"][k * 128:(k + 1) * 128, :],
                         ctx.ldS)
        for ti in range(TQ // NT):
            tsl = slice(ti * NT, (ti + 1) * NT)
            tw = ctx.barrier_waits()
            yw = 0
            for k in range(16):
                yw = ctx.dma("sync", y3t[:, k * NT:(k + 1) * NT], y3_scr[k * 128:(k + 1) * 128, tsl],
                             ctx.lsS, waits=tw)
            for o in range(DC):
                bi, p, tk = mm_group(ctx, [(wo[:, k * D + o * 128: k * D + (o + 1) * 128],
                                            y3t[:, k * NT:(k + 1) * NT]) for k in range(16)],
                                     waits=[(ctx.ldS, wt), (ctx.lsS, yw)])
                lx = ctx.dma("sync", x1t[:, :], x1_scr[o * 128:(o + 1) * 128, tsl], ctx.lsS,
                             waits=[(ctx.vS, ctx.vS.count)])
                lx = ctx.dma("sync", g1t[:, :], g1_scr[o * 128:(o + 1) * 128, tsl], ctx.lsS)
                ctx.op("vector", lambda v, p=p: v.tensor_mul(ev[:, :], p[:, 0:NT], g1t[:, :]),
                       inc=ctx.vS, waits=[(ctx.mm, tk), (ctx.lsS, lx), (ctx.stS, 1)])
                etk = ctx.op("vector", lambda v: v.tensor_add(ev[:, :], ev[:, :], x1t[:, :]),
                             inc=ctx.vS)
                ctx.pool.release(ctx, "vector", bi, [(ctx.vS, etk)])
                ctx.dma("sync", out_dst[o * 128:(o + 1) * 128, tsl], ev[:, :], ctx.stS,
                        waits=[(ctx.vS, etk)])
    ctx.wait("sync", ctx.stS, ctx.stS.count)
    ctx.wait("gpsimd", ctx.ldS, ctx.ldS.count)


def build_fused():
    ctx = Ctx()
    nc = ctx.nc
    TH = TQ + 3
    DI = 2048

    def din(name, shape, dt=F32):
        return nc.dram_tensor(name, shape, dt, kind="ExternalInput").ap()

    x_full = din("x_full", [D, T])
    x_own = din("x_own", [D, TH])
    x_dup = din("x_dup", [D, TH])
    sw1_d = din("sw1", [D, 512])
    sw2_d = din("sw2", [512, 128])
    bw_d = din("bw", [D, 128])
    dw_d = din("dw", [D, D])
    gw_d = din("gw", [D, D])
    cw_d = din("cw", [128, D])
    sb1_d = din("sb1", [128, 4])
    sb2_d = din("sb2", [128, 1])
    bb_d = din("bb", [128, 1])
    cdb_d = din("cdb", [128, 16])
    gb_d = din("gb", [128, 8])
    A_d = din("A", [128, 128])
    I_d = din("ident", [128, 128])
    wg_d = din("wgd", [D, 3072], I32)
    wu_d = din("wud", [D, 3072], I32)
    wd_d = din("wdd", [3072, D], I32)
    win_d = din("win", [D, 2 * DI])
    convc_d = din("convc", [128, 64])
    convb_d = din("convb", [128, 16])
    wx_d = din("wx", [DI, 96])
    wdt_d = din("wdt", [64, DI])
    bdt_d = din("bdt", [128, 16])
    alog_d = din("alog", [DI, 16])
    dv_d = din("dv", [128, 16])
    wo_d = din("wo", [DI, D])
    w1_d = din("w1", [8 * D, 4096])
    w2_d = din("w2", [8 * 4096, D])
    b1_d = din("b1", [128, 8 * 32])
    b2_d = din("b2", [128, 8 * 8])
    wr_d = din("wr", [D, 8])
    br_d = din("br", [1, 8])
    cfg_d = din("cfg", [1, 16], I32)
    mg_d = din("mg", [1, 4], I32)

    out_o = nc.dram_tensor("out", [D, TQ], F32, kind="ExternalOutput").ap()

    import os as _os
    _dbg_set = set((_os.environ.get("KF_DBG") or "").split(",")) - {""}
    _nocomm = bool(_os.environ.get("KF_NOCOMM"))

    def scr(name, shape, dt=F32):
        kind = "ExternalOutput" if name in _dbg_set else "Internal"
        return nc.dram_tensor(name, shape, dt, kind=kind).ap()

    u_s = scr("u_s", [128, T], BF16)
    hs_s = scr("hs_s", [128, T + 3], BF16)
    hs_t = hs_s.tensor
    ios = {}
    for tg in ("d", "o"):
        ios[tg] = {
            "xn": scr(f"{tg}_xn_s", [D, TH], BF16),
            "g0": scr(f"{tg}_g0_s", [D, TH]),
            "x1": scr(f"{tg}_x1_s", [D, TQ]),
            "xn1": scr(f"{tg}_xn1_s", [D, TH], BF16),
            "uc": scr(f"{tg}_uc_s", [DI, TQ]),
            "ucb": scr(f"{tg}_ucb_s", [DI, TQ], BF16),
            "dt": scr(f"{tg}_dt_s", [DI, TQ]),
            "bct": scr(f"{tg}_bct_s", [TQ, 32]),
        }
    ios["o"]["sz"] = scr("o_sz_s", [DI, TQ])
    ios["o"]["y3"] = scr("o_y3_s", [DI, TQ], BF16)
    g1_s = scr("g1_s", [D, TQ])
    if "dbgw" in _dbg_set:
        W_dbgw = nc.dram_tensor("dbgw", [128, 3072], BF16, kind="ExternalOutput").ap()
    else:
        W_dbgw = None
    shp = nc.dram_tensor("shp", [7 * 128, 256], F32, addr_space="Shared")
    flg = nc.dram_tensor("flgs", [1, 64], I32, addr_space="Shared")

    ctx.pool = PsumPool(ctx)
    W = {}
    gw_sb = {}
    cw = ctx.sbuf("g_cw", [128, D], BF16)
    wxb = ctx.sbuf("g_wxb", [128, 16 * 96], BF16)
    wdtb = ctx.sbuf("g_wdtb", [64, DI], BF16)
    wrb = ctx.sbuf("g_wrb", [128, DC * 8], BF16)
    ctx.dma("gpsimd", cw[:, :], cw_d[:, :], ctx.ldS)
    ctx.dma("gpsimd", wxb[:, :], wx_d.rearrange("(c p) n -> p c n", p=128), ctx.ldS)
    ctx.dma("gpsimd", wdtb[0:64, :], wdt_d[:, :], ctx.ldS)
    ctx.dma("gpsimd", wrb[:, :], wr_d.rearrange("(c p) e -> p c e", p=128), ctx.ldS)
    bias_tiles = {}
    for nm, src, w_ in (("sb1", sb1_d, 4), ("sb2", sb2_d, 1), ("bb", bb_d, 1), ("cdb", cdb_d, 16),
                        ("gb", gb_d, 8), ("convc", convc_d, 64), ("convb", convb_d, 16),
                        ("bdt", bdt_d, 16), ("idf", I_d, 128), ("b1c", b1_d, 256),
                        ("b2c", b2_d, 64), ("dvc", dv_d, 16)):
        t = ctx.sbuf(f"g_{nm}", [128, w_])
        ctx.dma("sync", t[:, :], src[:, :], ctx.lsS)
        bias_tiles[nm] = t
    brr = ctx.sbuf("g_brr", [1, 8])
    ctx.dma("sync", brr[0:1, :], br_d[0:1, :], ctx.lsS)
    cfg_sb = ctx.sbuf("g_cfg", [1, 16], I32)
    mg_sb = ctx.sbuf("g_mg", [1, 4], I32)
    ctx.dma("sync", cfg_sb[0:1, :], cfg_d[0:1, :], ctx.lsS)
    ctx.dma("sync", mg_sb[0:1, :], mg_d[0:1, :], ctx.lsS)
    asb = ctx.sbuf("g_asb", [128, 16 * 16])
    ctx.dma("gpsimd", asb[:, :], alog_d.rearrange("(g p) s -> p g s", p=128), ctx.ldS)
    al_tok = ctx.ldS.count
    ctx.op("scalar", lambda sx: sx.activation(out=asb[:, :], in_=asb[:, :], func=AF.Exp),
           inc=ctx.sS, waits=[(ctx.ldS, al_tok), (ctx.lsS, ctx.lsS.count)])
    ctx.op("vector", lambda v: v.tensor_scalar_mul(asb[:, :], asb[:, :], -1.0), inc=ctx.vS,
           waits=[(ctx.sS, ctx.sS.count)])
    ydb = ctx.sbuf("g_ydb", [128, 8])
    ctx.op("vector", lambda v: v.tensor_add(ydb[:, :], bias_tiles["cdb"][:, 0:8],
                                            bias_tiles["cdb"][:, 8:16]), inc=ctx.vS)
    zt = ctx.sbuf("g_zt", [128, 256])
    dupstate = ctx.sbuf("g_dst", [128, 256])
    ownstate = ctx.sbuf("g_ost", [128, 256])
    Hini = ctx.sbuf("g_hini", [128, 256])
    ctmp1 = ctx.sbuf("g_ct1", [128, 256])
    ctmp2 = ctx.sbuf("g_ct2", [128, 256])
    ctmp3 = ctx.sbuf("g_ct3", [128, 256])
    lam = ctx.sbuf("g_lam", [128, 256])
    dtsum = ctx.sbuf("g_dts", [128, 16])
    flgsb = ctx.sbuf("g_flgsb", [1, 1], I32)
    ctx.op("gpsimd", lambda g: g.memset(zt[:, :], 0.0), inc=ctx.gS)
    zt_tok = ctx.gS.count

    W["_dbgw"] = W_dbgw
    W.update({"sw1_d": sw1_d, "sw2_d": sw2_d, "bw_d": bw_d, "cw": cw, "wxb": wxb, "wdtb": wdtb,
              "wrb": wrb, "brr": brr, "ydb": ydb, "asb": asb,
              "wg_d": wg_d, "wu_d": wu_d, "wd_d": wd_d, "win_d": win_d,
              "dw_d": dw_d, "gw_d": gw_d, "wo_d": wo_d, "w1_d": w1_d, "w2_d": w2_d})
    for k, v in bias_tiles.items():
        W[k] = v
    emit_barrier(ctx)

    cfg_i = cfg_sb
    def pub_zeros(g):
        g.dma_start(out=bass.AP(shp, 6 * 128 * 256, [[256, 128], [1, 256]]),
                    in_=zt[:, :]).then_inc(ctx.ldS.h, 16)
    ctx.wait("gpsimd", ctx.gS, zt_tok)
    ctx.wait("gpsimd", ctx.lsS, ctx.lsS.count)
    if not _nocomm:
        ctx.ldS.count += 16
        _zdone = ctx.ldS.count
        ctx.q["gpsimd"].append(pub_zeros)
    else:
        _zdone = ctx.ldS.count

    def pub_hello(g, _zdone=_zdone):
        g.wait_ge(ctx.ldS.h, _zdone)
        with g.register("ph1") as r1, g.register("ph2") as r2:
            g.load(r1, cfg_i[0:1, 8:9])
            g.load(r2, cfg_i[0:1, 9:10])
            g.dma_start(out=bass.AP(flg, r1, [[1, 1], [1, 1]]), in_=mg_sb[0:1, 1:2]).then_inc(ctx.ldS.h, 16)
            g.dma_start(out=bass.AP(flg, r2, [[1, 1], [1, 1]]), in_=mg_sb[0:1, 2:3]).then_inc(ctx.ldS.h, 16)
    if not _nocomm:
        ctx.ldS.count += 32
        ctx.q["gpsimd"].append(pub_hello)

    hs_own = scr("hs_own", [128, TQ + 3], BF16)
    hs_dup = scr("hs_dup", [128, TQ + 3], BF16)
    emit_l0_upath(ctx, W, x_full, T, u_s)
    zpad = ctx.sbuf("g_zpad", [128, 3], BF16)
    ctx.op("gpsimd", lambda g: g.memset(zpad[:, :], 0.0), inc=ctx.gS)
    ctx.dma("gpsimd", hs_s[:, 0:3], zpad[:, :], ctx.ldS, waits=[(ctx.gS, ctx.gS.count)])
    emit_l0_scan(ctx, u_s, A_d, I_d, hs_s[:, 3:])

    def hs_windows(g):
        with g.register("whoff") as ro, g.register("whof2") as rd:
            g.load(ro, cfg_i[0:1, 1:2])
            g.load(rd, cfg_i[0:1, 0:1])
            g.dma_start(out=hs_own[:, :],
                        in_=bass.AP(hs_t, ro, [[T + 3, 128], [1, TQ + 3]])).then_inc(ctx.ldS.h, 16)
            g.dma_start(out=hs_dup[:, :],
                        in_=bass.AP(hs_t, rd, [[T + 3, 128], [1, TQ + 3]])).then_inc(ctx.ldS.h, 16)
    ctx.wait("gpsimd", ctx.stS, ctx.stS.count)
    ctx.wait("gpsimd", ctx.lsS, ctx.lsS.count)
    if _nocomm:
        ctx.dma("gpsimd", hs_own[:, :], hs_s[:, 0:TQ + 3], ctx.ldS)
        ctx.dma("gpsimd", hs_dup[:, :], hs_s[:, 0:TQ + 3], ctx.ldS)
    else:
        ctx.ldS.count += 32
        ctx.q["gpsimd"].append(hs_windows)
    emit_barrier(ctx)

    emit_quarter_pipeline(ctx, W, "d", x_dup, hs_dup,
                          ios["d"]["xn"], ios["d"]["g0"],
                          ios["d"]["x1"], ios["d"]["xn1"], ios["d"], with_z=False)
    with ctx.scope():
        dtrow = ctx.sbuf("s_dtrow", [128, TQ])
        for g_ in range(16):
            lw = ctx.dma("sync", dtrow[:, :], ios["d"]["dt"][g_ * 128:(g_ + 1) * 128, :], ctx.lsS,
                         waits=[(ctx.vS, ctx.vS.count)])
            ctx.op("vector", lambda v, g_=g_, dtrow=dtrow: v.tensor_reduce(
                dtsum[:, g_:g_ + 1], dtrow.ap().unsqueeze(1), AX.X, ALU.add),
                inc=ctx.vS, waits=[(ctx.lsS, lw)])
    emit_barrier(ctx)
    a3v = asb.ap().rearrange("p (g s) -> p g s", s=16)
    l3v = lam.ap().rearrange("p (g s) -> p g s", s=16)
    dbc = dtsum.ap().unsqueeze(2).broadcast_to([128, 16, 16])
    ctx.op("vector", lambda v: v.tensor_tensor(l3v, a3v, dbc, ALU.mult), inc=ctx.vS)
    lam_tok = ctx.vS.count
    emit_scan2(ctx, W, "ds", ios["d"], with_y=False, init_tile=None, state_out=dupstate)
    dstate_tok = ctx.vS.count

    ctx.wait("gpsimd", ctx.vS, max(lam_tok, dstate_tok))
    _base = ctx.ldS.count

    def pub_r1(g, _base=_base):
        with g.register("pr1a") as ra, g.register("pr1b") as rb, g.register("pr1c") as rc:
            g.load(ra, cfg_i[0:1, 2:3])
            g.reg_alu(ra, ra, 32768, ALU.mult)
            g.load(rb, cfg_i[0:1, 3:4])
            g.reg_alu(rb, rb, 32768, ALU.mult)
            g.load(rc, cfg_i[0:1, 10:11])
            g.dma_start(out=bass.AP(shp, ra, [[256, 128], [1, 256]]),
                        in_=dupstate[:, :]).then_inc(ctx.ldS.h, 16)
            g.dma_start(out=bass.AP(shp, rb, [[256, 128], [1, 256]]),
                        in_=lam[:, :]).then_inc(ctx.ldS.h, 16)
            g.wait_ge(ctx.ldS.h, _base + 32)
            g.dma_start(out=bass.AP(flg, rc, [[1, 1], [1, 1]]), in_=mg_sb[0:1, 1:2]).then_inc(ctx.ldS.h, 16)
    if not _nocomm:
        ctx.ldS.count += 48
        ctx.q["gpsimd"].append(pub_r1)

    emit_quarter_pipeline(ctx, W, "o", x_own, hs_own, ios["o"]["xn"], ios["o"]["g0"],
                          ios["o"]["x1"], ios["o"]["xn1"], ios["o"], with_z=True)
    emit_moe2(ctx, W, "m", ios["o"]["xn1"], g1_s)

    emit_barrier(ctx)

    psem1 = ctx.sem(dma=True)

    def wait_r1(g):
        with g.register("nd1") as nd, g.register("tgt1") as tgt, g.register("fv1") as fv, \
             g.register("ix1") as ix, g.register("mg1") as mgv:
            g.load(ix, cfg_i[0:1, 11:12])
            g.load(mgv, mg_sb[0:1, 1:2])
            g.reg_mov(nd, 1)
            g.reg_mov(tgt, 0)
            with g.While(nd):
                g.reg_add(tgt, tgt, 16)
                g.dma_start(out=flgsb[0:1, 0:1],
                            in_=bass.AP(flg, ix, [[1, 1], [1, 1]])).then_inc(psem1.h, 16)
                g.wait_ge(psem1.h, tgt)
                g.load(fv, flgsb[0:1, 0:1])
                g.reg_alu(nd, fv, mgv, ALU.not_equal)

    if not _nocomm:
        ctx.q["gpsimd"].append(wait_r1)
    ctx.op("gpsimd", lambda g: g.nop(), inc=ctx.gS)
    r1w_tok = ctx.gS.count

    def rd_r1(g):
        with g.register("rr1a") as ra, g.register("rr1b") as rb, g.register("rr1c") as rc:
            g.load(ra, cfg_i[0:1, 4:5])
            g.reg_alu(ra, ra, 32768, ALU.mult)
            g.load(rb, cfg_i[0:1, 5:6])
            g.reg_alu(rb, rb, 32768, ALU.mult)
            g.load(rc, cfg_i[0:1, 6:7])
            g.reg_alu(rc, rc, 32768, ALU.mult)
            g.dma_start(out=ctmp1[:, :], in_=bass.AP(shp, ra, [[256, 128], [1, 256]])).then_inc(ctx.ldS.h, 16)
            g.dma_start(out=ctmp2[:, :], in_=bass.AP(shp, rb, [[256, 128], [1, 256]])).then_inc(ctx.ldS.h, 16)
            g.dma_start(out=ctmp3[:, :], in_=bass.AP(shp, rc, [[256, 128], [1, 256]])).then_inc(ctx.ldS.h, 16)
    ctx.wait("gpsimd", ctx.gS, r1w_tok)
    if not _nocomm:
        ctx.ldS.count += 48
        ctx.q["gpsimd"].append(rd_r1)
    else:
        ctx.op("gpsimd", lambda g: g.memset(ctmp1[:, :], 0.0), inc=ctx.gS)
        ctx.op("gpsimd", lambda g: g.memset(ctmp2[:, :], 0.0), inc=ctx.gS)
        ctx.op("gpsimd", lambda g: g.memset(ctmp3[:, :], 0.0), inc=ctx.gS)
    r1r_tok = ctx.ldS.count
    ctx.op("scalar", lambda sx: sx.activation(out=ctmp1[:, :], in_=ctmp1[:, :], func=AF.Exp),
           inc=ctx.sS, waits=[(ctx.ldS, r1r_tok), (ctx.gS, r1w_tok)])
    ctx.op("vector", lambda v: v.tensor_mul(Hini[:, :], ctmp1[:, :], ctmp2[:, :]), inc=ctx.vS,
           waits=[(ctx.sS, ctx.sS.count)])
    ctx.op("vector", lambda v: v.tensor_add(Hini[:, :], Hini[:, :], ctmp3[:, :]), inc=ctx.vS)
    psem2 = ctx.sem(dma=True)

    def wait_r2(g):
        with g.register("nd2") as nd, g.register("tgt2") as tgt, g.register("fv2") as fv, \
             g.register("ix2") as ix, g.register("mg2") as mgv:
            g.load(ix, cfg_i[0:1, 12:13])
            g.load(mgv, mg_sb[0:1, 2:3])
            g.reg_mov(nd, 1)
            g.reg_mov(tgt, 0)
            with g.While(nd):
                g.reg_add(tgt, tgt, 16)
                g.dma_start(out=flgsb[0:1, 0:1],
                            in_=bass.AP(flg, ix, [[1, 1], [1, 1]])).then_inc(psem2.h, 16)
                g.wait_ge(psem2.h, tgt)
                g.load(fv, flgsb[0:1, 0:1])
                g.reg_alu(nd, fv, mgv, ALU.not_equal)

    if not _nocomm:
        ctx.q["gpsimd"].append(wait_r2)
    ctx.op("gpsimd", lambda g: g.nop(), inc=ctx.gS)
    r2w_tok = ctx.gS.count

    def rd_r2(g):
        with g.register("rr2") as ra:
            g.load(ra, cfg_i[0:1, 7:8])
            g.reg_alu(ra, ra, 32768, ALU.mult)
            g.dma_start(out=ctmp1[:, :], in_=bass.AP(shp, ra, [[256, 128], [1, 256]])).then_inc(ctx.ldS.h, 16)
    ctx.wait("gpsimd", ctx.gS, r2w_tok)
    ctx.wait("gpsimd", ctx.sS, ctx.sS.count)
    if not _nocomm:
        ctx.ldS.count += 16
        ctx.q["gpsimd"].append(rd_r2)
    r2r_tok = ctx.ldS.count
    hini_tok = ctx.op("vector", lambda v: v.tensor_add(Hini[:, :], Hini[:, :], ctmp1[:, :]),
                      inc=ctx.vS, waits=[(ctx.ldS, r2r_tok)])
    emit_barrier(ctx)

    emit_scan2(ctx, W, "os", ios["o"], with_y=True, init_tile=Hini, state_out=ownstate)
    ostate_tok = ctx.vS.count

    ctx.wait("gpsimd", ctx.vS, ostate_tok)
    _b2 = ctx.ldS.count
    if _nocomm:
        _b2 = None

    def pub_r2(g, _b2=_b2):
        with g.register("p2a") as ra, g.register("p2b") as rb:
            g.load(ra, cfg_i[0:1, 13:14])
            g.reg_alu(ra, ra, 32768, ALU.mult)
            g.load(rb, cfg_i[0:1, 14:15])
            g.dma_start(out=bass.AP(shp, ra, [[256, 128], [1, 256]]),
                        in_=ownstate[:, :]).then_inc(ctx.ldS.h, 16)
            g.wait_ge(ctx.ldS.h, _b2 + 16)
            g.dma_start(out=bass.AP(flg, rb, [[1, 1], [1, 1]]), in_=mg_sb[0:1, 2:3]).then_inc(ctx.ldS.h, 16)
    if not _nocomm:
        ctx.ldS.count += 32
        ctx.q["gpsimd"].append(pub_r2)

    emit_final2(ctx, W, "f", ios["o"]["y3"], ios["o"]["x1"], g1_s, out_o)
    return ctx


def host_fused(inputs, core, magic):
    b, lid = core // 4, core % 4
    p = lid % 2
    x = np.asarray(inputs["L0_attn_x"])[b]
    xf = np.ascontiguousarray(x.T).astype(np.float32)

    def xslice(q):
        s = q * TQ
        halo = np.zeros((D, 3), np.float32) if s == 0 else xf[:, s - 3:s]
        return np.ascontiguousarray(np.concatenate([halo, xf[:, s:s + TQ]], 1))

    dupq = lid % 2
    W1 = np.asarray(inputs["L1_ffn_W1"]).astype(np.float32)
    W2 = np.asarray(inputs["L1_ffn_W2"]).astype(np.float32)
    b1 = np.asarray(inputs["L1_ffn_b1"]).astype(np.float32)
    b2 = np.asarray(inputs["L1_ffn_b2"]).astype(np.float32)
    conv_w = np.asarray(inputs["L1_attn_conv_w"]).astype(np.float32)
    cfg = np.zeros(16, np.int32)
    cfg[0] = dupq * TQ
    cfg[1] = lid * TQ
    cfg[2] = p * 3 + 0
    cfg[3] = p * 3 + 1
    if lid == 2:
        cfg[4], cfg[5], cfg[6] = (1 - p) * 3 + 1, p * 3 + 0, (1 - p) * 3 + 0
    else:
        cfg[4], cfg[5], cfg[6] = 6, 6, 6
    cfg[7] = (1 - p) * 3 + 2 if lid in (1, 3) else 6
    cfg[8] = p * 8 + 3
    cfg[9] = p * 8 + 4
    cfg[10] = p * 8 + 1
    cfg[11] = (1 - p) * 8 + 1 if lid == 2 else p * 8 + 3
    cfg[12] = (1 - p) * 8 + 2 if lid in (1, 3) else p * 8 + 4
    cfg[13] = p * 3 + 2
    cfg[14] = p * 8 + 2
    return {
        "x_full": xf,
        "x_own": xslice(lid),
        "x_dup": xslice(dupq),
        "sw1": _fm(inputs["L0_attn_s_w1"]).astype(np.float32),
        "sw2": _fm(inputs["L0_attn_s_w2"]).astype(np.float32),
        "bw": _fm(inputs["L0_attn_B_w"]).astype(np.float32),
        "dw": _fm(inputs["L0_attn_D_w"]).astype(np.float32),
        "gw": _fm(inputs["L0_attn_g_w"]).astype(np.float32),
        "cw": np.ascontiguousarray(np.asarray(inputs["L0_attn_C_w"]).T).astype(np.float32),
        "sb1": _bcols(inputs["L0_attn_s_b1"], 4),
        "sb2": _bcols(inputs["L0_attn_s_b2"], 1),
        "bb": _bcols(inputs["L0_attn_B_b"], 1),
        "cdb": np.concatenate([_bcols(inputs["L0_attn_C_b"], 8), _bcols(inputs["L0_attn_D_b"], 8)], 1),
        "gb": _bcols(inputs["L0_attn_g_b"], 8),
        "A": np.ascontiguousarray(np.asarray(inputs["L0_attn_A"]).astype(np.float32)),
        "ident": np.eye(128, dtype=np.float32),
        "wgd": _fm(inputs["L0_ffn_w_gate"]).astype(np.int32),
        "wud": _fm(inputs["L0_ffn_w_up"]).astype(np.int32),
        "wdd": _fm(inputs["L0_ffn_w_down"]).astype(np.int32),
        "win": np.ascontiguousarray(np.asarray(inputs["L1_attn_W_in"]).astype(np.float32).T),
        "convc": np.ascontiguousarray(conv_w.reshape(16, 128, 4).transpose(1, 0, 2).reshape(128, 64)),
        "convb": _bcols(inputs["L1_attn_conv_b"], 16),
        "wx": np.ascontiguousarray(np.asarray(inputs["L1_attn_W_x"]).astype(np.float32).T),
        "wdt": np.ascontiguousarray(np.asarray(inputs["L1_attn_W_dt"]).astype(np.float32).T),
        "bdt": _bcols(inputs["L1_attn_b_dt"], 16),
        "alog": np.ascontiguousarray(np.asarray(inputs["L1_attn_A_log"]).astype(np.float32)),
        "dv": np.ascontiguousarray(np.asarray(inputs["L1_attn_D"]).astype(np.float32).reshape(16, 128).T),
        "wo": np.ascontiguousarray(np.asarray(inputs["L1_attn_W_out"]).astype(np.float32).T),
        "w1": np.ascontiguousarray(W1.reshape(8 * 1024, 4096)),
        "w2": np.ascontiguousarray(W2.reshape(8 * 4096, 1024)),
        "b1": np.ascontiguousarray(np.concatenate([_bcols(b1[e], 32) for e in range(8)], 1)),
        "b2": np.ascontiguousarray(np.concatenate([_bcols(b2[e], 8) for e in range(8)], 1)),
        "wr": np.ascontiguousarray(np.asarray(inputs["L1_ffn_Wr"]).astype(np.float32)),
        "br": np.ascontiguousarray(np.asarray(inputs["L1_ffn_br"]).astype(np.float32).reshape(1, 8)),
        "cfg": cfg.reshape(1, 16),
        "mg": np.asarray([magic, magic + 1, magic + 2, magic + 3], np.int32).reshape(1, 4),
    }


def kernel_fused(**inputs):
    import random
    magic = random.randint(10_000_000, 2_000_000_000)
    ctx = build_fused()
    maps = [host_fused(inputs, c, magic) for c in range(NCORES)]
    res = ctx.run(maps)
    out = np.empty((2, T, D), np.float32)
    for c in range(NCORES):
        b, q = c // 4, c % 4
        out[b, q * TQ:(q + 1) * TQ, :] = np.asarray(res[c]["out"]).T
    return out


# version 04
# speedup vs baseline: 1.1869x; 1.1869x over previous
import contextlib
import numpy as np
import ml_dtypes
import concourse.bass as bass
import concourse.mybir as mybir
from concourse.bass_utils import run_bass_kernel_spmd

F32 = mybir.dt.float32
BF16 = mybir.dt.bfloat16
I32 = mybir.dt.int32
AF = mybir.ActivationFunctionType
ALU = mybir.AluOpType
AX = mybir.AxisListType
BF = ml_dtypes.bfloat16

NCORES = 8
T = 16384
TQ = T // 4
D = 1024
DC = D // 128
ENGINES = ("tensor", "vector", "scalar", "gpsimd", "sync")

GELU_K = 1.5957691216057308
GELU_C = 0.044715
LN_EPS = 1e-5
DEQ = 1.0 / 127.0
C0 = 64


class Sem:
    __slots__ = ("h", "count", "dma")

    def __init__(self, h, dma=False):
        self.h = h
        self.count = 0
        self.dma = dma


class Ctx:
    def __init__(self):
        self.nc = bass.Bass("TRN2", target_bir_lowering=False)
        self.es = contextlib.ExitStack()
        self.q = {e: [] for e in ENGINES}
        self.hwm = {e: {} for e in ENGINES}
        self._semi = 0
        self.sems = []
        self.mm = self.sem()
        self.vS = self.sem()
        self.sS = self.sem()
        self.gS = self.sem()
        self.ldS = self.sem(dma=True)
        self.lsS = self.sem(dma=True)
        self.stS = self.sem(dma=True)
        self.pool = None
        self._scan_war = {}
        self._scope = None
        self.serial = False
        self.eng_sem = {"vector": self.vS, "scalar": self.sS, "gpsimd": self.gS}
        self.eng_last = {"vector": 0, "scalar": 0, "gpsimd": 0}
        import os as _os
        self.auto_serial = not _os.environ.get("KNOSER")

    def sem(self, dma=False):
        self._semi += 1
        s = Sem(self.es.enter_context(self.nc.semaphore(f"s{self._semi}")), dma)
        self.sems.append(s)
        return s

    def sbuf(self, name, shape, dt=F32):
        tgt = self._scope if self._scope is not None else self.es
        return tgt.enter_context(self.nc.sbuf_tensor(name, shape, dt))

    def scope(self):
        import contextlib as _cl

        @_cl.contextmanager
        def _s():
            old = self._scope
            self._scope = contextlib.ExitStack()
            try:
                yield
            finally:
                self._scope.close()
                self._scope = old

        return _s()

    def wait(self, eng, sem, val):
        if sem.dma:
            val = sem.count
        if val <= 0:
            return
        hw = self.hwm[eng]
        if hw.get(id(sem), 0) >= val:
            return
        hw[id(sem)] = val
        self.q[eng].append(lambda e, s=sem, v=val: e.wait_ge(s.h, v))

    def op(self, eng, fn, inc=None, waits=()):
        if self.serial:
            waits = list(waits) + self.barrier_waits()
        es = self.eng_sem.get(eng)
        if es is not None:
            if self.auto_serial:
                self.wait(eng, es, self.eng_last[eng])
            if inc is None:
                inc = es
        for (s, v) in waits:
            self.wait(eng, s, v)
        if inc is None:
            self.q[eng].append(fn)
            return None
        inc.count += 1

        def run(e, fn=fn, s=inc):
            fn(e).then_inc(s.h, 1)

        self.q[eng].append(run)
        if inc is es and es is not None:
            self.eng_last[eng] = inc.count
        return inc.count

    def dma(self, eng, out, in_, sem, waits=(), **kw):
        if self.serial:
            waits = list(waits) + self.barrier_waits()
        for (s, v) in waits:
            self.wait(eng, s, v)
        sem.count += 16

        def run(e, out=out, in_=in_, s=sem, kw=kw):
            e.dma_start(out=out, in_=in_, **kw).then_inc(s.h, 16)

        self.q[eng].append(run)
        return sem.count

    def barrier_waits(self):
        return [(s, s.count) for s in (self.mm, self.vS, self.sS, self.gS, self.ldS, self.lsS, self.stS)]

    def run(self, in_maps):
        nc = self.nc
        with nc.Block() as block:
            for eng in ENGINES:
                ops = self.q[eng]

                def section(obj, ops=ops):
                    for fn in ops:
                        fn(obj)

                getattr(block, eng)(section)
        res = run_bass_kernel_spmd(nc, in_maps, core_ids=list(range(len(in_maps))))
        self.es.close()
        return res.results


class PsumPool:
    def __init__(self, ctx, n=8, cols=512):
        self.ctx = ctx
        self.tiles = [ctx.es.enter_context(ctx.nc.psum_tensor(f"ps{i}", [128, cols], F32)) for i in range(n)]
        self.free = [ctx.sem() for _ in range(n)]
        self.uses = [0] * n
        self.rr = 0
        self.n = n

    def acquire(self):
        i = self.rr % self.n
        self.rr += 1
        need = self.uses[i]
        self.uses[i] += 1
        return i, self.tiles[i], (self.free[i], need)

    def release(self, ctx, eng, i, waits):
        ctx.op(eng, lambda e: e.nop(), inc=self.free[i], waits=waits)


def mm_group(ctx, pairs, out_slice=None, waits=(), psum=None):
    pool = ctx.pool
    i, tile, (fsem, fneed) = pool.acquire()
    ctx.wait("tensor", fsem, fneed)
    if ctx.serial:
        waits = list(waits) + ctx.barrier_waits()
    for (s, v) in waits:
        ctx.wait("tensor", s, v)
    n = len(pairs)
    ctx.mm.count += 1
    tok = ctx.mm.count

    def run(t, pairs=pairs, tile=tile, sem=ctx.mm, out_slice=out_slice, n=n):
        dst = tile[:, 0:pairs[0][1].shape[-1]] if out_slice is None else tile[out_slice[0], out_slice[1]]
        inst = None
        for j, (lh, rh) in enumerate(pairs):
            inst = t.matmul(dst, lh, rh, start=(j == 0), stop=(j == n - 1))
        inst.then_inc(sem.h, 1)

    ctx.q["tensor"].append(run)
    return i, tile, tok


def emit_ln(ctx, consts, xt, ncols, xn_write_fn, ld_waits, stride=None):
    onecol, onerow, epsc, rows, sq = (consts["onecol"], consts["onerow"], consts["epsc"],
                                      consts["rows"], consts["sq"])
    NT = ncols
    ST = stride if stride is not None else ncols
    bi_m, pm, tok_m = mm_group(
        ctx, [(onecol[:, 0:1], xt[:, c * ST: c * ST + NT]) for c in range(DC)],
        out_slice=(slice(0, 1), slice(0, NT)), waits=ld_waits)
    sqt = []
    for c in range(DC):
        sqt.append(ctx.op("scalar",
                          lambda s, c=c: s.activation(out=sq[:, c * ST: c * ST + NT],
                                                      in_=xt[:, c * ST: c * ST + NT], func=AF.Square),
                          inc=ctx.sS, waits=ld_waits))
    bi_q, pq, tok_q = mm_group(
        ctx, [(onecol[:, 0:1], sq[:, c * ST: c * ST + NT]) for c in range(DC)],
        out_slice=(slice(0, 1), slice(0, NT)), waits=[(ctx.sS, sqt[-1])])
    R_m = rows[0:1, 0 * NT:1 * NT]
    R_t = rows[0:1, 1 * NT:2 * NT]
    R_v = rows[0:1, 2 * NT:3 * NT]
    R_r = rows[0:1, 3 * NT:4 * NT]
    ctx.op("vector", lambda v: v.tensor_scalar_mul(R_m, pm[0:1, 0:NT], 1.0 / D), inc=ctx.vS,
           waits=[(ctx.mm, tok_m)])
    ctx.op("vector", lambda v: v.tensor_scalar_mul(R_v, pq[0:1, 0:NT], 1.0 / D), inc=ctx.vS,
           waits=[(ctx.mm, tok_q)])
    ctx.op("vector", lambda v: v.tensor_mul(R_t, R_m, R_m), inc=ctx.vS)
    ctx.op("vector", lambda v: v.tensor_sub(R_v, R_v, R_t), inc=ctx.vS)
    ctx.op("scalar", lambda s: s.activation(out=R_t, in_=R_v, func=AF.Sqrt, bias=epsc[0:1, 0:1]),
           inc=ctx.sS, waits=[(ctx.vS, ctx.vS.count)])
    ctx.op("vector", lambda v: v.reciprocal(R_r, R_t), inc=ctx.vS, waits=[(ctx.sS, ctx.sS.count)])
    row_tok = ctx.vS.count
    ctx.pool.release(ctx, "vector", bi_m, [(ctx.vS, row_tok)])
    ctx.pool.release(ctx, "vector", bi_q, [(ctx.vS, row_tok)])
    bi_M, pM, tok_M = mm_group(ctx, [(onerow[0:1, :], R_m)], waits=[(ctx.vS, row_tok)])
    bi_R, pR, tok_R = mm_group(ctx, [(onerow[0:1, :], R_r)], waits=[(ctx.vS, row_tok)])
    for c in range(DC):
        dst = xn_write_fn(c)
        ctx.op("vector", lambda v, c=c: v.tensor_sub(xt[:, c * ST: c * ST + NT],
                                                     xt[:, c * ST: c * ST + NT], pM[:, 0:NT]),
               inc=ctx.vS, waits=[(ctx.mm, tok_M), (ctx.sS, sqt[-1])])
        ctx.op("vector", lambda v, c=c, dst=dst: v.tensor_mul(dst, xt[:, c * ST: c * ST + NT],
                                                              pR[:, 0:NT]),
               inc=ctx.vS, waits=[(ctx.mm, tok_R)])
    xn_tok = ctx.vS.count
    ctx.pool.release(ctx, "vector", bi_M, [(ctx.vS, xn_tok)])
    ctx.pool.release(ctx, "vector", bi_R, [(ctx.vS, xn_tok)])
    return xn_tok


def ln_consts(ctx, NT):
    consts = {
        "onecol": ctx.sbuf("onecol", [128, 1]),
        "onerow": ctx.sbuf("onerow", [1, 128]),
        "epsc": ctx.sbuf("epsc", [1, 1]),
        "rows": ctx.sbuf("rows", [1, 4 * NT]),
        "sq": ctx.sbuf("sq", [128, DC * NT]),
    }
    ctx.op("gpsimd", lambda g: g.memset(consts["onecol"][:, :], 1.0), inc=ctx.gS)
    ctx.op("gpsimd", lambda g: g.memset(consts["onerow"][:, :], 1.0), inc=ctx.gS)
    ctx.op("gpsimd", lambda g: g.memset(consts["epsc"][:, :], LN_EPS), inc=ctx.gS)
    return consts, ctx.gS.count


def _fm(x):
    return np.ascontiguousarray(np.asarray(x).T)


def _bcols(b, nchunks):
    return np.ascontiguousarray(np.asarray(b).reshape(nchunks, 128).T).astype(np.float32)


def build_l0main():
    ctx = Ctx()
    import os as _os
    ctx.serial = bool(_os.environ.get("KSERIAL"))
    _parts = _os.environ.get("KPARTS", "lhusgy")
    _ntile_env = int(_os.environ.get("KNT", "0"))
    nc = ctx.nc
    NT = 512
    ntiles = _ntile_env or (TQ // NT)

    x_d = nc.dram_tensor("x", [D, TQ], F32, kind="ExternalInput").ap()
    sw1_d = nc.dram_tensor("sw1", [D, 512], F32, kind="ExternalInput").ap()
    sw2_d = nc.dram_tensor("sw2", [512, 128], F32, kind="ExternalInput").ap()
    bw_d = nc.dram_tensor("bw", [D, 128], F32, kind="ExternalInput").ap()
    dw_d = nc.dram_tensor("dw", [D, D], F32, kind="ExternalInput").ap()
    gw_d = nc.dram_tensor("gw", [D, D], F32, kind="ExternalInput").ap()
    sb1_d = nc.dram_tensor("sb1", [128, 4], F32, kind="ExternalInput").ap()
    sb2_d = nc.dram_tensor("sb2", [128, 1], F32, kind="ExternalInput").ap()
    bb_d = nc.dram_tensor("bb", [128, 1], F32, kind="ExternalInput").ap()
    cdb_d = nc.dram_tensor("cdb", [128, 16], F32, kind="ExternalInput").ap()
    gb_d = nc.dram_tensor("gb", [128, 8], F32, kind="ExternalInput").ap()

    xn_o = nc.dram_tensor("xn", [D, TQ], BF16, kind="ExternalOutput").ap()
    u_o = nc.dram_tensor("u", [128, TQ], BF16, kind="ExternalOutput").ap()
    gate_o = nc.dram_tensor("gate", [D, TQ], F32, kind="ExternalOutput").ap()
    y0d_o = nc.dram_tensor("y0d", [D, TQ], F32, kind="ExternalOutput").ap()

    xn = ctx.sbuf("xns", [128, DC * TQ], BF16)
    xt = ctx.sbuf("xt", [128, DC * NT])
    sw1 = ctx.sbuf("sw1s", [128, 4 * 512], BF16)
    sw2 = ctx.sbuf("sw2s", [128, 4 * 128], BF16)
    bw = ctx.sbuf("bws", [128, DC * 128], BF16)
    dw = ctx.sbuf("dws", [128, DC * D], BF16)
    gw = ctx.sbuf("gws", [128, DC * D], BF16)
    sw1f = ctx.sbuf("sw1f", [128, DC * 512], BF16)
    sb1 = ctx.sbuf("sb1s", [128, 4])
    sb2 = ctx.sbuf("sb2s", [128, 1])
    bb = ctx.sbuf("bbs", [128, 1])
    cdb = ctx.sbuf("cdbs", [128, 16])
    ydb = ctx.sbuf("ydbs", [128, 8])
    gb = ctx.sbuf("gbs", [128, 8])
    h1 = ctx.sbuf("h1", [128, 4 * NT], BF16)
    sg = ctx.sbuf("sg", [128, NT])
    ev1 = ctx.sbuf("ev1", [128, NT])
    ev2 = ctx.sbuf("ev2", [128, NT])

    ctx.pool = PsumPool(ctx)
    consts, memset_tok = ln_consts(ctx, NT)

    wt = {}
    for c in range(DC):
        wt[("sw1", c)] = ctx.dma("gpsimd", sw1f[:, c * 512:(c + 1) * 512], sw1_d[c * 128:(c + 1) * 128, :], ctx.ldS)
        wt[("bw", c)] = ctx.dma("gpsimd", bw[:, c * 128:(c + 1) * 128], bw_d[c * 128:(c + 1) * 128, :], ctx.ldS)
        wt[("dw", c)] = ctx.dma("gpsimd", dw[:, c * D:(c + 1) * D], dw_d[c * 128:(c + 1) * 128, :], ctx.ldS)
        wt[("gw", c)] = ctx.dma("gpsimd", gw[:, c * D:(c + 1) * D], gw_d[c * 128:(c + 1) * 128, :], ctx.ldS)
    for c in range(4):
        wt[("sw2", c)] = ctx.dma("gpsimd", sw2[:, c * 128:(c + 1) * 128], sw2_d[c * 128:(c + 1) * 128, :], ctx.ldS)
    wld = ctx.ldS.count
    btoks = []
    for dst, src in ((sb1, sb1_d), (sb2, sb2_d), (bb, bb_d), (cdb, cdb_d), (gb, gb_d)):
        btoks.append(ctx.dma("sync", dst[:, :], src[:, :], ctx.lsS))
    bld = ctx.lsS.count
    ctx.op("vector", lambda v: v.tensor_add(ydb[:, :], cdb[:, 0:8], cdb[:, 8:16]),
           inc=ctx.vS, waits=[(ctx.lsS, bld)])
    ydb_tok = ctx.vS.count

    for ti in range(ntiles):
        tsl = slice(ti * NT, (ti + 1) * NT)
        bw_waits = ctx.barrier_waits()
        xw = 0
        for c in range(DC):
            xw = ctx.dma("sync", xt[:, c * NT:(c + 1) * NT], x_d[c * 128:(c + 1) * 128, tsl],
                         ctx.lsS, waits=bw_waits)
        ld_waits = [(ctx.lsS, xw), (ctx.gS, memset_tok)]
        xn_tok = emit_ln(ctx, consts, xt, NT,
                         lambda c: xn[:, c * TQ + ti * NT: c * TQ + (ti + 1) * NT], ld_waits)
        for c in range(DC):
            ctx.dma("sync", xn_o[c * 128:(c + 1) * 128, tsl],
                    xn[:, c * TQ + ti * NT: c * TQ + (ti + 1) * NT], ctx.stS, waits=[(ctx.vS, xn_tok)])

        def xnsl(c):
            return xn[:, c * TQ + ti * NT: c * TQ + (ti + 1) * NT]

        for o in range(4 if "h" in _parts else 0):
            bi, p, tk = mm_group(ctx, [(sw1f[:, c * 512 + o * 128: c * 512 + (o + 1) * 128], xnsl(c))
                                       for c in range(DC)],
                                 waits=[(ctx.ldS, wld), (ctx.vS, xn_tok)])
            etk = ctx.op("scalar", lambda s, o=o, p=p: s.activation(
                out=h1[:, o * NT:(o + 1) * NT], in_=p[:, 0:NT], func=AF.Silu, bias=sb1[:, o:o + 1]),
                inc=ctx.sS, waits=[(ctx.mm, tk), (ctx.lsS, bld)])
            ctx.pool.release(ctx, "scalar", bi, [(ctx.sS, etk)])
        h1_tok = ctx.sS.count
        if "s" not in _parts:
            continue
        bi, p, tk = mm_group(ctx, [(sw2[:, c * 128:(c + 1) * 128], h1[:, c * NT:(c + 1) * NT])
                                   for c in range(4)], waits=[(ctx.sS, h1_tok)])
        etk = ctx.op("scalar", lambda s, p=p: s.activation(out=sg[:, :], in_=p[:, 0:NT],
                                                           func=AF.Sigmoid, bias=sb2[:, 0:1]),
                     inc=ctx.sS, waits=[(ctx.mm, tk)])
        ctx.pool.release(ctx, "scalar", bi, [(ctx.sS, etk)])
        sg_tok = etk
        if "u" not in _parts:
            continue
        bi, p, tk = mm_group(ctx, [(bw[:, c * 128:(c + 1) * 128], xnsl(c)) for c in range(DC)],
                             waits=[(ctx.vS, xn_tok)])
        etk = ctx.op("scalar", lambda s, p=p: s.activation(out=ev1[:, :], in_=p[:, 0:NT],
                                                           func=AF.Identity, bias=bb[:, 0:1]),
                     inc=ctx.sS, waits=[(ctx.mm, tk)])
        ctx.pool.release(ctx, "scalar", bi, [(ctx.sS, etk)])
        utk = ctx.op("vector", lambda v: v.tensor_mul(ev2[:, :], ev1[:, :], sg[:, :]),
                     inc=ctx.vS, waits=[(ctx.sS, max(etk, sg_tok))])
        ctx.dma("gpsimd", u_o[:, tsl], ev2[:, :], ctx.ldS, waits=[(ctx.vS, utk)])
        _sel = [(gw, gb, gate_o, AF.Sigmoid)] if "g" in _parts else []
        _sel += [(dw, ydb, y0d_o, AF.Identity)] if "y" in _parts else []
        for (wsb, bias_sb, out_d, func) in _sel:
            for o in range(DC):
                bi, p, tk = mm_group(ctx, [(wsb[:, c * D + o * 128: c * D + (o + 1) * 128], xnsl(c))
                                           for c in range(DC)], waits=[(ctx.vS, xn_tok)])
                etk = ctx.op("scalar", lambda s, p=p, o=o, b=bias_sb, f=func: s.activation(
                    out=ev1[:, :], in_=p[:, 0:NT], func=f, bias=b[:, o:o + 1]),
                    inc=ctx.sS, waits=[(ctx.mm, tk), (ctx.vS, ctx.vS.count), (ctx.stS, ctx.stS.count),
                                       (ctx.ldS, ctx.ldS.count)])
                ctx.pool.release(ctx, "scalar", bi, [(ctx.sS, etk)])
                ctx.dma("sync", out_d[o * 128:(o + 1) * 128, tsl], ev1[:, :], ctx.stS,
                        waits=[(ctx.sS, etk)])
    ctx.wait("sync", ctx.stS, ctx.stS.count)
    ctx.wait("gpsimd", ctx.ldS, ctx.ldS.count)
    return ctx


def host_l0main(inputs, core):
    b, q = core // 4, core % 4
    x = np.asarray(inputs["L0_attn_x"])[b, q * TQ:(q + 1) * TQ, :]
    return {
        "x": _fm(x).astype(np.float32),
        "sw1": _fm(inputs["L0_attn_s_w1"]).astype(np.float32),
        "sw2": _fm(inputs["L0_attn_s_w2"]).astype(np.float32),
        "bw": _fm(inputs["L0_attn_B_w"]).astype(np.float32),
        "dw": _fm(inputs["L0_attn_D_w"]).astype(np.float32),
        "gw": _fm(inputs["L0_attn_g_w"]).astype(np.float32),
        "sb1": _bcols(inputs["L0_attn_s_b1"], 4),
        "sb2": _bcols(inputs["L0_attn_s_b2"], 1),
        "bb": _bcols(inputs["L0_attn_B_b"], 1),
        "cdb": np.concatenate([_bcols(inputs["L0_attn_C_b"], 8), _bcols(inputs["L0_attn_D_b"], 8)], 1),
        "gb": _bcols(inputs["L0_attn_g_b"], 8),
    }


def build_l0ffn():
    ctx = Ctx()
    nc = ctx.nc
    NT = 256
    ntiles = TQ // NT
    H = 3072
    HC = H // 128

    xn_d = nc.dram_tensor("xn", [D, TQ], BF16, kind="ExternalInput").ap()
    wg_d = nc.dram_tensor("wg", [D, H], I32, kind="ExternalInput").ap()
    wu_d = nc.dram_tensor("wu", [D, H], I32, kind="ExternalInput").ap()
    wd_d = nc.dram_tensor("wd", [H, D], I32, kind="ExternalInput").ap()
    g0_o = nc.dram_tensor("g0", [D, TQ], F32, kind="ExternalOutput").ap()

    wg = ctx.sbuf("wgs", [128, DC * H], BF16)
    wu = ctx.sbuf("wus", [128, DC * H], BF16)
    wd = ctx.sbuf("wds", [128, HC * D], BF16)
    xnt = ctx.sbuf("xnt", [128, DC * NT], BF16)
    gsp = ctx.sbuf("gsp", [128, HC * NT])
    hbf = ctx.sbuf("hbf", [128, HC * NT], BF16)
    ev1 = ctx.sbuf("ev1", [128, NT])

    ctx.pool = PsumPool(ctx)
    for c in range(DC):
        ctx.dma("gpsimd", wg[:, c * H:(c + 1) * H], wg_d[c * 128:(c + 1) * 128, :], ctx.ldS)
        ctx.dma("gpsimd", wu[:, c * H:(c + 1) * H], wu_d[c * 128:(c + 1) * 128, :], ctx.ldS)
    for c in range(HC):
        ctx.dma("gpsimd", wd[:, c * D:(c + 1) * D], wd_d[c * 128:(c + 1) * 128, :], ctx.ldS)
    wld = ctx.ldS.count

    for ti in range(ntiles):
        tsl = slice(ti * NT, (ti + 1) * NT)
        bw_waits = ctx.barrier_waits()
        xw = 0
        for c in range(DC):
            xw = ctx.dma("sync", xnt[:, c * NT:(c + 1) * NT], xn_d[c * 128:(c + 1) * 128, tsl],
                         ctx.lsS, waits=bw_waits)

        def xnsl(c):
            return xnt[:, c * NT:(c + 1) * NT]

        for o in range(HC):
            bi, p, tk = mm_group(ctx, [(wg[:, c * H + o * 128: c * H + (o + 1) * 128], xnsl(c))
                                       for c in range(DC)],
                                 waits=[(ctx.ldS, wld), (ctx.lsS, xw)])
            etk = ctx.op("scalar", lambda s, o=o, p=p: s.activation(
                out=gsp[:, o * NT:(o + 1) * NT], in_=p[:, 0:NT], func=AF.Silu, scale=DEQ),
                inc=ctx.sS, waits=[(ctx.mm, tk)])
            ctx.pool.release(ctx, "scalar", bi, [(ctx.sS, etk)])
        for o in range(HC):
            bi, p, tk = mm_group(ctx, [(wu[:, c * H + o * 128: c * H + (o + 1) * 128], xnsl(c))
                                       for c in range(DC)], waits=[(ctx.lsS, xw)])
            etk = ctx.op("vector", lambda v, o=o, p=p: v.scalar_tensor_tensor(
                hbf[:, o * NT:(o + 1) * NT], p[:, 0:NT], DEQ, gsp[:, o * NT:(o + 1) * NT],
                ALU.mult, ALU.mult),
                inc=ctx.vS, waits=[(ctx.mm, tk), (ctx.sS, ctx.sS.count)])
            ctx.pool.release(ctx, "vector", bi, [(ctx.vS, etk)])
        h_tok = ctx.vS.count
        for o in range(DC):
            bi, p, tk = mm_group(ctx, [(wd[:, c * D + o * 128: c * D + (o + 1) * 128],
                                        hbf[:, c * NT:(c + 1) * NT]) for c in range(HC)],
                                 waits=[(ctx.vS, h_tok)])
            etk = ctx.op("scalar", lambda s, p=p: s.activation(
                out=ev1[:, :], in_=p[:, 0:NT], func=AF.Silu, scale=DEQ),
                inc=ctx.sS, waits=[(ctx.mm, tk), (ctx.stS, ctx.stS.count)])
            ctx.pool.release(ctx, "scalar", bi, [(ctx.sS, etk)])
            ctx.dma("sync", g0_o[o * 128:(o + 1) * 128, tsl], ev1[:, :], ctx.stS,
                    waits=[(ctx.sS, etk)])
    ctx.wait("sync", ctx.stS, ctx.stS.count)
    return ctx


def host_l0ffn(inputs, core, st):
    return {
        "xn": st["xn"][core],
        "wg": _fm(inputs["L0_ffn_w_gate"]).astype(np.int32),
        "wu": _fm(inputs["L0_ffn_w_up"]).astype(np.int32),
        "wd": _fm(inputs["L0_ffn_w_down"]).astype(np.int32),
    }


def build_l0scan():
    ctx = Ctx()
    nc = ctx.nc
    c0 = C0
    TE = T + c0
    u_d = nc.dram_tensor("u", [128, T], BF16, kind="ExternalInput").ap()
    A_d = nc.dram_tensor("A", [128, 128], F32, kind="ExternalInput").ap()
    I_d = nc.dram_tensor("ident", [128, 128], F32, kind="ExternalInput").ap()
    hs_o = nc.dram_tensor("hs", [128, T], BF16, kind="ExternalOutput").ap()

    ue = ctx.sbuf("ue", [128, TE], BF16)
    P = ctx.sbuf("P", [128, (c0 + 1) * 128], BF16)
    Fb = ctx.sbuf("F", [128, TE])
    h = ctx.sbuf("h", [128, TE], BF16)
    A_sb = ctx.sbuf("A_sb", [128, 128], BF16)

    ctx.pool = PsumPool(ctx)
    ctx.op("gpsimd", lambda g: g.memset(ue[:, 0:c0], 0.0), inc=ctx.gS)
    ms_tok = ctx.gS.count
    ctx.dma("gpsimd", ue[:, c0:], u_d[:, :], ctx.ldS)
    ctx.dma("gpsimd", A_sb[:, :], A_d[:, :], ctx.ldS)
    ctx.dma("gpsimd", P[:, 0:128], I_d[:, :], ctx.ldS)
    ld_tok = ctx.ldS.count

    for k in range(1, c0 + 1):
        bi, p, tk = mm_group(ctx, [(A_sb[:, :], P[:, (k - 1) * 128: k * 128])],
                             waits=[(ctx.ldS, ld_tok), (ctx.vS, ctx.vS.count)])
        etk = ctx.op("vector", lambda v, k=k, p=p: v.tensor_copy(P[:, k * 128:(k + 1) * 128],
                                                                 p[:, 0:128]),
                     inc=ctx.vS, waits=[(ctx.mm, tk)])
        ctx.pool.release(ctx, "vector", bi, [(ctx.vS, etk)])
    pow_tok = ctx.vS.count

    tiles = [(i * 512, 512) for i in range(TE // 512)]
    if TE % 512:
        tiles.append((TE - TE % 512, TE % 512))
    for (ts, n) in tiles:
        i, tile, (fsem, fneed) = ctx.pool.acquire()
        ctx.wait("tensor", fsem, fneed)
        ctx.wait("tensor", ctx.vS, pow_tok)
        ctx.wait("tensor", ctx.gS, ms_tok)
        ctx.mm.count += 1
        tok = ctx.mm.count

        def run(t, ts=ts, n=n, tile=tile, sem=ctx.mm):
            inst = None
            first = True
            for k in range(c0):
                if ts - k + n <= 0:
                    continue
                lo = max(0, k - ts)
                src0 = ts - k + lo
                inst = t.matmul(tile[:, lo:n], P[:, k * 128:(k + 1) * 128],
                                ue[:, src0: src0 + (n - lo)], start=first, stop=(k == c0 - 1))
                first = False
            inst.then_inc(sem.h, 1)

        ctx.q["tensor"].append(run)
        etk = ctx.op("vector", lambda v, ts=ts, n=n, tile=tile: v.tensor_copy(
            Fb[:, ts:ts + n], tile[:, 0:n]), inc=ctx.vS, waits=[(ctx.mm, tok)])
        ctx.pool.release(ctx, "vector", i, [(ctx.vS, etk)])
    f_tok = ctx.vS.count

    ctx.op("vector", lambda v: v.tensor_copy(h[:, 0:c0], Fb[:, 0:c0]), inc=ctx.vS,
           waits=[(ctx.vS, f_tok)])
    for r in range(1, TE // c0):
        bi, p, tk = mm_group(ctx, [(P[:, c0 * 128:(c0 + 1) * 128], h[:, (r - 1) * c0: r * c0])],
                             waits=[(ctx.vS, ctx.vS.count)])
        etk = ctx.op("vector", lambda v, r=r, p=p: v.tensor_add(
            h[:, r * c0:(r + 1) * c0], Fb[:, r * c0:(r + 1) * c0], p[:, 0:c0]),
            inc=ctx.vS, waits=[(ctx.mm, tk)])
        ctx.pool.release(ctx, "vector", bi, [(ctx.vS, etk)])
    ctx.dma("sync", hs_o[:, :], h[:, c0:], ctx.stS, waits=[(ctx.vS, ctx.vS.count)])
    ctx.wait("sync", ctx.stS, ctx.stS.count)
    return ctx


def build_combine1():
    ctx = Ctx()
    nc = ctx.nc
    NT = 512
    ntiles = TQ // NT

    x_d = nc.dram_tensor("x", [D, TQ], F32, kind="ExternalInput").ap()
    hs_d = nc.dram_tensor("hs", [128, TQ], BF16, kind="ExternalInput").ap()
    y0d_d = nc.dram_tensor("y0d", [D, TQ], F32, kind="ExternalInput").ap()
    gate_d = nc.dram_tensor("gate", [D, TQ], F32, kind="ExternalInput").ap()
    g0_d = nc.dram_tensor("g0", [D, TQ], F32, kind="ExternalInput").ap()
    cw_d = nc.dram_tensor("cw", [128, D], F32, kind="ExternalInput").ap()
    x1_o = nc.dram_tensor("x1", [D, TQ], F32, kind="ExternalOutput").ap()
    xn1_o = nc.dram_tensor("xn1", [D, TQ], BF16, kind="ExternalOutput").ap()

    cw = ctx.sbuf("cws", [128, D], BF16)
    hs = ctx.sbuf("hss", [128, TQ], BF16)
    xt = ctx.sbuf("xt", [128, DC * NT])
    aux = ctx.sbuf("aux", [128, 3 * DC * NT])
    xnt = ctx.sbuf("xnt", [128, DC * NT], BF16)

    ctx.pool = PsumPool(ctx)
    consts, memset_tok = ln_consts(ctx, NT)
    ctx.dma("gpsimd", cw[:, :], cw_d[:, :], ctx.ldS)
    ctx.dma("sync", hs[:, :], hs_d[:, :], ctx.lsS)
    wld, hld = ctx.ldS.count, ctx.lsS.count

    for ti in range(ntiles):
        tsl = slice(ti * NT, (ti + 1) * NT)
        bw_waits = ctx.barrier_waits()
        xw = 0
        for c in range(DC):
            xw = ctx.dma("sync", xt[:, c * NT:(c + 1) * NT], x_d[c * 128:(c + 1) * 128, tsl],
                         ctx.lsS, waits=bw_waits)
            xw = ctx.dma("sync", aux[:, (0 * DC + c) * NT:(0 * DC + c + 1) * NT],
                         y0d_d[c * 128:(c + 1) * 128, tsl], ctx.lsS)
            xw = ctx.dma("sync", aux[:, (1 * DC + c) * NT:(1 * DC + c + 1) * NT],
                         gate_d[c * 128:(c + 1) * 128, tsl], ctx.lsS)
            xw = ctx.dma("sync", aux[:, (2 * DC + c) * NT:(2 * DC + c + 1) * NT],
                         g0_d[c * 128:(c + 1) * 128, tsl], ctx.lsS)
        for o in range(DC):
            bi, p, tk = mm_group(ctx, [(cw[:, o * 128:(o + 1) * 128], hs[:, tsl])],
                                 waits=[(ctx.ldS, wld), (ctx.lsS, max(xw, hld))])
            y0sl = aux[:, (0 * DC + o) * NT:(0 * DC + o + 1) * NT]
            gsl = aux[:, (1 * DC + o) * NT:(1 * DC + o + 1) * NT]
            g0sl = aux[:, (2 * DC + o) * NT:(2 * DC + o + 1) * NT]
            xsl = xt[:, o * NT:(o + 1) * NT]
            ctx.op("vector", lambda v, p=p, y0sl=y0sl: v.tensor_add(y0sl, y0sl, p[:, 0:NT]),
                   inc=ctx.vS, waits=[(ctx.mm, tk)])
            ctx.pool.release(ctx, "vector", bi, [(ctx.vS, ctx.vS.count)])
            ctx.op("vector", lambda v, y0sl=y0sl, gsl=gsl: v.tensor_mul(y0sl, y0sl, gsl), inc=ctx.vS)
            ctx.op("vector", lambda v, y0sl=y0sl, g0sl=g0sl: v.tensor_mul(y0sl, y0sl, g0sl), inc=ctx.vS)
            ctx.op("vector", lambda v, xsl=xsl, y0sl=y0sl: v.tensor_add(xsl, xsl, y0sl), inc=ctx.vS)
        x1_tok = ctx.vS.count
        for c in range(DC):
            ctx.dma("sync", x1_o[c * 128:(c + 1) * 128, tsl], xt[:, c * NT:(c + 1) * NT],
                    ctx.stS, waits=[(ctx.vS, x1_tok)])
        xn_tok = emit_ln(ctx, consts, xt, NT, lambda c: xnt[:, c * NT:(c + 1) * NT],
                         [(ctx.vS, x1_tok), (ctx.stS, ctx.stS.count), (ctx.gS, memset_tok)])
        for c in range(DC):
            ctx.dma("sync", xn1_o[c * 128:(c + 1) * 128, tsl], xnt[:, c * NT:(c + 1) * NT],
                    ctx.stS, waits=[(ctx.vS, xn_tok)])
    ctx.wait("sync", ctx.stS, ctx.stS.count)
    return ctx


_CTX_CACHE = {}


import os
def _run(name, builder, in_maps):
    ctx = builder()
    return ctx.run(in_maps)


def kernel(**inputs):
    import os as _os
    if not _os.environ.get("KUNFUSED"):
        return kernel_fused(**inputs)
    st = {}
    maps = [host_l0main(inputs, c) for c in range(NCORES)]
    res = _run("l0main", build_l0main, maps)
    st["xn"] = [np.asarray(r["xn"]) for r in res]
    st["u"] = [np.asarray(r["u"]) for r in res]
    st["gate"] = [np.asarray(r["gate"]) for r in res]
    st["y0d"] = [np.asarray(r["y0d"]) for r in res]
    maps = [host_l0ffn(inputs, c, st) for c in range(NCORES)]
    res = _run("l0ffn", build_l0ffn, maps)
    st["g0"] = [np.asarray(r["g0"]) for r in res]
    A = np.asarray(inputs["L0_attn_A"]).astype(np.float32)
    ident = np.eye(128, dtype=np.float32)
    maps = []
    for c in range(NCORES):
        b = c // 4
        ufull = np.concatenate([st["u"][4 * b + q] for q in range(4)], axis=1)
        maps.append({"u": ufull, "A": A, "ident": ident})
    res = _run("l0scan", build_l0scan, maps)
    st["hs"] = [np.asarray(res[4 * b]["hs"]) for b in range(2)]
    maps = []
    for c in range(NCORES):
        b, q = c // 4, c % 4
        maps.append({
            "x": _fm(np.asarray(inputs["L0_attn_x"])[b, q * TQ:(q + 1) * TQ, :]).astype(np.float32),
            "hs": np.ascontiguousarray(st["hs"][b][:, q * TQ:(q + 1) * TQ]),
            "y0d": st["y0d"][c], "gate": st["gate"][c], "g0": st["g0"][c],
            "cw": np.ascontiguousarray(np.asarray(inputs["L0_attn_C_w"]).T).astype(np.float32),
        })
    res = _run("combine1", build_combine1, maps)
    st["x1"] = [np.asarray(r["x1"]) for r in res]
    st["xn1"] = [np.asarray(r["xn1"]) for r in res]

    maps = [host_moe(inputs, c, st) for c in range(NCORES)]
    res = _run("moe", build_moe, maps)
    st["g1"] = [np.asarray(r["g1"]) for r in res]
    maps = [host_mamba_a(inputs, c, st) for c in range(NCORES)]
    res = _run("mamba_a", build_mamba_a, maps)
    for k in ("uc", "ucb", "dt", "sz", "bct"):
        st[k] = [np.asarray(r[k]) for r in res]
    maps = [host_mamba_scan(inputs, c, st) for c in range(NCORES)]
    res = _run("mamba_scan", build_mamba_scan, maps)
    st["y3s"] = [np.asarray(r["y3"]) for r in res]
    maps = [host_final(inputs, c, st) for c in range(NCORES)]
    res = _run("final", build_final, maps)
    out = np.empty((2, T, D), np.float32)
    for c in range(NCORES):
        b, q = c // 4, c % 4
        out[b, q * TQ:(q + 1) * TQ, :] = np.asarray(res[c]["out"]).T
    return out


def build_moe():
    ctx = Ctx()
    nc = ctx.nc
    NE, HP = 8, 4096
    QW = 1024
    xn_d = nc.dram_tensor("xn1", [D, TQ], BF16, kind="ExternalInput").ap()
    w1_d = nc.dram_tensor("w1", [NE * D, HP], F32, kind="ExternalInput").ap()
    w2_d = nc.dram_tensor("w2", [NE * HP, D], F32, kind="ExternalInput").ap()
    b1_d = nc.dram_tensor("b1", [128, NE * 32], F32, kind="ExternalInput").ap()
    b2_d = nc.dram_tensor("b2", [128, NE * 8], F32, kind="ExternalInput").ap()
    wr_d = nc.dram_tensor("wr", [D, NE], F32, kind="ExternalInput").ap()
    br_d = nc.dram_tensor("br", [1, NE], F32, kind="ExternalInput").ap()
    id_d = nc.dram_tensor("ident", [128, 128], F32, kind="ExternalInput").ap()
    g1_o = nc.dram_tensor("g1", [D, TQ], F32, kind="ExternalOutput").ap()
    import os as _os
    _dbg = bool(_os.environ.get("KMOE_DBG"))
    if _dbg:
        wf_d = nc.dram_tensor("wfd", [NE, TQ], F32, kind="ExternalOutput").ap()
        sc_o = nc.dram_tensor("sc_o", [128, 32 * NE], F32, kind="ExternalOutput").ap()
    else:
        wf_d = nc.dram_tensor("wfd", [NE, TQ], F32).ap()

    xnt = ctx.sbuf("xnt", [128, DC * 512], BF16)
    xnq = ctx.sbuf("xnq", [128, DC * QW], BF16)
    wrb = ctx.sbuf("wrbs", [128, DC * NE], BF16)
    brr = ctx.sbuf("brrs", [1, NE])
    brbc = ctx.sbuf("brbc", [128, NE])
    idf = ctx.sbuf("idfs", [128, 128])
    sc = ctx.sbuf("scs", [128, 32 * NE])
    m1 = ctx.sbuf("m1", [128, 32])
    m2 = ctx.sbuf("m2", [128, 32])
    lt = ctx.sbuf("lt", [128, 32 * NE])
    msk = ctx.sbuf("msk", [128, 32 * NE])
    dd = ctx.sbuf("dd", [128, 32])
    w1v = ctx.sbuf("w1v", [128, 32])
    w2v = ctx.sbuf("w2v", [128, 32])
    eq = ctx.sbuf("eq", [128, 32 * NE])
    wsb = ctx.sbuf("wsb", [128, 32 * NE])
    wrows = ctx.sbuf("wrows", [8, 128])
    wfrow = ctx.sbuf("wfrow", [1, 512])
    onerow = ctx.sbuf("onerow", [1, 128])
    b1c = ctx.sbuf("b1c", [128, NE * 32])
    b2c = ctx.sbuf("b2c", [128, NE * 8])
    w1b = ctx.sbuf("w1b", [128, DC * 128], BF16)
    w2b = ctx.sbuf("w2b", [128, 32 * 128], BF16)
    gsb = ctx.sbuf("gsb", [128, 32 * QW], BF16)
    acc = ctx.sbuf("acc", [128, DC * QW])
    wbc = ctx.sbuf("wbc", [128, 2 * 512])
    xbf = ctx.sbuf("xbf", [128, 512])
    sqf = ctx.sbuf("sqf", [128, 512])
    inn = ctx.sbuf("inn", [128, 512])
    ev1 = ctx.sbuf("ev1", [128, 512])
    yw = ctx.sbuf("yw", [128, 512])

    ctx.pool = PsumPool(ctx)
    ctx.op("gpsimd", lambda g: g.memset(onerow[:, :], 1.0))
    ms_tok = ctx.gS.count
    ctx.dma("gpsimd", wrb[:, :], wr_d.rearrange("(c p) e -> p c e", p=128), ctx.ldS)
    ctx.dma("sync", brr[0:1, :], br_d[0:1, :], ctx.lsS)
    ctx.dma("sync", idf[:, :], id_d[:, :], ctx.lsS)
    ctx.dma("sync", b1c[:, :], b1_d[:, :], ctx.lsS)
    ctx.dma("sync", b2c[:, :], b2_d[:, :], ctx.lsS)
    cst_tok = ctx.lsS.count

    bi, p, tk = mm_group(ctx, [(onerow[0:1, :], brr[0:1, :])],
                         waits=[(ctx.lsS, cst_tok), (ctx.gS, ms_tok)])
    etk = ctx.op("vector", lambda v, p=p: v.tensor_copy(brbc[:, :], p[:, 0:NE]),
                 inc=ctx.vS, waits=[(ctx.mm, tk)])
    ctx.pool.release(ctx, "vector", bi, [(ctx.vS, etk)])

    wr_ld = ctx.ldS.count
    for tt in range(8):
        xw = 0
        for c in range(DC):
            xw = ctx.dma("sync", xnt[:, c * 512:(c + 1) * 512],
                         xn_d[c * 128:(c + 1) * 128, tt * 512:(tt + 1) * 512], ctx.lsS,
                         waits=[(ctx.mm, ctx.mm.count)])
        for j in range(4):
            tc = tt * 4 + j
            bi, p, tk = mm_group(ctx, [(xnt[:, c * 512 + j * 128: c * 512 + (j + 1) * 128],
                                        wrb[:, c * NE:(c + 1) * NE]) for c in range(DC)],
                                 waits=[(ctx.ldS, wr_ld), (ctx.lsS, xw)])
            etk = ctx.op("vector", lambda v, tc=tc, p=p: v.tensor_add(
                sc[:, tc * NE:(tc + 1) * NE], p[:, 0:NE], brbc[:, :]),
                inc=ctx.vS, waits=[(ctx.mm, tk)])
            ctx.pool.release(ctx, "vector", bi, [(ctx.vS, etk)])
    sc3 = sc.ap().rearrange("p (t e) -> p t e", e=NE)
    ctx.op("vector", lambda v: v.tensor_reduce(m1[:, :], sc3, AX.X, ALU.max), inc=ctx.vS)
    m1bc = m1.ap().unsqueeze(2).broadcast_to([128, 32, NE])
    lt3 = lt.ap().rearrange("p (t e) -> p t e", e=NE)
    ctx.op("vector", lambda v: v.tensor_tensor(lt3, sc3, m1bc, ALU.is_ge), inc=ctx.vS)
    ctx.op("vector", lambda v: v.scalar_tensor_tensor(msk[:, :], lt[:, :], -1.0e4, sc[:, :],
                                                      ALU.mult, ALU.add), inc=ctx.vS)
    msk3 = msk.ap().rearrange("p (t e) -> p t e", e=NE)
    ctx.op("vector", lambda v: v.tensor_reduce(m2[:, :], msk3, AX.X, ALU.max), inc=ctx.vS)
    ctx.op("vector", lambda v: v.tensor_sub(dd[:, :], m2[:, :], m1[:, :]), inc=ctx.vS)
    ctx.op("scalar", lambda sx: sx.activation(out=dd[:, :], in_=dd[:, :], func=AF.Exp),
           inc=ctx.sS, waits=[(ctx.vS, ctx.vS.count)])
    ctx.op("vector", lambda v: v.tensor_scalar_add(w1v[:, :], dd[:, :], 1.0), inc=ctx.vS,
           waits=[(ctx.sS, ctx.sS.count)])
    ctx.op("vector", lambda v: v.reciprocal(w1v[:, :], w1v[:, :]), inc=ctx.vS)
    ctx.op("vector", lambda v: v.tensor_mul(w2v[:, :], dd[:, :], w1v[:, :]), inc=ctx.vS)
    eq3 = eq.ap().rearrange("p (t e) -> p t e", e=NE)
    wsb3 = wsb.ap().rearrange("p (t e) -> p t e", e=NE)
    w1bc = w1v.ap().unsqueeze(2).broadcast_to([128, 32, NE])
    w2bc = w2v.ap().unsqueeze(2).broadcast_to([128, 32, NE])
    m2bc = m2.ap().unsqueeze(2).broadcast_to([128, 32, NE])
    ctx.op("vector", lambda v: v.tensor_tensor(eq3, sc3, m1bc, ALU.is_equal), inc=ctx.vS)
    ctx.op("vector", lambda v: v.tensor_tensor(wsb3, eq3, w1bc, ALU.mult), inc=ctx.vS)
    ctx.op("vector", lambda v: v.tensor_tensor(eq3, sc3, m2bc, ALU.is_equal), inc=ctx.vS)
    ctx.op("vector", lambda v: v.tensor_tensor(eq3, eq3, w2bc, ALU.mult), inc=ctx.vS)
    ctx.op("vector", lambda v: v.tensor_add(wsb[:, :], wsb[:, :], eq[:, :]), inc=ctx.vS)
    wsb_tok = ctx.vS.count
    for j in range(32):
        i, tile, (fsem, fneed) = ctx.pool.acquire()
        ctx.wait("tensor", fsem, fneed)
        ctx.wait("tensor", ctx.vS, wsb_tok)
        ctx.wait("tensor", ctx.lsS, cst_tok)
        ctx.mm.count += 1
        tk = ctx.mm.count

        def runt(t, j=j, tile=tile, sem=ctx.mm):
            t.transpose(tile[0:NE, 0:128], wsb[:, j * NE:(j + 1) * NE], idf[:, :]).then_inc(sem.h, 1)

        ctx.q["tensor"].append(runt)
        etk = ctx.op("vector", lambda v, tile=tile: v.tensor_copy(
            wrows[0:NE, :], tile[0:NE, 0:128]),
            inc=ctx.vS, waits=[(ctx.mm, tk), (ctx.ldS, ctx.ldS.count)])
        ctx.pool.release(ctx, "vector", i, [(ctx.vS, etk)])
        ctx.dma("gpsimd", wf_d[0:NE, j * 128:(j + 1) * 128], wrows[0:NE, :], ctx.ldS,
                waits=[(ctx.vS, etk)])
    wflat_tok = ctx.ldS.count
    if _dbg:
        ctx.dma("sync", sc_o[:, :], wsb[:, :], ctx.stS, waits=[(ctx.vS, wsb_tok)])

    GC = GELU_C
    GK = GELU_K
    for q in range(4):
        qsl = slice(q * QW, (q + 1) * QW)
        bwq = ctx.barrier_waits()
        xq = 0
        for c in range(DC):
            xq = ctx.dma("sync", xnq[:, c * QW:(c + 1) * QW],
                         xn_d[c * 128:(c + 1) * 128, qsl], ctx.lsS, waits=bwq)
        ctx.op("gpsimd", lambda g: g.memset(acc[:, :], 0.0), waits=bwq)
        acc0_tok = ctx.gS.count
        for e in range(NE):
            for o in range(32):
                w1t = ctx.dma("gpsimd", w1b[:, :],
                              w1_d[e * D:(e + 1) * D, o * 128:(o + 1) * 128]
                              .rearrange("(c p) n -> p c n", p=128), ctx.ldS,
                              waits=[(ctx.mm, ctx.mm.count)])
                for t2 in range(2):
                    bi, p, tk = mm_group(ctx, [(w1b[:, c * 128:(c + 1) * 128],
                                                xnq[:, c * QW + t2 * 512: c * QW + t2 * 512 + 512])
                                               for c in range(DC)],
                                         waits=[(ctx.ldS, w1t), (ctx.lsS, xq)])
                    bcol = b1c[:, e * 32 + o: e * 32 + o + 1]
                    ctx.op("scalar", lambda sx, p=p, bcol=bcol: sx.activation(
                        out=xbf[:, :], in_=p[:, 0:512], func=AF.Identity, bias=bcol),
                        inc=ctx.sS, waits=[(ctx.mm, tk), (ctx.lsS, cst_tok), (ctx.vS, ctx.vS.count)])
                    etk = ctx.op("scalar", lambda sx: sx.activation(
                        out=sqf[:, :], in_=xbf[:, :], func=AF.Square), inc=ctx.sS)
                    ctx.pool.release(ctx, "scalar", bi, [(ctx.sS, etk)])
                    ctx.op("vector", lambda v: v.tensor_scalar(inn[:, :], sqf[:, :], GC, 1.0,
                                                               ALU.mult, ALU.add),
                           inc=ctx.vS, waits=[(ctx.sS, etk)])
                    ctx.op("vector", lambda v: v.tensor_mul(inn[:, :], inn[:, :], xbf[:, :]),
                           inc=ctx.vS)
                    stk = ctx.op("scalar", lambda sx: sx.activation(
                        out=inn[:, :], in_=inn[:, :], func=AF.Sigmoid, scale=GK),
                        inc=ctx.sS, waits=[(ctx.vS, ctx.vS.count)])
                    gdst = gsb[:, o * QW + t2 * 512: o * QW + t2 * 512 + 512]
                    ctx.op("vector", lambda v, gdst=gdst: v.tensor_mul(gdst, inn[:, :], xbf[:, :]),
                           inc=ctx.vS, waits=[(ctx.sS, stk)])
            g_tok = ctx.vS.count
            for t2 in range(2):
                wr0 = q * QW + t2 * 512
                wl = ctx.dma("sync", wfrow[0:1, :], wf_d[e:e + 1, wr0:wr0 + 512], ctx.lsS,
                             waits=[(ctx.ldS, wflat_tok), (ctx.mm, ctx.mm.count)])
                bi, p, tk = mm_group(ctx, [(onerow[0:1, :], wfrow[0:1, :])],
                                     waits=[(ctx.lsS, wl), (ctx.gS, ms_tok)])
                etk = ctx.op("vector", lambda v, p=p, t2=t2: v.tensor_copy(
                    wbc[:, t2 * 512:(t2 + 1) * 512], p[:, 0:512]),
                    inc=ctx.vS, waits=[(ctx.mm, tk)])
                ctx.pool.release(ctx, "vector", bi, [(ctx.vS, etk)])
                for oc in range(DC):
                    w2t = ctx.dma("gpsimd", w2b[:, :],
                                  w2_d[e * HP: (e + 1) * HP, oc * 128:(oc + 1) * 128]
                                  .rearrange("(k p) n -> p k n", p=128), ctx.ldS,
                                  waits=[(ctx.mm, ctx.mm.count)])
                    bi, p, tk = mm_group(ctx, [(w2b[:, k * 128:(k + 1) * 128],
                                                gsb[:, k * QW + t2 * 512: k * QW + t2 * 512 + 512])
                                               for k in range(32)],
                                         waits=[(ctx.ldS, w2t), (ctx.vS, g_tok)])
                    b2col = b2c[:, e * 8 + oc: e * 8 + oc + 1]
                    ctx.op("vector", lambda v, p=p, b2col=b2col, t2=t2: v.scalar_tensor_tensor(
                        yw[:, :], p[:, 0:512], b2col, wbc[:, t2 * 512:(t2 + 1) * 512],
                        ALU.add, ALU.mult),
                        inc=ctx.vS, waits=[(ctx.mm, tk), (ctx.lsS, cst_tok), (ctx.gS, acc0_tok)])
                    adst = acc[:, oc * QW + t2 * 512: oc * QW + t2 * 512 + 512]
                    etk = ctx.op("vector", lambda v, adst=adst: v.tensor_add(adst, adst, yw[:, :]),
                                 inc=ctx.vS)
                    ctx.pool.release(ctx, "vector", bi, [(ctx.vS, etk)])
        for oc in range(DC):
            etk = ctx.op("scalar", lambda sx, oc=oc: sx.activation(
                out=ev1[:, 0:512], in_=acc[:, oc * QW: oc * QW + 512], func=AF.Silu),
                inc=ctx.sS, waits=[(ctx.vS, ctx.vS.count), (ctx.stS, ctx.stS.count)])
            ctx.dma("sync", g1_o[oc * 128:(oc + 1) * 128, q * QW: q * QW + 512], ev1[:, 0:512],
                    ctx.stS, waits=[(ctx.sS, etk)])
            etk = ctx.op("scalar", lambda sx, oc=oc: sx.activation(
                out=ev1[:, 0:512], in_=acc[:, oc * QW + 512: (oc + 1) * QW], func=AF.Silu),
                inc=ctx.sS, waits=[(ctx.stS, ctx.stS.count)])
            ctx.dma("sync", g1_o[oc * 128:(oc + 1) * 128, q * QW + 512: (q + 1) * QW], ev1[:, 0:512],
                    ctx.stS, waits=[(ctx.sS, etk)])
    ctx.wait("sync", ctx.stS, ctx.stS.count)
    return ctx


def host_moe(inputs, core, st):
    W1 = np.asarray(inputs["L1_ffn_W1"]).astype(np.float32)
    W2 = np.asarray(inputs["L1_ffn_W2"]).astype(np.float32)
    b1 = np.asarray(inputs["L1_ffn_b1"]).astype(np.float32)
    b2 = np.asarray(inputs["L1_ffn_b2"]).astype(np.float32)
    return {
        "xn1": st["xn1"][core],
        "w1": np.ascontiguousarray(W1.reshape(8 * 1024, 4096)),
        "w2": np.ascontiguousarray(W2.reshape(8 * 4096, 1024)),
        "b1": np.ascontiguousarray(np.concatenate([_bcols(b1[e], 32) for e in range(8)], 1)),
        "b2": np.ascontiguousarray(np.concatenate([_bcols(b2[e], 8) for e in range(8)], 1)),
        "wr": np.ascontiguousarray(np.asarray(inputs["L1_ffn_Wr"]).astype(np.float32)),
        "br": np.ascontiguousarray(np.asarray(inputs["L1_ffn_br"]).astype(np.float32).reshape(1, 8)),
        "ident": np.eye(128, dtype=np.float32),
    }


def build_mamba_a():
    ctx = Ctx()
    import os as _os
    _mp = _os.environ.get("KMA_PARTS", "uzxdb")
    nc = ctx.nc
    TH = TQ + 3
    DI = 2048

    xnh_d = nc.dram_tensor("xnh", [D, TH], BF16, kind="ExternalInput").ap()
    win_d = nc.dram_tensor("win", [D, 2 * DI], F32, kind="ExternalInput").ap()
    convc_d = nc.dram_tensor("convc", [128, 16 * 4], F32, kind="ExternalInput").ap()
    convb_d = nc.dram_tensor("convb", [128, 16], F32, kind="ExternalInput").ap()
    wx_d = nc.dram_tensor("wx", [DI, 96], F32, kind="ExternalInput").ap()
    wdt_d = nc.dram_tensor("wdt", [64, DI], F32, kind="ExternalInput").ap()
    bdt_d = nc.dram_tensor("bdt", [128, 16], F32, kind="ExternalInput").ap()
    id_d = nc.dram_tensor("ident", [128, 128], F32, kind="ExternalInput").ap()
    uc_o = nc.dram_tensor("uc", [DI, TQ], F32, kind="ExternalOutput").ap()
    ucb_o = nc.dram_tensor("ucb", [DI, TQ], BF16, kind="ExternalOutput").ap()
    dt_o = nc.dram_tensor("dt", [DI, TQ], F32, kind="ExternalOutput").ap()
    sz_o = nc.dram_tensor("sz", [DI, TQ], F32, kind="ExternalOutput").ap()
    bct_o = nc.dram_tensor("bct", [TQ, 32], F32, kind="ExternalOutput").ap()
    _madbg = bool(_os.environ.get("KMA_DBG"))
    if _madbg:
        xd_dbg = nc.dram_tensor("xd_dbg", [128, TQ], F32, kind="ExternalOutput").ap()
        xdb_dbg = nc.dram_tensor("xdb_dbg", [128, TQ], BF16, kind="ExternalOutput").ap()

    xnh = ctx.sbuf("xnhs", [128, DC * TH], BF16)
    wb = ctx.sbuf("wbs", [128, DC * 128], BF16)
    convc = ctx.sbuf("convcs", [128, 64])
    convb = ctx.sbuf("convbs", [128, 16])
    uo = ctx.sbuf("uos", [128, TH])
    ca = ctx.sbuf("ca", [128, TQ])
    cb = ctx.sbuf("cb", [128, TQ])
    ucf = ctx.sbuf("ucf", [128, TQ])
    ucc = ctx.sbuf("ucc", [128, TQ], BF16)
    wxb = ctx.sbuf("wxb", [128, 16 * 96], BF16)
    wdtb = ctx.sbuf("wdtb", [64, DI], BF16)
    bdt = ctx.sbuf("bdts", [128, 16])
    idf = ctx.sbuf("idfs", [128, 128])
    onec = ctx.sbuf("onec", [128, 1])
    xdbl = ctx.sbuf("xdbls", [128, TQ])
    xdblb = ctx.sbuf("xdblb", [128, TQ], BF16)
    eva = ctx.sbuf("eva", [128, 1024])
    evb = ctx.sbuf("evb", [128, 1024])
    evc = ctx.sbuf("evc", [128, 1024])
    bcsb = ctx.sbuf("bcsb", [128, 32 * 32])

    ctx.pool = PsumPool(ctx)
    ctx.op("gpsimd", lambda g: g.memset(onec[:, :], 1.0))
    ms_tok = ctx.gS.count
    xw = 0
    for c in range(DC):
        xw = ctx.dma("sync", xnh[:, c * TH:(c + 1) * TH], xnh_d[c * 128:(c + 1) * 128, :], ctx.lsS)
    ctx.dma("sync", convc[:, :], convc_d[:, :], ctx.lsS)
    ctx.dma("sync", convb[:, :], convb_d[:, :], ctx.lsS)
    ctx.dma("sync", bdt[:, :], bdt_d[:, :], ctx.lsS)
    ctx.dma("sync", idf[:, :], id_d[:, :], ctx.lsS)
    cst_tok = ctx.lsS.count
    ctx.dma("gpsimd", wxb[:, :], wx_d.rearrange("(c p) n -> p c n", p=128), ctx.ldS)
    ctx.dma("gpsimd", wdtb[0:64, :], wdt_d[:, :], ctx.ldS)
    wx_tok = ctx.ldS.count

    utiles = [(t * 512, 512) for t in range(8)] + [(4096, 3)]

    for o in range(16 if "u" in _mp else 0):
        wt = ctx.dma("gpsimd", wb[:, :],
                     win_d[:, o * 128:(o + 1) * 128].rearrange("(c p) n -> p c n", p=128),
                     ctx.ldS, waits=[(ctx.mm, ctx.mm.count)])
        for (ts, n) in utiles:
            bi, p, tk = mm_group(ctx, [(wb[:, c * 128:(c + 1) * 128],
                                        xnh[:, c * TH + ts: c * TH + ts + n]) for c in range(DC)],
                                 waits=[(ctx.ldS, wt), (ctx.lsS, xw)])
            etk = ctx.op("vector", lambda v, ts=ts, n=n, p=p: v.tensor_copy(
                uo[:, ts:ts + n], p[:, 0:n]), inc=ctx.vS, waits=[(ctx.mm, tk)])
            ctx.pool.release(ctx, "vector", bi, [(ctx.vS, etk)])
        ctx.op("vector", lambda v, o=o: v.tensor_scalar(
            ca[:, :], uo[:, 0:TQ], convc[:, o * 4: o * 4 + 1], None, ALU.mult),
            inc=ctx.vS, waits=[(ctx.lsS, cst_tok), (ctx.sS, ctx.sS.count)])
        ctx.op("vector", lambda v, o=o: v.scalar_tensor_tensor(
            cb[:, :], uo[:, 1:TQ + 1], convc[:, o * 4 + 1: o * 4 + 2], ca[:, :], ALU.mult, ALU.add),
            inc=ctx.vS)
        ctx.op("vector", lambda v, o=o: v.scalar_tensor_tensor(
            ca[:, :], uo[:, 2:TQ + 2], convc[:, o * 4 + 2: o * 4 + 3], cb[:, :], ALU.mult, ALU.add),
            inc=ctx.vS)
        ctx.op("vector", lambda v, o=o: v.scalar_tensor_tensor(
            cb[:, :], uo[:, 3:TQ + 3], convc[:, o * 4 + 3: o * 4 + 4], ca[:, :], ALU.mult, ALU.add),
            inc=ctx.vS)
        stk = ctx.op("scalar", lambda sx, o=o: sx.activation(
            out=ucf[:, :], in_=cb[:, :], func=AF.Silu, bias=convb[:, o:o + 1]),
            inc=ctx.sS, waits=[(ctx.vS, ctx.vS.count), (ctx.stS, ctx.stS.count),
                               (ctx.ldS, ctx.ldS.count)])
        ctx.dma("sync", uc_o[o * 128:(o + 1) * 128, :], ucf[:, :], ctx.stS, waits=[(ctx.sS, stk)])
        ctx.dma("gpsimd", ucb_o[o * 128:(o + 1) * 128, :], ucf[:, :], ctx.ldS, waits=[(ctx.sS, stk)])
    for o in range(16 if "z" in _mp else 0):
        wt = ctx.dma("gpsimd", wb[:, :],
                     win_d[:, DI + o * 128: DI + (o + 1) * 128].rearrange("(c p) n -> p c n", p=128),
                     ctx.ldS, waits=[(ctx.mm, ctx.mm.count)])
        for t in range(8):
            ts = 3 + t * 512
            bi, p, tk = mm_group(ctx, [(wb[:, c * 128:(c + 1) * 128],
                                        xnh[:, c * TH + ts: c * TH + ts + 512]) for c in range(DC)],
                                 waits=[(ctx.ldS, wt)])
            etk = ctx.op("scalar", lambda sx, p=p: sx.activation(
                out=eva[:, 0:512], in_=p[:, 0:512], func=AF.Silu),
                inc=ctx.sS, waits=[(ctx.mm, tk), (ctx.stS, ctx.stS.count)])
            ctx.pool.release(ctx, "scalar", bi, [(ctx.sS, etk)])
            ctx.dma("sync", sz_o[o * 128:(o + 1) * 128, t * 512:(t + 1) * 512], eva[:, 0:512],
                    ctx.stS, waits=[(ctx.sS, etk)])
    if "x" not in _mp:
        ctx.wait("sync", ctx.stS, ctx.stS.count)
        ctx.wait("gpsimd", ctx.ldS, ctx.ldS.count)
        return ctx
    ctx.wait("tensor", ctx.ldS, ctx.ldS.count)
    for i in range(8):
        ctx.wait("tensor", ctx.pool.free[i], ctx.pool.uses[i])
        ctx.pool.uses[i] += 1
    xd_tok = []
    for c in range(16):
        lt = ctx.dma("sync", ucc[:, :], ucb_o[c * 128:(c + 1) * 128, :], ctx.lsS,
                     waits=[(ctx.mm, ctx.mm.count)] + ([(ctx.ldS, ctx.ldS.count)] if c == 0 else []))
        for t in range(8):
            ctx.wait("tensor", ctx.lsS, lt)
            ctx.mm.count += 1

            def runx(tt, c=c, t=t, sem=ctx.mm):
                tt.matmul(ctx.pool.tiles[t][0:96, 0:512], wxb[:, c * 96:(c + 1) * 96],
                          ucc[:, t * 512:(t + 1) * 512], start=(c == 0),
                          stop=(c == 15)).then_inc(sem.h, 1)

            ctx.q["tensor"].append(runx)
            if c == 15:
                xd_tok.append(ctx.mm.count)
    for t in range(8):
        etk = ctx.op("vector", lambda v, t=t: v.tensor_copy(
            xdbl[0:96, t * 512:(t + 1) * 512], ctx.pool.tiles[t][0:96, 0:512]),
            inc=ctx.vS, waits=[(ctx.mm, xd_tok[t])])
        ctx.pool.release(ctx, "vector", t, [(ctx.vS, etk)])
    ctx.op("vector", lambda v: v.tensor_copy(xdblb[0:64, :], xdbl[0:64, :]), inc=ctx.vS)
    xdbl_tok = ctx.vS.count
    if _madbg:
        ctx.dma("sync", xd_dbg[:, :], xdbl[:, :], ctx.stS, waits=[(ctx.vS, xdbl_tok)])
        ctx.dma("sync", xdb_dbg[:, :], xdblb[:, :], ctx.stS, waits=[(ctx.vS, xdbl_tok)])
    for o in range(16 if "d" in _mp else 0):
        for t in range(8):
            bi, p, tk = mm_group(ctx, [(wdtb[0:64, o * 128:(o + 1) * 128],
                                        xdblb[0:64, t * 512:(t + 1) * 512])],
                                 waits=[(ctx.vS, xdbl_tok), (ctx.ldS, wx_tok)])
            bcol = bdt[:, o:o + 1]
            par = ((o * 8 + t) % 2) * 512
            A_ = eva[:, par:par + 512]
            B_ = evb[:, par:par + 512]
            C_ = evc[:, par:par + 512]
            ctx.op("vector", lambda v, p=p, bcol=bcol, C_=C_: v.tensor_scalar_add(
                C_, p[:, 0:512], bcol),
                inc=ctx.vS, waits=[(ctx.mm, tk), (ctx.lsS, cst_tok), (ctx.sS, ctx.sS.count)])
            ctx.op("vector", lambda v, A_=A_, C_=C_: v.tensor_scalar_max(A_, C_, 0.0), inc=ctx.vS,
                   waits=[(ctx.stS, 1)])
            ctx.op("vector", lambda v, B_=B_, C_=C_: v.tensor_scalar_mul(B_, C_, -1.0), inc=ctx.vS)
            ctx.op("vector", lambda v, B_=B_, C_=C_: v.tensor_max(B_, B_, C_), inc=ctx.vS)
            vt = ctx.vS.count
            ctx.op("scalar", lambda sx, B_=B_, C_=C_: sx.activation(
                out=C_, in_=B_, func=AF.Exp, scale=-1.0), inc=ctx.sS,
                waits=[(ctx.vS, vt)])
            etk2 = ctx.op("scalar", lambda sx, B_=B_, C_=C_: sx.activation(
                out=B_, in_=C_, func=AF.Ln, bias=onec[:, 0:1]),
                inc=ctx.sS, waits=[(ctx.gS, ms_tok)])
            ctx.pool.release(ctx, "scalar", bi, [(ctx.sS, etk2)])
            etk = ctx.op("vector", lambda v, A_=A_, B_=B_: v.tensor_add(A_, A_, B_),
                         inc=ctx.vS, waits=[(ctx.sS, etk2), (ctx.stS, ctx.stS.count)])
            ctx.dma("sync", dt_o[o * 128:(o + 1) * 128, t * 512:(t + 1) * 512], A_,
                    ctx.stS, waits=[(ctx.vS, etk)])
    for j in range(32 if "b" in _mp else 0):
        i, tile, (fsem, fneed) = ctx.pool.acquire()
        ctx.wait("tensor", fsem, fneed)
        ctx.wait("tensor", ctx.vS, xdbl_tok)
        ctx.wait("tensor", ctx.lsS, cst_tok)
        ctx.mm.count += 1
        tk = ctx.mm.count

        def runt(t, j=j, tile=tile, sem=ctx.mm):
            t.transpose(tile[:, 0:32], xdbl[64:96, j * 128:(j + 1) * 128],
                        idf[64:96, 64:96]).then_inc(sem.h, 1)

        ctx.q["tensor"].append(runt)
        etk = ctx.op("vector", lambda v, j=j, tile=tile: v.tensor_copy(
            bcsb[:, j * 32:(j + 1) * 32], tile[:, 0:32]), inc=ctx.vS, waits=[(ctx.mm, tk)])
        ctx.pool.release(ctx, "vector", i, [(ctx.vS, etk)])
    bc_tok = ctx.vS.count
    if "b" in _mp:
        ctx.dma("sync", bct_o.rearrange("(j p) s -> p j s", p=128), bcsb.ap().rearrange("p (j s) -> p j s", s=32),
                ctx.stS, waits=[(ctx.vS, bc_tok)])
    ctx.wait("sync", ctx.stS, ctx.stS.count)
    ctx.wait("gpsimd", ctx.ldS, ctx.ldS.count)
    return ctx


def host_mamba_a(inputs, core, st):
    b, q = core // 4, core % 4
    xn1 = st["xn1"][core]
    if q == 0:
        halo = np.zeros((D, 3), dtype=xn1.dtype)
    else:
        halo = st["xn1"][core - 1][:, -3:]
    xnh = np.concatenate([halo, xn1], axis=1)
    W_in = np.asarray(inputs["L1_attn_W_in"]).astype(np.float32)
    conv_w = np.asarray(inputs["L1_attn_conv_w"]).astype(np.float32)
    return {
        "xnh": np.ascontiguousarray(xnh),
        "win": np.ascontiguousarray(W_in.T),
        "convc": np.ascontiguousarray(conv_w.reshape(16, 128, 4).transpose(1, 0, 2).reshape(128, 64)),
        "convb": _bcols(inputs["L1_attn_conv_b"], 16),
        "wx": np.ascontiguousarray(np.asarray(inputs["L1_attn_W_x"]).astype(np.float32).T),
        "wdt": np.ascontiguousarray(np.asarray(inputs["L1_attn_W_dt"]).astype(np.float32).T),
        "bdt": _bcols(inputs["L1_attn_b_dt"], 16),
        "ident": np.eye(128, dtype=np.float32),
    }


def build_mamba_scan():
    ctx = Ctx()
    nc = ctx.nc
    NT = 512
    NS = 16
    ntiles = T // NT

    dt_d = nc.dram_tensor("dt", [512, T], F32, kind="ExternalInput").ap()
    uc_d = nc.dram_tensor("uc", [512, T], F32, kind="ExternalInput").ap()
    sz_d = nc.dram_tensor("sz", [512, T], F32, kind="ExternalInput").ap()
    bct_d = nc.dram_tensor("bct", [T, 32], F32, kind="ExternalInput").ap()
    alog_d = nc.dram_tensor("alog", [512, NS], F32, kind="ExternalInput").ap()
    dv_d = nc.dram_tensor("dv", [128, 4], F32, kind="ExternalInput").ap()
    y3_o = nc.dram_tensor("y3", [512, T], BF16, kind="ExternalOutput").ap()

    bufA = ctx.sbuf("bufA", [128, NT * NS])
    bufB = ctx.sbuf("bufB", [128, NT * NS])
    bufC = ctx.sbuf("bufC", [128, NT * NS])
    Bbc = ctx.sbuf("Bbc", [128, NT * NS])
    Cbc = ctx.sbuf("Cbc", [128, NT * NS])
    dtg = ctx.sbuf("dtgs", [128, NT])
    ucg = ctx.sbuf("ucg", [128, NT])
    szg = ctx.sbuf("szg", [128, NT])
    gg = ctx.sbuf("gg", [128, NT])
    yb = ctx.sbuf("yb", [128, NT])
    yb2 = ctx.sbuf("yb2", [128, NT])
    y3b = ctx.sbuf("y3b", [128, NT], BF16)
    rowb = ctx.sbuf("rowb", [1, 2 * 512])
    asb = ctx.sbuf("asbs", [128, 4 * NS])
    dvc = ctx.sbuf("dvcs", [128, 4])
    state = ctx.sbuf("state", [128, 4 * NS])
    onerow = ctx.sbuf("onerow", [1, 128])

    ctx.pool = PsumPool(ctx)
    ctx.op("gpsimd", lambda g: g.memset(onerow[:, :], 1.0))
    ms_tok = ctx.gS.count
    ctx.dma("gpsimd", asb[:, :], alog_d.rearrange("(g p) s -> p g s", p=128), ctx.ldS)
    ctx.dma("gpsimd", dvc[:, :], dv_d[:, :], ctx.ldS)
    al_tok = ctx.ldS.count
    ctx.op("scalar", lambda sx: sx.activation(out=asb[:, :], in_=asb[:, :], func=AF.Exp),
           inc=ctx.sS, waits=[(ctx.ldS, al_tok)])
    atk = ctx.op("vector", lambda v: v.tensor_scalar_mul(asb[:, :], asb[:, :], -1.0),
                 inc=ctx.vS, waits=[(ctx.sS, ctx.sS.count)])

    for ti in range(ntiles):
        tw = ctx.barrier_waits()
        for (coff, dst) in ((0, Bbc), (16, Cbc)):
            for k in range(NS):
                par = (k % 2) * 512
                lw = ctx.dma("sync", rowb[0:1, par:par + 512].rearrange("p (t s) -> p t s", s=16),
                             bct_d[ti * NT + k * 32: ti * NT + (k + 1) * 32, coff:coff + 16]
                             .unsqueeze(0),
                             ctx.lsS, waits=tw + [(ctx.mm, ctx.mm.count)])
                bi, p, tk = mm_group(ctx, [(onerow[0:1, :], rowb[0:1, par:par + 512])],
                                     waits=[(ctx.lsS, lw), (ctx.gS, ms_tok)])
                etk = ctx.op("vector", lambda v, p=p, dst=dst, k=k: v.tensor_copy(
                    dst[:, k * 512:(k + 1) * 512], p[:, 0:512]), inc=ctx.vS, waits=[(ctx.mm, tk)])
                ctx.pool.release(ctx, "vector", bi, [(ctx.vS, etk)])
        bc_tok = ctx.vS.count
        for g in range(4):
            ld = ctx.dma("sync", dtg[:, :], dt_d[g * 128:(g + 1) * 128, ti * NT:(ti + 1) * NT],
                         ctx.lsS, waits=[(ctx.vS, ctx.vS.count), (ctx.sS, ctx.sS.count)])
            ld = ctx.dma("sync", ucg[:, :], uc_d[g * 128:(g + 1) * 128, ti * NT:(ti + 1) * NT], ctx.lsS)
            ld = ctx.dma("sync", szg[:, :], sz_d[g * 128:(g + 1) * 128, ti * NT:(ti + 1) * NT], ctx.lsS)
            ctx.op("vector", lambda v: v.tensor_mul(gg[:, :], dtg[:, :], ucg[:, :]),
                   inc=ctx.vS, waits=[(ctx.lsS, ld), (ctx.vS, atk)])
            a3 = bufA.ap().rearrange("p (t s) -> p t s", s=NS)
            dtbc = dtg.ap().unsqueeze(2).broadcast_to([128, NT, NS])
            abc = asb.ap()[:, g * NS:(g + 1) * NS].unsqueeze(1).broadcast_to([128, NT, NS])
            ctx.op("vector", lambda v, a3=a3, dtbc=dtbc, abc=abc: v.tensor_tensor(
                a3, dtbc, abc, ALU.mult), inc=ctx.vS)
            eE = ctx.vS.count
            sE = ctx.op("scalar", lambda sx: sx.activation(out=bufB[:, :], in_=bufA[:, :],
                                                           func=AF.Exp),
                        inc=ctx.sS, waits=[(ctx.vS, eE)])
            gbc = gg.ap().unsqueeze(2).broadcast_to([128, NT, NS])
            b3 = Bbc.ap().rearrange("p (t s) -> p t s", s=NS)
            ctx.op("vector", lambda v, a3=a3, gbc=gbc, b3=b3: v.tensor_tensor(
                a3, gbc, b3, ALU.mult), inc=ctx.vS, waits=[(ctx.sS, sE), (ctx.vS, bc_tok)])
            aR = bufA.ap().rearrange("p (t s) -> p s t", s=NS)
            bR = bufB.ap().rearrange("p (t s) -> p s t", s=NS)
            cR = bufC.ap().rearrange("p (t s) -> p s t", s=NS)
            for sidx in range(NS):
                if ti == 0:
                    init = 0.0
                else:
                    init = state[:, g * NS + sidx: g * NS + sidx + 1]
                ctx.op("vector", lambda v, sidx=sidx, init=init, aR=aR, bR=bR, cR=cR:
                       v.tensor_tensor_scan(cR[:, sidx, :], bR[:, sidx, :], aR[:, sidx, :],
                                            init, ALU.mult, ALU.add), inc=ctx.vS)
            ctx.op("vector", lambda v, g=g: v.tensor_copy(state[:, g * NS:(g + 1) * NS],
                                                          bufC[:, (NT - 1) * NS: NT * NS]),
                   inc=ctx.vS)
            c3 = Cbc.ap().rearrange("p (t s) -> p t s", s=NS)
            h3 = bufC.ap().rearrange("p (t s) -> p t s", s=NS)
            ctx.op("vector", lambda v, a3=a3, h3=h3, c3=c3: v.tensor_tensor(a3, h3, c3, ALU.mult),
                   inc=ctx.vS)
            ctx.op("vector", lambda v, a3=a3: v.tensor_reduce(yb[:, :], a3, AX.X, ALU.add),
                   inc=ctx.vS)
            ctx.op("vector", lambda v, g=g: v.scalar_tensor_tensor(
                yb2[:, :], ucg[:, :], dvc[:, g:g + 1], yb[:, :], ALU.mult, ALU.add), inc=ctx.vS)
            etk = ctx.op("vector", lambda v: v.tensor_mul(y3b[:, :], yb2[:, :], szg[:, :]),
                         inc=ctx.vS, waits=[(ctx.stS, ctx.stS.count)])
            ctx.dma("sync", y3_o[g * 128:(g + 1) * 128, ti * NT:(ti + 1) * NT], y3b[:, :],
                    ctx.stS, waits=[(ctx.vS, etk)])
    ctx.wait("sync", ctx.stS, ctx.stS.count)
    return ctx


def host_mamba_scan(inputs, core, st):
    b, j = core // 4, core % 4
    rs = slice(512 * j, 512 * (j + 1))
    dt = np.concatenate([st["dt"][4 * b + q][rs, :] for q in range(4)], axis=1)
    uc = np.concatenate([st["uc"][4 * b + q][rs, :] for q in range(4)], axis=1)
    sz = np.concatenate([st["sz"][4 * b + q][rs, :] for q in range(4)], axis=1)
    bct = np.concatenate([st["bct"][4 * b + q] for q in range(4)], axis=0)
    alog = np.asarray(inputs["L1_attn_A_log"]).astype(np.float32)[rs, :]
    dvec = np.asarray(inputs["L1_attn_D"]).astype(np.float32)[rs]
    return {
        "dt": np.ascontiguousarray(dt), "uc": np.ascontiguousarray(uc),
        "sz": np.ascontiguousarray(sz), "bct": np.ascontiguousarray(bct),
        "alog": np.ascontiguousarray(alog),
        "dv": np.ascontiguousarray(dvec.reshape(4, 128).T),
    }


def build_final():
    ctx = Ctx()
    nc = ctx.nc
    NT = 512
    DI = 2048

    y3_d = nc.dram_tensor("y3", [DI, TQ], BF16, kind="ExternalInput").ap()
    wo_d = nc.dram_tensor("wo", [DI, D], F32, kind="ExternalInput").ap()
    x1_d = nc.dram_tensor("x1", [D, TQ], F32, kind="ExternalInput").ap()
    g1_d = nc.dram_tensor("g1", [D, TQ], F32, kind="ExternalInput").ap()
    out_o = nc.dram_tensor("out", [D, TQ], F32, kind="ExternalOutput").ap()

    wo = ctx.sbuf("wos", [128, 16 * D], BF16)
    y3t = ctx.sbuf("y3t", [128, 16 * NT], BF16)
    x1t = ctx.sbuf("x1t", [128, NT])
    g1t = ctx.sbuf("g1t", [128, NT])
    ev = ctx.sbuf("ev", [128, NT])

    ctx.pool = PsumPool(ctx)
    wt = 0
    for k in range(16):
        wt = ctx.dma("gpsimd", wo[:, k * D:(k + 1) * D], wo_d[k * 128:(k + 1) * 128, :], ctx.ldS)
    for ti in range(TQ // NT):
        tsl = slice(ti * NT, (ti + 1) * NT)
        tw = ctx.barrier_waits()
        yw = 0
        for k in range(16):
            yw = ctx.dma("sync", y3t[:, k * NT:(k + 1) * NT], y3_d[k * 128:(k + 1) * 128, tsl],
                         ctx.lsS, waits=tw)
        for o in range(DC):
            bi, p, tk = mm_group(ctx, [(wo[:, k * D + o * 128: k * D + (o + 1) * 128],
                                        y3t[:, k * NT:(k + 1) * NT]) for k in range(16)],
                                 waits=[(ctx.ldS, wt), (ctx.lsS, yw)])
            lx = ctx.dma("sync", x1t[:, :], x1_d[o * 128:(o + 1) * 128, tsl], ctx.lsS,
                         waits=[(ctx.vS, ctx.vS.count)])
            lx = ctx.dma("sync", g1t[:, :], g1_d[o * 128:(o + 1) * 128, tsl], ctx.lsS)
            ctx.op("vector", lambda v, p=p: v.tensor_mul(ev[:, :], p[:, 0:NT], g1t[:, :]),
                   inc=ctx.vS, waits=[(ctx.mm, tk), (ctx.lsS, lx), (ctx.stS, ctx.stS.count)])
            etk = ctx.op("vector", lambda v: v.tensor_add(ev[:, :], ev[:, :], x1t[:, :]),
                         inc=ctx.vS)
            ctx.pool.release(ctx, "vector", bi, [(ctx.vS, etk)])
            ctx.dma("sync", out_o[o * 128:(o + 1) * 128, tsl], ev[:, :], ctx.stS,
                    waits=[(ctx.vS, etk)])
    ctx.wait("sync", ctx.stS, ctx.stS.count)
    return ctx


def host_final(inputs, core, st):
    b, q = core // 4, core % 4
    y3 = np.concatenate([st["y3s"][4 * b + j][:, q * TQ:(q + 1) * TQ] for j in range(4)], axis=0)
    return {
        "y3": np.ascontiguousarray(y3),
        "wo": np.ascontiguousarray(np.asarray(inputs["L1_attn_W_out"]).astype(np.float32).T),
        "x1": st["x1"][core],
        "g1": st["g1"][core],
    }


def _dyn_cols(t, col_expr, nrows, ncols, rowstride):
    return bass.AP(t, col_expr, [[rowstride, nrows], [1, ncols]])


def emit_barrier(ctx):
    sems = [ctx.mm, ctx.vS, ctx.sS, ctx.gS, ctx.ldS, ctx.lsS, ctx.stS]
    for eng in ENGINES:
        for s in sems:
            ctx.wait(eng, s, s.count)


def emit_spinwait(ctx, flg_t, flgsb, idx_val, magic_val):
    psem = ctx.sem(dma=True)

    def run(g, flg_t=flg_t, flgsb=flgsb, idx_val=idx_val, magic_val=magic_val, psem=psem):
        with g.register("nd") as nd, g.register("tgt") as tgt, g.register("fv") as fv, \
             g.register("mgr") as mgr:
            g.reg_alu(mgr, magic_val, 0, ALU.add)
            g.reg_mov(nd, 1)
            g.reg_mov(tgt, 0)
            with g.While(nd):
                g.reg_add(tgt, tgt, 16)
                g.dma_start(out=flgsb[0:1, 0:1],
                            in_=bass.AP(flg_t, idx_val, [[1, 1], [1, 1]])).then_inc(psem.h, 16)
                g.wait_ge(psem.h, tgt)
                g.load(fv, flgsb[0:1, 0:1])
                g.reg_alu(nd, fv, mgr, ALU.not_equal)

    ctx.q["gpsimd"].append(run)
    ctx.op("gpsimd", lambda g: g.nop(), inc=ctx.gS)
    return ctx.gS.count


def emit_l0_upath(ctx, W, x_src, ntok, u_dst):
    NT = 512
    ntiles = ntok // NT
    with ctx.scope():
        xt = ctx.sbuf("p1_xt", [128, DC * NT])
        xn = ctx.sbuf("p1_xn", [128, DC * NT], BF16)
        h1 = ctx.sbuf("p1_h1", [128, 4 * NT], BF16)
        sg = ctx.sbuf("p1_sg", [128, NT])
        ev1 = ctx.sbuf("p1_ev1", [128, NT])
        ev2 = ctx.sbuf("p1_ev2", [128, NT])
        sw1 = ctx.sbuf("p1_sw1", [128, DC * 512], BF16)
        sw2 = ctx.sbuf("p1_sw2", [128, 4 * 128], BF16)
        bwt = ctx.sbuf("p1_bw", [128, DC * 128], BF16)
        for c in range(DC):
            ctx.dma("gpsimd", sw1[:, c * 512:(c + 1) * 512], W["sw1_d"][c * 128:(c + 1) * 128, :], ctx.ldS)
            ctx.dma("gpsimd", bwt[:, c * 128:(c + 1) * 128], W["bw_d"][c * 128:(c + 1) * 128, :], ctx.ldS)
        for c in range(4):
            ctx.dma("gpsimd", sw2[:, c * 128:(c + 1) * 128], W["sw2_d"][c * 128:(c + 1) * 128, :], ctx.ldS)
        W = dict(W)
        W["sw1"], W["sw2"], W["bw"] = sw1, sw2, bwt
        consts, ms_tok = ln_consts2(ctx, NT, "p1")
        for ti in range(ntiles):
            tsl = slice(ti * NT, (ti + 1) * NT)
            bw = ctx.barrier_waits()
            xw = 0
            for c in range(DC):
                xw = ctx.dma("sync", xt[:, c * NT:(c + 1) * NT], x_src[c * 128:(c + 1) * 128, tsl],
                             ctx.lsS, waits=bw)
            xn_tok = emit_ln(ctx, consts, xt, NT, lambda c: xn[:, c * NT:(c + 1) * NT],
                             [(ctx.lsS, xw), (ctx.gS, ms_tok)])

            def xnsl(c):
                return xn[:, c * NT:(c + 1) * NT]

            for o in range(4):
                bi, p, tk = mm_group(ctx, [(W["sw1"][:, c * 512 + o * 128: c * 512 + (o + 1) * 128],
                                            xnsl(c)) for c in range(DC)],
                                     waits=[(ctx.ldS, 1), (ctx.vS, xn_tok)])
                etk = ctx.op("scalar", lambda s, o=o, p=p: s.activation(
                    out=h1[:, o * NT:(o + 1) * NT], in_=p[:, 0:NT], func=AF.Silu,
                    bias=W["sb1"][:, o:o + 1]), inc=ctx.sS, waits=[(ctx.mm, tk), (ctx.lsS, 1)])
                ctx.pool.release(ctx, "scalar", bi, [(ctx.sS, etk)])
            h1_tok = ctx.sS.count
            bi, p, tk = mm_group(ctx, [(W["sw2"][:, c * 128:(c + 1) * 128], h1[:, c * NT:(c + 1) * NT])
                                       for c in range(4)], waits=[(ctx.sS, h1_tok)])
            sg_tok = ctx.op("scalar", lambda s, p=p: s.activation(
                out=sg[:, :], in_=p[:, 0:NT], func=AF.Sigmoid, bias=W["sb2"][:, 0:1]),
                inc=ctx.sS, waits=[(ctx.mm, tk)])
            ctx.pool.release(ctx, "scalar", bi, [(ctx.sS, sg_tok)])
            bi, p, tk = mm_group(ctx, [(W["bw"][:, c * 128:(c + 1) * 128], xnsl(c)) for c in range(DC)],
                                 waits=[(ctx.vS, xn_tok)])
            etk = ctx.op("scalar", lambda s, p=p: s.activation(
                out=ev1[:, :], in_=p[:, 0:NT], func=AF.Identity, bias=W["bb"][:, 0:1]),
                inc=ctx.sS, waits=[(ctx.mm, tk)])
            ctx.pool.release(ctx, "scalar", bi, [(ctx.sS, etk)])
            utk = ctx.op("vector", lambda v: v.tensor_mul(ev2[:, :], ev1[:, :], sg[:, :]),
                         inc=ctx.vS, waits=[(ctx.sS, ctx.sS.count), (ctx.ldS, 1)])
            ctx.dma("gpsimd", u_dst[:, tsl], ev2[:, :], ctx.ldS, waits=[(ctx.vS, utk)])
    emit_barrier(ctx)


def ln_consts2(ctx, NT, tag):
    consts = {
        "onecol": ctx.sbuf(f"{tag}_onecol", [128, 1]),
        "onerow": ctx.sbuf(f"{tag}_onerow", [1, 128]),
        "epsc": ctx.sbuf(f"{tag}_epsc", [1, 1]),
        "rows": ctx.sbuf(f"{tag}_rows", [1, 4 * NT]),
        "sq": ctx.sbuf(f"{tag}_sq", [128, DC * NT]),
    }
    ctx.op("gpsimd", lambda g: g.memset(consts["onecol"][:, :], 1.0), inc=ctx.gS)
    ctx.op("gpsimd", lambda g: g.memset(consts["onerow"][:, :], 1.0), inc=ctx.gS)
    ctx.op("gpsimd", lambda g: g.memset(consts["epsc"][:, :], LN_EPS), inc=ctx.gS)
    return consts, ctx.gS.count


def emit_l0_scan(ctx, u_src, A_src, I_src, hs_dst):
    c0 = C0
    TE = T + c0
    with ctx.scope():
        ue = ctx.sbuf("p2_ue", [128, TE], BF16)
        P = ctx.sbuf("p2_P", [128, (c0 + 1) * 128], BF16)
        Fb = ctx.sbuf("p2_F", [128, TE])
        h = ctx.sbuf("p2_h", [128, TE], BF16)
        A_sb = ctx.sbuf("p2_A", [128, 128], BF16)
        ctx.op("gpsimd", lambda g: g.memset(ue[:, 0:c0], 0.0), inc=ctx.gS)
        ms_tok = ctx.gS.count
        ctx.dma("gpsimd", ue[:, c0:], u_src[:, :], ctx.ldS)
        ctx.dma("gpsimd", A_sb[:, :], A_src[:, :], ctx.ldS)
        ctx.dma("gpsimd", P[:, 0:128], I_src[:, :], ctx.ldS)
        ld_tok = ctx.ldS.count
        for k in range(1, c0 + 1):
            bi, p, tk = mm_group(ctx, [(A_sb[:, :], P[:, (k - 1) * 128: k * 128])],
                                 waits=[(ctx.ldS, ld_tok), (ctx.vS, ctx.vS.count)])
            etk = ctx.op("vector", lambda v, k=k, p=p: v.tensor_copy(P[:, k * 128:(k + 1) * 128],
                                                                     p[:, 0:128]),
                         inc=ctx.vS, waits=[(ctx.mm, tk)])
            ctx.pool.release(ctx, "vector", bi, [(ctx.vS, etk)])
        pow_tok = ctx.vS.count
        tiles = [(i * 512, 512) for i in range(TE // 512)]
        if TE % 512:
            tiles.append((TE - TE % 512, TE % 512))
        for (ts, n) in tiles:
            i, tile, (fsem, fneed) = ctx.pool.acquire()
            ctx.wait("tensor", fsem, fneed)
            ctx.wait("tensor", ctx.vS, pow_tok)
            ctx.wait("tensor", ctx.gS, ms_tok)
            ctx.wait("tensor", ctx.ldS, ctx.ldS.count)
            ctx.mm.count += 1
            tok = ctx.mm.count

            def run(t, ts=ts, n=n, tile=tile, sem=ctx.mm):
                inst = None
                first = True
                for k in range(c0):
                    if ts - k + n <= 0:
                        continue
                    lo = max(0, k - ts)
                    src0 = ts - k + lo
                    inst = t.matmul(tile[:, lo:n], P[:, k * 128:(k + 1) * 128],
                                    ue[:, src0: src0 + (n - lo)], start=first, stop=(k == c0 - 1))
                    first = False
                inst.then_inc(sem.h, 1)

            ctx.q["tensor"].append(run)
            etk = ctx.op("vector", lambda v, ts=ts, n=n, tile=tile: v.tensor_copy(
                Fb[:, ts:ts + n], tile[:, 0:n]), inc=ctx.vS, waits=[(ctx.mm, tok)])
            ctx.pool.release(ctx, "vector", i, [(ctx.vS, etk)])
        f_tok = ctx.vS.count
        ctx.op("vector", lambda v: v.tensor_copy(h[:, 0:c0], Fb[:, 0:c0]), inc=ctx.vS,
               waits=[(ctx.vS, f_tok)])
        for r in range(1, TE // c0):
            bi, p, tk = mm_group(ctx, [(P[:, c0 * 128:(c0 + 1) * 128], h[:, (r - 1) * c0: r * c0])],
                                 waits=[(ctx.vS, ctx.vS.count)])
            etk = ctx.op("vector", lambda v, r=r, p=p: v.tensor_add(
                h[:, r * c0:(r + 1) * c0], Fb[:, r * c0:(r + 1) * c0], p[:, 0:c0]),
                inc=ctx.vS, waits=[(ctx.mm, tk)])
            ctx.pool.release(ctx, "vector", bi, [(ctx.vS, etk)])
        ctx.dma("sync", hs_dst[:, :], h[:, c0:], ctx.stS, waits=[(ctx.vS, ctx.vS.count)])
    emit_barrier(ctx)


def emit_quarter_pipeline(ctx, W, tag, x_src, hs_win, xn_scr, g0_scr, x1_dst, xn1_dst,
                          io, with_z):
    TH = TQ + 3
    tiles9 = [(i * 512, 512) for i in range(8)] + [(4096, 3)]
    NT = 512
    with ctx.scope():
        xt = ctx.sbuf(f"{tag}a_xt", [128, DC * NT])
        xn = ctx.sbuf(f"{tag}a_xn", [128, DC * NT], BF16)
        consts, ms_tok = ln_consts2(ctx, NT, f"{tag}a")
        for (ts, n) in tiles9:
            bw = ctx.barrier_waits()
            xw = 0
            for c in range(DC):
                xw = ctx.dma("sync", xt[:, c * NT: c * NT + n],
                             x_src[c * 128:(c + 1) * 128, ts:ts + n], ctx.lsS, waits=bw)
            xn_tok = emit_ln(ctx, consts, xt, n, lambda c: xn[:, c * NT: c * NT + n],
                             [(ctx.lsS, xw), (ctx.gS, ms_tok)], stride=NT)
            for c in range(DC):
                ctx.dma("sync", xn_scr[c * 128:(c + 1) * 128, ts:ts + n],
                        xn[:, c * NT: c * NT + n], ctx.stS, waits=[(ctx.vS, xn_tok)])
    emit_barrier(ctx)
    H = 3072
    HC = 24
    NT2 = 256
    with ctx.scope():
        wg = ctx.sbuf(f"{tag}b_wg", [128, DC * H], BF16)
        wu = ctx.sbuf(f"{tag}b_wu", [128, DC * H], BF16)
        wd = ctx.sbuf(f"{tag}b_wd", [128, HC * D], BF16)
        xnt = ctx.sbuf(f"{tag}b_xnt", [128, DC * NT2], BF16)
        gsp = ctx.sbuf(f"{tag}b_gsp", [128, HC * NT2])
        hbf = ctx.sbuf(f"{tag}b_hbf", [128, HC * NT2], BF16)
        fev1 = ctx.sbuf(f"{tag}b_ev1", [128, NT2])
        for c in range(DC):
            ctx.dma("gpsimd", wg[:, c * H:(c + 1) * H], W["wg_d"][c * 128:(c + 1) * 128, :], ctx.ldS)
            ctx.dma("gpsimd", wu[:, c * H:(c + 1) * H], W["wu_d"][c * 128:(c + 1) * 128, :], ctx.ldS)
        for c in range(HC):
            ctx.dma("gpsimd", wd[:, c * D:(c + 1) * D], W["wd_d"][c * 128:(c + 1) * 128, :], ctx.ldS)
        wld = ctx.ldS.count
        if W.get("_dbgw") is not None and tag == "o":
            ctx.dma("sync", W["_dbgw"][:, :], wg[:, 0:3072], ctx.stS, waits=[(ctx.ldS, wld)])
        ftiles = [(i * NT2, NT2) for i in range(16)] + [(4096, 3)]
        for (ts, n) in ftiles:
            bw = ctx.barrier_waits()
            xw = 0
            for c in range(DC):
                xw = ctx.dma("sync", xnt[:, c * NT2: c * NT2 + n],
                             xn_scr[c * 128:(c + 1) * 128, ts:ts + n], ctx.lsS, waits=bw)

            def xnsl(c, n=n):
                return xnt[:, c * NT2: c * NT2 + n]

            for o in range(HC):
                bi, p, tk = mm_group(ctx, [(wg[:, c * H + o * 128: c * H + (o + 1) * 128], xnsl(c))
                                           for c in range(DC)],
                                     waits=[(ctx.ldS, wld), (ctx.lsS, xw)])
                etk = ctx.op("scalar", lambda s, o=o, p=p, n=n: s.activation(
                    out=gsp[:, o * NT2: o * NT2 + n], in_=p[:, 0:n], func=AF.Silu, scale=DEQ),
                    inc=ctx.sS, waits=[(ctx.mm, tk)])
                ctx.pool.release(ctx, "scalar", bi, [(ctx.sS, etk)])
            for o in range(HC):
                bi, p, tk = mm_group(ctx, [(wu[:, c * H + o * 128: c * H + (o + 1) * 128], xnsl(c))
                                           for c in range(DC)], waits=[(ctx.lsS, xw)])
                etk = ctx.op("vector", lambda v, o=o, p=p, n=n: v.scalar_tensor_tensor(
                    hbf[:, o * NT2: o * NT2 + n], p[:, 0:n], DEQ, gsp[:, o * NT2: o * NT2 + n],
                    ALU.mult, ALU.mult),
                    inc=ctx.vS, waits=[(ctx.mm, tk), (ctx.sS, ctx.sS.count)])
                ctx.pool.release(ctx, "vector", bi, [(ctx.vS, etk)])
            h_tok = ctx.vS.count
            for o in range(DC):
                bi, p, tk = mm_group(ctx, [(wd[:, c * D + o * 128: c * D + (o + 1) * 128],
                                            hbf[:, c * NT2: c * NT2 + n]) for c in range(HC)],
                                     waits=[(ctx.vS, h_tok)])
                etk = ctx.op("scalar", lambda s, p=p, n=n, fev1=fev1: s.activation(
                    out=fev1[:, 0:n], in_=p[:, 0:n], func=AF.Silu, scale=DEQ),
                    inc=ctx.sS, waits=[(ctx.mm, tk), (ctx.stS, 1)])
                ctx.pool.release(ctx, "scalar", bi, [(ctx.sS, etk)])
                ctx.dma("sync", g0_scr[o * 128:(o + 1) * 128, ts:ts + n], fev1[:, 0:n],
                        ctx.stS, waits=[(ctx.sS, etk)])
    emit_barrier(ctx)
    with ctx.scope():
        dw = ctx.sbuf(f"{tag}d_dw", [128, DC * D], BF16)
        gw = ctx.sbuf(f"{tag}d_gw", [128, DC * D], BF16)
        xt = ctx.sbuf(f"{tag}d_xt", [128, DC * NT])
        xn = ctx.sbuf(f"{tag}d_xn", [128, DC * NT], BF16)
        hst = ctx.sbuf(f"{tag}d_hs", [128, NT], BF16)
        s0t = ctx.sbuf(f"{tag}d_s0", [128, DC * NT])
        ga = ctx.sbuf(f"{tag}d_ga", [128, NT])
        ev1 = ctx.sbuf(f"{tag}d_ev1", [128, NT])
        consts, ms_tok = ln_consts2(ctx, NT, f"{tag}d")
        for c in range(DC):
            ctx.dma("gpsimd", dw[:, c * D:(c + 1) * D], W["dw_d"][c * 128:(c + 1) * 128, :], ctx.ldS)
            ctx.dma("gpsimd", gw[:, c * D:(c + 1) * D], W["gw_d"][c * 128:(c + 1) * 128, :], ctx.ldS)
        wld = ctx.ldS.count
        for (ts, n) in tiles9:
            bw = ctx.barrier_waits()
            xw = 0
            for c in range(DC):
                xw = ctx.dma("sync", xt[:, c * NT: c * NT + n],
                             x_src[c * 128:(c + 1) * 128, ts:ts + n], ctx.lsS, waits=bw)
                xw = ctx.dma("sync", xn[:, c * NT: c * NT + n],
                             xn_scr[c * 128:(c + 1) * 128, ts:ts + n], ctx.lsS)

            hw = ctx.dma("sync", hst[:, 0:n], hs_win[:, ts:ts + n], ctx.lsS)

            def xnsl(c, n=n):
                return xn[:, c * NT: c * NT + n]

            for o in range(DC):
                bi, py, tky = mm_group(ctx, [(W["cw"][:, o * 128:(o + 1) * 128], hst[:, 0:n])] +
                                       [(dw[:, c * D + o * 128: c * D + (o + 1) * 128], xnsl(c))
                                        for c in range(DC)],
                                       waits=[(ctx.lsS, hw), (ctx.ldS, wld)])
                y0tk = ctx.op("vector", lambda v, py=py, o=o, n=n: v.tensor_scalar(
                    s0t[:, o * NT: o * NT + n], py[:, 0:n], W["ydb"][:, o:o + 1], None, ALU.add),
                    inc=ctx.vS, waits=[(ctx.mm, tky), (ctx.sS, ctx.sS.count)])
                ctx.pool.release(ctx, "vector", bi, [(ctx.vS, y0tk)])
                bi, pg, tkg = mm_group(ctx, [(gw[:, c * D + o * 128: c * D + (o + 1) * 128],
                                              xnsl(c)) for c in range(DC)], waits=())
                gtk = ctx.op("scalar", lambda s, pg=pg, o=o, n=n: s.activation(
                    out=ga[:, 0:n], in_=pg[:, 0:n], func=AF.Sigmoid, bias=W["gb"][:, o:o + 1]),
                    inc=ctx.sS, waits=[(ctx.mm, tkg), (ctx.vS, y0tk)])
                ctx.pool.release(ctx, "scalar", bi, [(ctx.sS, gtk)])
                g0w = ctx.dma("sync", ev1[:, 0:n], g0_scr[o * 128:(o + 1) * 128, ts:ts + n],
                              ctx.lsS, waits=[(ctx.vS, ctx.vS.count), (ctx.sS, ctx.sS.count)])
                ctx.op("vector", lambda v, o=o, n=n: v.tensor_mul(
                    s0t[:, o * NT: o * NT + n], s0t[:, o * NT: o * NT + n], ga[:, 0:n]),
                    inc=ctx.vS, waits=[(ctx.sS, gtk), (ctx.lsS, g0w)])
                ctx.op("vector", lambda v, o=o, n=n: v.tensor_mul(
                    s0t[:, o * NT: o * NT + n], s0t[:, o * NT: o * NT + n], ev1[:, 0:n]), inc=ctx.vS)
                ctx.op("vector", lambda v, o=o, n=n: v.tensor_add(
                    xt[:, o * NT: o * NT + n], xt[:, o * NT: o * NT + n],
                    s0t[:, o * NT: o * NT + n]), inc=ctx.vS)
            x1_tok = ctx.vS.count
            for c in range(DC):
                lo = max(0, 3 - ts)
                if n - lo > 0:
                    ctx.dma("sync", x1_dst[c * 128:(c + 1) * 128, ts - 3 + lo: ts - 3 + n],
                            xt[:, c * NT + lo: c * NT + n], ctx.stS, waits=[(ctx.vS, x1_tok)])
            xn1_tok = emit_ln(ctx, consts, xt, n, lambda c: xn[:, c * NT: c * NT + n],
                              [(ctx.vS, x1_tok), (ctx.stS, 1)], stride=NT)
            for c in range(DC):
                ctx.dma("sync", xn1_dst[c * 128:(c + 1) * 128, ts:ts + n],
                        xn[:, c * NT: c * NT + n], ctx.stS, waits=[(ctx.vS, xn1_tok)])
    emit_barrier(ctx)
    emit_mamba_a2(ctx, W, tag, xn1_dst, io, with_z)


def emit_mamba_a2(ctx, W, tag, xn1_src, io, with_z):
    TH = TQ + 3
    DI = 2048
    with ctx.scope():
        xnh = ctx.sbuf(f"{tag}m_xnh", [128, DC * TH], BF16)
        wb = ctx.sbuf(f"{tag}m_wb", [128, DC * 128], BF16)
        uo = ctx.sbuf(f"{tag}m_uo", [128, TH])
        ca = ctx.sbuf(f"{tag}m_ca", [128, TQ])
        cb = ctx.sbuf(f"{tag}m_cb", [128, TQ])
        ucf = ctx.sbuf(f"{tag}m_ucf", [128, TQ])
        ucc = ctx.sbuf(f"{tag}m_ucc", [128, TQ], BF16)
        xdbl = ctx.sbuf(f"{tag}m_xdbl", [128, TQ])
        xdblb = ctx.sbuf(f"{tag}m_xdblb", [128, TQ], BF16)
        eva = ctx.sbuf(f"{tag}m_eva", [128, 1024])
        evb = ctx.sbuf(f"{tag}m_evb", [128, 1024])
        evc = ctx.sbuf(f"{tag}m_evc", [128, 1024])
        bcsb = ctx.sbuf(f"{tag}m_bcsb", [128, 32 * 32])
        onec = ctx.sbuf(f"{tag}m_onec", [128, 1])
        ctx.op("gpsimd", lambda g: g.memset(onec[:, :], 1.0), inc=ctx.gS)
        ms_tok = ctx.gS.count
        xw = 0
        for c in range(DC):
            xw = ctx.dma("sync", xnh[:, c * TH:(c + 1) * TH], xn1_src[c * 128:(c + 1) * 128, :], ctx.lsS)
        utiles = [(t * 512, 512) for t in range(8)] + [(4096, 3)]
        for o in range(16):
            wt = ctx.dma("gpsimd", wb[:, :],
                         W["win_d"][:, o * 128:(o + 1) * 128].rearrange("(c p) n -> p c n", p=128),
                         ctx.ldS, waits=[(ctx.mm, ctx.mm.count)])
            for (ts, n) in utiles:
                bi, p, tk = mm_group(ctx, [(wb[:, c * 128:(c + 1) * 128],
                                            xnh[:, c * TH + ts: c * TH + ts + n]) for c in range(DC)],
                                     waits=[(ctx.ldS, wt), (ctx.lsS, xw)])
                etk = ctx.op("vector", lambda v, ts=ts, n=n, p=p: v.tensor_copy(
                    uo[:, ts:ts + n], p[:, 0:n]), inc=ctx.vS, waits=[(ctx.mm, tk)])
                ctx.pool.release(ctx, "vector", bi, [(ctx.vS, etk)])
            ctx.op("vector", lambda v, o=o: v.tensor_scalar(
                ca[:, :], uo[:, 0:TQ], W["convc"][:, o * 4: o * 4 + 1], None, ALU.mult),
                inc=ctx.vS, waits=[(ctx.lsS, 1), (ctx.sS, ctx.sS.count)])
            ctx.op("vector", lambda v, o=o: v.scalar_tensor_tensor(
                cb[:, :], uo[:, 1:TQ + 1], W["convc"][:, o * 4 + 1: o * 4 + 2], ca[:, :],
                ALU.mult, ALU.add), inc=ctx.vS)
            ctx.op("vector", lambda v, o=o: v.scalar_tensor_tensor(
                ca[:, :], uo[:, 2:TQ + 2], W["convc"][:, o * 4 + 2: o * 4 + 3], cb[:, :],
                ALU.mult, ALU.add), inc=ctx.vS)
            ctx.op("vector", lambda v, o=o: v.scalar_tensor_tensor(
                cb[:, :], uo[:, 3:TQ + 3], W["convc"][:, o * 4 + 3: o * 4 + 4], ca[:, :],
                ALU.mult, ALU.add), inc=ctx.vS)
            stk = ctx.op("scalar", lambda sx, o=o: sx.activation(
                out=ucf[:, :], in_=cb[:, :], func=AF.Silu, bias=W["convb"][:, o:o + 1]),
                inc=ctx.sS, waits=[(ctx.vS, ctx.vS.count), (ctx.stS, 1), (ctx.ldS, 1)])
            ctx.dma("sync", io["uc"][o * 128:(o + 1) * 128, :], ucf[:, :], ctx.stS,
                    waits=[(ctx.sS, stk)])
            ctx.dma("gpsimd", io["ucb"][o * 128:(o + 1) * 128, :], ucf[:, :], ctx.ldS,
                    waits=[(ctx.sS, stk)])
        if with_z:
            for o in range(16):
                wt = ctx.dma("gpsimd", wb[:, :],
                             W["win_d"][:, DI + o * 128: DI + (o + 1) * 128]
                             .rearrange("(c p) n -> p c n", p=128),
                             ctx.ldS, waits=[(ctx.mm, ctx.mm.count)])
                for t in range(8):
                    ts = 3 + t * 512
                    bi, p, tk = mm_group(ctx, [(wb[:, c * 128:(c + 1) * 128],
                                                xnh[:, c * TH + ts: c * TH + ts + 512])
                                               for c in range(DC)],
                                         waits=[(ctx.ldS, wt)])
                    etk = ctx.op("scalar", lambda sx, p=p: sx.activation(
                        out=eva[:, 0:512], in_=p[:, 0:512], func=AF.Silu),
                        inc=ctx.sS, waits=[(ctx.mm, tk), (ctx.stS, 1), (ctx.vS, ctx.vS.count)])
                    ctx.pool.release(ctx, "scalar", bi, [(ctx.sS, etk)])
                    ctx.dma("sync", io["sz"][o * 128:(o + 1) * 128, t * 512:(t + 1) * 512],
                            eva[:, 0:512], ctx.stS, waits=[(ctx.sS, etk)])
        ctx.wait("tensor", ctx.ldS, ctx.ldS.count)
        for i in range(8):
            ctx.wait("tensor", ctx.pool.free[i], ctx.pool.uses[i])
            ctx.pool.uses[i] += 1
        xd_tok = []
        for c in range(16):
            lt = ctx.dma("sync", ucc[:, :], io["ucb"][c * 128:(c + 1) * 128, :], ctx.lsS,
                         waits=[(ctx.mm, ctx.mm.count), (ctx.ldS, 1)])
            for t in range(8):
                ctx.wait("tensor", ctx.lsS, lt)
                ctx.mm.count += 1

                def runx(tt, c=c, t=t, sem=ctx.mm):
                    tt.matmul(ctx.pool.tiles[t][0:96, 0:512], W["wxb"][:, c * 96:(c + 1) * 96],
                              ucc[:, t * 512:(t + 1) * 512], start=(c == 0),
                              stop=(c == 15)).then_inc(sem.h, 1)

                ctx.q["tensor"].append(runx)
                if c == 15:
                    xd_tok.append(ctx.mm.count)
        for t in range(8):
            etk = ctx.op("vector", lambda v, t=t: v.tensor_copy(
                xdbl[0:96, t * 512:(t + 1) * 512], ctx.pool.tiles[t][0:96, 0:512]),
                inc=ctx.vS, waits=[(ctx.mm, xd_tok[t])])
            ctx.pool.release(ctx, "vector", t, [(ctx.vS, etk)])
        ctx.op("vector", lambda v: v.tensor_copy(xdblb[0:64, :], xdbl[0:64, :]), inc=ctx.vS)
        xdbl_tok = ctx.vS.count
        for o in range(16):
            for t in range(8):
                bi, p, tk = mm_group(ctx, [(W["wdtb"][0:64, o * 128:(o + 1) * 128],
                                            xdblb[0:64, t * 512:(t + 1) * 512])],
                                     waits=[(ctx.vS, xdbl_tok), (ctx.ldS, 1)])
                bcol = W["bdt"][:, o:o + 1]
                par = ((o * 8 + t) % 2) * 512
                A_ = eva[:, par:par + 512]
                B_ = evb[:, par:par + 512]
                C_ = evc[:, par:par + 512]
                ctx.op("vector", lambda v, p=p, bcol=bcol, C_=C_: v.tensor_scalar_add(
                    C_, p[:, 0:512], bcol),
                    inc=ctx.vS, waits=[(ctx.mm, tk), (ctx.lsS, 1), (ctx.sS, ctx.sS.count)])
                ctx.op("vector", lambda v, A_=A_, C_=C_: v.tensor_scalar_max(A_, C_, 0.0),
                       inc=ctx.vS, waits=[(ctx.stS, 1)])
                ctx.op("vector", lambda v, B_=B_, C_=C_: v.tensor_scalar_mul(B_, C_, -1.0), inc=ctx.vS)
                ctx.op("vector", lambda v, B_=B_, C_=C_: v.tensor_max(B_, B_, C_), inc=ctx.vS)
                vt = ctx.vS.count
                ctx.op("scalar", lambda sx, B_=B_, C_=C_: sx.activation(
                    out=C_, in_=B_, func=AF.Exp, scale=-1.0), inc=ctx.sS, waits=[(ctx.vS, vt)])
                etk2 = ctx.op("scalar", lambda sx, B_=B_, C_=C_: sx.activation(
                    out=B_, in_=C_, func=AF.Ln, bias=onec[:, 0:1]),
                    inc=ctx.sS, waits=[(ctx.gS, ms_tok)])
                ctx.pool.release(ctx, "scalar", bi, [(ctx.sS, etk2)])
                etk = ctx.op("vector", lambda v, A_=A_, B_=B_: v.tensor_add(A_, A_, B_),
                             inc=ctx.vS, waits=[(ctx.sS, etk2), (ctx.stS, 1)])
                ctx.dma("sync", io["dt"][o * 128:(o + 1) * 128, t * 512:(t + 1) * 512], A_,
                        ctx.stS, waits=[(ctx.vS, etk)])
        for j in range(32):
            i, tile, (fsem, fneed) = ctx.pool.acquire()
            ctx.wait("tensor", fsem, fneed)
            ctx.wait("tensor", ctx.vS, xdbl_tok)
            ctx.wait("tensor", ctx.lsS, ctx.lsS.count)
            ctx.mm.count += 1
            tk = ctx.mm.count

            def runt(t, j=j, tile=tile, sem=ctx.mm):
                t.transpose(tile[:, 0:32], xdbl[64:96, j * 128:(j + 1) * 128],
                            W["idf"][64:96, 64:96]).then_inc(sem.h, 1)

            ctx.q["tensor"].append(runt)
            etk = ctx.op("vector", lambda v, j=j, tile=tile: v.tensor_copy(
                bcsb[:, j * 32:(j + 1) * 32], tile[:, 0:32]), inc=ctx.vS, waits=[(ctx.mm, tk)])
            ctx.pool.release(ctx, "vector", i, [(ctx.vS, etk)])
        bc_tok = ctx.vS.count
        ctx.dma("sync", io["bct"].rearrange("(j p) s -> p j s", p=128),
                bcsb.ap().rearrange("p (j s) -> p j s", s=32), ctx.stS, waits=[(ctx.vS, bc_tok)])
    emit_barrier(ctx)


def emit_scan2(ctx, W, tag, io, with_y, init_tile, state_out):
    NT = 512
    NS = 16
    ntiles = TQ // NT
    NG = 16
    with ctx.scope():
        bufA = ctx.sbuf(f"{tag}s_A", [128, NT * NS])
        bufB = ctx.sbuf(f"{tag}s_B", [128, NT * NS])
        bufC = ctx.sbuf(f"{tag}s_C", [128, NT * NS])
        Bbc = ctx.sbuf(f"{tag}s_Bb", [128, NT * NS], BF16)
        Cbc = ctx.sbuf(f"{tag}s_Cb", [128, NT * NS], BF16) if with_y else None
        dtg2 = ctx.sbuf(f"{tag}s_dt", [128, 2 * NT])
        ucg2 = ctx.sbuf(f"{tag}s_uc", [128, 2 * NT])
        szg2 = ctx.sbuf(f"{tag}s_sz", [128, 2 * NT]) if with_y else None
        gg = ctx.sbuf(f"{tag}s_g", [128, NT])
        yb = ctx.sbuf(f"{tag}s_y", [128, NT]) if with_y else None
        yb2 = ctx.sbuf(f"{tag}s_y2", [128, NT]) if with_y else None
        y3b = ctx.sbuf(f"{tag}s_y3", [128, NT], BF16) if with_y else None
        rowb = ctx.sbuf(f"{tag}s_row", [1, NT * NS])
        onerow = ctx.sbuf(f"{tag}s_or", [1, 128])
        ctx.op("gpsimd", lambda g: g.memset(onerow[:, :], 1.0), inc=ctx.gS)
        ms_tok = ctx.gS.count
        for ti in range(ntiles):
            tw = ctx.barrier_waits()
            pairs = [(0, Bbc)] + ([(16, Cbc)] if with_y else [])
            for (ci, (coff, dst)) in enumerate(pairs):
                lw = ctx.dma("sync", rowb[0:1, :].rearrange("p (t s) -> p t s", s=16),
                             io["bct"][ti * NT:(ti + 1) * NT, coff:coff + 16]
                             .unsqueeze(0), ctx.lsS, waits=tw + [(ctx.mm, ctx.mm.count)])
                for k in range(NS):
                    bi, p, tk = mm_group(ctx, [(onerow[0:1, :], rowb[0:1, k * 512:(k + 1) * 512])],
                                         waits=[(ctx.lsS, lw), (ctx.gS, ms_tok)])
                    etk = ctx.op("vector", lambda v, p=p, dst=dst, k=k: v.tensor_copy(
                        dst[:, k * 512:(k + 1) * 512], p[:, 0:512]), inc=ctx.vS,
                        waits=[(ctx.mm, tk)])
                    ctx.pool.release(ctx, "vector", bi, [(ctx.vS, etk)])
            for g in range(NG):
                gpar = (g % 2) * NT
                dtg = dtg2[:, gpar:gpar + NT]
                ucg = ucg2[:, gpar:gpar + NT]
                szg = szg2[:, gpar:gpar + NT] if with_y else None
                war = ctx._scan_war.get((tag, g % 2), (0, 0))
                ld = ctx.dma("sync", dtg, io["dt"][g * 128:(g + 1) * 128, ti * NT:(ti + 1) * NT],
                             ctx.lsS, waits=[(ctx.vS, war[0]), (ctx.sS, war[1])])
                ld = ctx.dma("sync", ucg, io["uc"][g * 128:(g + 1) * 128, ti * NT:(ti + 1) * NT],
                             ctx.lsS)
                if with_y:
                    ld = ctx.dma("sync", szg, io["sz"][g * 128:(g + 1) * 128, ti * NT:(ti + 1) * NT],
                                 ctx.lsS)
                ctx.op("vector", lambda v, gg=gg, dtg=dtg, ucg=ucg: v.tensor_mul(gg[:, :], dtg, ucg),
                       inc=ctx.vS, waits=[(ctx.lsS, ld)])
                a3 = bufA.ap().rearrange("p (t s) -> p t s", s=NS)
                dtbc = dtg.unsqueeze(2).broadcast_to([128, NT, NS])
                abc = W["asb"].ap()[:, g * NS:(g + 1) * NS].unsqueeze(1).broadcast_to([128, NT, NS])
                ctx.op("vector", lambda v, a3=a3, dtbc=dtbc, abc=abc: v.tensor_tensor(
                    a3, dtbc, abc, ALU.mult), inc=ctx.vS)
                eE = ctx.vS.count
                sE = ctx.op("scalar", lambda sx: sx.activation(out=bufB[:, :], in_=bufA[:, :],
                                                               func=AF.Exp),
                            inc=ctx.sS, waits=[(ctx.vS, eE)])
                gbc = gg.ap().unsqueeze(2).broadcast_to([128, NT, NS])
                b3 = Bbc.ap().rearrange("p (t s) -> p t s", s=NS)
                ctx.op("vector", lambda v, a3=a3, gbc=gbc, b3=b3: v.tensor_tensor(
                    a3, gbc, b3, ALU.mult), inc=ctx.vS, waits=[(ctx.sS, sE)])
                aR = bufA.ap().rearrange("p (t s) -> p s t", s=NS)
                bR = bufB.ap().rearrange("p (t s) -> p s t", s=NS)
                cR = bufC.ap().rearrange("p (t s) -> p s t", s=NS)
                for sidx in range(NS):
                    if ti == 0:
                        init = 0.0 if init_tile is None else init_tile[:, g * NS + sidx: g * NS + sidx + 1]
                    else:
                        init = state_out[:, g * NS + sidx: g * NS + sidx + 1]
                    ctx.op("vector", lambda v, sidx=sidx, init=init, aR=aR, bR=bR, cR=cR:
                           v.tensor_tensor_scan(cR[:, sidx, :], bR[:, sidx, :], aR[:, sidx, :],
                                                init, ALU.mult, ALU.add), inc=ctx.vS)
                ctx.op("vector", lambda v, g=g: v.tensor_copy(state_out[:, g * NS:(g + 1) * NS],
                                                              bufC[:, (NT - 1) * NS: NT * NS]),
                       inc=ctx.vS)
                if with_y:
                    c3 = Cbc.ap().rearrange("p (t s) -> p t s", s=NS)
                    h3 = bufC.ap().rearrange("p (t s) -> p t s", s=NS)
                    ctx.op("vector", lambda v, a3=a3, h3=h3, c3=c3: v.tensor_tensor(
                        a3, h3, c3, ALU.mult), inc=ctx.vS)
                    ctx.op("vector", lambda v, a3=a3: v.tensor_reduce(yb[:, :], a3, AX.X, ALU.add),
                           inc=ctx.vS)
                    ctx.op("vector", lambda v, g=g, ucg=ucg: v.scalar_tensor_tensor(
                        yb2[:, :], ucg, W["dvc"][:, g:g + 1], yb[:, :], ALU.mult, ALU.add),
                        inc=ctx.vS)
                    etk = ctx.op("vector", lambda v, szg=szg: v.tensor_mul(y3b[:, :], yb2[:, :], szg),
                                 inc=ctx.vS, waits=[(ctx.stS, 1)])
                    ctx.dma("sync", io["y3"][g * 128:(g + 1) * 128, ti * NT:(ti + 1) * NT], y3b[:, :],
                            ctx.stS, waits=[(ctx.vS, etk)])
                ctx._scan_war[(tag, g % 2)] = (ctx.vS.count, ctx.sS.count)
    emit_barrier(ctx)


def emit_moe2(ctx, W, tag, xn1_scr, g1_dst):
    NE, HP = 8, 4096
    QW = 1024
    nc = ctx.nc
    wf_d = nc.dram_tensor(f"{tag}_wfd", [NE, TQ], F32).ap()
    with ctx.scope():
        xnt = ctx.sbuf(f"{tag}e_xnt", [128, DC * 512], BF16)
        xnq = ctx.sbuf(f"{tag}e_xnq", [128, DC * QW], BF16)
        sc = ctx.sbuf(f"{tag}e_sc", [128, 32 * NE])
        m1 = ctx.sbuf(f"{tag}e_m1", [128, 32])
        m2 = ctx.sbuf(f"{tag}e_m2", [128, 32])
        lt = ctx.sbuf(f"{tag}e_lt", [128, 32 * NE])
        msk = ctx.sbuf(f"{tag}e_msk", [128, 32 * NE])
        dd = ctx.sbuf(f"{tag}e_dd", [128, 32])
        w1v = ctx.sbuf(f"{tag}e_w1v", [128, 32])
        w2v = ctx.sbuf(f"{tag}e_w2v", [128, 32])
        eq = ctx.sbuf(f"{tag}e_eq", [128, 32 * NE])
        wsb = ctx.sbuf(f"{tag}e_wsb", [128, 32 * NE])
        wrows = ctx.sbuf(f"{tag}e_wr", [8, 128])
        wfrow = ctx.sbuf(f"{tag}e_wf", [1, 512])
        brbc = ctx.sbuf(f"{tag}e_brbc", [128, NE])
        w1b = ctx.sbuf(f"{tag}e_w1b", [128, 2 * DC * 128], BF16)
        w2b = ctx.sbuf(f"{tag}e_w2b", [128, 2 * 32 * 128], BF16)
        gsb = ctx.sbuf(f"{tag}e_gsb", [128, 32 * QW], BF16)
        acc = ctx.sbuf(f"{tag}e_acc", [128, DC * QW])
        wbc = ctx.sbuf(f"{tag}e_wbc", [128, 2 * 512])
        xbf = ctx.sbuf(f"{tag}e_xbf", [128, 1024])
        sqf = ctx.sbuf(f"{tag}e_sqf", [128, 1024])
        inn = ctx.sbuf(f"{tag}e_inn", [128, 1024])
        ev1 = ctx.sbuf(f"{tag}e_ev1", [128, 512])
        yw = ctx.sbuf(f"{tag}e_yw", [128, 512])
        onerow = ctx.sbuf(f"{tag}e_or", [1, 128])
        ctx.op("gpsimd", lambda g: g.memset(onerow[:, :], 1.0), inc=ctx.gS)
        ms_tok = ctx.gS.count

        bi, p, tk = mm_group(ctx, [(onerow[0:1, :], W["brr"][0:1, :])],
                             waits=[(ctx.lsS, 1), (ctx.gS, ms_tok), (ctx.ldS, 1)])
        etk = ctx.op("vector", lambda v, p=p: v.tensor_copy(brbc[:, :], p[:, 0:NE]),
                     inc=ctx.vS, waits=[(ctx.mm, tk)])
        ctx.pool.release(ctx, "vector", bi, [(ctx.vS, etk)])
        for tt in range(8):
            xw = 0
            for c in range(DC):
                xw = ctx.dma("sync", xnt[:, c * 512:(c + 1) * 512],
                             xn1_scr[c * 128:(c + 1) * 128, 3 + tt * 512: 3 + (tt + 1) * 512],
                             ctx.lsS, waits=[(ctx.mm, ctx.mm.count)])
            for j in range(4):
                tc = tt * 4 + j
                bi, p, tk = mm_group(ctx, [(xnt[:, c * 512 + j * 128: c * 512 + (j + 1) * 128],
                                            W["wrb"][:, c * NE:(c + 1) * NE]) for c in range(DC)],
                                     waits=[(ctx.ldS, 1), (ctx.lsS, xw)])
                etk = ctx.op("vector", lambda v, tc=tc, p=p: v.tensor_add(
                    sc[:, tc * NE:(tc + 1) * NE], p[:, 0:NE], brbc[:, :]),
                    inc=ctx.vS, waits=[(ctx.mm, tk)])
                ctx.pool.release(ctx, "vector", bi, [(ctx.vS, etk)])
        sc3 = sc.ap().rearrange("p (t e) -> p t e", e=NE)
        ctx.op("vector", lambda v: v.tensor_reduce(m1[:, :], sc3, AX.X, ALU.max), inc=ctx.vS)
        m1bc = m1.ap().unsqueeze(2).broadcast_to([128, 32, NE])
        lt3 = lt.ap().rearrange("p (t e) -> p t e", e=NE)
        ctx.op("vector", lambda v: v.tensor_tensor(lt3, sc3, m1bc, ALU.is_ge), inc=ctx.vS)
        ctx.op("vector", lambda v: v.scalar_tensor_tensor(msk[:, :], lt[:, :], -1.0e4, sc[:, :],
                                                          ALU.mult, ALU.add), inc=ctx.vS)
        msk3 = msk.ap().rearrange("p (t e) -> p t e", e=NE)
        ctx.op("vector", lambda v: v.tensor_reduce(m2[:, :], msk3, AX.X, ALU.max), inc=ctx.vS)
        ctx.op("vector", lambda v: v.tensor_sub(dd[:, :], m2[:, :], m1[:, :]), inc=ctx.vS)
        ctx.op("scalar", lambda sx: sx.activation(out=dd[:, :], in_=dd[:, :], func=AF.Exp),
               inc=ctx.sS, waits=[(ctx.vS, ctx.vS.count)])
        ctx.op("vector", lambda v: v.tensor_scalar_add(w1v[:, :], dd[:, :], 1.0), inc=ctx.vS,
               waits=[(ctx.sS, ctx.sS.count)])
        ctx.op("vector", lambda v: v.reciprocal(w1v[:, :], w1v[:, :]), inc=ctx.vS)
        ctx.op("vector", lambda v: v.tensor_mul(w2v[:, :], dd[:, :], w1v[:, :]), inc=ctx.vS)
        eq3 = eq.ap().rearrange("p (t e) -> p t e", e=NE)
        wsb3 = wsb.ap().rearrange("p (t e) -> p t e", e=NE)
        w1bc = w1v.ap().unsqueeze(2).broadcast_to([128, 32, NE])
        w2bc = w2v.ap().unsqueeze(2).broadcast_to([128, 32, NE])
        m2bc = m2.ap().unsqueeze(2).broadcast_to([128, 32, NE])
        ctx.op("vector", lambda v: v.tensor_tensor(eq3, sc3, m1bc, ALU.is_equal), inc=ctx.vS)
        ctx.op("vector", lambda v: v.tensor_tensor(wsb3, eq3, w1bc, ALU.mult), inc=ctx.vS)
        ctx.op("vector", lambda v: v.tensor_tensor(eq3, sc3, m2bc, ALU.is_equal), inc=ctx.vS)
        ctx.op("vector", lambda v: v.tensor_tensor(eq3, eq3, w2bc, ALU.mult), inc=ctx.vS)
        ctx.op("vector", lambda v: v.tensor_add(wsb[:, :], wsb[:, :], eq[:, :]), inc=ctx.vS)
        wsb_tok = ctx.vS.count
        for j in range(32):
            i, tile, (fsem, fneed) = ctx.pool.acquire()
            ctx.wait("tensor", fsem, fneed)
            ctx.wait("tensor", ctx.vS, wsb_tok)
            ctx.wait("tensor", ctx.lsS, ctx.lsS.count)
            ctx.mm.count += 1
            tk = ctx.mm.count

            def runt(t, j=j, tile=tile, sem=ctx.mm):
                t.transpose(tile[0:NE, 0:128], wsb[:, j * NE:(j + 1) * NE],
                            W["idf"][:, :]).then_inc(sem.h, 1)

            ctx.q["tensor"].append(runt)
            etk = ctx.op("vector", lambda v, tile=tile: v.tensor_copy(
                wrows[0:NE, :], tile[0:NE, 0:128]),
                inc=ctx.vS, waits=[(ctx.mm, tk), (ctx.ldS, 1)])
            ctx.pool.release(ctx, "vector", i, [(ctx.vS, etk)])
            ctx.dma("gpsimd", wf_d[0:NE, j * 128:(j + 1) * 128], wrows[0:NE, :], ctx.ldS,
                    waits=[(ctx.vS, etk)])
        wflat_tok = ctx.ldS.count
        GC = GELU_C
        GK = GELU_K
        mm_hist = {}
        w2_hist = {}
        for q in range(4):
            qsl = slice(3 + q * QW, 3 + (q + 1) * QW)
            bwq = ctx.barrier_waits()
            xq = 0
            for c in range(DC):
                xq = ctx.dma("sync", xnq[:, c * QW:(c + 1) * QW],
                             xn1_scr[c * 128:(c + 1) * 128, qsl], ctx.lsS, waits=bwq)
            ctx.op("gpsimd", lambda g: g.memset(acc[:, :], 0.0), waits=bwq)
            acc0_tok = ctx.gS.count
            for e in range(NE):
                for o in range(32):
                    par = (o % 2) * DC * 128
                    w1t = ctx.dma("gpsimd", w1b[:, par:par + DC * 128],
                                  W["w1_d"][e * D:(e + 1) * D, o * 128:(o + 1) * 128]
                                  .rearrange("(c p) n -> p c n", p=128), ctx.ldS,
                                  waits=[(ctx.mm, mm_hist.get(o % 2, 0))])
                    banks = []
                    for t2 in range(2):
                        bi, p, tk = mm_group(ctx, [(w1b[:, par + c * 128: par + (c + 1) * 128],
                                                    xnq[:, c * QW + t2 * 512: c * QW + t2 * 512 + 512])
                                                   for c in range(DC)],
                                             waits=[(ctx.ldS, w1t), (ctx.lsS, xq)])
                        banks.append((bi, p, tk))
                    mm_hist[o % 2] = ctx.mm.count
                    bcol = W["b1c"][:, e * 32 + o: e * 32 + o + 1]
                    (b0, p0, tk0), (b1, p1, tk1) = banks
                    ctx.op("scalar", lambda sx, p0=p0, bcol=bcol: sx.activation(
                        out=xbf[:, 0:512], in_=p0[:, 0:512], func=AF.Identity, bias=bcol),
                        inc=ctx.sS, waits=[(ctx.mm, tk0), (ctx.lsS, 1), (ctx.vS, ctx.vS.count)])
                    etk = ctx.op("scalar", lambda sx, p1=p1, bcol=bcol: sx.activation(
                        out=xbf[:, 512:1024], in_=p1[:, 0:512], func=AF.Identity, bias=bcol),
                        inc=ctx.sS, waits=[(ctx.mm, tk1)])
                    ctx.pool.release(ctx, "scalar", b0, [(ctx.sS, etk)])
                    ctx.pool.release(ctx, "scalar", b1, [(ctx.sS, etk)])
                    etk = ctx.op("scalar", lambda sx: sx.activation(
                        out=sqf[:, :], in_=xbf[:, :], func=AF.Square), inc=ctx.sS)
                    ctx.op("vector", lambda v: v.tensor_scalar(inn[:, :], sqf[:, :], GC, 1.0,
                                                               ALU.mult, ALU.add),
                           inc=ctx.vS, waits=[(ctx.sS, etk)])
                    ctx.op("vector", lambda v: v.tensor_mul(inn[:, :], inn[:, :], xbf[:, :]),
                           inc=ctx.vS)
                    stk = ctx.op("scalar", lambda sx: sx.activation(
                        out=inn[:, :], in_=inn[:, :], func=AF.Sigmoid, scale=GK),
                        inc=ctx.sS, waits=[(ctx.vS, ctx.vS.count)])
                    gdst = gsb[:, o * QW: o * QW + QW]
                    ctx.op("vector", lambda v, gdst=gdst: v.tensor_mul(gdst, inn[:, :], xbf[:, :]),
                           inc=ctx.vS, waits=[(ctx.sS, stk)])
                g_tok = ctx.vS.count
                for t2 in range(2):
                    wr0 = q * QW + t2 * 512
                    wl = ctx.dma("sync", wfrow[0:1, :], wf_d[e:e + 1, wr0:wr0 + 512], ctx.lsS,
                                 waits=[(ctx.ldS, wflat_tok), (ctx.mm, ctx.mm.count)])
                    bi, p, tk = mm_group(ctx, [(onerow[0:1, :], wfrow[0:1, :])],
                                         waits=[(ctx.lsS, wl), (ctx.gS, ms_tok)])
                    etk = ctx.op("vector", lambda v, p=p, t2=t2: v.tensor_copy(
                        wbc[:, t2 * 512:(t2 + 1) * 512], p[:, 0:512]),
                        inc=ctx.vS, waits=[(ctx.mm, tk)])
                    ctx.pool.release(ctx, "vector", bi, [(ctx.vS, etk)])
                    for oc in range(DC):
                        par2 = (oc % 2) * 32 * 128
                        w2t = ctx.dma("gpsimd", w2b[:, par2:par2 + 32 * 128],
                                      W["w2_d"][e * HP: (e + 1) * HP, oc * 128:(oc + 1) * 128]
                                      .rearrange("(k p) n -> p k n", p=128), ctx.ldS,
                                      waits=[(ctx.mm, w2_hist.get(oc % 2, 0))])
                        bi, p, tk = mm_group(ctx, [(w2b[:, par2 + k * 128: par2 + (k + 1) * 128],
                                                    gsb[:, k * QW + t2 * 512: k * QW + t2 * 512 + 512])
                                                   for k in range(32)],
                                             waits=[(ctx.ldS, w2t), (ctx.vS, g_tok)])
                        w2_hist[oc % 2] = ctx.mm.count
                        b2col = W["b2c"][:, e * 8 + oc: e * 8 + oc + 1]
                        ctx.op("vector", lambda v, p=p, b2col=b2col, t2=t2: v.scalar_tensor_tensor(
                            yw[:, :], p[:, 0:512], b2col, wbc[:, t2 * 512:(t2 + 1) * 512],
                            ALU.add, ALU.mult),
                            inc=ctx.vS, waits=[(ctx.mm, tk), (ctx.lsS, 1), (ctx.gS, acc0_tok)])
                        adst = acc[:, oc * QW + t2 * 512: oc * QW + t2 * 512 + 512]
                        etk = ctx.op("vector", lambda v, adst=adst: v.tensor_add(adst, adst, yw[:, :]),
                                     inc=ctx.vS)
                        ctx.pool.release(ctx, "vector", bi, [(ctx.vS, etk)])
            for oc in range(DC):
                for t2 in range(2):
                    etk = ctx.op("scalar", lambda sx, oc=oc, t2=t2: sx.activation(
                        out=ev1[:, :], in_=acc[:, oc * QW + t2 * 512: oc * QW + t2 * 512 + 512],
                        func=AF.Silu),
                        inc=ctx.sS, waits=[(ctx.vS, ctx.vS.count), (ctx.stS, 1)])
                    ctx.dma("sync", g1_dst[oc * 128:(oc + 1) * 128,
                                           q * QW + t2 * 512: q * QW + t2 * 512 + 512],
                            ev1[:, :], ctx.stS, waits=[(ctx.sS, etk)])
    emit_barrier(ctx)


def emit_final2(ctx, W, tag, y3_scr, x1_scr, g1_scr, out_dst):
    NT = 512
    with ctx.scope():
        wo = ctx.sbuf(f"{tag}f_wo", [128, 16 * D], BF16)
        y3t = ctx.sbuf(f"{tag}f_y3", [128, 16 * NT], BF16)
        x1t = ctx.sbuf(f"{tag}f_x1", [128, NT])
        g1t = ctx.sbuf(f"{tag}f_g1", [128, NT])
        ev = ctx.sbuf(f"{tag}f_ev", [128, NT])
        wt = 0
        for k in range(16):
            wt = ctx.dma("gpsimd", wo[:, k * D:(k + 1) * D], W["wo_d"][k * 128:(k + 1) * 128, :],
                         ctx.ldS)
        for ti in range(TQ // NT):
            tsl = slice(ti * NT, (ti + 1) * NT)
            tw = ctx.barrier_waits()
            yw = 0
            for k in range(16):
                yw = ctx.dma("sync", y3t[:, k * NT:(k + 1) * NT], y3_scr[k * 128:(k + 1) * 128, tsl],
                             ctx.lsS, waits=tw)
            for o in range(DC):
                bi, p, tk = mm_group(ctx, [(wo[:, k * D + o * 128: k * D + (o + 1) * 128],
                                            y3t[:, k * NT:(k + 1) * NT]) for k in range(16)],
                                     waits=[(ctx.ldS, wt), (ctx.lsS, yw)])
                lx = ctx.dma("sync", x1t[:, :], x1_scr[o * 128:(o + 1) * 128, tsl], ctx.lsS,
                             waits=[(ctx.vS, ctx.vS.count)])
                lx = ctx.dma("sync", g1t[:, :], g1_scr[o * 128:(o + 1) * 128, tsl], ctx.lsS)
                ctx.op("vector", lambda v, p=p: v.tensor_mul(ev[:, :], p[:, 0:NT], g1t[:, :]),
                       inc=ctx.vS, waits=[(ctx.mm, tk), (ctx.lsS, lx), (ctx.stS, 1)])
                etk = ctx.op("vector", lambda v: v.tensor_add(ev[:, :], ev[:, :], x1t[:, :]),
                             inc=ctx.vS)
                ctx.pool.release(ctx, "vector", bi, [(ctx.vS, etk)])
                ctx.dma("sync", out_dst[o * 128:(o + 1) * 128, tsl], ev[:, :], ctx.stS,
                        waits=[(ctx.vS, etk)])
    ctx.wait("sync", ctx.stS, ctx.stS.count)
    ctx.wait("gpsimd", ctx.ldS, ctx.ldS.count)


def build_fused():
    ctx = Ctx()
    nc = ctx.nc
    TH = TQ + 3
    DI = 2048

    def din(name, shape, dt=F32):
        return nc.dram_tensor(name, shape, dt, kind="ExternalInput").ap()

    x_full = din("x_full", [D, T])
    x_own = din("x_own", [D, TH])
    x_dup = din("x_dup", [D, TH])
    sw1_d = din("sw1", [D, 512])
    sw2_d = din("sw2", [512, 128])
    bw_d = din("bw", [D, 128])
    dw_d = din("dw", [D, D])
    gw_d = din("gw", [D, D])
    cw_d = din("cw", [128, D])
    sb1_d = din("sb1", [128, 4])
    sb2_d = din("sb2", [128, 1])
    bb_d = din("bb", [128, 1])
    cdb_d = din("cdb", [128, 16])
    gb_d = din("gb", [128, 8])
    A_d = din("A", [128, 128])
    I_d = din("ident", [128, 128])
    wg_d = din("wgd", [D, 3072], I32)
    wu_d = din("wud", [D, 3072], I32)
    wd_d = din("wdd", [3072, D], I32)
    win_d = din("win", [D, 2 * DI])
    convc_d = din("convc", [128, 64])
    convb_d = din("convb", [128, 16])
    wx_d = din("wx", [DI, 96])
    wdt_d = din("wdt", [64, DI])
    bdt_d = din("bdt", [128, 16])
    alog_d = din("alog", [DI, 16])
    dv_d = din("dv", [128, 16])
    wo_d = din("wo", [DI, D])
    w1_d = din("w1", [8 * D, 4096])
    w2_d = din("w2", [8 * 4096, D])
    b1_d = din("b1", [128, 8 * 32])
    b2_d = din("b2", [128, 8 * 8])
    wr_d = din("wr", [D, 8])
    br_d = din("br", [1, 8])
    cfg_d = din("cfg", [1, 16], I32)
    mg_d = din("mg", [1, 4], I32)

    out_o = nc.dram_tensor("out", [D, TQ], F32, kind="ExternalOutput").ap()

    import os as _os
    _dbg_set = set((_os.environ.get("KF_DBG") or "").split(",")) - {""}
    _nocomm = bool(_os.environ.get("KF_NOCOMM"))

    def scr(name, shape, dt=F32):
        kind = "ExternalOutput" if name in _dbg_set else "Internal"
        return nc.dram_tensor(name, shape, dt, kind=kind).ap()

    u_s = scr("u_s", [128, T], BF16)
    hs_s = scr("hs_s", [128, T + 3], BF16)
    hs_t = hs_s.tensor
    ios = {}
    for tg in ("d", "o"):
        ios[tg] = {
            "xn": scr(f"{tg}_xn_s", [D, TH], BF16),
            "g0": scr(f"{tg}_g0_s", [D, TH]),
            "x1": scr(f"{tg}_x1_s", [D, TQ]),
            "xn1": scr(f"{tg}_xn1_s", [D, TH], BF16),
            "uc": scr(f"{tg}_uc_s", [DI, TQ]),
            "ucb": scr(f"{tg}_ucb_s", [DI, TQ], BF16),
            "dt": scr(f"{tg}_dt_s", [DI, TQ]),
            "bct": scr(f"{tg}_bct_s", [TQ, 32]),
        }
    ios["o"]["sz"] = scr("o_sz_s", [DI, TQ])
    ios["o"]["y3"] = scr("o_y3_s", [DI, TQ], BF16)
    g1_s = scr("g1_s", [D, TQ])
    if "dbgw" in _dbg_set:
        W_dbgw = nc.dram_tensor("dbgw", [128, 3072], BF16, kind="ExternalOutput").ap()
    else:
        W_dbgw = None
    shp = nc.dram_tensor("shp", [7 * 128, 256], F32, addr_space="Shared")
    flg = nc.dram_tensor("flgs", [1, 64], I32, addr_space="Shared")

    ctx.pool = PsumPool(ctx)
    W = {}
    gw_sb = {}
    cw = ctx.sbuf("g_cw", [128, D], BF16)
    wxb = ctx.sbuf("g_wxb", [128, 16 * 96], BF16)
    wdtb = ctx.sbuf("g_wdtb", [64, DI], BF16)
    wrb = ctx.sbuf("g_wrb", [128, DC * 8], BF16)
    ctx.dma("gpsimd", cw[:, :], cw_d[:, :], ctx.ldS)
    ctx.dma("gpsimd", wxb[:, :], wx_d.rearrange("(c p) n -> p c n", p=128), ctx.ldS)
    ctx.dma("gpsimd", wdtb[0:64, :], wdt_d[:, :], ctx.ldS)
    ctx.dma("gpsimd", wrb[:, :], wr_d.rearrange("(c p) e -> p c e", p=128), ctx.ldS)
    bias_tiles = {}
    for nm, src, w_ in (("sb1", sb1_d, 4), ("sb2", sb2_d, 1), ("bb", bb_d, 1), ("cdb", cdb_d, 16),
                        ("gb", gb_d, 8), ("convc", convc_d, 64), ("convb", convb_d, 16),
                        ("bdt", bdt_d, 16), ("idf", I_d, 128), ("b1c", b1_d, 256),
                        ("b2c", b2_d, 64), ("dvc", dv_d, 16)):
        t = ctx.sbuf(f"g_{nm}", [128, w_])
        ctx.dma("sync", t[:, :], src[:, :], ctx.lsS)
        bias_tiles[nm] = t
    brr = ctx.sbuf("g_brr", [1, 8])
    ctx.dma("sync", brr[0:1, :], br_d[0:1, :], ctx.lsS)
    cfg_sb = ctx.sbuf("g_cfg", [1, 16], I32)
    mg_sb = ctx.sbuf("g_mg", [1, 4], I32)
    ctx.dma("sync", cfg_sb[0:1, :], cfg_d[0:1, :], ctx.lsS)
    ctx.dma("sync", mg_sb[0:1, :], mg_d[0:1, :], ctx.lsS)
    asb = ctx.sbuf("g_asb", [128, 16 * 16])
    ctx.dma("gpsimd", asb[:, :], alog_d.rearrange("(g p) s -> p g s", p=128), ctx.ldS)
    al_tok = ctx.ldS.count
    ctx.op("scalar", lambda sx: sx.activation(out=asb[:, :], in_=asb[:, :], func=AF.Exp),
           inc=ctx.sS, waits=[(ctx.ldS, al_tok), (ctx.lsS, ctx.lsS.count)])
    ctx.op("vector", lambda v: v.tensor_scalar_mul(asb[:, :], asb[:, :], -1.0), inc=ctx.vS,
           waits=[(ctx.sS, ctx.sS.count)])
    ydb = ctx.sbuf("g_ydb", [128, 8])
    ctx.op("vector", lambda v: v.tensor_add(ydb[:, :], bias_tiles["cdb"][:, 0:8],
                                            bias_tiles["cdb"][:, 8:16]), inc=ctx.vS)
    zt = ctx.sbuf("g_zt", [128, 256])
    dupstate = ctx.sbuf("g_dst", [128, 256])
    ownstate = ctx.sbuf("g_ost", [128, 256])
    Hini = ctx.sbuf("g_hini", [128, 256])
    ctmp1 = ctx.sbuf("g_ct1", [128, 256])
    ctmp2 = ctx.sbuf("g_ct2", [128, 256])
    ctmp3 = ctx.sbuf("g_ct3", [128, 256])
    lam = ctx.sbuf("g_lam", [128, 256])
    dtsum = ctx.sbuf("g_dts", [128, 16])
    flgsb = ctx.sbuf("g_flgsb", [1, 1], I32)
    ctx.op("gpsimd", lambda g: g.memset(zt[:, :], 0.0), inc=ctx.gS)
    zt_tok = ctx.gS.count

    W["_dbgw"] = W_dbgw
    W.update({"sw1_d": sw1_d, "sw2_d": sw2_d, "bw_d": bw_d, "cw": cw, "wxb": wxb, "wdtb": wdtb,
              "wrb": wrb, "brr": brr, "ydb": ydb, "asb": asb,
              "wg_d": wg_d, "wu_d": wu_d, "wd_d": wd_d, "win_d": win_d,
              "dw_d": dw_d, "gw_d": gw_d, "wo_d": wo_d, "w1_d": w1_d, "w2_d": w2_d})
    for k, v in bias_tiles.items():
        W[k] = v
    emit_barrier(ctx)

    cfg_i = cfg_sb
    def pub_zeros(g):
        g.dma_start(out=bass.AP(shp, 6 * 128 * 256, [[256, 128], [1, 256]]),
                    in_=zt[:, :]).then_inc(ctx.ldS.h, 16)
    ctx.wait("gpsimd", ctx.gS, zt_tok)
    ctx.wait("gpsimd", ctx.lsS, ctx.lsS.count)
    if not _nocomm:
        ctx.ldS.count += 16
        _zdone = ctx.ldS.count
        ctx.q["gpsimd"].append(pub_zeros)
    else:
        _zdone = ctx.ldS.count

    def pub_hello(g, _zdone=_zdone):
        g.wait_ge(ctx.ldS.h, _zdone)
        with g.register("ph1") as r1, g.register("ph2") as r2:
            g.load(r1, cfg_i[0:1, 8:9])
            g.load(r2, cfg_i[0:1, 9:10])
            g.dma_start(out=bass.AP(flg, r1, [[1, 1], [1, 1]]), in_=mg_sb[0:1, 1:2]).then_inc(ctx.ldS.h, 16)
            g.dma_start(out=bass.AP(flg, r2, [[1, 1], [1, 1]]), in_=mg_sb[0:1, 2:3]).then_inc(ctx.ldS.h, 16)
    if not _nocomm:
        ctx.ldS.count += 32
        ctx.q["gpsimd"].append(pub_hello)

    hs_own = scr("hs_own", [128, TQ + 3], BF16)
    hs_dup = scr("hs_dup", [128, TQ + 3], BF16)
    emit_l0_upath(ctx, W, x_full, T, u_s)
    zpad = ctx.sbuf("g_zpad", [128, 3], BF16)
    ctx.op("gpsimd", lambda g: g.memset(zpad[:, :], 0.0), inc=ctx.gS)
    ctx.dma("gpsimd", hs_s[:, 0:3], zpad[:, :], ctx.ldS, waits=[(ctx.gS, ctx.gS.count)])
    emit_l0_scan(ctx, u_s, A_d, I_d, hs_s[:, 3:])

    def hs_windows(g):
        with g.register("whoff") as ro, g.register("whof2") as rd:
            g.load(ro, cfg_i[0:1, 1:2])
            g.load(rd, cfg_i[0:1, 0:1])
            g.dma_start(out=hs_own[:, :],
                        in_=bass.AP(hs_t, ro, [[T + 3, 128], [1, TQ + 3]])).then_inc(ctx.ldS.h, 16)
            g.dma_start(out=hs_dup[:, :],
                        in_=bass.AP(hs_t, rd, [[T + 3, 128], [1, TQ + 3]])).then_inc(ctx.ldS.h, 16)
    ctx.wait("gpsimd", ctx.stS, ctx.stS.count)
    ctx.wait("gpsimd", ctx.lsS, ctx.lsS.count)
    if _nocomm:
        ctx.dma("gpsimd", hs_own[:, :], hs_s[:, 0:TQ + 3], ctx.ldS)
        ctx.dma("gpsimd", hs_dup[:, :], hs_s[:, 0:TQ + 3], ctx.ldS)
    else:
        ctx.ldS.count += 32
        ctx.q["gpsimd"].append(hs_windows)
    emit_barrier(ctx)

    emit_quarter_pipeline(ctx, W, "d", x_dup, hs_dup,
                          ios["d"]["xn"], ios["d"]["g0"],
                          ios["d"]["x1"], ios["d"]["xn1"], ios["d"], with_z=False)
    with ctx.scope():
        dtrow = ctx.sbuf("s_dtrow", [128, TQ])
        for g_ in range(16):
            lw = ctx.dma("sync", dtrow[:, :], ios["d"]["dt"][g_ * 128:(g_ + 1) * 128, :], ctx.lsS,
                         waits=[(ctx.vS, ctx.vS.count)])
            ctx.op("vector", lambda v, g_=g_, dtrow=dtrow: v.tensor_reduce(
                dtsum[:, g_:g_ + 1], dtrow.ap().unsqueeze(1), AX.X, ALU.add),
                inc=ctx.vS, waits=[(ctx.lsS, lw)])
    emit_barrier(ctx)
    a3v = asb.ap().rearrange("p (g s) -> p g s", s=16)
    l3v = lam.ap().rearrange("p (g s) -> p g s", s=16)
    dbc = dtsum.ap().unsqueeze(2).broadcast_to([128, 16, 16])
    ctx.op("vector", lambda v: v.tensor_tensor(l3v, a3v, dbc, ALU.mult), inc=ctx.vS)
    lam_tok = ctx.vS.count
    emit_scan2(ctx, W, "ds", ios["d"], with_y=False, init_tile=None, state_out=dupstate)
    dstate_tok = ctx.vS.count

    ctx.wait("gpsimd", ctx.vS, max(lam_tok, dstate_tok))
    _base = ctx.ldS.count

    def pub_r1(g, _base=_base):
        with g.register("pr1a") as ra, g.register("pr1b") as rb, g.register("pr1c") as rc:
            g.load(ra, cfg_i[0:1, 2:3])
            g.reg_alu(ra, ra, 32768, ALU.mult)
            g.load(rb, cfg_i[0:1, 3:4])
            g.reg_alu(rb, rb, 32768, ALU.mult)
            g.load(rc, cfg_i[0:1, 10:11])
            g.dma_start(out=bass.AP(shp, ra, [[256, 128], [1, 256]]),
                        in_=dupstate[:, :]).then_inc(ctx.ldS.h, 16)
            g.dma_start(out=bass.AP(shp, rb, [[256, 128], [1, 256]]),
                        in_=lam[:, :]).then_inc(ctx.ldS.h, 16)
            g.wait_ge(ctx.ldS.h, _base + 32)
            g.dma_start(out=bass.AP(flg, rc, [[1, 1], [1, 1]]), in_=mg_sb[0:1, 1:2]).then_inc(ctx.ldS.h, 16)
    if not _nocomm:
        ctx.ldS.count += 48
        ctx.q["gpsimd"].append(pub_r1)

    emit_quarter_pipeline(ctx, W, "o", x_own, hs_own, ios["o"]["xn"], ios["o"]["g0"],
                          ios["o"]["x1"], ios["o"]["xn1"], ios["o"], with_z=True)
    emit_moe2(ctx, W, "m", ios["o"]["xn1"], g1_s)

    emit_barrier(ctx)

    psem1 = ctx.sem(dma=True)

    def wait_r1(g):
        with g.register("nd1") as nd, g.register("tgt1") as tgt, g.register("fv1") as fv, \
             g.register("ix1") as ix, g.register("mg1") as mgv:
            g.load(ix, cfg_i[0:1, 11:12])
            g.load(mgv, mg_sb[0:1, 1:2])
            g.reg_mov(nd, 1)
            g.reg_mov(tgt, 0)
            with g.While(nd):
                g.reg_add(tgt, tgt, 16)
                g.dma_start(out=flgsb[0:1, 0:1],
                            in_=bass.AP(flg, ix, [[1, 1], [1, 1]])).then_inc(psem1.h, 16)
                g.wait_ge(psem1.h, tgt)
                g.load(fv, flgsb[0:1, 0:1])
                g.reg_alu(nd, fv, mgv, ALU.not_equal)

    if not _nocomm:
        ctx.q["gpsimd"].append(wait_r1)
    ctx.op("gpsimd", lambda g: g.nop(), inc=ctx.gS)
    r1w_tok = ctx.gS.count

    def rd_r1(g):
        with g.register("rr1a") as ra, g.register("rr1b") as rb, g.register("rr1c") as rc:
            g.load(ra, cfg_i[0:1, 4:5])
            g.reg_alu(ra, ra, 32768, ALU.mult)
            g.load(rb, cfg_i[0:1, 5:6])
            g.reg_alu(rb, rb, 32768, ALU.mult)
            g.load(rc, cfg_i[0:1, 6:7])
            g.reg_alu(rc, rc, 32768, ALU.mult)
            g.dma_start(out=ctmp1[:, :], in_=bass.AP(shp, ra, [[256, 128], [1, 256]])).then_inc(ctx.ldS.h, 16)
            g.dma_start(out=ctmp2[:, :], in_=bass.AP(shp, rb, [[256, 128], [1, 256]])).then_inc(ctx.ldS.h, 16)
            g.dma_start(out=ctmp3[:, :], in_=bass.AP(shp, rc, [[256, 128], [1, 256]])).then_inc(ctx.ldS.h, 16)
    ctx.wait("gpsimd", ctx.gS, r1w_tok)
    if not _nocomm:
        ctx.ldS.count += 48
        ctx.q["gpsimd"].append(rd_r1)
    else:
        ctx.op("gpsimd", lambda g: g.memset(ctmp1[:, :], 0.0), inc=ctx.gS)
        ctx.op("gpsimd", lambda g: g.memset(ctmp2[:, :], 0.0), inc=ctx.gS)
        ctx.op("gpsimd", lambda g: g.memset(ctmp3[:, :], 0.0), inc=ctx.gS)
    r1r_tok = ctx.ldS.count
    ctx.op("scalar", lambda sx: sx.activation(out=ctmp1[:, :], in_=ctmp1[:, :], func=AF.Exp),
           inc=ctx.sS, waits=[(ctx.ldS, r1r_tok), (ctx.gS, r1w_tok)])
    ctx.op("vector", lambda v: v.tensor_mul(Hini[:, :], ctmp1[:, :], ctmp2[:, :]), inc=ctx.vS,
           waits=[(ctx.sS, ctx.sS.count)])
    ctx.op("vector", lambda v: v.tensor_add(Hini[:, :], Hini[:, :], ctmp3[:, :]), inc=ctx.vS)
    psem2 = ctx.sem(dma=True)

    def wait_r2(g):
        with g.register("nd2") as nd, g.register("tgt2") as tgt, g.register("fv2") as fv, \
             g.register("ix2") as ix, g.register("mg2") as mgv:
            g.load(ix, cfg_i[0:1, 12:13])
            g.load(mgv, mg_sb[0:1, 2:3])
            g.reg_mov(nd, 1)
            g.reg_mov(tgt, 0)
            with g.While(nd):
                g.reg_add(tgt, tgt, 16)
                g.dma_start(out=flgsb[0:1, 0:1],
                            in_=bass.AP(flg, ix, [[1, 1], [1, 1]])).then_inc(psem2.h, 16)
                g.wait_ge(psem2.h, tgt)
                g.load(fv, flgsb[0:1, 0:1])
                g.reg_alu(nd, fv, mgv, ALU.not_equal)

    if not _nocomm:
        ctx.q["gpsimd"].append(wait_r2)
    ctx.op("gpsimd", lambda g: g.nop(), inc=ctx.gS)
    r2w_tok = ctx.gS.count

    def rd_r2(g):
        with g.register("rr2") as ra:
            g.load(ra, cfg_i[0:1, 7:8])
            g.reg_alu(ra, ra, 32768, ALU.mult)
            g.dma_start(out=ctmp1[:, :], in_=bass.AP(shp, ra, [[256, 128], [1, 256]])).then_inc(ctx.ldS.h, 16)
    ctx.wait("gpsimd", ctx.gS, r2w_tok)
    ctx.wait("gpsimd", ctx.sS, ctx.sS.count)
    if not _nocomm:
        ctx.ldS.count += 16
        ctx.q["gpsimd"].append(rd_r2)
    r2r_tok = ctx.ldS.count
    hini_tok = ctx.op("vector", lambda v: v.tensor_add(Hini[:, :], Hini[:, :], ctmp1[:, :]),
                      inc=ctx.vS, waits=[(ctx.ldS, r2r_tok)])
    emit_barrier(ctx)

    emit_scan2(ctx, W, "os", ios["o"], with_y=True, init_tile=Hini, state_out=ownstate)
    ostate_tok = ctx.vS.count

    ctx.wait("gpsimd", ctx.vS, ostate_tok)
    _b2 = ctx.ldS.count
    if _nocomm:
        _b2 = None

    def pub_r2(g, _b2=_b2):
        with g.register("p2a") as ra, g.register("p2b") as rb:
            g.load(ra, cfg_i[0:1, 13:14])
            g.reg_alu(ra, ra, 32768, ALU.mult)
            g.load(rb, cfg_i[0:1, 14:15])
            g.dma_start(out=bass.AP(shp, ra, [[256, 128], [1, 256]]),
                        in_=ownstate[:, :]).then_inc(ctx.ldS.h, 16)
            g.wait_ge(ctx.ldS.h, _b2 + 16)
            g.dma_start(out=bass.AP(flg, rb, [[1, 1], [1, 1]]), in_=mg_sb[0:1, 2:3]).then_inc(ctx.ldS.h, 16)
    if not _nocomm:
        ctx.ldS.count += 32
        ctx.q["gpsimd"].append(pub_r2)

    emit_final2(ctx, W, "f", ios["o"]["y3"], ios["o"]["x1"], g1_s, out_o)
    return ctx


def host_fused(inputs, core, magic):
    b, lid = core // 4, core % 4
    p = lid % 2
    x = np.asarray(inputs["L0_attn_x"])[b]
    xf = np.ascontiguousarray(x.T).astype(np.float32)

    def xslice(q):
        s = q * TQ
        halo = np.zeros((D, 3), np.float32) if s == 0 else xf[:, s - 3:s]
        return np.ascontiguousarray(np.concatenate([halo, xf[:, s:s + TQ]], 1))

    dupq = lid % 2
    W1 = np.asarray(inputs["L1_ffn_W1"]).astype(np.float32)
    W2 = np.asarray(inputs["L1_ffn_W2"]).astype(np.float32)
    b1 = np.asarray(inputs["L1_ffn_b1"]).astype(np.float32)
    b2 = np.asarray(inputs["L1_ffn_b2"]).astype(np.float32)
    conv_w = np.asarray(inputs["L1_attn_conv_w"]).astype(np.float32)
    cfg = np.zeros(16, np.int32)
    cfg[0] = dupq * TQ
    cfg[1] = lid * TQ
    cfg[2] = p * 3 + 0
    cfg[3] = p * 3 + 1
    if lid == 2:
        cfg[4], cfg[5], cfg[6] = (1 - p) * 3 + 1, p * 3 + 0, (1 - p) * 3 + 0
    else:
        cfg[4], cfg[5], cfg[6] = 6, 6, 6
    cfg[7] = (1 - p) * 3 + 2 if lid in (1, 3) else 6
    cfg[8] = p * 8 + 3
    cfg[9] = p * 8 + 4
    cfg[10] = p * 8 + 1
    cfg[11] = (1 - p) * 8 + 1 if lid == 2 else p * 8 + 3
    cfg[12] = (1 - p) * 8 + 2 if lid in (1, 3) else p * 8 + 4
    cfg[13] = p * 3 + 2
    cfg[14] = p * 8 + 2
    return {
        "x_full": xf,
        "x_own": xslice(lid),
        "x_dup": xslice(dupq),
        "sw1": _fm(inputs["L0_attn_s_w1"]).astype(np.float32),
        "sw2": _fm(inputs["L0_attn_s_w2"]).astype(np.float32),
        "bw": _fm(inputs["L0_attn_B_w"]).astype(np.float32),
        "dw": _fm(inputs["L0_attn_D_w"]).astype(np.float32),
        "gw": _fm(inputs["L0_attn_g_w"]).astype(np.float32),
        "cw": np.ascontiguousarray(np.asarray(inputs["L0_attn_C_w"]).T).astype(np.float32),
        "sb1": _bcols(inputs["L0_attn_s_b1"], 4),
        "sb2": _bcols(inputs["L0_attn_s_b2"], 1),
        "bb": _bcols(inputs["L0_attn_B_b"], 1),
        "cdb": np.concatenate([_bcols(inputs["L0_attn_C_b"], 8), _bcols(inputs["L0_attn_D_b"], 8)], 1),
        "gb": _bcols(inputs["L0_attn_g_b"], 8),
        "A": np.ascontiguousarray(np.asarray(inputs["L0_attn_A"]).astype(np.float32)),
        "ident": np.eye(128, dtype=np.float32),
        "wgd": _fm(inputs["L0_ffn_w_gate"]).astype(np.int32),
        "wud": _fm(inputs["L0_ffn_w_up"]).astype(np.int32),
        "wdd": _fm(inputs["L0_ffn_w_down"]).astype(np.int32),
        "win": np.ascontiguousarray(np.asarray(inputs["L1_attn_W_in"]).astype(np.float32).T),
        "convc": np.ascontiguousarray(conv_w.reshape(16, 128, 4).transpose(1, 0, 2).reshape(128, 64)),
        "convb": _bcols(inputs["L1_attn_conv_b"], 16),
        "wx": np.ascontiguousarray(np.asarray(inputs["L1_attn_W_x"]).astype(np.float32).T),
        "wdt": np.ascontiguousarray(np.asarray(inputs["L1_attn_W_dt"]).astype(np.float32).T),
        "bdt": _bcols(inputs["L1_attn_b_dt"], 16),
        "alog": np.ascontiguousarray(np.asarray(inputs["L1_attn_A_log"]).astype(np.float32)),
        "dv": np.ascontiguousarray(np.asarray(inputs["L1_attn_D"]).astype(np.float32).reshape(16, 128).T),
        "wo": np.ascontiguousarray(np.asarray(inputs["L1_attn_W_out"]).astype(np.float32).T),
        "w1": np.ascontiguousarray(W1.reshape(8 * 1024, 4096)),
        "w2": np.ascontiguousarray(W2.reshape(8 * 4096, 1024)),
        "b1": np.ascontiguousarray(np.concatenate([_bcols(b1[e], 32) for e in range(8)], 1)),
        "b2": np.ascontiguousarray(np.concatenate([_bcols(b2[e], 8) for e in range(8)], 1)),
        "wr": np.ascontiguousarray(np.asarray(inputs["L1_ffn_Wr"]).astype(np.float32)),
        "br": np.ascontiguousarray(np.asarray(inputs["L1_ffn_br"]).astype(np.float32).reshape(1, 8)),
        "cfg": cfg.reshape(1, 16),
        "mg": np.asarray([magic, magic + 1, magic + 2, magic + 3], np.int32).reshape(1, 4),
    }


def kernel_fused(**inputs):
    import random
    magic = random.randint(10_000_000, 2_000_000_000)
    ctx = build_fused()
    maps = [host_fused(inputs, c, magic) for c in range(NCORES)]
    res = ctx.run(maps)
    out = np.empty((2, T, D), np.float32)
    for c in range(NCORES):
        b, q = c // 4, c % 4
        out[b, q * TQ:(q + 1) * TQ, :] = np.asarray(res[c]["out"]).T
    return out


# version 05
# speedup vs baseline: 1.2418x; 1.0462x over previous
import contextlib
import numpy as np
import ml_dtypes
import concourse.bass as bass
import concourse.mybir as mybir
from concourse.bass_utils import run_bass_kernel_spmd

F32 = mybir.dt.float32
BF16 = mybir.dt.bfloat16
I32 = mybir.dt.int32
AF = mybir.ActivationFunctionType
ALU = mybir.AluOpType
AX = mybir.AxisListType
BF = ml_dtypes.bfloat16

NCORES = 8
T = 16384
TQ = T // 4
D = 1024
DC = D // 128
ENGINES = ("tensor", "vector", "scalar", "gpsimd", "sync")

GELU_K = 1.5957691216057308
GELU_C = 0.044715
LN_EPS = 1e-5
DEQ = 1.0 / 127.0
C0 = 64


class Sem:
    __slots__ = ("h", "count", "dma")

    def __init__(self, h, dma=False):
        self.h = h
        self.count = 0
        self.dma = dma


class Ctx:
    def __init__(self):
        self.nc = bass.Bass("TRN2", target_bir_lowering=False)
        self.es = contextlib.ExitStack()
        self.q = {e: [] for e in ENGINES}
        self.hwm = {e: {} for e in ENGINES}
        self._semi = 0
        self.sems = []
        self.mm = self.sem()
        self.vS = self.sem()
        self.sS = self.sem()
        self.gS = self.sem()
        self.ldS = self.sem(dma=True)
        self.lsS = self.sem(dma=True)
        self.stS = self.sem(dma=True)
        self.pool = None
        self._scan_war = {}
        self._scope = None
        self.serial = False
        self.eng_sem = {"vector": self.vS, "scalar": self.sS, "gpsimd": self.gS}
        self.eng_last = {"vector": 0, "scalar": 0, "gpsimd": 0}
        import os as _os
        self.auto_serial = not _os.environ.get("KNOSER")

    def sem(self, dma=False):
        self._semi += 1
        s = Sem(self.es.enter_context(self.nc.semaphore(f"s{self._semi}")), dma)
        self.sems.append(s)
        return s

    def sbuf(self, name, shape, dt=F32):
        tgt = self._scope if self._scope is not None else self.es
        return tgt.enter_context(self.nc.sbuf_tensor(name, shape, dt))

    def scope(self):
        import contextlib as _cl

        @_cl.contextmanager
        def _s():
            old = self._scope
            self._scope = contextlib.ExitStack()
            try:
                yield
            finally:
                self._scope.close()
                self._scope = old

        return _s()

    def wait(self, eng, sem, val):
        if sem.dma:
            val = sem.count
        if val <= 0:
            return
        hw = self.hwm[eng]
        if hw.get(id(sem), 0) >= val:
            return
        hw[id(sem)] = val
        self.q[eng].append(lambda e, s=sem, v=val: e.wait_ge(s.h, v))

    def op(self, eng, fn, inc=None, waits=()):
        if self.serial:
            waits = list(waits) + self.barrier_waits()
        es = self.eng_sem.get(eng)
        if es is not None:
            if self.auto_serial:
                self.wait(eng, es, self.eng_last[eng])
            if inc is None:
                inc = es
        for (s, v) in waits:
            self.wait(eng, s, v)
        if inc is None:
            self.q[eng].append(fn)
            return None
        inc.count += 1

        def run(e, fn=fn, s=inc):
            fn(e).then_inc(s.h, 1)

        self.q[eng].append(run)
        if inc is es and es is not None:
            self.eng_last[eng] = inc.count
        return inc.count

    def dma(self, eng, out, in_, sem, waits=(), **kw):
        if self.serial:
            waits = list(waits) + self.barrier_waits()
        for (s, v) in waits:
            self.wait(eng, s, v)
        sem.count += 16

        def run(e, out=out, in_=in_, s=sem, kw=kw):
            e.dma_start(out=out, in_=in_, **kw).then_inc(s.h, 16)

        self.q[eng].append(run)
        return sem.count

    def barrier_waits(self):
        return [(s, s.count) for s in (self.mm, self.vS, self.sS, self.gS, self.ldS, self.lsS, self.stS)]

    def compute_waits(self):
        return [(s, s.count) for s in (self.mm, self.vS, self.sS)]

    def run(self, in_maps):
        nc = self.nc
        with nc.Block() as block:
            for eng in ENGINES:
                ops = self.q[eng]

                def section(obj, ops=ops):
                    for fn in ops:
                        fn(obj)

                getattr(block, eng)(section)
        res = run_bass_kernel_spmd(nc, in_maps, core_ids=list(range(len(in_maps))))
        self.es.close()
        return res.results


class PsumPool:
    def __init__(self, ctx, n=8, cols=512):
        self.ctx = ctx
        self.tiles = [ctx.es.enter_context(ctx.nc.psum_tensor(f"ps{i}", [128, cols], F32)) for i in range(n)]
        self.free = [ctx.sem() for _ in range(n)]
        self.uses = [0] * n
        self.rr = 0
        self.n = n

    def acquire(self):
        i = self.rr % self.n
        self.rr += 1
        need = self.uses[i]
        self.uses[i] += 1
        return i, self.tiles[i], (self.free[i], need)

    def release(self, ctx, eng, i, waits):
        ctx.op(eng, lambda e: e.nop(), inc=self.free[i], waits=waits)


def mm_group(ctx, pairs, out_slice=None, waits=(), psum=None):
    pool = ctx.pool
    i, tile, (fsem, fneed) = pool.acquire()
    ctx.wait("tensor", fsem, fneed)
    if ctx.serial:
        waits = list(waits) + ctx.barrier_waits()
    for (s, v) in waits:
        ctx.wait("tensor", s, v)
    n = len(pairs)
    ctx.mm.count += 1
    tok = ctx.mm.count

    def run(t, pairs=pairs, tile=tile, sem=ctx.mm, out_slice=out_slice, n=n):
        dst = tile[:, 0:pairs[0][1].shape[-1]] if out_slice is None else tile[out_slice[0], out_slice[1]]
        inst = None
        for j, (lh, rh) in enumerate(pairs):
            inst = t.matmul(dst, lh, rh, start=(j == 0), stop=(j == n - 1))
        inst.then_inc(sem.h, 1)

    ctx.q["tensor"].append(run)
    return i, tile, tok


def emit_ln(ctx, consts, xt, ncols, xn_write_fn, ld_waits, stride=None):
    onecol, onerow, epsc, rows, sq = (consts["onecol"], consts["onerow"], consts["epsc"],
                                      consts["rows"], consts["sq"])
    NT = ncols
    ST = stride if stride is not None else ncols
    bi_m, pm, tok_m = mm_group(
        ctx, [(onecol[:, 0:1], xt[:, c * ST: c * ST + NT]) for c in range(DC)],
        out_slice=(slice(0, 1), slice(0, NT)), waits=ld_waits)
    sqt = []
    for c in range(DC):
        sqt.append(ctx.op("scalar",
                          lambda s, c=c: s.activation(out=sq[:, c * ST: c * ST + NT],
                                                      in_=xt[:, c * ST: c * ST + NT], func=AF.Square),
                          inc=ctx.sS, waits=ld_waits))
    bi_q, pq, tok_q = mm_group(
        ctx, [(onecol[:, 0:1], sq[:, c * ST: c * ST + NT]) for c in range(DC)],
        out_slice=(slice(0, 1), slice(0, NT)), waits=[(ctx.sS, sqt[-1])])
    R_m = rows[0:1, 0 * NT:1 * NT]
    R_t = rows[0:1, 1 * NT:2 * NT]
    R_v = rows[0:1, 2 * NT:3 * NT]
    R_r = rows[0:1, 3 * NT:4 * NT]
    ctx.op("vector", lambda v: v.tensor_scalar_mul(R_m, pm[0:1, 0:NT], 1.0 / D), inc=ctx.vS,
           waits=[(ctx.mm, tok_m)])
    ctx.op("vector", lambda v: v.tensor_scalar_mul(R_v, pq[0:1, 0:NT], 1.0 / D), inc=ctx.vS,
           waits=[(ctx.mm, tok_q)])
    ctx.op("vector", lambda v: v.tensor_mul(R_t, R_m, R_m), inc=ctx.vS)
    ctx.op("vector", lambda v: v.tensor_sub(R_v, R_v, R_t), inc=ctx.vS)
    ctx.op("scalar", lambda s: s.activation(out=R_t, in_=R_v, func=AF.Sqrt, bias=epsc[0:1, 0:1]),
           inc=ctx.sS, waits=[(ctx.vS, ctx.vS.count)])
    ctx.op("vector", lambda v: v.reciprocal(R_r, R_t), inc=ctx.vS, waits=[(ctx.sS, ctx.sS.count)])
    row_tok = ctx.vS.count
    ctx.pool.release(ctx, "vector", bi_m, [(ctx.vS, row_tok)])
    ctx.pool.release(ctx, "vector", bi_q, [(ctx.vS, row_tok)])
    bi_M, pM, tok_M = mm_group(ctx, [(onerow[0:1, :], R_m)], waits=[(ctx.vS, row_tok)])
    bi_R, pR, tok_R = mm_group(ctx, [(onerow[0:1, :], R_r)], waits=[(ctx.vS, row_tok)])
    for c in range(DC):
        dst = xn_write_fn(c)
        ctx.op("vector", lambda v, c=c: v.tensor_sub(xt[:, c * ST: c * ST + NT],
                                                     xt[:, c * ST: c * ST + NT], pM[:, 0:NT]),
               inc=ctx.vS, waits=[(ctx.mm, tok_M), (ctx.sS, sqt[-1])])
        ctx.op("vector", lambda v, c=c, dst=dst: v.tensor_mul(dst, xt[:, c * ST: c * ST + NT],
                                                              pR[:, 0:NT]),
               inc=ctx.vS, waits=[(ctx.mm, tok_R)])
    xn_tok = ctx.vS.count
    ctx.pool.release(ctx, "vector", bi_M, [(ctx.vS, xn_tok)])
    ctx.pool.release(ctx, "vector", bi_R, [(ctx.vS, xn_tok)])
    return xn_tok


def ln_consts(ctx, NT):
    consts = {
        "onecol": ctx.sbuf("onecol", [128, 1]),
        "onerow": ctx.sbuf("onerow", [1, 128]),
        "epsc": ctx.sbuf("epsc", [1, 1]),
        "rows": ctx.sbuf("rows", [1, 4 * NT]),
        "sq": ctx.sbuf("sq", [128, DC * NT]),
    }
    ctx.op("gpsimd", lambda g: g.memset(consts["onecol"][:, :], 1.0), inc=ctx.gS)
    ctx.op("gpsimd", lambda g: g.memset(consts["onerow"][:, :], 1.0), inc=ctx.gS)
    ctx.op("gpsimd", lambda g: g.memset(consts["epsc"][:, :], LN_EPS), inc=ctx.gS)
    return consts, ctx.gS.count


def _fm(x):
    return np.ascontiguousarray(np.asarray(x).T)


def _bcols(b, nchunks):
    return np.ascontiguousarray(np.asarray(b).reshape(nchunks, 128).T).astype(np.float32)


def build_l0main():
    ctx = Ctx()
    import os as _os
    ctx.serial = bool(_os.environ.get("KSERIAL"))
    _parts = _os.environ.get("KPARTS", "lhusgy")
    _ntile_env = int(_os.environ.get("KNT", "0"))
    nc = ctx.nc
    NT = 512
    ntiles = _ntile_env or (TQ // NT)

    x_d = nc.dram_tensor("x", [D, TQ], F32, kind="ExternalInput").ap()
    sw1_d = nc.dram_tensor("sw1", [D, 512], F32, kind="ExternalInput").ap()
    sw2_d = nc.dram_tensor("sw2", [512, 128], F32, kind="ExternalInput").ap()
    bw_d = nc.dram_tensor("bw", [D, 128], F32, kind="ExternalInput").ap()
    dw_d = nc.dram_tensor("dw", [D, D], F32, kind="ExternalInput").ap()
    gw_d = nc.dram_tensor("gw", [D, D], F32, kind="ExternalInput").ap()
    sb1_d = nc.dram_tensor("sb1", [128, 4], F32, kind="ExternalInput").ap()
    sb2_d = nc.dram_tensor("sb2", [128, 1], F32, kind="ExternalInput").ap()
    bb_d = nc.dram_tensor("bb", [128, 1], F32, kind="ExternalInput").ap()
    cdb_d = nc.dram_tensor("cdb", [128, 16], F32, kind="ExternalInput").ap()
    gb_d = nc.dram_tensor("gb", [128, 8], F32, kind="ExternalInput").ap()

    xn_o = nc.dram_tensor("xn", [D, TQ], BF16, kind="ExternalOutput").ap()
    u_o = nc.dram_tensor("u", [128, TQ], BF16, kind="ExternalOutput").ap()
    gate_o = nc.dram_tensor("gate", [D, TQ], F32, kind="ExternalOutput").ap()
    y0d_o = nc.dram_tensor("y0d", [D, TQ], F32, kind="ExternalOutput").ap()

    xn = ctx.sbuf("xns", [128, DC * TQ], BF16)
    xt = ctx.sbuf("xt", [128, DC * NT])
    sw1 = ctx.sbuf("sw1s", [128, 4 * 512], BF16)
    sw2 = ctx.sbuf("sw2s", [128, 4 * 128], BF16)
    bw = ctx.sbuf("bws", [128, DC * 128], BF16)
    dw = ctx.sbuf("dws", [128, DC * D], BF16)
    gw = ctx.sbuf("gws", [128, DC * D], BF16)
    sw1f = ctx.sbuf("sw1f", [128, DC * 512], BF16)
    sb1 = ctx.sbuf("sb1s", [128, 4])
    sb2 = ctx.sbuf("sb2s", [128, 1])
    bb = ctx.sbuf("bbs", [128, 1])
    cdb = ctx.sbuf("cdbs", [128, 16])
    ydb = ctx.sbuf("ydbs", [128, 8])
    gb = ctx.sbuf("gbs", [128, 8])
    h1 = ctx.sbuf("h1", [128, 4 * NT], BF16)
    sg = ctx.sbuf("sg", [128, NT])
    ev1 = ctx.sbuf("ev1", [128, NT])
    ev2 = ctx.sbuf("ev2", [128, NT])

    ctx.pool = PsumPool(ctx)
    consts, memset_tok = ln_consts(ctx, NT)

    wt = {}
    for c in range(DC):
        wt[("sw1", c)] = ctx.dma("gpsimd", sw1f[:, c * 512:(c + 1) * 512], sw1_d[c * 128:(c + 1) * 128, :], ctx.ldS)
        wt[("bw", c)] = ctx.dma("gpsimd", bw[:, c * 128:(c + 1) * 128], bw_d[c * 128:(c + 1) * 128, :], ctx.ldS)
        wt[("dw", c)] = ctx.dma("gpsimd", dw[:, c * D:(c + 1) * D], dw_d[c * 128:(c + 1) * 128, :], ctx.ldS)
        wt[("gw", c)] = ctx.dma("gpsimd", gw[:, c * D:(c + 1) * D], gw_d[c * 128:(c + 1) * 128, :], ctx.ldS)
    for c in range(4):
        wt[("sw2", c)] = ctx.dma("gpsimd", sw2[:, c * 128:(c + 1) * 128], sw2_d[c * 128:(c + 1) * 128, :], ctx.ldS)
    wld = ctx.ldS.count
    btoks = []
    for dst, src in ((sb1, sb1_d), (sb2, sb2_d), (bb, bb_d), (cdb, cdb_d), (gb, gb_d)):
        btoks.append(ctx.dma("sync", dst[:, :], src[:, :], ctx.lsS))
    bld = ctx.lsS.count
    ctx.op("vector", lambda v: v.tensor_add(ydb[:, :], cdb[:, 0:8], cdb[:, 8:16]),
           inc=ctx.vS, waits=[(ctx.lsS, bld)])
    ydb_tok = ctx.vS.count

    for ti in range(ntiles):
        tsl = slice(ti * NT, (ti + 1) * NT)
        bw_waits = ctx.barrier_waits()
        xw = 0
        for c in range(DC):
            xw = ctx.dma("sync", xt[:, c * NT:(c + 1) * NT], x_d[c * 128:(c + 1) * 128, tsl],
                         ctx.lsS, waits=bw_waits)
        ld_waits = [(ctx.lsS, xw), (ctx.gS, memset_tok)]
        xn_tok = emit_ln(ctx, consts, xt, NT,
                         lambda c: xn[:, c * TQ + ti * NT: c * TQ + (ti + 1) * NT], ld_waits)
        for c in range(DC):
            ctx.dma("sync", xn_o[c * 128:(c + 1) * 128, tsl],
                    xn[:, c * TQ + ti * NT: c * TQ + (ti + 1) * NT], ctx.stS, waits=[(ctx.vS, xn_tok)])

        def xnsl(c):
            return xn[:, c * TQ + ti * NT: c * TQ + (ti + 1) * NT]

        for o in range(4 if "h" in _parts else 0):
            bi, p, tk = mm_group(ctx, [(sw1f[:, c * 512 + o * 128: c * 512 + (o + 1) * 128], xnsl(c))
                                       for c in range(DC)],
                                 waits=[(ctx.ldS, wld), (ctx.vS, xn_tok)])
            etk = ctx.op("scalar", lambda s, o=o, p=p: s.activation(
                out=h1[:, o * NT:(o + 1) * NT], in_=p[:, 0:NT], func=AF.Silu, bias=sb1[:, o:o + 1]),
                inc=ctx.sS, waits=[(ctx.mm, tk), (ctx.lsS, bld)])
            ctx.pool.release(ctx, "scalar", bi, [(ctx.sS, etk)])
        h1_tok = ctx.sS.count
        if "s" not in _parts:
            continue
        bi, p, tk = mm_group(ctx, [(sw2[:, c * 128:(c + 1) * 128], h1[:, c * NT:(c + 1) * NT])
                                   for c in range(4)], waits=[(ctx.sS, h1_tok)])
        etk = ctx.op("scalar", lambda s, p=p: s.activation(out=sg[:, :], in_=p[:, 0:NT],
                                                           func=AF.Sigmoid, bias=sb2[:, 0:1]),
                     inc=ctx.sS, waits=[(ctx.mm, tk)])
        ctx.pool.release(ctx, "scalar", bi, [(ctx.sS, etk)])
        sg_tok = etk
        if "u" not in _parts:
            continue
        bi, p, tk = mm_group(ctx, [(bw[:, c * 128:(c + 1) * 128], xnsl(c)) for c in range(DC)],
                             waits=[(ctx.vS, xn_tok)])
        etk = ctx.op("scalar", lambda s, p=p: s.activation(out=ev1[:, :], in_=p[:, 0:NT],
                                                           func=AF.Identity, bias=bb[:, 0:1]),
                     inc=ctx.sS, waits=[(ctx.mm, tk)])
        ctx.pool.release(ctx, "scalar", bi, [(ctx.sS, etk)])
        utk = ctx.op("vector", lambda v: v.tensor_mul(ev2[:, :], ev1[:, :], sg[:, :]),
                     inc=ctx.vS, waits=[(ctx.sS, max(etk, sg_tok))])
        ctx.dma("gpsimd", u_o[:, tsl], ev2[:, :], ctx.ldS, waits=[(ctx.vS, utk)])
        _sel = [(gw, gb, gate_o, AF.Sigmoid)] if "g" in _parts else []
        _sel += [(dw, ydb, y0d_o, AF.Identity)] if "y" in _parts else []
        for (wsb, bias_sb, out_d, func) in _sel:
            for o in range(DC):
                bi, p, tk = mm_group(ctx, [(wsb[:, c * D + o * 128: c * D + (o + 1) * 128], xnsl(c))
                                           for c in range(DC)], waits=[(ctx.vS, xn_tok)])
                etk = ctx.op("scalar", lambda s, p=p, o=o, b=bias_sb, f=func: s.activation(
                    out=ev1[:, :], in_=p[:, 0:NT], func=f, bias=b[:, o:o + 1]),
                    inc=ctx.sS, waits=[(ctx.mm, tk), (ctx.vS, ctx.vS.count), (ctx.stS, ctx.stS.count),
                                       (ctx.ldS, ctx.ldS.count)])
                ctx.pool.release(ctx, "scalar", bi, [(ctx.sS, etk)])
                ctx.dma("sync", out_d[o * 128:(o + 1) * 128, tsl], ev1[:, :], ctx.stS,
                        waits=[(ctx.sS, etk)])
    ctx.wait("sync", ctx.stS, ctx.stS.count)
    ctx.wait("gpsimd", ctx.ldS, ctx.ldS.count)
    return ctx


def host_l0main(inputs, core):
    b, q = core // 4, core % 4
    x = np.asarray(inputs["L0_attn_x"])[b, q * TQ:(q + 1) * TQ, :]
    return {
        "x": _fm(x).astype(np.float32),
        "sw1": _fm(inputs["L0_attn_s_w1"]).astype(np.float32),
        "sw2": _fm(inputs["L0_attn_s_w2"]).astype(np.float32),
        "bw": _fm(inputs["L0_attn_B_w"]).astype(np.float32),
        "dw": _fm(inputs["L0_attn_D_w"]).astype(np.float32),
        "gw": _fm(inputs["L0_attn_g_w"]).astype(np.float32),
        "sb1": _bcols(inputs["L0_attn_s_b1"], 4),
        "sb2": _bcols(inputs["L0_attn_s_b2"], 1),
        "bb": _bcols(inputs["L0_attn_B_b"], 1),
        "cdb": np.concatenate([_bcols(inputs["L0_attn_C_b"], 8), _bcols(inputs["L0_attn_D_b"], 8)], 1),
        "gb": _bcols(inputs["L0_attn_g_b"], 8),
    }


def build_l0ffn():
    ctx = Ctx()
    nc = ctx.nc
    NT = 256
    ntiles = TQ // NT
    H = 3072
    HC = H // 128

    xn_d = nc.dram_tensor("xn", [D, TQ], BF16, kind="ExternalInput").ap()
    wg_d = nc.dram_tensor("wg", [D, H], I32, kind="ExternalInput").ap()
    wu_d = nc.dram_tensor("wu", [D, H], I32, kind="ExternalInput").ap()
    wd_d = nc.dram_tensor("wd", [H, D], I32, kind="ExternalInput").ap()
    g0_o = nc.dram_tensor("g0", [D, TQ], F32, kind="ExternalOutput").ap()

    wg = ctx.sbuf("wgs", [128, DC * H], BF16)
    wu = ctx.sbuf("wus", [128, DC * H], BF16)
    wd = ctx.sbuf("wds", [128, HC * D], BF16)
    xnt = ctx.sbuf("xnt", [128, DC * NT], BF16)
    gsp = ctx.sbuf("gsp", [128, HC * NT])
    hbf = ctx.sbuf("hbf", [128, HC * NT], BF16)
    ev1 = ctx.sbuf("ev1", [128, NT])

    ctx.pool = PsumPool(ctx)
    for c in range(DC):
        ctx.dma("gpsimd", wg[:, c * H:(c + 1) * H], wg_d[c * 128:(c + 1) * 128, :], ctx.ldS)
        ctx.dma("gpsimd", wu[:, c * H:(c + 1) * H], wu_d[c * 128:(c + 1) * 128, :], ctx.ldS)
    for c in range(HC):
        ctx.dma("gpsimd", wd[:, c * D:(c + 1) * D], wd_d[c * 128:(c + 1) * 128, :], ctx.ldS)
    wld = ctx.ldS.count

    for ti in range(ntiles):
        tsl = slice(ti * NT, (ti + 1) * NT)
        bw_waits = ctx.barrier_waits()
        xw = 0
        for c in range(DC):
            xw = ctx.dma("sync", xnt[:, c * NT:(c + 1) * NT], xn_d[c * 128:(c + 1) * 128, tsl],
                         ctx.lsS, waits=bw_waits)

        def xnsl(c):
            return xnt[:, c * NT:(c + 1) * NT]

        for o in range(HC):
            bi, p, tk = mm_group(ctx, [(wg[:, c * H + o * 128: c * H + (o + 1) * 128], xnsl(c))
                                       for c in range(DC)],
                                 waits=[(ctx.ldS, wld), (ctx.lsS, xw)])
            etk = ctx.op("scalar", lambda s, o=o, p=p: s.activation(
                out=gsp[:, o * NT:(o + 1) * NT], in_=p[:, 0:NT], func=AF.Silu, scale=DEQ),
                inc=ctx.sS, waits=[(ctx.mm, tk)])
            ctx.pool.release(ctx, "scalar", bi, [(ctx.sS, etk)])
        for o in range(HC):
            bi, p, tk = mm_group(ctx, [(wu[:, c * H + o * 128: c * H + (o + 1) * 128], xnsl(c))
                                       for c in range(DC)], waits=[(ctx.lsS, xw)])
            etk = ctx.op("vector", lambda v, o=o, p=p: v.scalar_tensor_tensor(
                hbf[:, o * NT:(o + 1) * NT], p[:, 0:NT], DEQ, gsp[:, o * NT:(o + 1) * NT],
                ALU.mult, ALU.mult),
                inc=ctx.vS, waits=[(ctx.mm, tk), (ctx.sS, ctx.sS.count)])
            ctx.pool.release(ctx, "vector", bi, [(ctx.vS, etk)])
        h_tok = ctx.vS.count
        for o in range(DC):
            bi, p, tk = mm_group(ctx, [(wd[:, c * D + o * 128: c * D + (o + 1) * 128],
                                        hbf[:, c * NT:(c + 1) * NT]) for c in range(HC)],
                                 waits=[(ctx.vS, h_tok)])
            etk = ctx.op("scalar", lambda s, p=p: s.activation(
                out=ev1[:, :], in_=p[:, 0:NT], func=AF.Silu, scale=DEQ),
                inc=ctx.sS, waits=[(ctx.mm, tk), (ctx.stS, ctx.stS.count)])
            ctx.pool.release(ctx, "scalar", bi, [(ctx.sS, etk)])
            ctx.dma("sync", g0_o[o * 128:(o + 1) * 128, tsl], ev1[:, :], ctx.stS,
                    waits=[(ctx.sS, etk)])
    ctx.wait("sync", ctx.stS, ctx.stS.count)
    return ctx


def host_l0ffn(inputs, core, st):
    return {
        "xn": st["xn"][core],
        "wg": _fm(inputs["L0_ffn_w_gate"]).astype(np.int32),
        "wu": _fm(inputs["L0_ffn_w_up"]).astype(np.int32),
        "wd": _fm(inputs["L0_ffn_w_down"]).astype(np.int32),
    }


def build_l0scan():
    ctx = Ctx()
    nc = ctx.nc
    c0 = C0
    TE = T + c0
    u_d = nc.dram_tensor("u", [128, T], BF16, kind="ExternalInput").ap()
    A_d = nc.dram_tensor("A", [128, 128], F32, kind="ExternalInput").ap()
    I_d = nc.dram_tensor("ident", [128, 128], F32, kind="ExternalInput").ap()
    hs_o = nc.dram_tensor("hs", [128, T], BF16, kind="ExternalOutput").ap()

    ue = ctx.sbuf("ue", [128, TE], BF16)
    P = ctx.sbuf("P", [128, (c0 + 1) * 128], BF16)
    Fb = ctx.sbuf("F", [128, TE])
    h = ctx.sbuf("h", [128, TE], BF16)
    A_sb = ctx.sbuf("A_sb", [128, 128], BF16)

    ctx.pool = PsumPool(ctx)
    ctx.op("gpsimd", lambda g: g.memset(ue[:, 0:c0], 0.0), inc=ctx.gS)
    ms_tok = ctx.gS.count
    ctx.dma("gpsimd", ue[:, c0:], u_d[:, :], ctx.ldS)
    ctx.dma("gpsimd", A_sb[:, :], A_d[:, :], ctx.ldS)
    ctx.dma("gpsimd", P[:, 0:128], I_d[:, :], ctx.ldS)
    ld_tok = ctx.ldS.count

    for k in range(1, c0 + 1):
        bi, p, tk = mm_group(ctx, [(A_sb[:, :], P[:, (k - 1) * 128: k * 128])],
                             waits=[(ctx.ldS, ld_tok), (ctx.vS, ctx.vS.count)])
        etk = ctx.op("vector", lambda v, k=k, p=p: v.tensor_copy(P[:, k * 128:(k + 1) * 128],
                                                                 p[:, 0:128]),
                     inc=ctx.vS, waits=[(ctx.mm, tk)])
        ctx.pool.release(ctx, "vector", bi, [(ctx.vS, etk)])
    pow_tok = ctx.vS.count

    tiles = [(i * 512, 512) for i in range(TE // 512)]
    if TE % 512:
        tiles.append((TE - TE % 512, TE % 512))
    for (ts, n) in tiles:
        i, tile, (fsem, fneed) = ctx.pool.acquire()
        ctx.wait("tensor", fsem, fneed)
        ctx.wait("tensor", ctx.vS, pow_tok)
        ctx.wait("tensor", ctx.gS, ms_tok)
        ctx.mm.count += 1
        tok = ctx.mm.count

        def run(t, ts=ts, n=n, tile=tile, sem=ctx.mm):
            inst = None
            first = True
            for k in range(c0):
                if ts - k + n <= 0:
                    continue
                lo = max(0, k - ts)
                src0 = ts - k + lo
                inst = t.matmul(tile[:, lo:n], P[:, k * 128:(k + 1) * 128],
                                ue[:, src0: src0 + (n - lo)], start=first, stop=(k == c0 - 1))
                first = False
            inst.then_inc(sem.h, 1)

        ctx.q["tensor"].append(run)
        etk = ctx.op("vector", lambda v, ts=ts, n=n, tile=tile: v.tensor_copy(
            Fb[:, ts:ts + n], tile[:, 0:n]), inc=ctx.vS, waits=[(ctx.mm, tok)])
        ctx.pool.release(ctx, "vector", i, [(ctx.vS, etk)])
    f_tok = ctx.vS.count

    ctx.op("vector", lambda v: v.tensor_copy(h[:, 0:c0], Fb[:, 0:c0]), inc=ctx.vS,
           waits=[(ctx.vS, f_tok)])
    for r in range(1, TE // c0):
        bi, p, tk = mm_group(ctx, [(P[:, c0 * 128:(c0 + 1) * 128], h[:, (r - 1) * c0: r * c0])],
                             waits=[(ctx.vS, ctx.vS.count)])
        etk = ctx.op("vector", lambda v, r=r, p=p: v.tensor_add(
            h[:, r * c0:(r + 1) * c0], Fb[:, r * c0:(r + 1) * c0], p[:, 0:c0]),
            inc=ctx.vS, waits=[(ctx.mm, tk)])
        ctx.pool.release(ctx, "vector", bi, [(ctx.vS, etk)])
    ctx.dma("sync", hs_o[:, :], h[:, c0:], ctx.stS, waits=[(ctx.vS, ctx.vS.count)])
    ctx.wait("sync", ctx.stS, ctx.stS.count)
    return ctx


def build_combine1():
    ctx = Ctx()
    nc = ctx.nc
    NT = 512
    ntiles = TQ // NT

    x_d = nc.dram_tensor("x", [D, TQ], F32, kind="ExternalInput").ap()
    hs_d = nc.dram_tensor("hs", [128, TQ], BF16, kind="ExternalInput").ap()
    y0d_d = nc.dram_tensor("y0d", [D, TQ], F32, kind="ExternalInput").ap()
    gate_d = nc.dram_tensor("gate", [D, TQ], F32, kind="ExternalInput").ap()
    g0_d = nc.dram_tensor("g0", [D, TQ], F32, kind="ExternalInput").ap()
    cw_d = nc.dram_tensor("cw", [128, D], F32, kind="ExternalInput").ap()
    x1_o = nc.dram_tensor("x1", [D, TQ], F32, kind="ExternalOutput").ap()
    xn1_o = nc.dram_tensor("xn1", [D, TQ], BF16, kind="ExternalOutput").ap()

    cw = ctx.sbuf("cws", [128, D], BF16)
    hs = ctx.sbuf("hss", [128, TQ], BF16)
    xt = ctx.sbuf("xt", [128, DC * NT])
    aux = ctx.sbuf("aux", [128, 3 * DC * NT])
    xnt = ctx.sbuf("xnt", [128, DC * NT], BF16)

    ctx.pool = PsumPool(ctx)
    consts, memset_tok = ln_consts(ctx, NT)
    ctx.dma("gpsimd", cw[:, :], cw_d[:, :], ctx.ldS)
    ctx.dma("sync", hs[:, :], hs_d[:, :], ctx.lsS)
    wld, hld = ctx.ldS.count, ctx.lsS.count

    for ti in range(ntiles):
        tsl = slice(ti * NT, (ti + 1) * NT)
        bw_waits = ctx.barrier_waits()
        xw = 0
        for c in range(DC):
            xw = ctx.dma("sync", xt[:, c * NT:(c + 1) * NT], x_d[c * 128:(c + 1) * 128, tsl],
                         ctx.lsS, waits=bw_waits)
            xw = ctx.dma("sync", aux[:, (0 * DC + c) * NT:(0 * DC + c + 1) * NT],
                         y0d_d[c * 128:(c + 1) * 128, tsl], ctx.lsS)
            xw = ctx.dma("sync", aux[:, (1 * DC + c) * NT:(1 * DC + c + 1) * NT],
                         gate_d[c * 128:(c + 1) * 128, tsl], ctx.lsS)
            xw = ctx.dma("sync", aux[:, (2 * DC + c) * NT:(2 * DC + c + 1) * NT],
                         g0_d[c * 128:(c + 1) * 128, tsl], ctx.lsS)
        for o in range(DC):
            bi, p, tk = mm_group(ctx, [(cw[:, o * 128:(o + 1) * 128], hs[:, tsl])],
                                 waits=[(ctx.ldS, wld), (ctx.lsS, max(xw, hld))])
            y0sl = aux[:, (0 * DC + o) * NT:(0 * DC + o + 1) * NT]
            gsl = aux[:, (1 * DC + o) * NT:(1 * DC + o + 1) * NT]
            g0sl = aux[:, (2 * DC + o) * NT:(2 * DC + o + 1) * NT]
            xsl = xt[:, o * NT:(o + 1) * NT]
            ctx.op("vector", lambda v, p=p, y0sl=y0sl: v.tensor_add(y0sl, y0sl, p[:, 0:NT]),
                   inc=ctx.vS, waits=[(ctx.mm, tk)])
            ctx.pool.release(ctx, "vector", bi, [(ctx.vS, ctx.vS.count)])
            ctx.op("vector", lambda v, y0sl=y0sl, gsl=gsl: v.tensor_mul(y0sl, y0sl, gsl), inc=ctx.vS)
            ctx.op("vector", lambda v, y0sl=y0sl, g0sl=g0sl: v.tensor_mul(y0sl, y0sl, g0sl), inc=ctx.vS)
            ctx.op("vector", lambda v, xsl=xsl, y0sl=y0sl: v.tensor_add(xsl, xsl, y0sl), inc=ctx.vS)
        x1_tok = ctx.vS.count
        for c in range(DC):
            ctx.dma("sync", x1_o[c * 128:(c + 1) * 128, tsl], xt[:, c * NT:(c + 1) * NT],
                    ctx.stS, waits=[(ctx.vS, x1_tok)])
        xn_tok = emit_ln(ctx, consts, xt, NT, lambda c: xnt[:, c * NT:(c + 1) * NT],
                         [(ctx.vS, x1_tok), (ctx.stS, ctx.stS.count), (ctx.gS, memset_tok)])
        for c in range(DC):
            ctx.dma("sync", xn1_o[c * 128:(c + 1) * 128, tsl], xnt[:, c * NT:(c + 1) * NT],
                    ctx.stS, waits=[(ctx.vS, xn_tok)])
    ctx.wait("sync", ctx.stS, ctx.stS.count)
    return ctx


_CTX_CACHE = {}


import os
def _run(name, builder, in_maps):
    ctx = builder()
    return ctx.run(in_maps)


def kernel(**inputs):
    import os as _os
    if not _os.environ.get("KUNFUSED"):
        return kernel_fused(**inputs)
    st = {}
    maps = [host_l0main(inputs, c) for c in range(NCORES)]
    res = _run("l0main", build_l0main, maps)
    st["xn"] = [np.asarray(r["xn"]) for r in res]
    st["u"] = [np.asarray(r["u"]) for r in res]
    st["gate"] = [np.asarray(r["gate"]) for r in res]
    st["y0d"] = [np.asarray(r["y0d"]) for r in res]
    maps = [host_l0ffn(inputs, c, st) for c in range(NCORES)]
    res = _run("l0ffn", build_l0ffn, maps)
    st["g0"] = [np.asarray(r["g0"]) for r in res]
    A = np.asarray(inputs["L0_attn_A"]).astype(np.float32)
    ident = np.eye(128, dtype=np.float32)
    maps = []
    for c in range(NCORES):
        b = c // 4
        ufull = np.concatenate([st["u"][4 * b + q] for q in range(4)], axis=1)
        maps.append({"u": ufull, "A": A, "ident": ident})
    res = _run("l0scan", build_l0scan, maps)
    st["hs"] = [np.asarray(res[4 * b]["hs"]) for b in range(2)]
    maps = []
    for c in range(NCORES):
        b, q = c // 4, c % 4
        maps.append({
            "x": _fm(np.asarray(inputs["L0_attn_x"])[b, q * TQ:(q + 1) * TQ, :]).astype(np.float32),
            "hs": np.ascontiguousarray(st["hs"][b][:, q * TQ:(q + 1) * TQ]),
            "y0d": st["y0d"][c], "gate": st["gate"][c], "g0": st["g0"][c],
            "cw": np.ascontiguousarray(np.asarray(inputs["L0_attn_C_w"]).T).astype(np.float32),
        })
    res = _run("combine1", build_combine1, maps)
    st["x1"] = [np.asarray(r["x1"]) for r in res]
    st["xn1"] = [np.asarray(r["xn1"]) for r in res]

    maps = [host_moe(inputs, c, st) for c in range(NCORES)]
    res = _run("moe", build_moe, maps)
    st["g1"] = [np.asarray(r["g1"]) for r in res]
    maps = [host_mamba_a(inputs, c, st) for c in range(NCORES)]
    res = _run("mamba_a", build_mamba_a, maps)
    for k in ("uc", "ucb", "dt", "sz", "bct"):
        st[k] = [np.asarray(r[k]) for r in res]
    maps = [host_mamba_scan(inputs, c, st) for c in range(NCORES)]
    res = _run("mamba_scan", build_mamba_scan, maps)
    st["y3s"] = [np.asarray(r["y3"]) for r in res]
    maps = [host_final(inputs, c, st) for c in range(NCORES)]
    res = _run("final", build_final, maps)
    out = np.empty((2, T, D), np.float32)
    for c in range(NCORES):
        b, q = c // 4, c % 4
        out[b, q * TQ:(q + 1) * TQ, :] = np.asarray(res[c]["out"]).T
    return out


def build_moe():
    ctx = Ctx()
    nc = ctx.nc
    NE, HP = 8, 4096
    QW = 1024
    xn_d = nc.dram_tensor("xn1", [D, TQ], BF16, kind="ExternalInput").ap()
    w1_d = nc.dram_tensor("w1", [NE * D, HP], F32, kind="ExternalInput").ap()
    w2_d = nc.dram_tensor("w2", [NE * HP, D], F32, kind="ExternalInput").ap()
    b1_d = nc.dram_tensor("b1", [128, NE * 32], F32, kind="ExternalInput").ap()
    b2_d = nc.dram_tensor("b2", [128, NE * 8], F32, kind="ExternalInput").ap()
    wr_d = nc.dram_tensor("wr", [D, NE], F32, kind="ExternalInput").ap()
    br_d = nc.dram_tensor("br", [1, NE], F32, kind="ExternalInput").ap()
    id_d = nc.dram_tensor("ident", [128, 128], F32, kind="ExternalInput").ap()
    g1_o = nc.dram_tensor("g1", [D, TQ], F32, kind="ExternalOutput").ap()
    import os as _os
    _dbg = bool(_os.environ.get("KMOE_DBG"))
    if _dbg:
        wf_d = nc.dram_tensor("wfd", [NE, TQ], F32, kind="ExternalOutput").ap()
        sc_o = nc.dram_tensor("sc_o", [128, 32 * NE], F32, kind="ExternalOutput").ap()
    else:
        wf_d = nc.dram_tensor("wfd", [NE, TQ], F32).ap()

    xnt = ctx.sbuf("xnt", [128, DC * 512], BF16)
    xnq = ctx.sbuf("xnq", [128, DC * QW], BF16)
    wrb = ctx.sbuf("wrbs", [128, DC * NE], BF16)
    brr = ctx.sbuf("brrs", [1, NE])
    brbc = ctx.sbuf("brbc", [128, NE])
    idf = ctx.sbuf("idfs", [128, 128])
    sc = ctx.sbuf("scs", [128, 32 * NE])
    m1 = ctx.sbuf("m1", [128, 32])
    m2 = ctx.sbuf("m2", [128, 32])
    lt = ctx.sbuf("lt", [128, 32 * NE])
    msk = ctx.sbuf("msk", [128, 32 * NE])
    dd = ctx.sbuf("dd", [128, 32])
    w1v = ctx.sbuf("w1v", [128, 32])
    w2v = ctx.sbuf("w2v", [128, 32])
    eq = ctx.sbuf("eq", [128, 32 * NE])
    wsb = ctx.sbuf("wsb", [128, 32 * NE])
    wrows = ctx.sbuf("wrows", [8, 128])
    wfrow = ctx.sbuf("wfrow", [1, 512])
    onerow = ctx.sbuf("onerow", [1, 128])
    b1c = ctx.sbuf("b1c", [128, NE * 32])
    b2c = ctx.sbuf("b2c", [128, NE * 8])
    w1b = ctx.sbuf("w1b", [128, DC * 128], BF16)
    w2b = ctx.sbuf("w2b", [128, 32 * 128], BF16)
    gsb = ctx.sbuf("gsb", [128, 32 * QW], BF16)
    acc = ctx.sbuf("acc", [128, DC * QW])
    wbc = ctx.sbuf("wbc", [128, 2 * 512])
    xbf = ctx.sbuf("xbf", [128, 512])
    sqf = ctx.sbuf("sqf", [128, 512])
    inn = ctx.sbuf("inn", [128, 512])
    ev1 = ctx.sbuf("ev1", [128, 512])
    yw = ctx.sbuf("yw", [128, 512])

    ctx.pool = PsumPool(ctx)
    ctx.op("gpsimd", lambda g: g.memset(onerow[:, :], 1.0))
    ms_tok = ctx.gS.count
    ctx.dma("gpsimd", wrb[:, :], wr_d.rearrange("(c p) e -> p c e", p=128), ctx.ldS)
    ctx.dma("sync", brr[0:1, :], br_d[0:1, :], ctx.lsS)
    ctx.dma("sync", idf[:, :], id_d[:, :], ctx.lsS)
    ctx.dma("sync", b1c[:, :], b1_d[:, :], ctx.lsS)
    ctx.dma("sync", b2c[:, :], b2_d[:, :], ctx.lsS)
    cst_tok = ctx.lsS.count

    bi, p, tk = mm_group(ctx, [(onerow[0:1, :], brr[0:1, :])],
                         waits=[(ctx.lsS, cst_tok), (ctx.gS, ms_tok)])
    etk = ctx.op("vector", lambda v, p=p: v.tensor_copy(brbc[:, :], p[:, 0:NE]),
                 inc=ctx.vS, waits=[(ctx.mm, tk)])
    ctx.pool.release(ctx, "vector", bi, [(ctx.vS, etk)])

    wr_ld = ctx.ldS.count
    for tt in range(8):
        xw = 0
        for c in range(DC):
            xw = ctx.dma("sync", xnt[:, c * 512:(c + 1) * 512],
                         xn_d[c * 128:(c + 1) * 128, tt * 512:(tt + 1) * 512], ctx.lsS,
                         waits=[(ctx.mm, ctx.mm.count)])
        for j in range(4):
            tc = tt * 4 + j
            bi, p, tk = mm_group(ctx, [(xnt[:, c * 512 + j * 128: c * 512 + (j + 1) * 128],
                                        wrb[:, c * NE:(c + 1) * NE]) for c in range(DC)],
                                 waits=[(ctx.ldS, wr_ld), (ctx.lsS, xw)])
            etk = ctx.op("vector", lambda v, tc=tc, p=p: v.tensor_add(
                sc[:, tc * NE:(tc + 1) * NE], p[:, 0:NE], brbc[:, :]),
                inc=ctx.vS, waits=[(ctx.mm, tk)])
            ctx.pool.release(ctx, "vector", bi, [(ctx.vS, etk)])
    sc3 = sc.ap().rearrange("p (t e) -> p t e", e=NE)
    ctx.op("vector", lambda v: v.tensor_reduce(m1[:, :], sc3, AX.X, ALU.max), inc=ctx.vS)
    m1bc = m1.ap().unsqueeze(2).broadcast_to([128, 32, NE])
    lt3 = lt.ap().rearrange("p (t e) -> p t e", e=NE)
    ctx.op("vector", lambda v: v.tensor_tensor(lt3, sc3, m1bc, ALU.is_ge), inc=ctx.vS)
    ctx.op("vector", lambda v: v.scalar_tensor_tensor(msk[:, :], lt[:, :], -1.0e4, sc[:, :],
                                                      ALU.mult, ALU.add), inc=ctx.vS)
    msk3 = msk.ap().rearrange("p (t e) -> p t e", e=NE)
    ctx.op("vector", lambda v: v.tensor_reduce(m2[:, :], msk3, AX.X, ALU.max), inc=ctx.vS)
    ctx.op("vector", lambda v: v.tensor_sub(dd[:, :], m2[:, :], m1[:, :]), inc=ctx.vS)
    ctx.op("scalar", lambda sx: sx.activation(out=dd[:, :], in_=dd[:, :], func=AF.Exp),
           inc=ctx.sS, waits=[(ctx.vS, ctx.vS.count)])
    ctx.op("vector", lambda v: v.tensor_scalar_add(w1v[:, :], dd[:, :], 1.0), inc=ctx.vS,
           waits=[(ctx.sS, ctx.sS.count)])
    ctx.op("vector", lambda v: v.reciprocal(w1v[:, :], w1v[:, :]), inc=ctx.vS)
    ctx.op("vector", lambda v: v.tensor_mul(w2v[:, :], dd[:, :], w1v[:, :]), inc=ctx.vS)
    eq3 = eq.ap().rearrange("p (t e) -> p t e", e=NE)
    wsb3 = wsb.ap().rearrange("p (t e) -> p t e", e=NE)
    w1bc = w1v.ap().unsqueeze(2).broadcast_to([128, 32, NE])
    w2bc = w2v.ap().unsqueeze(2).broadcast_to([128, 32, NE])
    m2bc = m2.ap().unsqueeze(2).broadcast_to([128, 32, NE])
    ctx.op("vector", lambda v: v.tensor_tensor(eq3, sc3, m1bc, ALU.is_equal), inc=ctx.vS)
    ctx.op("vector", lambda v: v.tensor_tensor(wsb3, eq3, w1bc, ALU.mult), inc=ctx.vS)
    ctx.op("vector", lambda v: v.tensor_tensor(eq3, sc3, m2bc, ALU.is_equal), inc=ctx.vS)
    ctx.op("vector", lambda v: v.tensor_tensor(eq3, eq3, w2bc, ALU.mult), inc=ctx.vS)
    ctx.op("vector", lambda v: v.tensor_add(wsb[:, :], wsb[:, :], eq[:, :]), inc=ctx.vS)
    wsb_tok = ctx.vS.count
    for j in range(32):
        i, tile, (fsem, fneed) = ctx.pool.acquire()
        ctx.wait("tensor", fsem, fneed)
        ctx.wait("tensor", ctx.vS, wsb_tok)
        ctx.wait("tensor", ctx.lsS, cst_tok)
        ctx.mm.count += 1
        tk = ctx.mm.count

        def runt(t, j=j, tile=tile, sem=ctx.mm):
            t.transpose(tile[0:NE, 0:128], wsb[:, j * NE:(j + 1) * NE], idf[:, :]).then_inc(sem.h, 1)

        ctx.q["tensor"].append(runt)
        etk = ctx.op("vector", lambda v, tile=tile: v.tensor_copy(
            wrows[0:NE, :], tile[0:NE, 0:128]),
            inc=ctx.vS, waits=[(ctx.mm, tk), (ctx.ldS, ctx.ldS.count)])
        ctx.pool.release(ctx, "vector", i, [(ctx.vS, etk)])
        ctx.dma("gpsimd", wf_d[0:NE, j * 128:(j + 1) * 128], wrows[0:NE, :], ctx.ldS,
                waits=[(ctx.vS, etk)])
    wflat_tok = ctx.ldS.count
    if _dbg:
        ctx.dma("sync", sc_o[:, :], wsb[:, :], ctx.stS, waits=[(ctx.vS, wsb_tok)])

    GC = GELU_C
    GK = GELU_K
    for q in range(4):
        qsl = slice(q * QW, (q + 1) * QW)
        bwq = ctx.barrier_waits()
        xq = 0
        for c in range(DC):
            xq = ctx.dma("sync", xnq[:, c * QW:(c + 1) * QW],
                         xn_d[c * 128:(c + 1) * 128, qsl], ctx.lsS, waits=bwq)
        ctx.op("gpsimd", lambda g: g.memset(acc[:, :], 0.0), waits=bwq)
        acc0_tok = ctx.gS.count
        for e in range(NE):
            for o in range(32):
                w1t = ctx.dma("gpsimd", w1b[:, :],
                              w1_d[e * D:(e + 1) * D, o * 128:(o + 1) * 128]
                              .rearrange("(c p) n -> p c n", p=128), ctx.ldS,
                              waits=[(ctx.mm, ctx.mm.count)])
                for t2 in range(2):
                    bi, p, tk = mm_group(ctx, [(w1b[:, c * 128:(c + 1) * 128],
                                                xnq[:, c * QW + t2 * 512: c * QW + t2 * 512 + 512])
                                               for c in range(DC)],
                                         waits=[(ctx.ldS, w1t), (ctx.lsS, xq)])
                    bcol = b1c[:, e * 32 + o: e * 32 + o + 1]
                    ctx.op("scalar", lambda sx, p=p, bcol=bcol: sx.activation(
                        out=xbf[:, :], in_=p[:, 0:512], func=AF.Identity, bias=bcol),
                        inc=ctx.sS, waits=[(ctx.mm, tk), (ctx.lsS, cst_tok), (ctx.vS, ctx.vS.count)])
                    etk = ctx.op("scalar", lambda sx: sx.activation(
                        out=sqf[:, :], in_=xbf[:, :], func=AF.Square), inc=ctx.sS)
                    ctx.pool.release(ctx, "scalar", bi, [(ctx.sS, etk)])
                    ctx.op("vector", lambda v: v.tensor_scalar(inn[:, :], sqf[:, :], GC, 1.0,
                                                               ALU.mult, ALU.add),
                           inc=ctx.vS, waits=[(ctx.sS, etk)])
                    ctx.op("vector", lambda v: v.tensor_mul(inn[:, :], inn[:, :], xbf[:, :]),
                           inc=ctx.vS)
                    stk = ctx.op("scalar", lambda sx: sx.activation(
                        out=inn[:, :], in_=inn[:, :], func=AF.Sigmoid, scale=GK),
                        inc=ctx.sS, waits=[(ctx.vS, ctx.vS.count)])
                    gdst = gsb[:, o * QW + t2 * 512: o * QW + t2 * 512 + 512]
                    ctx.op("vector", lambda v, gdst=gdst: v.tensor_mul(gdst, inn[:, :], xbf[:, :]),
                           inc=ctx.vS, waits=[(ctx.sS, stk)])
            g_tok = ctx.vS.count
            for t2 in range(2):
                wr0 = q * QW + t2 * 512
                wl = ctx.dma("sync", wfrow[0:1, :], wf_d[e:e + 1, wr0:wr0 + 512], ctx.lsS,
                             waits=[(ctx.ldS, wflat_tok), (ctx.mm, ctx.mm.count)])
                bi, p, tk = mm_group(ctx, [(onerow[0:1, :], wfrow[0:1, :])],
                                     waits=[(ctx.lsS, wl), (ctx.gS, ms_tok)])
                etk = ctx.op("vector", lambda v, p=p, t2=t2: v.tensor_copy(
                    wbc[:, t2 * 512:(t2 + 1) * 512], p[:, 0:512]),
                    inc=ctx.vS, waits=[(ctx.mm, tk)])
                ctx.pool.release(ctx, "vector", bi, [(ctx.vS, etk)])
                for oc in range(DC):
                    w2t = ctx.dma("gpsimd", w2b[:, :],
                                  w2_d[e * HP: (e + 1) * HP, oc * 128:(oc + 1) * 128]
                                  .rearrange("(k p) n -> p k n", p=128), ctx.ldS,
                                  waits=[(ctx.mm, ctx.mm.count)])
                    bi, p, tk = mm_group(ctx, [(w2b[:, k * 128:(k + 1) * 128],
                                                gsb[:, k * QW + t2 * 512: k * QW + t2 * 512 + 512])
                                               for k in range(32)],
                                         waits=[(ctx.ldS, w2t), (ctx.vS, g_tok)])
                    b2col = b2c[:, e * 8 + oc: e * 8 + oc + 1]
                    ctx.op("vector", lambda v, p=p, b2col=b2col, t2=t2: v.scalar_tensor_tensor(
                        yw[:, :], p[:, 0:512], b2col, wbc[:, t2 * 512:(t2 + 1) * 512],
                        ALU.add, ALU.mult),
                        inc=ctx.vS, waits=[(ctx.mm, tk), (ctx.lsS, cst_tok), (ctx.gS, acc0_tok)])
                    adst = acc[:, oc * QW + t2 * 512: oc * QW + t2 * 512 + 512]
                    etk = ctx.op("vector", lambda v, adst=adst: v.tensor_add(adst, adst, yw[:, :]),
                                 inc=ctx.vS)
                    ctx.pool.release(ctx, "vector", bi, [(ctx.vS, etk)])
        for oc in range(DC):
            etk = ctx.op("scalar", lambda sx, oc=oc: sx.activation(
                out=ev1[:, 0:512], in_=acc[:, oc * QW: oc * QW + 512], func=AF.Silu),
                inc=ctx.sS, waits=[(ctx.vS, ctx.vS.count), (ctx.stS, ctx.stS.count)])
            ctx.dma("sync", g1_o[oc * 128:(oc + 1) * 128, q * QW: q * QW + 512], ev1[:, 0:512],
                    ctx.stS, waits=[(ctx.sS, etk)])
            etk = ctx.op("scalar", lambda sx, oc=oc: sx.activation(
                out=ev1[:, 0:512], in_=acc[:, oc * QW + 512: (oc + 1) * QW], func=AF.Silu),
                inc=ctx.sS, waits=[(ctx.stS, ctx.stS.count)])
            ctx.dma("sync", g1_o[oc * 128:(oc + 1) * 128, q * QW + 512: (q + 1) * QW], ev1[:, 0:512],
                    ctx.stS, waits=[(ctx.sS, etk)])
    ctx.wait("sync", ctx.stS, ctx.stS.count)
    return ctx


def host_moe(inputs, core, st):
    W1 = np.asarray(inputs["L1_ffn_W1"]).astype(np.float32)
    W2 = np.asarray(inputs["L1_ffn_W2"]).astype(np.float32)
    b1 = np.asarray(inputs["L1_ffn_b1"]).astype(np.float32)
    b2 = np.asarray(inputs["L1_ffn_b2"]).astype(np.float32)
    return {
        "xn1": st["xn1"][core],
        "w1": np.ascontiguousarray(W1.reshape(8 * 1024, 4096)),
        "w2": np.ascontiguousarray(W2.reshape(8 * 4096, 1024)),
        "b1": np.ascontiguousarray(np.concatenate([_bcols(b1[e], 32) for e in range(8)], 1)),
        "b2": np.ascontiguousarray(np.concatenate([_bcols(b2[e], 8) for e in range(8)], 1)),
        "wr": np.ascontiguousarray(np.asarray(inputs["L1_ffn_Wr"]).astype(np.float32)),
        "br": np.ascontiguousarray(np.asarray(inputs["L1_ffn_br"]).astype(np.float32).reshape(1, 8)),
        "ident": np.eye(128, dtype=np.float32),
    }


def build_mamba_a():
    ctx = Ctx()
    import os as _os
    _mp = _os.environ.get("KMA_PARTS", "uzxdb")
    nc = ctx.nc
    TH = TQ + 3
    DI = 2048

    xnh_d = nc.dram_tensor("xnh", [D, TH], BF16, kind="ExternalInput").ap()
    win_d = nc.dram_tensor("win", [D, 2 * DI], F32, kind="ExternalInput").ap()
    convc_d = nc.dram_tensor("convc", [128, 16 * 4], F32, kind="ExternalInput").ap()
    convb_d = nc.dram_tensor("convb", [128, 16], F32, kind="ExternalInput").ap()
    wx_d = nc.dram_tensor("wx", [DI, 96], F32, kind="ExternalInput").ap()
    wdt_d = nc.dram_tensor("wdt", [64, DI], F32, kind="ExternalInput").ap()
    bdt_d = nc.dram_tensor("bdt", [128, 16], F32, kind="ExternalInput").ap()
    id_d = nc.dram_tensor("ident", [128, 128], F32, kind="ExternalInput").ap()
    uc_o = nc.dram_tensor("uc", [DI, TQ], F32, kind="ExternalOutput").ap()
    ucb_o = nc.dram_tensor("ucb", [DI, TQ], BF16, kind="ExternalOutput").ap()
    dt_o = nc.dram_tensor("dt", [DI, TQ], F32, kind="ExternalOutput").ap()
    sz_o = nc.dram_tensor("sz", [DI, TQ], F32, kind="ExternalOutput").ap()
    bct_o = nc.dram_tensor("bct", [TQ, 32], F32, kind="ExternalOutput").ap()
    _madbg = bool(_os.environ.get("KMA_DBG"))
    if _madbg:
        xd_dbg = nc.dram_tensor("xd_dbg", [128, TQ], F32, kind="ExternalOutput").ap()
        xdb_dbg = nc.dram_tensor("xdb_dbg", [128, TQ], BF16, kind="ExternalOutput").ap()

    xnh = ctx.sbuf("xnhs", [128, DC * TH], BF16)
    wb = ctx.sbuf("wbs", [128, DC * 128], BF16)
    convc = ctx.sbuf("convcs", [128, 64])
    convb = ctx.sbuf("convbs", [128, 16])
    uo = ctx.sbuf("uos", [128, TH])
    ca = ctx.sbuf("ca", [128, TQ])
    cb = ctx.sbuf("cb", [128, TQ])
    ucf = ctx.sbuf("ucf", [128, TQ])
    ucc = ctx.sbuf("ucc", [128, TQ], BF16)
    wxb = ctx.sbuf("wxb", [128, 16 * 96], BF16)
    wdtb = ctx.sbuf("wdtb", [64, DI], BF16)
    bdt = ctx.sbuf("bdts", [128, 16])
    idf = ctx.sbuf("idfs", [128, 128])
    onec = ctx.sbuf("onec", [128, 1])
    xdbl = ctx.sbuf("xdbls", [128, TQ])
    xdblb = ctx.sbuf("xdblb", [128, TQ], BF16)
    eva = ctx.sbuf("eva", [128, 1024])
    evb = ctx.sbuf("evb", [128, 1024])
    evc = ctx.sbuf("evc", [128, 1024])
    bcsb = ctx.sbuf("bcsb", [128, 32 * 32])

    ctx.pool = PsumPool(ctx)
    ctx.op("gpsimd", lambda g: g.memset(onec[:, :], 1.0))
    ms_tok = ctx.gS.count
    xw = 0
    for c in range(DC):
        xw = ctx.dma("sync", xnh[:, c * TH:(c + 1) * TH], xnh_d[c * 128:(c + 1) * 128, :], ctx.lsS)
    ctx.dma("sync", convc[:, :], convc_d[:, :], ctx.lsS)
    ctx.dma("sync", convb[:, :], convb_d[:, :], ctx.lsS)
    ctx.dma("sync", bdt[:, :], bdt_d[:, :], ctx.lsS)
    ctx.dma("sync", idf[:, :], id_d[:, :], ctx.lsS)
    cst_tok = ctx.lsS.count
    ctx.dma("gpsimd", wxb[:, :], wx_d.rearrange("(c p) n -> p c n", p=128), ctx.ldS)
    ctx.dma("gpsimd", wdtb[0:64, :], wdt_d[:, :], ctx.ldS)
    wx_tok = ctx.ldS.count

    utiles = [(t * 512, 512) for t in range(8)] + [(4096, 3)]

    for o in range(16 if "u" in _mp else 0):
        wt = ctx.dma("gpsimd", wb[:, :],
                     win_d[:, o * 128:(o + 1) * 128].rearrange("(c p) n -> p c n", p=128),
                     ctx.ldS, waits=[(ctx.mm, ctx.mm.count)])
        for (ts, n) in utiles:
            bi, p, tk = mm_group(ctx, [(wb[:, c * 128:(c + 1) * 128],
                                        xnh[:, c * TH + ts: c * TH + ts + n]) for c in range(DC)],
                                 waits=[(ctx.ldS, wt), (ctx.lsS, xw)])
            etk = ctx.op("vector", lambda v, ts=ts, n=n, p=p: v.tensor_copy(
                uo[:, ts:ts + n], p[:, 0:n]), inc=ctx.vS, waits=[(ctx.mm, tk)])
            ctx.pool.release(ctx, "vector", bi, [(ctx.vS, etk)])
        ctx.op("vector", lambda v, o=o: v.tensor_scalar(
            ca[:, :], uo[:, 0:TQ], convc[:, o * 4: o * 4 + 1], None, ALU.mult),
            inc=ctx.vS, waits=[(ctx.lsS, cst_tok), (ctx.sS, ctx.sS.count)])
        ctx.op("vector", lambda v, o=o: v.scalar_tensor_tensor(
            cb[:, :], uo[:, 1:TQ + 1], convc[:, o * 4 + 1: o * 4 + 2], ca[:, :], ALU.mult, ALU.add),
            inc=ctx.vS)
        ctx.op("vector", lambda v, o=o: v.scalar_tensor_tensor(
            ca[:, :], uo[:, 2:TQ + 2], convc[:, o * 4 + 2: o * 4 + 3], cb[:, :], ALU.mult, ALU.add),
            inc=ctx.vS)
        ctx.op("vector", lambda v, o=o: v.scalar_tensor_tensor(
            cb[:, :], uo[:, 3:TQ + 3], convc[:, o * 4 + 3: o * 4 + 4], ca[:, :], ALU.mult, ALU.add),
            inc=ctx.vS)
        stk = ctx.op("scalar", lambda sx, o=o: sx.activation(
            out=ucf[:, :], in_=cb[:, :], func=AF.Silu, bias=convb[:, o:o + 1]),
            inc=ctx.sS, waits=[(ctx.vS, ctx.vS.count), (ctx.stS, ctx.stS.count),
                               (ctx.ldS, ctx.ldS.count)])
        ctx.dma("sync", uc_o[o * 128:(o + 1) * 128, :], ucf[:, :], ctx.stS, waits=[(ctx.sS, stk)])
        ctx.dma("gpsimd", ucb_o[o * 128:(o + 1) * 128, :], ucf[:, :], ctx.ldS, waits=[(ctx.sS, stk)])
    for o in range(16 if "z" in _mp else 0):
        wt = ctx.dma("gpsimd", wb[:, :],
                     win_d[:, DI + o * 128: DI + (o + 1) * 128].rearrange("(c p) n -> p c n", p=128),
                     ctx.ldS, waits=[(ctx.mm, ctx.mm.count)])
        for t in range(8):
            ts = 3 + t * 512
            bi, p, tk = mm_group(ctx, [(wb[:, c * 128:(c + 1) * 128],
                                        xnh[:, c * TH + ts: c * TH + ts + 512]) for c in range(DC)],
                                 waits=[(ctx.ldS, wt)])
            etk = ctx.op("scalar", lambda sx, p=p: sx.activation(
                out=eva[:, 0:512], in_=p[:, 0:512], func=AF.Silu),
                inc=ctx.sS, waits=[(ctx.mm, tk), (ctx.stS, ctx.stS.count)])
            ctx.pool.release(ctx, "scalar", bi, [(ctx.sS, etk)])
            ctx.dma("sync", sz_o[o * 128:(o + 1) * 128, t * 512:(t + 1) * 512], eva[:, 0:512],
                    ctx.stS, waits=[(ctx.sS, etk)])
    if "x" not in _mp:
        ctx.wait("sync", ctx.stS, ctx.stS.count)
        ctx.wait("gpsimd", ctx.ldS, ctx.ldS.count)
        return ctx
    ctx.wait("tensor", ctx.ldS, ctx.ldS.count)
    for i in range(8):
        ctx.wait("tensor", ctx.pool.free[i], ctx.pool.uses[i])
        ctx.pool.uses[i] += 1
    xd_tok = []
    for c in range(16):
        lt = ctx.dma("sync", ucc[:, :], ucb_o[c * 128:(c + 1) * 128, :], ctx.lsS,
                     waits=[(ctx.mm, ctx.mm.count)] + ([(ctx.ldS, ctx.ldS.count)] if c == 0 else []))
        for t in range(8):
            ctx.wait("tensor", ctx.lsS, lt)
            ctx.mm.count += 1

            def runx(tt, c=c, t=t, sem=ctx.mm):
                tt.matmul(ctx.pool.tiles[t][0:96, 0:512], wxb[:, c * 96:(c + 1) * 96],
                          ucc[:, t * 512:(t + 1) * 512], start=(c == 0),
                          stop=(c == 15)).then_inc(sem.h, 1)

            ctx.q["tensor"].append(runx)
            if c == 15:
                xd_tok.append(ctx.mm.count)
    for t in range(8):
        etk = ctx.op("vector", lambda v, t=t: v.tensor_copy(
            xdbl[0:96, t * 512:(t + 1) * 512], ctx.pool.tiles[t][0:96, 0:512]),
            inc=ctx.vS, waits=[(ctx.mm, xd_tok[t])])
        ctx.pool.release(ctx, "vector", t, [(ctx.vS, etk)])
    ctx.op("vector", lambda v: v.tensor_copy(xdblb[0:64, :], xdbl[0:64, :]), inc=ctx.vS)
    xdbl_tok = ctx.vS.count
    if _madbg:
        ctx.dma("sync", xd_dbg[:, :], xdbl[:, :], ctx.stS, waits=[(ctx.vS, xdbl_tok)])
        ctx.dma("sync", xdb_dbg[:, :], xdblb[:, :], ctx.stS, waits=[(ctx.vS, xdbl_tok)])
    for o in range(16 if "d" in _mp else 0):
        for t in range(8):
            bi, p, tk = mm_group(ctx, [(wdtb[0:64, o * 128:(o + 1) * 128],
                                        xdblb[0:64, t * 512:(t + 1) * 512])],
                                 waits=[(ctx.vS, xdbl_tok), (ctx.ldS, wx_tok)])
            bcol = bdt[:, o:o + 1]
            par = ((o * 8 + t) % 2) * 512
            A_ = eva[:, par:par + 512]
            B_ = evb[:, par:par + 512]
            C_ = evc[:, par:par + 512]
            ctx.op("vector", lambda v, p=p, bcol=bcol, C_=C_: v.tensor_scalar_add(
                C_, p[:, 0:512], bcol),
                inc=ctx.vS, waits=[(ctx.mm, tk), (ctx.lsS, cst_tok), (ctx.sS, ctx.sS.count)])
            ctx.op("vector", lambda v, A_=A_, C_=C_: v.tensor_scalar_max(A_, C_, 0.0), inc=ctx.vS,
                   waits=[(ctx.stS, 1)])
            ctx.op("vector", lambda v, B_=B_, C_=C_: v.tensor_scalar_mul(B_, C_, -1.0), inc=ctx.vS)
            ctx.op("vector", lambda v, B_=B_, C_=C_: v.tensor_max(B_, B_, C_), inc=ctx.vS)
            vt = ctx.vS.count
            ctx.op("scalar", lambda sx, B_=B_, C_=C_: sx.activation(
                out=C_, in_=B_, func=AF.Exp, scale=-1.0), inc=ctx.sS,
                waits=[(ctx.vS, vt)])
            etk2 = ctx.op("scalar", lambda sx, B_=B_, C_=C_: sx.activation(
                out=B_, in_=C_, func=AF.Ln, bias=onec[:, 0:1]),
                inc=ctx.sS, waits=[(ctx.gS, ms_tok)])
            ctx.pool.release(ctx, "scalar", bi, [(ctx.sS, etk2)])
            etk = ctx.op("vector", lambda v, A_=A_, B_=B_: v.tensor_add(A_, A_, B_),
                         inc=ctx.vS, waits=[(ctx.sS, etk2), (ctx.stS, ctx.stS.count)])
            ctx.dma("sync", dt_o[o * 128:(o + 1) * 128, t * 512:(t + 1) * 512], A_,
                    ctx.stS, waits=[(ctx.vS, etk)])
    for j in range(32 if "b" in _mp else 0):
        i, tile, (fsem, fneed) = ctx.pool.acquire()
        ctx.wait("tensor", fsem, fneed)
        ctx.wait("tensor", ctx.vS, xdbl_tok)
        ctx.wait("tensor", ctx.lsS, cst_tok)
        ctx.mm.count += 1
        tk = ctx.mm.count

        def runt(t, j=j, tile=tile, sem=ctx.mm):
            t.transpose(tile[:, 0:32], xdbl[64:96, j * 128:(j + 1) * 128],
                        idf[64:96, 64:96]).then_inc(sem.h, 1)

        ctx.q["tensor"].append(runt)
        etk = ctx.op("vector", lambda v, j=j, tile=tile: v.tensor_copy(
            bcsb[:, j * 32:(j + 1) * 32], tile[:, 0:32]), inc=ctx.vS, waits=[(ctx.mm, tk)])
        ctx.pool.release(ctx, "vector", i, [(ctx.vS, etk)])
    bc_tok = ctx.vS.count
    if "b" in _mp:
        ctx.dma("sync", bct_o.rearrange("(j p) s -> p j s", p=128), bcsb.ap().rearrange("p (j s) -> p j s", s=32),
                ctx.stS, waits=[(ctx.vS, bc_tok)])
    ctx.wait("sync", ctx.stS, ctx.stS.count)
    ctx.wait("gpsimd", ctx.ldS, ctx.ldS.count)
    return ctx


def host_mamba_a(inputs, core, st):
    b, q = core // 4, core % 4
    xn1 = st["xn1"][core]
    if q == 0:
        halo = np.zeros((D, 3), dtype=xn1.dtype)
    else:
        halo = st["xn1"][core - 1][:, -3:]
    xnh = np.concatenate([halo, xn1], axis=1)
    W_in = np.asarray(inputs["L1_attn_W_in"]).astype(np.float32)
    conv_w = np.asarray(inputs["L1_attn_conv_w"]).astype(np.float32)
    return {
        "xnh": np.ascontiguousarray(xnh),
        "win": np.ascontiguousarray(W_in.T),
        "convc": np.ascontiguousarray(conv_w.reshape(16, 128, 4).transpose(1, 0, 2).reshape(128, 64)),
        "convb": _bcols(inputs["L1_attn_conv_b"], 16),
        "wx": np.ascontiguousarray(np.asarray(inputs["L1_attn_W_x"]).astype(np.float32).T),
        "wdt": np.ascontiguousarray(np.asarray(inputs["L1_attn_W_dt"]).astype(np.float32).T),
        "bdt": _bcols(inputs["L1_attn_b_dt"], 16),
        "ident": np.eye(128, dtype=np.float32),
    }


def build_mamba_scan():
    ctx = Ctx()
    nc = ctx.nc
    NT = 512
    NS = 16
    ntiles = T // NT

    dt_d = nc.dram_tensor("dt", [512, T], F32, kind="ExternalInput").ap()
    uc_d = nc.dram_tensor("uc", [512, T], F32, kind="ExternalInput").ap()
    sz_d = nc.dram_tensor("sz", [512, T], F32, kind="ExternalInput").ap()
    bct_d = nc.dram_tensor("bct", [T, 32], F32, kind="ExternalInput").ap()
    alog_d = nc.dram_tensor("alog", [512, NS], F32, kind="ExternalInput").ap()
    dv_d = nc.dram_tensor("dv", [128, 4], F32, kind="ExternalInput").ap()
    y3_o = nc.dram_tensor("y3", [512, T], BF16, kind="ExternalOutput").ap()

    bufA = ctx.sbuf("bufA", [128, NT * NS])
    bufB = ctx.sbuf("bufB", [128, NT * NS])
    bufC = ctx.sbuf("bufC", [128, NT * NS])
    Bbc = ctx.sbuf("Bbc", [128, NT * NS])
    Cbc = ctx.sbuf("Cbc", [128, NT * NS])
    dtg = ctx.sbuf("dtgs", [128, NT])
    ucg = ctx.sbuf("ucg", [128, NT])
    szg = ctx.sbuf("szg", [128, NT])
    gg = ctx.sbuf("gg", [128, NT])
    yb = ctx.sbuf("yb", [128, NT])
    yb2 = ctx.sbuf("yb2", [128, NT])
    y3b = ctx.sbuf("y3b", [128, NT], BF16)
    rowb = ctx.sbuf("rowb", [1, 2 * 512])
    asb = ctx.sbuf("asbs", [128, 4 * NS])
    dvc = ctx.sbuf("dvcs", [128, 4])
    state = ctx.sbuf("state", [128, 4 * NS])
    onerow = ctx.sbuf("onerow", [1, 128])

    ctx.pool = PsumPool(ctx)
    ctx.op("gpsimd", lambda g: g.memset(onerow[:, :], 1.0))
    ms_tok = ctx.gS.count
    ctx.dma("gpsimd", asb[:, :], alog_d.rearrange("(g p) s -> p g s", p=128), ctx.ldS)
    ctx.dma("gpsimd", dvc[:, :], dv_d[:, :], ctx.ldS)
    al_tok = ctx.ldS.count
    ctx.op("scalar", lambda sx: sx.activation(out=asb[:, :], in_=asb[:, :], func=AF.Exp),
           inc=ctx.sS, waits=[(ctx.ldS, al_tok)])
    atk = ctx.op("vector", lambda v: v.tensor_scalar_mul(asb[:, :], asb[:, :], -1.0),
                 inc=ctx.vS, waits=[(ctx.sS, ctx.sS.count)])

    for ti in range(ntiles):
        tw = ctx.barrier_waits()
        for (coff, dst) in ((0, Bbc), (16, Cbc)):
            for k in range(NS):
                par = (k % 2) * 512
                lw = ctx.dma("sync", rowb[0:1, par:par + 512].rearrange("p (t s) -> p t s", s=16),
                             bct_d[ti * NT + k * 32: ti * NT + (k + 1) * 32, coff:coff + 16]
                             .unsqueeze(0),
                             ctx.lsS, waits=tw + [(ctx.mm, ctx.mm.count)])
                bi, p, tk = mm_group(ctx, [(onerow[0:1, :], rowb[0:1, par:par + 512])],
                                     waits=[(ctx.lsS, lw), (ctx.gS, ms_tok)])
                etk = ctx.op("vector", lambda v, p=p, dst=dst, k=k: v.tensor_copy(
                    dst[:, k * 512:(k + 1) * 512], p[:, 0:512]), inc=ctx.vS, waits=[(ctx.mm, tk)])
                ctx.pool.release(ctx, "vector", bi, [(ctx.vS, etk)])
        bc_tok = ctx.vS.count
        for g in range(4):
            ld = ctx.dma("sync", dtg[:, :], dt_d[g * 128:(g + 1) * 128, ti * NT:(ti + 1) * NT],
                         ctx.lsS, waits=[(ctx.vS, ctx.vS.count), (ctx.sS, ctx.sS.count)])
            ld = ctx.dma("sync", ucg[:, :], uc_d[g * 128:(g + 1) * 128, ti * NT:(ti + 1) * NT], ctx.lsS)
            ld = ctx.dma("sync", szg[:, :], sz_d[g * 128:(g + 1) * 128, ti * NT:(ti + 1) * NT], ctx.lsS)
            ctx.op("vector", lambda v: v.tensor_mul(gg[:, :], dtg[:, :], ucg[:, :]),
                   inc=ctx.vS, waits=[(ctx.lsS, ld), (ctx.vS, atk)])
            a3 = bufA.ap().rearrange("p (t s) -> p t s", s=NS)
            dtbc = dtg.ap().unsqueeze(2).broadcast_to([128, NT, NS])
            abc = asb.ap()[:, g * NS:(g + 1) * NS].unsqueeze(1).broadcast_to([128, NT, NS])
            ctx.op("vector", lambda v, a3=a3, dtbc=dtbc, abc=abc: v.tensor_tensor(
                a3, dtbc, abc, ALU.mult), inc=ctx.vS)
            eE = ctx.vS.count
            sE = ctx.op("scalar", lambda sx: sx.activation(out=bufB[:, :], in_=bufA[:, :],
                                                           func=AF.Exp),
                        inc=ctx.sS, waits=[(ctx.vS, eE)])
            gbc = gg.ap().unsqueeze(2).broadcast_to([128, NT, NS])
            b3 = Bbc.ap().rearrange("p (t s) -> p t s", s=NS)
            ctx.op("vector", lambda v, a3=a3, gbc=gbc, b3=b3: v.tensor_tensor(
                a3, gbc, b3, ALU.mult), inc=ctx.vS, waits=[(ctx.sS, sE), (ctx.vS, bc_tok)])
            aR = bufA.ap().rearrange("p (t s) -> p s t", s=NS)
            bR = bufB.ap().rearrange("p (t s) -> p s t", s=NS)
            cR = bufC.ap().rearrange("p (t s) -> p s t", s=NS)
            for sidx in range(NS):
                if ti == 0:
                    init = 0.0
                else:
                    init = state[:, g * NS + sidx: g * NS + sidx + 1]
                ctx.op("vector", lambda v, sidx=sidx, init=init, aR=aR, bR=bR, cR=cR:
                       v.tensor_tensor_scan(cR[:, sidx, :], bR[:, sidx, :], aR[:, sidx, :],
                                            init, ALU.mult, ALU.add), inc=ctx.vS)
            ctx.op("vector", lambda v, g=g: v.tensor_copy(state[:, g * NS:(g + 1) * NS],
                                                          bufC[:, (NT - 1) * NS: NT * NS]),
                   inc=ctx.vS)
            c3 = Cbc.ap().rearrange("p (t s) -> p t s", s=NS)
            h3 = bufC.ap().rearrange("p (t s) -> p t s", s=NS)
            ctx.op("vector", lambda v, a3=a3, h3=h3, c3=c3: v.tensor_tensor(a3, h3, c3, ALU.mult),
                   inc=ctx.vS)
            ctx.op("vector", lambda v, a3=a3: v.tensor_reduce(yb[:, :], a3, AX.X, ALU.add),
                   inc=ctx.vS)
            ctx.op("vector", lambda v, g=g: v.scalar_tensor_tensor(
                yb2[:, :], ucg[:, :], dvc[:, g:g + 1], yb[:, :], ALU.mult, ALU.add), inc=ctx.vS)
            etk = ctx.op("vector", lambda v: v.tensor_mul(y3b[:, :], yb2[:, :], szg[:, :]),
                         inc=ctx.vS, waits=[(ctx.stS, ctx.stS.count)])
            ctx.dma("sync", y3_o[g * 128:(g + 1) * 128, ti * NT:(ti + 1) * NT], y3b[:, :],
                    ctx.stS, waits=[(ctx.vS, etk)])
    ctx.wait("sync", ctx.stS, ctx.stS.count)
    return ctx


def host_mamba_scan(inputs, core, st):
    b, j = core // 4, core % 4
    rs = slice(512 * j, 512 * (j + 1))
    dt = np.concatenate([st["dt"][4 * b + q][rs, :] for q in range(4)], axis=1)
    uc = np.concatenate([st["uc"][4 * b + q][rs, :] for q in range(4)], axis=1)
    sz = np.concatenate([st["sz"][4 * b + q][rs, :] for q in range(4)], axis=1)
    bct = np.concatenate([st["bct"][4 * b + q] for q in range(4)], axis=0)
    alog = np.asarray(inputs["L1_attn_A_log"]).astype(np.float32)[rs, :]
    dvec = np.asarray(inputs["L1_attn_D"]).astype(np.float32)[rs]
    return {
        "dt": np.ascontiguousarray(dt), "uc": np.ascontiguousarray(uc),
        "sz": np.ascontiguousarray(sz), "bct": np.ascontiguousarray(bct),
        "alog": np.ascontiguousarray(alog),
        "dv": np.ascontiguousarray(dvec.reshape(4, 128).T),
    }


def build_final():
    ctx = Ctx()
    nc = ctx.nc
    NT = 512
    DI = 2048

    y3_d = nc.dram_tensor("y3", [DI, TQ], BF16, kind="ExternalInput").ap()
    wo_d = nc.dram_tensor("wo", [DI, D], F32, kind="ExternalInput").ap()
    x1_d = nc.dram_tensor("x1", [D, TQ], F32, kind="ExternalInput").ap()
    g1_d = nc.dram_tensor("g1", [D, TQ], F32, kind="ExternalInput").ap()
    out_o = nc.dram_tensor("out", [D, TQ], F32, kind="ExternalOutput").ap()

    wo = ctx.sbuf("wos", [128, 16 * D], BF16)
    y3t = ctx.sbuf("y3t", [128, 16 * NT], BF16)
    x1t = ctx.sbuf("x1t", [128, NT])
    g1t = ctx.sbuf("g1t", [128, NT])
    ev = ctx.sbuf("ev", [128, NT])

    ctx.pool = PsumPool(ctx)
    wt = 0
    for k in range(16):
        wt = ctx.dma("gpsimd", wo[:, k * D:(k + 1) * D], wo_d[k * 128:(k + 1) * 128, :], ctx.ldS)
    for ti in range(TQ // NT):
        tsl = slice(ti * NT, (ti + 1) * NT)
        tw = ctx.barrier_waits()
        yw = 0
        for k in range(16):
            yw = ctx.dma("sync", y3t[:, k * NT:(k + 1) * NT], y3_d[k * 128:(k + 1) * 128, tsl],
                         ctx.lsS, waits=tw)
        for o in range(DC):
            bi, p, tk = mm_group(ctx, [(wo[:, k * D + o * 128: k * D + (o + 1) * 128],
                                        y3t[:, k * NT:(k + 1) * NT]) for k in range(16)],
                                 waits=[(ctx.ldS, wt), (ctx.lsS, yw)])
            lx = ctx.dma("sync", x1t[:, :], x1_d[o * 128:(o + 1) * 128, tsl], ctx.lsS,
                         waits=[(ctx.vS, ctx.vS.count)])
            lx = ctx.dma("sync", g1t[:, :], g1_d[o * 128:(o + 1) * 128, tsl], ctx.lsS)
            ctx.op("vector", lambda v, p=p: v.tensor_mul(ev[:, :], p[:, 0:NT], g1t[:, :]),
                   inc=ctx.vS, waits=[(ctx.mm, tk), (ctx.lsS, lx), (ctx.stS, ctx.stS.count)])
            etk = ctx.op("vector", lambda v: v.tensor_add(ev[:, :], ev[:, :], x1t[:, :]),
                         inc=ctx.vS)
            ctx.pool.release(ctx, "vector", bi, [(ctx.vS, etk)])
            ctx.dma("sync", out_o[o * 128:(o + 1) * 128, tsl], ev[:, :], ctx.stS,
                    waits=[(ctx.vS, etk)])
    ctx.wait("sync", ctx.stS, ctx.stS.count)
    return ctx


def host_final(inputs, core, st):
    b, q = core // 4, core % 4
    y3 = np.concatenate([st["y3s"][4 * b + j][:, q * TQ:(q + 1) * TQ] for j in range(4)], axis=0)
    return {
        "y3": np.ascontiguousarray(y3),
        "wo": np.ascontiguousarray(np.asarray(inputs["L1_attn_W_out"]).astype(np.float32).T),
        "x1": st["x1"][core],
        "g1": st["g1"][core],
    }


def _dyn_cols(t, col_expr, nrows, ncols, rowstride):
    return bass.AP(t, col_expr, [[rowstride, nrows], [1, ncols]])


def emit_barrier(ctx):
    sems = [ctx.mm, ctx.vS, ctx.sS, ctx.gS, ctx.ldS, ctx.lsS, ctx.stS]
    for eng in ENGINES:
        for s in sems:
            ctx.wait(eng, s, s.count)


def emit_spinwait(ctx, flg_t, flgsb, idx_val, magic_val):
    psem = ctx.sem(dma=True)

    def run(g, flg_t=flg_t, flgsb=flgsb, idx_val=idx_val, magic_val=magic_val, psem=psem):
        with g.register("nd") as nd, g.register("tgt") as tgt, g.register("fv") as fv, \
             g.register("mgr") as mgr:
            g.reg_alu(mgr, magic_val, 0, ALU.add)
            g.reg_mov(nd, 1)
            g.reg_mov(tgt, 0)
            with g.While(nd):
                g.reg_add(tgt, tgt, 16)
                g.dma_start(out=flgsb[0:1, 0:1],
                            in_=bass.AP(flg_t, idx_val, [[1, 1], [1, 1]])).then_inc(psem.h, 16)
                g.wait_ge(psem.h, tgt)
                g.load(fv, flgsb[0:1, 0:1])
                g.reg_alu(nd, fv, mgr, ALU.not_equal)

    ctx.q["gpsimd"].append(run)
    ctx.op("gpsimd", lambda g: g.nop(), inc=ctx.gS)
    return ctx.gS.count


def emit_l0_upath(ctx, W, x_src, ntok, u_dst):
    NT = 512
    ntiles = ntok // NT
    with ctx.scope():
        xt = ctx.sbuf("p1_xt", [128, DC * NT])
        xn = ctx.sbuf("p1_xn", [128, DC * NT], BF16)
        h1 = ctx.sbuf("p1_h1", [128, 4 * NT], BF16)
        sg = ctx.sbuf("p1_sg", [128, NT])
        ev1 = ctx.sbuf("p1_ev1", [128, NT])
        ev2 = ctx.sbuf("p1_ev2", [128, NT])
        sw1 = ctx.sbuf("p1_sw1", [128, DC * 512], BF16)
        sw2 = ctx.sbuf("p1_sw2", [128, 4 * 128], BF16)
        bwt = ctx.sbuf("p1_bw", [128, DC * 128], BF16)
        for c in range(DC):
            ctx.dma("gpsimd", sw1[:, c * 512:(c + 1) * 512], W["sw1_d"][c * 128:(c + 1) * 128, :], ctx.ldS)
            ctx.dma("gpsimd", bwt[:, c * 128:(c + 1) * 128], W["bw_d"][c * 128:(c + 1) * 128, :], ctx.ldS)
        for c in range(4):
            ctx.dma("gpsimd", sw2[:, c * 128:(c + 1) * 128], W["sw2_d"][c * 128:(c + 1) * 128, :], ctx.ldS)
        W = dict(W)
        W["sw1"], W["sw2"], W["bw"] = sw1, sw2, bwt
        consts, ms_tok = ln_consts2(ctx, NT, "p1")
        for ti in range(ntiles):
            tsl = slice(ti * NT, (ti + 1) * NT)
            bw = ctx.compute_waits()
            xw = 0
            for c in range(DC):
                xw = ctx.dma("sync", xt[:, c * NT:(c + 1) * NT], x_src[c * 128:(c + 1) * 128, tsl],
                             ctx.lsS, waits=bw)
            xn_tok = emit_ln(ctx, consts, xt, NT, lambda c: xn[:, c * NT:(c + 1) * NT],
                             [(ctx.lsS, xw), (ctx.gS, ms_tok)])

            def xnsl(c):
                return xn[:, c * NT:(c + 1) * NT]

            for o in range(4):
                bi, p, tk = mm_group(ctx, [(W["sw1"][:, c * 512 + o * 128: c * 512 + (o + 1) * 128],
                                            xnsl(c)) for c in range(DC)],
                                     waits=[(ctx.ldS, 1), (ctx.vS, xn_tok)])
                etk = ctx.op("scalar", lambda s, o=o, p=p: s.activation(
                    out=h1[:, o * NT:(o + 1) * NT], in_=p[:, 0:NT], func=AF.Silu,
                    bias=W["sb1"][:, o:o + 1]), inc=ctx.sS, waits=[(ctx.mm, tk), (ctx.lsS, 1)])
                ctx.pool.release(ctx, "scalar", bi, [(ctx.sS, etk)])
            h1_tok = ctx.sS.count
            bi, p, tk = mm_group(ctx, [(W["sw2"][:, c * 128:(c + 1) * 128], h1[:, c * NT:(c + 1) * NT])
                                       for c in range(4)], waits=[(ctx.sS, h1_tok)])
            sg_tok = ctx.op("scalar", lambda s, p=p: s.activation(
                out=sg[:, :], in_=p[:, 0:NT], func=AF.Sigmoid, bias=W["sb2"][:, 0:1]),
                inc=ctx.sS, waits=[(ctx.mm, tk)])
            ctx.pool.release(ctx, "scalar", bi, [(ctx.sS, sg_tok)])
            bi, p, tk = mm_group(ctx, [(W["bw"][:, c * 128:(c + 1) * 128], xnsl(c)) for c in range(DC)],
                                 waits=[(ctx.vS, xn_tok)])
            etk = ctx.op("scalar", lambda s, p=p: s.activation(
                out=ev1[:, :], in_=p[:, 0:NT], func=AF.Identity, bias=W["bb"][:, 0:1]),
                inc=ctx.sS, waits=[(ctx.mm, tk)])
            ctx.pool.release(ctx, "scalar", bi, [(ctx.sS, etk)])
            utk = ctx.op("vector", lambda v: v.tensor_mul(ev2[:, :], ev1[:, :], sg[:, :]),
                         inc=ctx.vS, waits=[(ctx.sS, ctx.sS.count), (ctx.ldS, 1)])
            ctx.dma("gpsimd", u_dst[:, tsl], ev2[:, :], ctx.ldS, waits=[(ctx.vS, utk)])
    emit_barrier(ctx)


def ln_consts2(ctx, NT, tag):
    consts = {
        "onecol": ctx.sbuf(f"{tag}_onecol", [128, 1]),
        "onerow": ctx.sbuf(f"{tag}_onerow", [1, 128]),
        "epsc": ctx.sbuf(f"{tag}_epsc", [1, 1]),
        "rows": ctx.sbuf(f"{tag}_rows", [1, 4 * NT]),
        "sq": ctx.sbuf(f"{tag}_sq", [128, DC * NT]),
    }
    ctx.op("gpsimd", lambda g: g.memset(consts["onecol"][:, :], 1.0), inc=ctx.gS)
    ctx.op("gpsimd", lambda g: g.memset(consts["onerow"][:, :], 1.0), inc=ctx.gS)
    ctx.op("gpsimd", lambda g: g.memset(consts["epsc"][:, :], LN_EPS), inc=ctx.gS)
    return consts, ctx.gS.count


def emit_l0_scan(ctx, u_src, A_src, I_src, hs_dst):
    c0 = C0
    TE = T + c0
    with ctx.scope():
        ue = ctx.sbuf("p2_ue", [128, TE], BF16)
        P = ctx.sbuf("p2_P", [128, (c0 + 1) * 128], BF16)
        Fb = ctx.sbuf("p2_F", [128, TE])
        h = ctx.sbuf("p2_h", [128, TE], BF16)
        A_sb = ctx.sbuf("p2_A", [128, 128], BF16)
        ctx.op("gpsimd", lambda g: g.memset(ue[:, 0:c0], 0.0), inc=ctx.gS)
        ms_tok = ctx.gS.count
        ctx.dma("gpsimd", ue[:, c0:], u_src[:, :], ctx.ldS)
        ctx.dma("gpsimd", A_sb[:, :], A_src[:, :], ctx.ldS)
        ctx.dma("gpsimd", P[:, 0:128], I_src[:, :], ctx.ldS)
        ld_tok = ctx.ldS.count
        for k in range(1, c0 + 1):
            bi, p, tk = mm_group(ctx, [(A_sb[:, :], P[:, (k - 1) * 128: k * 128])],
                                 waits=[(ctx.ldS, ld_tok), (ctx.vS, ctx.vS.count)])
            etk = ctx.op("vector", lambda v, k=k, p=p: v.tensor_copy(P[:, k * 128:(k + 1) * 128],
                                                                     p[:, 0:128]),
                         inc=ctx.vS, waits=[(ctx.mm, tk)])
            ctx.pool.release(ctx, "vector", bi, [(ctx.vS, etk)])
        pow_tok = ctx.vS.count
        tiles = [(i * 512, 512) for i in range(TE // 512)]
        if TE % 512:
            tiles.append((TE - TE % 512, TE % 512))
        for (ts, n) in tiles:
            i, tile, (fsem, fneed) = ctx.pool.acquire()
            ctx.wait("tensor", fsem, fneed)
            ctx.wait("tensor", ctx.vS, pow_tok)
            ctx.wait("tensor", ctx.gS, ms_tok)
            ctx.wait("tensor", ctx.ldS, ctx.ldS.count)
            ctx.mm.count += 1
            tok = ctx.mm.count

            def run(t, ts=ts, n=n, tile=tile, sem=ctx.mm):
                inst = None
                first = True
                for k in range(c0):
                    if ts - k + n <= 0:
                        continue
                    lo = max(0, k - ts)
                    src0 = ts - k + lo
                    inst = t.matmul(tile[:, lo:n], P[:, k * 128:(k + 1) * 128],
                                    ue[:, src0: src0 + (n - lo)], start=first, stop=(k == c0 - 1))
                    first = False
                inst.then_inc(sem.h, 1)

            ctx.q["tensor"].append(run)
            etk = ctx.op("vector", lambda v, ts=ts, n=n, tile=tile: v.tensor_copy(
                Fb[:, ts:ts + n], tile[:, 0:n]), inc=ctx.vS, waits=[(ctx.mm, tok)])
            ctx.pool.release(ctx, "vector", i, [(ctx.vS, etk)])
        f_tok = ctx.vS.count
        ctx.op("vector", lambda v: v.tensor_copy(h[:, 0:c0], Fb[:, 0:c0]), inc=ctx.vS,
               waits=[(ctx.vS, f_tok)])
        for r in range(1, TE // c0):
            bi, p, tk = mm_group(ctx, [(P[:, c0 * 128:(c0 + 1) * 128], h[:, (r - 1) * c0: r * c0])],
                                 waits=[(ctx.vS, ctx.vS.count)])
            etk = ctx.op("vector", lambda v, r=r, p=p: v.tensor_add(
                h[:, r * c0:(r + 1) * c0], Fb[:, r * c0:(r + 1) * c0], p[:, 0:c0]),
                inc=ctx.vS, waits=[(ctx.mm, tk)])
            ctx.pool.release(ctx, "vector", bi, [(ctx.vS, etk)])
        ctx.dma("sync", hs_dst[:, :], h[:, c0:], ctx.stS, waits=[(ctx.vS, ctx.vS.count)])
    emit_barrier(ctx)


def emit_quarter_pipeline(ctx, W, tag, x_src, hs_win, xn_scr, g0_scr, x1_dst, xn1_dst,
                          io, with_z):
    TH = TQ + 3
    tiles9 = [(i * 512, 512) for i in range(8)] + [(4096, 3)]
    NT = 512
    with ctx.scope():
        xt = ctx.sbuf(f"{tag}a_xt", [128, DC * NT])
        xn = ctx.sbuf(f"{tag}a_xn", [128, DC * NT], BF16)
        consts, ms_tok = ln_consts2(ctx, NT, f"{tag}a")
        for (ts, n) in tiles9:
            bw = ctx.compute_waits()
            xw = 0
            for c in range(DC):
                xw = ctx.dma("sync", xt[:, c * NT: c * NT + n],
                             x_src[c * 128:(c + 1) * 128, ts:ts + n], ctx.lsS, waits=bw)
            xn_tok = emit_ln(ctx, consts, xt, n, lambda c: xn[:, c * NT: c * NT + n],
                             [(ctx.lsS, xw), (ctx.gS, ms_tok)], stride=NT)
            for c in range(DC):
                ctx.dma("sync", xn_scr[c * 128:(c + 1) * 128, ts:ts + n],
                        xn[:, c * NT: c * NT + n], ctx.stS, waits=[(ctx.vS, xn_tok)])
    emit_barrier(ctx)
    H = 3072
    HC = 24
    NT2 = 256
    with ctx.scope():
        wg = ctx.sbuf(f"{tag}b_wg", [128, DC * H], BF16)
        wu = ctx.sbuf(f"{tag}b_wu", [128, DC * H], BF16)
        wd = ctx.sbuf(f"{tag}b_wd", [128, HC * D], BF16)
        xnt = ctx.sbuf(f"{tag}b_xnt", [128, DC * NT2], BF16)
        gsp = ctx.sbuf(f"{tag}b_gsp", [128, HC * NT2])
        hbf = ctx.sbuf(f"{tag}b_hbf", [128, HC * NT2], BF16)
        fev1 = ctx.sbuf(f"{tag}b_ev1", [128, NT2])
        for c in range(DC):
            ctx.dma("gpsimd", wg[:, c * H:(c + 1) * H], W["wg_d"][c * 128:(c + 1) * 128, :], ctx.ldS)
            ctx.dma("gpsimd", wu[:, c * H:(c + 1) * H], W["wu_d"][c * 128:(c + 1) * 128, :], ctx.ldS)
        for c in range(HC):
            ctx.dma("gpsimd", wd[:, c * D:(c + 1) * D], W["wd_d"][c * 128:(c + 1) * 128, :], ctx.ldS)
        wld = ctx.ldS.count
        if W.get("_dbgw") is not None and tag == "o":
            ctx.dma("sync", W["_dbgw"][:, :], wg[:, 0:3072], ctx.stS, waits=[(ctx.ldS, wld)])
        ftiles = [(i * NT2, NT2) for i in range(16)] + [(4096, 3)]
        for (ts, n) in ftiles:
            bw = ctx.compute_waits()
            xw = 0
            for c in range(DC):
                xw = ctx.dma("sync", xnt[:, c * NT2: c * NT2 + n],
                             xn_scr[c * 128:(c + 1) * 128, ts:ts + n], ctx.lsS, waits=bw)

            def xnsl(c, n=n):
                return xnt[:, c * NT2: c * NT2 + n]

            for o in range(HC):
                bi, p, tk = mm_group(ctx, [(wg[:, c * H + o * 128: c * H + (o + 1) * 128], xnsl(c))
                                           for c in range(DC)],
                                     waits=[(ctx.ldS, wld), (ctx.lsS, xw)])
                etk = ctx.op("scalar", lambda s, o=o, p=p, n=n: s.activation(
                    out=gsp[:, o * NT2: o * NT2 + n], in_=p[:, 0:n], func=AF.Silu, scale=DEQ),
                    inc=ctx.sS, waits=[(ctx.mm, tk)])
                ctx.pool.release(ctx, "scalar", bi, [(ctx.sS, etk)])
            for o in range(HC):
                bi, p, tk = mm_group(ctx, [(wu[:, c * H + o * 128: c * H + (o + 1) * 128], xnsl(c))
                                           for c in range(DC)], waits=[(ctx.lsS, xw)])
                etk = ctx.op("vector", lambda v, o=o, p=p, n=n: v.scalar_tensor_tensor(
                    hbf[:, o * NT2: o * NT2 + n], p[:, 0:n], DEQ, gsp[:, o * NT2: o * NT2 + n],
                    ALU.mult, ALU.mult),
                    inc=ctx.vS, waits=[(ctx.mm, tk), (ctx.sS, ctx.sS.count)])
                ctx.pool.release(ctx, "vector", bi, [(ctx.vS, etk)])
            h_tok = ctx.vS.count
            for o in range(DC):
                bi, p, tk = mm_group(ctx, [(wd[:, c * D + o * 128: c * D + (o + 1) * 128],
                                            hbf[:, c * NT2: c * NT2 + n]) for c in range(HC)],
                                     waits=[(ctx.vS, h_tok)])
                etk = ctx.op("scalar", lambda s, p=p, n=n, fev1=fev1: s.activation(
                    out=fev1[:, 0:n], in_=p[:, 0:n], func=AF.Silu, scale=DEQ),
                    inc=ctx.sS, waits=[(ctx.mm, tk), (ctx.stS, 1)])
                ctx.pool.release(ctx, "scalar", bi, [(ctx.sS, etk)])
                ctx.dma("sync", g0_scr[o * 128:(o + 1) * 128, ts:ts + n], fev1[:, 0:n],
                        ctx.stS, waits=[(ctx.sS, etk)])
    emit_barrier(ctx)
    with ctx.scope():
        dw = ctx.sbuf(f"{tag}d_dw", [128, DC * D], BF16)
        gw = ctx.sbuf(f"{tag}d_gw", [128, DC * D], BF16)
        xt = ctx.sbuf(f"{tag}d_xt", [128, DC * NT])
        xn = ctx.sbuf(f"{tag}d_xn", [128, DC * NT], BF16)
        hst = ctx.sbuf(f"{tag}d_hs", [128, NT], BF16)
        s0t = ctx.sbuf(f"{tag}d_s0", [128, DC * NT])
        ga = ctx.sbuf(f"{tag}d_ga", [128, NT])
        ev1 = ctx.sbuf(f"{tag}d_ev1", [128, NT])
        consts, ms_tok = ln_consts2(ctx, NT, f"{tag}d")
        for c in range(DC):
            ctx.dma("gpsimd", dw[:, c * D:(c + 1) * D], W["dw_d"][c * 128:(c + 1) * 128, :], ctx.ldS)
            ctx.dma("gpsimd", gw[:, c * D:(c + 1) * D], W["gw_d"][c * 128:(c + 1) * 128, :], ctx.ldS)
        wld = ctx.ldS.count
        for (ts, n) in tiles9:
            bw = ctx.compute_waits()
            xw = 0
            for c in range(DC):
                xw = ctx.dma("sync", xt[:, c * NT: c * NT + n],
                             x_src[c * 128:(c + 1) * 128, ts:ts + n], ctx.lsS, waits=bw)
                xw = ctx.dma("sync", xn[:, c * NT: c * NT + n],
                             xn_scr[c * 128:(c + 1) * 128, ts:ts + n], ctx.lsS)

            hw = ctx.dma("sync", hst[:, 0:n], hs_win[:, ts:ts + n], ctx.lsS)

            def xnsl(c, n=n):
                return xn[:, c * NT: c * NT + n]

            for o in range(DC):
                bi, py, tky = mm_group(ctx, [(W["cw"][:, o * 128:(o + 1) * 128], hst[:, 0:n])] +
                                       [(dw[:, c * D + o * 128: c * D + (o + 1) * 128], xnsl(c))
                                        for c in range(DC)],
                                       waits=[(ctx.lsS, hw), (ctx.ldS, wld)])
                y0tk = ctx.op("vector", lambda v, py=py, o=o, n=n: v.tensor_scalar(
                    s0t[:, o * NT: o * NT + n], py[:, 0:n], W["ydb"][:, o:o + 1], None, ALU.add),
                    inc=ctx.vS, waits=[(ctx.mm, tky), (ctx.sS, ctx.sS.count)])
                ctx.pool.release(ctx, "vector", bi, [(ctx.vS, y0tk)])
                bi, pg, tkg = mm_group(ctx, [(gw[:, c * D + o * 128: c * D + (o + 1) * 128],
                                              xnsl(c)) for c in range(DC)], waits=())
                gtk = ctx.op("scalar", lambda s, pg=pg, o=o, n=n: s.activation(
                    out=ga[:, 0:n], in_=pg[:, 0:n], func=AF.Sigmoid, bias=W["gb"][:, o:o + 1]),
                    inc=ctx.sS, waits=[(ctx.mm, tkg), (ctx.vS, y0tk)])
                ctx.pool.release(ctx, "scalar", bi, [(ctx.sS, gtk)])
                g0w = ctx.dma("sync", ev1[:, 0:n], g0_scr[o * 128:(o + 1) * 128, ts:ts + n],
                              ctx.lsS, waits=[(ctx.vS, ctx.vS.count), (ctx.sS, ctx.sS.count)])
                ctx.op("vector", lambda v, o=o, n=n: v.tensor_mul(
                    s0t[:, o * NT: o * NT + n], s0t[:, o * NT: o * NT + n], ga[:, 0:n]),
                    inc=ctx.vS, waits=[(ctx.sS, gtk), (ctx.lsS, g0w)])
                ctx.op("vector", lambda v, o=o, n=n: v.tensor_mul(
                    s0t[:, o * NT: o * NT + n], s0t[:, o * NT: o * NT + n], ev1[:, 0:n]), inc=ctx.vS)
                ctx.op("vector", lambda v, o=o, n=n: v.tensor_add(
                    xt[:, o * NT: o * NT + n], xt[:, o * NT: o * NT + n],
                    s0t[:, o * NT: o * NT + n]), inc=ctx.vS)
            x1_tok = ctx.vS.count
            for c in range(DC):
                lo = max(0, 3 - ts)
                if n - lo > 0:
                    ctx.dma("sync", x1_dst[c * 128:(c + 1) * 128, ts - 3 + lo: ts - 3 + n],
                            xt[:, c * NT + lo: c * NT + n], ctx.stS, waits=[(ctx.vS, x1_tok)])
            xn1_tok = emit_ln(ctx, consts, xt, n, lambda c: xn[:, c * NT: c * NT + n],
                              [(ctx.vS, x1_tok), (ctx.stS, 1)], stride=NT)
            for c in range(DC):
                ctx.dma("sync", xn1_dst[c * 128:(c + 1) * 128, ts:ts + n],
                        xn[:, c * NT: c * NT + n], ctx.stS, waits=[(ctx.vS, xn1_tok)])
    emit_barrier(ctx)
    emit_mamba_a2(ctx, W, tag, xn1_dst, io, with_z)


def emit_mamba_a2(ctx, W, tag, xn1_src, io, with_z):
    TH = TQ + 3
    DI = 2048
    with ctx.scope():
        xnh = ctx.sbuf(f"{tag}m_xnh", [128, DC * TH], BF16)
        wb = ctx.sbuf(f"{tag}m_wb", [128, DC * 128], BF16)
        uo = ctx.sbuf(f"{tag}m_uo", [128, TH])
        ca = ctx.sbuf(f"{tag}m_ca", [128, TQ])
        cb = ctx.sbuf(f"{tag}m_cb", [128, TQ])
        ucf = ctx.sbuf(f"{tag}m_ucf", [128, TQ])
        ucc = ctx.sbuf(f"{tag}m_ucc", [128, TQ], BF16)
        xdbl = ctx.sbuf(f"{tag}m_xdbl", [128, TQ])
        xdblb = ctx.sbuf(f"{tag}m_xdblb", [128, TQ], BF16)
        eva = ctx.sbuf(f"{tag}m_eva", [128, 1024])
        evb = ctx.sbuf(f"{tag}m_evb", [128, 1024])
        evc = ctx.sbuf(f"{tag}m_evc", [128, 1024])
        bcsb = ctx.sbuf(f"{tag}m_bcsb", [128, 32 * 32])
        onec = ctx.sbuf(f"{tag}m_onec", [128, 1])
        ctx.op("gpsimd", lambda g: g.memset(onec[:, :], 1.0), inc=ctx.gS)
        ms_tok = ctx.gS.count
        xw = 0
        for c in range(DC):
            xw = ctx.dma("sync", xnh[:, c * TH:(c + 1) * TH], xn1_src[c * 128:(c + 1) * 128, :], ctx.lsS)
        utiles = [(t * 512, 512) for t in range(8)] + [(4096, 3)]
        for o in range(16):
            wt = ctx.dma("gpsimd", wb[:, :],
                         W["win_d"][:, o * 128:(o + 1) * 128].rearrange("(c p) n -> p c n", p=128),
                         ctx.ldS, waits=[(ctx.mm, ctx.mm.count)])
            for (ts, n) in utiles:
                bi, p, tk = mm_group(ctx, [(wb[:, c * 128:(c + 1) * 128],
                                            xnh[:, c * TH + ts: c * TH + ts + n]) for c in range(DC)],
                                     waits=[(ctx.ldS, wt), (ctx.lsS, xw)])
                etk = ctx.op("vector", lambda v, ts=ts, n=n, p=p: v.tensor_copy(
                    uo[:, ts:ts + n], p[:, 0:n]), inc=ctx.vS, waits=[(ctx.mm, tk)])
                ctx.pool.release(ctx, "vector", bi, [(ctx.vS, etk)])
            ctx.op("vector", lambda v, o=o: v.tensor_scalar(
                ca[:, :], uo[:, 0:TQ], W["convc"][:, o * 4: o * 4 + 1], None, ALU.mult),
                inc=ctx.vS, waits=[(ctx.lsS, 1), (ctx.sS, ctx.sS.count)])
            ctx.op("vector", lambda v, o=o: v.scalar_tensor_tensor(
                cb[:, :], uo[:, 1:TQ + 1], W["convc"][:, o * 4 + 1: o * 4 + 2], ca[:, :],
                ALU.mult, ALU.add), inc=ctx.vS)
            ctx.op("vector", lambda v, o=o: v.scalar_tensor_tensor(
                ca[:, :], uo[:, 2:TQ + 2], W["convc"][:, o * 4 + 2: o * 4 + 3], cb[:, :],
                ALU.mult, ALU.add), inc=ctx.vS)
            ctx.op("vector", lambda v, o=o: v.scalar_tensor_tensor(
                cb[:, :], uo[:, 3:TQ + 3], W["convc"][:, o * 4 + 3: o * 4 + 4], ca[:, :],
                ALU.mult, ALU.add), inc=ctx.vS)
            stk = ctx.op("scalar", lambda sx, o=o: sx.activation(
                out=ucf[:, :], in_=cb[:, :], func=AF.Silu, bias=W["convb"][:, o:o + 1]),
                inc=ctx.sS, waits=[(ctx.vS, ctx.vS.count), (ctx.stS, 1), (ctx.ldS, 1)])
            ctx.dma("sync", io["uc"][o * 128:(o + 1) * 128, :], ucf[:, :], ctx.stS,
                    waits=[(ctx.sS, stk)])
            ctx.dma("gpsimd", io["ucb"][o * 128:(o + 1) * 128, :], ucf[:, :], ctx.ldS,
                    waits=[(ctx.sS, stk)])
        if with_z:
            for o in range(16):
                wt = ctx.dma("gpsimd", wb[:, :],
                             W["win_d"][:, DI + o * 128: DI + (o + 1) * 128]
                             .rearrange("(c p) n -> p c n", p=128),
                             ctx.ldS, waits=[(ctx.mm, ctx.mm.count)])
                for t in range(8):
                    ts = 3 + t * 512
                    bi, p, tk = mm_group(ctx, [(wb[:, c * 128:(c + 1) * 128],
                                                xnh[:, c * TH + ts: c * TH + ts + 512])
                                               for c in range(DC)],
                                         waits=[(ctx.ldS, wt)])
                    etk = ctx.op("scalar", lambda sx, p=p: sx.activation(
                        out=eva[:, 0:512], in_=p[:, 0:512], func=AF.Silu),
                        inc=ctx.sS, waits=[(ctx.mm, tk), (ctx.stS, 1), (ctx.vS, ctx.vS.count)])
                    ctx.pool.release(ctx, "scalar", bi, [(ctx.sS, etk)])
                    ctx.dma("sync", io["sz"][o * 128:(o + 1) * 128, t * 512:(t + 1) * 512],
                            eva[:, 0:512], ctx.stS, waits=[(ctx.sS, etk)])
        ctx.wait("tensor", ctx.ldS, ctx.ldS.count)
        for i in range(8):
            ctx.wait("tensor", ctx.pool.free[i], ctx.pool.uses[i])
            ctx.pool.uses[i] += 1
        xd_tok = []
        for c in range(16):
            lt = ctx.dma("sync", ucc[:, :], io["ucb"][c * 128:(c + 1) * 128, :], ctx.lsS,
                         waits=[(ctx.mm, ctx.mm.count), (ctx.ldS, 1)])
            for t in range(8):
                ctx.wait("tensor", ctx.lsS, lt)
                ctx.mm.count += 1

                def runx(tt, c=c, t=t, sem=ctx.mm):
                    tt.matmul(ctx.pool.tiles[t][0:96, 0:512], W["wxb"][:, c * 96:(c + 1) * 96],
                              ucc[:, t * 512:(t + 1) * 512], start=(c == 0),
                              stop=(c == 15)).then_inc(sem.h, 1)

                ctx.q["tensor"].append(runx)
                if c == 15:
                    xd_tok.append(ctx.mm.count)
        for t in range(8):
            etk = ctx.op("vector", lambda v, t=t: v.tensor_copy(
                xdbl[0:96, t * 512:(t + 1) * 512], ctx.pool.tiles[t][0:96, 0:512]),
                inc=ctx.vS, waits=[(ctx.mm, xd_tok[t])])
            ctx.pool.release(ctx, "vector", t, [(ctx.vS, etk)])
        ctx.op("vector", lambda v: v.tensor_copy(xdblb[0:64, :], xdbl[0:64, :]), inc=ctx.vS)
        xdbl_tok = ctx.vS.count
        for o in range(16):
            for t in range(8):
                bi, p, tk = mm_group(ctx, [(W["wdtb"][0:64, o * 128:(o + 1) * 128],
                                            xdblb[0:64, t * 512:(t + 1) * 512])],
                                     waits=[(ctx.vS, xdbl_tok), (ctx.ldS, 1)])
                bcol = W["bdt"][:, o:o + 1]
                par = ((o * 8 + t) % 2) * 512
                A_ = eva[:, par:par + 512]
                B_ = evb[:, par:par + 512]
                C_ = evc[:, par:par + 512]
                ctx.op("vector", lambda v, p=p, bcol=bcol, C_=C_: v.tensor_scalar_add(
                    C_, p[:, 0:512], bcol),
                    inc=ctx.vS, waits=[(ctx.mm, tk), (ctx.lsS, 1), (ctx.sS, ctx.sS.count)])
                ctx.op("vector", lambda v, A_=A_, C_=C_: v.tensor_scalar_max(A_, C_, 0.0),
                       inc=ctx.vS, waits=[(ctx.stS, 1)])
                ctx.op("vector", lambda v, B_=B_, C_=C_: v.tensor_scalar_mul(B_, C_, -1.0), inc=ctx.vS)
                ctx.op("vector", lambda v, B_=B_, C_=C_: v.tensor_max(B_, B_, C_), inc=ctx.vS)
                vt = ctx.vS.count
                ctx.op("scalar", lambda sx, B_=B_, C_=C_: sx.activation(
                    out=C_, in_=B_, func=AF.Exp, scale=-1.0), inc=ctx.sS, waits=[(ctx.vS, vt)])
                etk2 = ctx.op("scalar", lambda sx, B_=B_, C_=C_: sx.activation(
                    out=B_, in_=C_, func=AF.Ln, bias=onec[:, 0:1]),
                    inc=ctx.sS, waits=[(ctx.gS, ms_tok)])
                ctx.pool.release(ctx, "scalar", bi, [(ctx.sS, etk2)])
                etk = ctx.op("vector", lambda v, A_=A_, B_=B_: v.tensor_add(A_, A_, B_),
                             inc=ctx.vS, waits=[(ctx.sS, etk2), (ctx.stS, 1)])
                ctx.dma("sync", io["dt"][o * 128:(o + 1) * 128, t * 512:(t + 1) * 512], A_,
                        ctx.stS, waits=[(ctx.vS, etk)])
        for j in range(32):
            i, tile, (fsem, fneed) = ctx.pool.acquire()
            ctx.wait("tensor", fsem, fneed)
            ctx.wait("tensor", ctx.vS, xdbl_tok)
            ctx.wait("tensor", ctx.lsS, ctx.lsS.count)
            ctx.mm.count += 1
            tk = ctx.mm.count

            def runt(t, j=j, tile=tile, sem=ctx.mm):
                t.transpose(tile[:, 0:32], xdbl[64:96, j * 128:(j + 1) * 128],
                            W["idf"][64:96, 64:96]).then_inc(sem.h, 1)

            ctx.q["tensor"].append(runt)
            etk = ctx.op("vector", lambda v, j=j, tile=tile: v.tensor_copy(
                bcsb[:, j * 32:(j + 1) * 32], tile[:, 0:32]), inc=ctx.vS, waits=[(ctx.mm, tk)])
            ctx.pool.release(ctx, "vector", i, [(ctx.vS, etk)])
        bc_tok = ctx.vS.count
        ctx.dma("sync", io["bct"].rearrange("(j p) s -> p j s", p=128),
                bcsb.ap().rearrange("p (j s) -> p j s", s=32), ctx.stS, waits=[(ctx.vS, bc_tok)])
    emit_barrier(ctx)


def emit_scan2(ctx, W, tag, io, with_y, init_tile, state_out):
    NT = 512
    NS = 16
    ntiles = TQ // NT
    NG = 16
    with ctx.scope():
        bufA = ctx.sbuf(f"{tag}s_A", [128, NT * NS])
        bufB = ctx.sbuf(f"{tag}s_B", [128, NT * NS])
        bufC = ctx.sbuf(f"{tag}s_C", [128, NT * NS])
        Bbc = ctx.sbuf(f"{tag}s_Bb", [128, NT * NS], BF16)
        Cbc = ctx.sbuf(f"{tag}s_Cb", [128, NT * NS], BF16) if with_y else None
        dtg2 = ctx.sbuf(f"{tag}s_dt", [128, 2 * NT])
        ucg2 = ctx.sbuf(f"{tag}s_uc", [128, 2 * NT])
        szg2 = ctx.sbuf(f"{tag}s_sz", [128, 2 * NT]) if with_y else None
        gg = ctx.sbuf(f"{tag}s_g", [128, NT])
        yb = ctx.sbuf(f"{tag}s_y", [128, NT]) if with_y else None
        yb2 = ctx.sbuf(f"{tag}s_y2", [128, NT]) if with_y else None
        y3b = ctx.sbuf(f"{tag}s_y3", [128, NT], BF16) if with_y else None
        rowb = ctx.sbuf(f"{tag}s_row", [1, NT * NS])
        onerow = ctx.sbuf(f"{tag}s_or", [1, 128])
        ctx.op("gpsimd", lambda g: g.memset(onerow[:, :], 1.0), inc=ctx.gS)
        ms_tok = ctx.gS.count
        for ti in range(ntiles):
            tw = ctx.compute_waits()
            pairs = [(0, Bbc)] + ([(16, Cbc)] if with_y else [])
            for (ci, (coff, dst)) in enumerate(pairs):
                lw = ctx.dma("sync", rowb[0:1, :].rearrange("p (t s) -> p t s", s=16),
                             io["bct"][ti * NT:(ti + 1) * NT, coff:coff + 16]
                             .unsqueeze(0), ctx.lsS, waits=tw + [(ctx.mm, ctx.mm.count)])
                for k in range(NS):
                    bi, p, tk = mm_group(ctx, [(onerow[0:1, :], rowb[0:1, k * 512:(k + 1) * 512])],
                                         waits=[(ctx.lsS, lw), (ctx.gS, ms_tok)])
                    etk = ctx.op("vector", lambda v, p=p, dst=dst, k=k: v.tensor_copy(
                        dst[:, k * 512:(k + 1) * 512], p[:, 0:512]), inc=ctx.vS,
                        waits=[(ctx.mm, tk)])
                    ctx.pool.release(ctx, "vector", bi, [(ctx.vS, etk)])
            for g in range(NG):
                gpar = (g % 2) * NT
                dtg = dtg2[:, gpar:gpar + NT]
                ucg = ucg2[:, gpar:gpar + NT]
                szg = szg2[:, gpar:gpar + NT] if with_y else None
                war = ctx._scan_war.get((tag, g % 2), (0, 0))
                ld = ctx.dma("sync", dtg, io["dt"][g * 128:(g + 1) * 128, ti * NT:(ti + 1) * NT],
                             ctx.lsS, waits=[(ctx.vS, war[0]), (ctx.sS, war[1])])
                ld = ctx.dma("sync", ucg, io["uc"][g * 128:(g + 1) * 128, ti * NT:(ti + 1) * NT],
                             ctx.lsS)
                if with_y:
                    ld = ctx.dma("sync", szg, io["sz"][g * 128:(g + 1) * 128, ti * NT:(ti + 1) * NT],
                                 ctx.lsS)
                ctx.op("vector", lambda v, gg=gg, dtg=dtg, ucg=ucg: v.tensor_mul(gg[:, :], dtg, ucg),
                       inc=ctx.vS, waits=[(ctx.lsS, ld)])
                a3 = bufA.ap().rearrange("p (t s) -> p t s", s=NS)
                dtbc = dtg.unsqueeze(2).broadcast_to([128, NT, NS])
                abc = W["asb"].ap()[:, g * NS:(g + 1) * NS].unsqueeze(1).broadcast_to([128, NT, NS])
                ctx.op("vector", lambda v, a3=a3, dtbc=dtbc, abc=abc: v.tensor_tensor(
                    a3, dtbc, abc, ALU.mult), inc=ctx.vS)
                eE = ctx.vS.count
                sE = ctx.op("scalar", lambda sx: sx.activation(out=bufB[:, :], in_=bufA[:, :],
                                                               func=AF.Exp),
                            inc=ctx.sS, waits=[(ctx.vS, eE)])
                gbc = gg.ap().unsqueeze(2).broadcast_to([128, NT, NS])
                b3 = Bbc.ap().rearrange("p (t s) -> p t s", s=NS)
                ctx.op("vector", lambda v, a3=a3, gbc=gbc, b3=b3: v.tensor_tensor(
                    a3, gbc, b3, ALU.mult), inc=ctx.vS, waits=[(ctx.sS, sE)])
                aR = bufA.ap().rearrange("p (t s) -> p s t", s=NS)
                bR = bufB.ap().rearrange("p (t s) -> p s t", s=NS)
                cR = bufC.ap().rearrange("p (t s) -> p s t", s=NS)
                for sidx in range(NS):
                    if ti == 0:
                        init = 0.0 if init_tile is None else init_tile[:, g * NS + sidx: g * NS + sidx + 1]
                    else:
                        init = state_out[:, g * NS + sidx: g * NS + sidx + 1]
                    ctx.op("vector", lambda v, sidx=sidx, init=init, aR=aR, bR=bR, cR=cR:
                           v.tensor_tensor_scan(cR[:, sidx, :], bR[:, sidx, :], aR[:, sidx, :],
                                                init, ALU.mult, ALU.add), inc=ctx.vS)
                ctx.op("vector", lambda v, g=g: v.tensor_copy(state_out[:, g * NS:(g + 1) * NS],
                                                              bufC[:, (NT - 1) * NS: NT * NS]),
                       inc=ctx.vS)
                if with_y:
                    c3 = Cbc.ap().rearrange("p (t s) -> p t s", s=NS)
                    h3 = bufC.ap().rearrange("p (t s) -> p t s", s=NS)
                    ctx.op("vector", lambda v, a3=a3, h3=h3, c3=c3: v.tensor_tensor(
                        a3, h3, c3, ALU.mult), inc=ctx.vS)
                    ctx.op("vector", lambda v, a3=a3: v.tensor_reduce(yb[:, :], a3, AX.X, ALU.add),
                           inc=ctx.vS)
                    ctx.op("vector", lambda v, g=g, ucg=ucg: v.scalar_tensor_tensor(
                        yb2[:, :], ucg, W["dvc"][:, g:g + 1], yb[:, :], ALU.mult, ALU.add),
                        inc=ctx.vS)
                    etk = ctx.op("vector", lambda v, szg=szg: v.tensor_mul(y3b[:, :], yb2[:, :], szg),
                                 inc=ctx.vS, waits=[(ctx.stS, 1)])
                    ctx.dma("sync", io["y3"][g * 128:(g + 1) * 128, ti * NT:(ti + 1) * NT], y3b[:, :],
                            ctx.stS, waits=[(ctx.vS, etk)])
                ctx._scan_war[(tag, g % 2)] = (ctx.vS.count, ctx.sS.count)
    emit_barrier(ctx)


def emit_moe2(ctx, W, tag, xn1_scr, g1_dst):
    NE, HP = 8, 4096
    QW = 1024
    nc = ctx.nc
    wf_d = nc.dram_tensor(f"{tag}_wfd", [NE, TQ], F32).ap()
    with ctx.scope():
        xnt = ctx.sbuf(f"{tag}e_xnt", [128, DC * 512], BF16)
        xnq = ctx.sbuf(f"{tag}e_xnq", [128, DC * QW], BF16)
        sc = ctx.sbuf(f"{tag}e_sc", [128, 32 * NE])
        m1 = ctx.sbuf(f"{tag}e_m1", [128, 32])
        m2 = ctx.sbuf(f"{tag}e_m2", [128, 32])
        lt = ctx.sbuf(f"{tag}e_lt", [128, 32 * NE])
        msk = ctx.sbuf(f"{tag}e_msk", [128, 32 * NE])
        dd = ctx.sbuf(f"{tag}e_dd", [128, 32])
        w1v = ctx.sbuf(f"{tag}e_w1v", [128, 32])
        w2v = ctx.sbuf(f"{tag}e_w2v", [128, 32])
        eq = ctx.sbuf(f"{tag}e_eq", [128, 32 * NE])
        wsb = ctx.sbuf(f"{tag}e_wsb", [128, 32 * NE])
        wrows = ctx.sbuf(f"{tag}e_wr", [8, 128])
        wfrow = ctx.sbuf(f"{tag}e_wf", [1, 512])
        brbc = ctx.sbuf(f"{tag}e_brbc", [128, NE])
        w1b = ctx.sbuf(f"{tag}e_w1b", [128, 2 * DC * 128], BF16)
        w2b = ctx.sbuf(f"{tag}e_w2b", [128, 2 * 32 * 128], BF16)
        gsb = ctx.sbuf(f"{tag}e_gsb", [128, 32 * QW], BF16)
        acc = ctx.sbuf(f"{tag}e_acc", [128, DC * QW])
        wbc = ctx.sbuf(f"{tag}e_wbc", [128, 2 * 512])
        xbf = ctx.sbuf(f"{tag}e_xbf", [128, 1024])
        sqf = ctx.sbuf(f"{tag}e_sqf", [128, 1024])
        inn = ctx.sbuf(f"{tag}e_inn", [128, 1024])
        ev1 = ctx.sbuf(f"{tag}e_ev1", [128, 512])
        yw = ctx.sbuf(f"{tag}e_yw", [128, 512])
        onerow = ctx.sbuf(f"{tag}e_or", [1, 128])
        ctx.op("gpsimd", lambda g: g.memset(onerow[:, :], 1.0), inc=ctx.gS)
        ms_tok = ctx.gS.count

        bi, p, tk = mm_group(ctx, [(onerow[0:1, :], W["brr"][0:1, :])],
                             waits=[(ctx.lsS, 1), (ctx.gS, ms_tok), (ctx.ldS, 1)])
        etk = ctx.op("vector", lambda v, p=p: v.tensor_copy(brbc[:, :], p[:, 0:NE]),
                     inc=ctx.vS, waits=[(ctx.mm, tk)])
        ctx.pool.release(ctx, "vector", bi, [(ctx.vS, etk)])
        for tt in range(8):
            xw = 0
            for c in range(DC):
                xw = ctx.dma("sync", xnt[:, c * 512:(c + 1) * 512],
                             xn1_scr[c * 128:(c + 1) * 128, 3 + tt * 512: 3 + (tt + 1) * 512],
                             ctx.lsS, waits=[(ctx.mm, ctx.mm.count)])
            for j in range(4):
                tc = tt * 4 + j
                bi, p, tk = mm_group(ctx, [(xnt[:, c * 512 + j * 128: c * 512 + (j + 1) * 128],
                                            W["wrb"][:, c * NE:(c + 1) * NE]) for c in range(DC)],
                                     waits=[(ctx.ldS, 1), (ctx.lsS, xw)])
                etk = ctx.op("vector", lambda v, tc=tc, p=p: v.tensor_add(
                    sc[:, tc * NE:(tc + 1) * NE], p[:, 0:NE], brbc[:, :]),
                    inc=ctx.vS, waits=[(ctx.mm, tk)])
                ctx.pool.release(ctx, "vector", bi, [(ctx.vS, etk)])
        sc3 = sc.ap().rearrange("p (t e) -> p t e", e=NE)
        ctx.op("vector", lambda v: v.tensor_reduce(m1[:, :], sc3, AX.X, ALU.max), inc=ctx.vS)
        m1bc = m1.ap().unsqueeze(2).broadcast_to([128, 32, NE])
        lt3 = lt.ap().rearrange("p (t e) -> p t e", e=NE)
        ctx.op("vector", lambda v: v.tensor_tensor(lt3, sc3, m1bc, ALU.is_ge), inc=ctx.vS)
        ctx.op("vector", lambda v: v.scalar_tensor_tensor(msk[:, :], lt[:, :], -1.0e4, sc[:, :],
                                                          ALU.mult, ALU.add), inc=ctx.vS)
        msk3 = msk.ap().rearrange("p (t e) -> p t e", e=NE)
        ctx.op("vector", lambda v: v.tensor_reduce(m2[:, :], msk3, AX.X, ALU.max), inc=ctx.vS)
        ctx.op("vector", lambda v: v.tensor_sub(dd[:, :], m2[:, :], m1[:, :]), inc=ctx.vS)
        ctx.op("scalar", lambda sx: sx.activation(out=dd[:, :], in_=dd[:, :], func=AF.Exp),
               inc=ctx.sS, waits=[(ctx.vS, ctx.vS.count)])
        ctx.op("vector", lambda v: v.tensor_scalar_add(w1v[:, :], dd[:, :], 1.0), inc=ctx.vS,
               waits=[(ctx.sS, ctx.sS.count)])
        ctx.op("vector", lambda v: v.reciprocal(w1v[:, :], w1v[:, :]), inc=ctx.vS)
        ctx.op("vector", lambda v: v.tensor_mul(w2v[:, :], dd[:, :], w1v[:, :]), inc=ctx.vS)
        eq3 = eq.ap().rearrange("p (t e) -> p t e", e=NE)
        wsb3 = wsb.ap().rearrange("p (t e) -> p t e", e=NE)
        w1bc = w1v.ap().unsqueeze(2).broadcast_to([128, 32, NE])
        w2bc = w2v.ap().unsqueeze(2).broadcast_to([128, 32, NE])
        m2bc = m2.ap().unsqueeze(2).broadcast_to([128, 32, NE])
        ctx.op("vector", lambda v: v.tensor_tensor(eq3, sc3, m1bc, ALU.is_equal), inc=ctx.vS)
        ctx.op("vector", lambda v: v.tensor_tensor(wsb3, eq3, w1bc, ALU.mult), inc=ctx.vS)
        ctx.op("vector", lambda v: v.tensor_tensor(eq3, sc3, m2bc, ALU.is_equal), inc=ctx.vS)
        ctx.op("vector", lambda v: v.tensor_tensor(eq3, eq3, w2bc, ALU.mult), inc=ctx.vS)
        ctx.op("vector", lambda v: v.tensor_add(wsb[:, :], wsb[:, :], eq[:, :]), inc=ctx.vS)
        wsb_tok = ctx.vS.count
        for j in range(32):
            i, tile, (fsem, fneed) = ctx.pool.acquire()
            ctx.wait("tensor", fsem, fneed)
            ctx.wait("tensor", ctx.vS, wsb_tok)
            ctx.wait("tensor", ctx.lsS, ctx.lsS.count)
            ctx.mm.count += 1
            tk = ctx.mm.count

            def runt(t, j=j, tile=tile, sem=ctx.mm):
                t.transpose(tile[0:NE, 0:128], wsb[:, j * NE:(j + 1) * NE],
                            W["idf"][:, :]).then_inc(sem.h, 1)

            ctx.q["tensor"].append(runt)
            etk = ctx.op("vector", lambda v, tile=tile: v.tensor_copy(
                wrows[0:NE, :], tile[0:NE, 0:128]),
                inc=ctx.vS, waits=[(ctx.mm, tk), (ctx.ldS, 1)])
            ctx.pool.release(ctx, "vector", i, [(ctx.vS, etk)])
            ctx.dma("gpsimd", wf_d[0:NE, j * 128:(j + 1) * 128], wrows[0:NE, :], ctx.ldS,
                    waits=[(ctx.vS, etk)])
        wflat_tok = ctx.ldS.count
        GC = GELU_C
        GK = GELU_K
        mm_hist = {}
        w2_hist = {}
        for q in range(4):
            qsl = slice(3 + q * QW, 3 + (q + 1) * QW)
            bwq = ctx.compute_waits()
            xq = 0
            for c in range(DC):
                xq = ctx.dma("sync", xnq[:, c * QW:(c + 1) * QW],
                             xn1_scr[c * 128:(c + 1) * 128, qsl], ctx.lsS, waits=bwq)
            ctx.op("gpsimd", lambda g: g.memset(acc[:, :], 0.0), waits=bwq)
            acc0_tok = ctx.gS.count
            for e in range(NE):
                for o in range(32):
                    par = (o % 2) * DC * 128
                    w1t = ctx.dma("gpsimd", w1b[:, par:par + DC * 128],
                                  W["w1_d"][e * D:(e + 1) * D, o * 128:(o + 1) * 128]
                                  .rearrange("(c p) n -> p c n", p=128), ctx.ldS,
                                  waits=[(ctx.mm, mm_hist.get(o % 2, 0))])
                    banks = []
                    for t2 in range(2):
                        bi, p, tk = mm_group(ctx, [(w1b[:, par + c * 128: par + (c + 1) * 128],
                                                    xnq[:, c * QW + t2 * 512: c * QW + t2 * 512 + 512])
                                                   for c in range(DC)],
                                             waits=[(ctx.ldS, w1t), (ctx.lsS, xq)])
                        banks.append((bi, p, tk))
                    mm_hist[o % 2] = ctx.mm.count
                    bcol = W["b1c"][:, e * 32 + o: e * 32 + o + 1]
                    (b0, p0, tk0), (b1, p1, tk1) = banks
                    ctx.op("scalar", lambda sx, p0=p0, bcol=bcol: sx.activation(
                        out=xbf[:, 0:512], in_=p0[:, 0:512], func=AF.Identity, bias=bcol),
                        inc=ctx.sS, waits=[(ctx.mm, tk0), (ctx.lsS, 1), (ctx.vS, ctx.vS.count)])
                    etk = ctx.op("scalar", lambda sx, p1=p1, bcol=bcol: sx.activation(
                        out=xbf[:, 512:1024], in_=p1[:, 0:512], func=AF.Identity, bias=bcol),
                        inc=ctx.sS, waits=[(ctx.mm, tk1)])
                    ctx.pool.release(ctx, "scalar", b0, [(ctx.sS, etk)])
                    ctx.pool.release(ctx, "scalar", b1, [(ctx.sS, etk)])
                    etk = ctx.op("scalar", lambda sx: sx.activation(
                        out=sqf[:, :], in_=xbf[:, :], func=AF.Square), inc=ctx.sS)
                    ctx.op("vector", lambda v: v.tensor_scalar(inn[:, :], sqf[:, :], GC, 1.0,
                                                               ALU.mult, ALU.add),
                           inc=ctx.vS, waits=[(ctx.sS, etk)])
                    ctx.op("vector", lambda v: v.tensor_mul(inn[:, :], inn[:, :], xbf[:, :]),
                           inc=ctx.vS)
                    stk = ctx.op("scalar", lambda sx: sx.activation(
                        out=inn[:, :], in_=inn[:, :], func=AF.Sigmoid, scale=GK),
                        inc=ctx.sS, waits=[(ctx.vS, ctx.vS.count)])
                    gdst = gsb[:, o * QW: o * QW + QW]
                    ctx.op("vector", lambda v, gdst=gdst: v.tensor_mul(gdst, inn[:, :], xbf[:, :]),
                           inc=ctx.vS, waits=[(ctx.sS, stk)])
                g_tok = ctx.vS.count
                for t2 in range(2):
                    wr0 = q * QW + t2 * 512
                    wl = ctx.dma("sync", wfrow[0:1, :], wf_d[e:e + 1, wr0:wr0 + 512], ctx.lsS,
                                 waits=[(ctx.ldS, wflat_tok), (ctx.mm, ctx.mm.count)])
                    bi, p, tk = mm_group(ctx, [(onerow[0:1, :], wfrow[0:1, :])],
                                         waits=[(ctx.lsS, wl), (ctx.gS, ms_tok)])
                    etk = ctx.op("vector", lambda v, p=p, t2=t2: v.tensor_copy(
                        wbc[:, t2 * 512:(t2 + 1) * 512], p[:, 0:512]),
                        inc=ctx.vS, waits=[(ctx.mm, tk)])
                    ctx.pool.release(ctx, "vector", bi, [(ctx.vS, etk)])
                    for oc in range(DC):
                        par2 = (oc % 2) * 32 * 128
                        w2t = ctx.dma("gpsimd", w2b[:, par2:par2 + 32 * 128],
                                      W["w2_d"][e * HP: (e + 1) * HP, oc * 128:(oc + 1) * 128]
                                      .rearrange("(k p) n -> p k n", p=128), ctx.ldS,
                                      waits=[(ctx.mm, w2_hist.get(oc % 2, 0))])
                        bi, p, tk = mm_group(ctx, [(w2b[:, par2 + k * 128: par2 + (k + 1) * 128],
                                                    gsb[:, k * QW + t2 * 512: k * QW + t2 * 512 + 512])
                                                   for k in range(32)],
                                             waits=[(ctx.ldS, w2t), (ctx.vS, g_tok)])
                        w2_hist[oc % 2] = ctx.mm.count
                        b2col = W["b2c"][:, e * 8 + oc: e * 8 + oc + 1]
                        ctx.op("vector", lambda v, p=p, b2col=b2col, t2=t2: v.scalar_tensor_tensor(
                            yw[:, :], p[:, 0:512], b2col, wbc[:, t2 * 512:(t2 + 1) * 512],
                            ALU.add, ALU.mult),
                            inc=ctx.vS, waits=[(ctx.mm, tk), (ctx.lsS, 1), (ctx.gS, acc0_tok)])
                        adst = acc[:, oc * QW + t2 * 512: oc * QW + t2 * 512 + 512]
                        etk = ctx.op("vector", lambda v, adst=adst: v.tensor_add(adst, adst, yw[:, :]),
                                     inc=ctx.vS)
                        ctx.pool.release(ctx, "vector", bi, [(ctx.vS, etk)])
            for oc in range(DC):
                for t2 in range(2):
                    etk = ctx.op("scalar", lambda sx, oc=oc, t2=t2: sx.activation(
                        out=ev1[:, :], in_=acc[:, oc * QW + t2 * 512: oc * QW + t2 * 512 + 512],
                        func=AF.Silu),
                        inc=ctx.sS, waits=[(ctx.vS, ctx.vS.count), (ctx.stS, 1)])
                    ctx.dma("sync", g1_dst[oc * 128:(oc + 1) * 128,
                                           q * QW + t2 * 512: q * QW + t2 * 512 + 512],
                            ev1[:, :], ctx.stS, waits=[(ctx.sS, etk)])
    emit_barrier(ctx)


def emit_final2(ctx, W, tag, y3_scr, x1_scr, g1_scr, out_dst):
    NT = 512
    with ctx.scope():
        wo = ctx.sbuf(f"{tag}f_wo", [128, 16 * D], BF16)
        y3t = ctx.sbuf(f"{tag}f_y3", [128, 16 * NT], BF16)
        x1t = ctx.sbuf(f"{tag}f_x1", [128, NT])
        g1t = ctx.sbuf(f"{tag}f_g1", [128, NT])
        ev = ctx.sbuf(f"{tag}f_ev", [128, NT])
        wt = 0
        for k in range(16):
            wt = ctx.dma("gpsimd", wo[:, k * D:(k + 1) * D], W["wo_d"][k * 128:(k + 1) * 128, :],
                         ctx.ldS)
        for ti in range(TQ // NT):
            tsl = slice(ti * NT, (ti + 1) * NT)
            tw = ctx.compute_waits()
            yw = 0
            for k in range(16):
                yw = ctx.dma("sync", y3t[:, k * NT:(k + 1) * NT], y3_scr[k * 128:(k + 1) * 128, tsl],
                             ctx.lsS, waits=tw)
            for o in range(DC):
                bi, p, tk = mm_group(ctx, [(wo[:, k * D + o * 128: k * D + (o + 1) * 128],
                                            y3t[:, k * NT:(k + 1) * NT]) for k in range(16)],
                                     waits=[(ctx.ldS, wt), (ctx.lsS, yw)])
                lx = ctx.dma("sync", x1t[:, :], x1_scr[o * 128:(o + 1) * 128, tsl], ctx.lsS,
                             waits=[(ctx.vS, ctx.vS.count)])
                lx = ctx.dma("sync", g1t[:, :], g1_scr[o * 128:(o + 1) * 128, tsl], ctx.lsS)
                ctx.op("vector", lambda v, p=p: v.tensor_mul(ev[:, :], p[:, 0:NT], g1t[:, :]),
                       inc=ctx.vS, waits=[(ctx.mm, tk), (ctx.lsS, lx), (ctx.stS, 1)])
                etk = ctx.op("vector", lambda v: v.tensor_add(ev[:, :], ev[:, :], x1t[:, :]),
                             inc=ctx.vS)
                ctx.pool.release(ctx, "vector", bi, [(ctx.vS, etk)])
                ctx.dma("sync", out_dst[o * 128:(o + 1) * 128, tsl], ev[:, :], ctx.stS,
                        waits=[(ctx.vS, etk)])
    ctx.wait("sync", ctx.stS, ctx.stS.count)
    ctx.wait("gpsimd", ctx.ldS, ctx.ldS.count)


def build_fused():
    ctx = Ctx()
    nc = ctx.nc
    TH = TQ + 3
    DI = 2048

    def din(name, shape, dt=F32):
        return nc.dram_tensor(name, shape, dt, kind="ExternalInput").ap()

    x_full = din("x_full", [D, T])
    x_own = din("x_own", [D, TH])
    x_dup = din("x_dup", [D, TH])
    sw1_d = din("sw1", [D, 512])
    sw2_d = din("sw2", [512, 128])
    bw_d = din("bw", [D, 128])
    dw_d = din("dw", [D, D])
    gw_d = din("gw", [D, D])
    cw_d = din("cw", [128, D])
    sb1_d = din("sb1", [128, 4])
    sb2_d = din("sb2", [128, 1])
    bb_d = din("bb", [128, 1])
    cdb_d = din("cdb", [128, 16])
    gb_d = din("gb", [128, 8])
    A_d = din("A", [128, 128])
    I_d = din("ident", [128, 128])
    wg_d = din("wgd", [D, 3072], I32)
    wu_d = din("wud", [D, 3072], I32)
    wd_d = din("wdd", [3072, D], I32)
    win_d = din("win", [D, 2 * DI])
    convc_d = din("convc", [128, 64])
    convb_d = din("convb", [128, 16])
    wx_d = din("wx", [DI, 96])
    wdt_d = din("wdt", [64, DI])
    bdt_d = din("bdt", [128, 16])
    alog_d = din("alog", [DI, 16])
    dv_d = din("dv", [128, 16])
    wo_d = din("wo", [DI, D])
    w1_d = din("w1", [8 * D, 4096])
    w2_d = din("w2", [8 * 4096, D])
    b1_d = din("b1", [128, 8 * 32])
    b2_d = din("b2", [128, 8 * 8])
    wr_d = din("wr", [D, 8])
    br_d = din("br", [1, 8])
    cfg_d = din("cfg", [1, 16], I32)
    mg_d = din("mg", [1, 4], I32)

    out_o = nc.dram_tensor("out", [D, TQ], F32, kind="ExternalOutput").ap()

    import os as _os
    _dbg_set = set((_os.environ.get("KF_DBG") or "").split(",")) - {""}
    _nocomm = bool(_os.environ.get("KF_NOCOMM"))

    def scr(name, shape, dt=F32):
        kind = "ExternalOutput" if name in _dbg_set else "Internal"
        return nc.dram_tensor(name, shape, dt, kind=kind).ap()

    u_s = scr("u_s", [128, T], BF16)
    hs_s = scr("hs_s", [128, T + 3], BF16)
    hs_t = hs_s.tensor
    ios = {}
    for tg in ("d", "o"):
        ios[tg] = {
            "xn": scr(f"{tg}_xn_s", [D, TH], BF16),
            "g0": scr(f"{tg}_g0_s", [D, TH]),
            "x1": scr(f"{tg}_x1_s", [D, TQ]),
            "xn1": scr(f"{tg}_xn1_s", [D, TH], BF16),
            "uc": scr(f"{tg}_uc_s", [DI, TQ]),
            "ucb": scr(f"{tg}_ucb_s", [DI, TQ], BF16),
            "dt": scr(f"{tg}_dt_s", [DI, TQ]),
            "bct": scr(f"{tg}_bct_s", [TQ, 32]),
        }
    ios["o"]["sz"] = scr("o_sz_s", [DI, TQ])
    ios["o"]["y3"] = scr("o_y3_s", [DI, TQ], BF16)
    g1_s = scr("g1_s", [D, TQ])
    if "dbgw" in _dbg_set:
        W_dbgw = nc.dram_tensor("dbgw", [128, 3072], BF16, kind="ExternalOutput").ap()
    else:
        W_dbgw = None
    shp = nc.dram_tensor("shp", [7 * 128, 256], F32, addr_space="Shared")
    flg = nc.dram_tensor("flgs", [1, 64], I32, addr_space="Shared")

    ctx.pool = PsumPool(ctx)
    W = {}
    gw_sb = {}
    cw = ctx.sbuf("g_cw", [128, D], BF16)
    wxb = ctx.sbuf("g_wxb", [128, 16 * 96], BF16)
    wdtb = ctx.sbuf("g_wdtb", [64, DI], BF16)
    wrb = ctx.sbuf("g_wrb", [128, DC * 8], BF16)
    ctx.dma("gpsimd", cw[:, :], cw_d[:, :], ctx.ldS)
    ctx.dma("gpsimd", wxb[:, :], wx_d.rearrange("(c p) n -> p c n", p=128), ctx.ldS)
    ctx.dma("gpsimd", wdtb[0:64, :], wdt_d[:, :], ctx.ldS)
    ctx.dma("gpsimd", wrb[:, :], wr_d.rearrange("(c p) e -> p c e", p=128), ctx.ldS)
    bias_tiles = {}
    for nm, src, w_ in (("sb1", sb1_d, 4), ("sb2", sb2_d, 1), ("bb", bb_d, 1), ("cdb", cdb_d, 16),
                        ("gb", gb_d, 8), ("convc", convc_d, 64), ("convb", convb_d, 16),
                        ("bdt", bdt_d, 16), ("idf", I_d, 128), ("b1c", b1_d, 256),
                        ("b2c", b2_d, 64), ("dvc", dv_d, 16)):
        t = ctx.sbuf(f"g_{nm}", [128, w_])
        ctx.dma("sync", t[:, :], src[:, :], ctx.lsS)
        bias_tiles[nm] = t
    brr = ctx.sbuf("g_brr", [1, 8])
    ctx.dma("sync", brr[0:1, :], br_d[0:1, :], ctx.lsS)
    cfg_sb = ctx.sbuf("g_cfg", [1, 16], I32)
    mg_sb = ctx.sbuf("g_mg", [1, 4], I32)
    ctx.dma("sync", cfg_sb[0:1, :], cfg_d[0:1, :], ctx.lsS)
    ctx.dma("sync", mg_sb[0:1, :], mg_d[0:1, :], ctx.lsS)
    asb = ctx.sbuf("g_asb", [128, 16 * 16])
    ctx.dma("gpsimd", asb[:, :], alog_d.rearrange("(g p) s -> p g s", p=128), ctx.ldS)
    al_tok = ctx.ldS.count
    ctx.op("scalar", lambda sx: sx.activation(out=asb[:, :], in_=asb[:, :], func=AF.Exp),
           inc=ctx.sS, waits=[(ctx.ldS, al_tok), (ctx.lsS, ctx.lsS.count)])
    ctx.op("vector", lambda v: v.tensor_scalar_mul(asb[:, :], asb[:, :], -1.0), inc=ctx.vS,
           waits=[(ctx.sS, ctx.sS.count)])
    ydb = ctx.sbuf("g_ydb", [128, 8])
    ctx.op("vector", lambda v: v.tensor_add(ydb[:, :], bias_tiles["cdb"][:, 0:8],
                                            bias_tiles["cdb"][:, 8:16]), inc=ctx.vS)
    zt = ctx.sbuf("g_zt", [128, 256])
    dupstate = ctx.sbuf("g_dst", [128, 256])
    ownstate = ctx.sbuf("g_ost", [128, 256])
    Hini = ctx.sbuf("g_hini", [128, 256])
    ctmp1 = ctx.sbuf("g_ct1", [128, 256])
    ctmp2 = ctx.sbuf("g_ct2", [128, 256])
    ctmp3 = ctx.sbuf("g_ct3", [128, 256])
    lam = ctx.sbuf("g_lam", [128, 256])
    dtsum = ctx.sbuf("g_dts", [128, 16])
    flgsb = ctx.sbuf("g_flgsb", [1, 1], I32)
    ctx.op("gpsimd", lambda g: g.memset(zt[:, :], 0.0), inc=ctx.gS)
    zt_tok = ctx.gS.count

    W["_dbgw"] = W_dbgw
    W.update({"sw1_d": sw1_d, "sw2_d": sw2_d, "bw_d": bw_d, "cw": cw, "wxb": wxb, "wdtb": wdtb,
              "wrb": wrb, "brr": brr, "ydb": ydb, "asb": asb,
              "wg_d": wg_d, "wu_d": wu_d, "wd_d": wd_d, "win_d": win_d,
              "dw_d": dw_d, "gw_d": gw_d, "wo_d": wo_d, "w1_d": w1_d, "w2_d": w2_d})
    for k, v in bias_tiles.items():
        W[k] = v
    emit_barrier(ctx)

    cfg_i = cfg_sb
    def pub_zeros(g):
        g.dma_start(out=bass.AP(shp, 6 * 128 * 256, [[256, 128], [1, 256]]),
                    in_=zt[:, :]).then_inc(ctx.ldS.h, 16)
    ctx.wait("gpsimd", ctx.gS, zt_tok)
    ctx.wait("gpsimd", ctx.lsS, ctx.lsS.count)
    if not _nocomm:
        ctx.ldS.count += 16
        _zdone = ctx.ldS.count
        ctx.q["gpsimd"].append(pub_zeros)
    else:
        _zdone = ctx.ldS.count

    def pub_hello(g, _zdone=_zdone):
        g.wait_ge(ctx.ldS.h, _zdone)
        with g.register("ph1") as r1, g.register("ph2") as r2:
            g.load(r1, cfg_i[0:1, 8:9])
            g.load(r2, cfg_i[0:1, 9:10])
            g.dma_start(out=bass.AP(flg, r1, [[1, 1], [1, 1]]), in_=mg_sb[0:1, 1:2]).then_inc(ctx.ldS.h, 16)
            g.dma_start(out=bass.AP(flg, r2, [[1, 1], [1, 1]]), in_=mg_sb[0:1, 2:3]).then_inc(ctx.ldS.h, 16)
    if not _nocomm:
        ctx.ldS.count += 32
        ctx.q["gpsimd"].append(pub_hello)

    hs_own = scr("hs_own", [128, TQ + 3], BF16)
    hs_dup = scr("hs_dup", [128, TQ + 3], BF16)
    emit_l0_upath(ctx, W, x_full, T, u_s)
    zpad = ctx.sbuf("g_zpad", [128, 3], BF16)
    ctx.op("gpsimd", lambda g: g.memset(zpad[:, :], 0.0), inc=ctx.gS)
    ctx.dma("gpsimd", hs_s[:, 0:3], zpad[:, :], ctx.ldS, waits=[(ctx.gS, ctx.gS.count)])
    emit_l0_scan(ctx, u_s, A_d, I_d, hs_s[:, 3:])

    def hs_windows(g):
        with g.register("whoff") as ro, g.register("whof2") as rd:
            g.load(ro, cfg_i[0:1, 1:2])
            g.load(rd, cfg_i[0:1, 0:1])
            g.dma_start(out=hs_own[:, :],
                        in_=bass.AP(hs_t, ro, [[T + 3, 128], [1, TQ + 3]])).then_inc(ctx.ldS.h, 16)
            g.dma_start(out=hs_dup[:, :],
                        in_=bass.AP(hs_t, rd, [[T + 3, 128], [1, TQ + 3]])).then_inc(ctx.ldS.h, 16)
    ctx.wait("gpsimd", ctx.stS, ctx.stS.count)
    ctx.wait("gpsimd", ctx.lsS, ctx.lsS.count)
    if _nocomm:
        ctx.dma("gpsimd", hs_own[:, :], hs_s[:, 0:TQ + 3], ctx.ldS)
        ctx.dma("gpsimd", hs_dup[:, :], hs_s[:, 0:TQ + 3], ctx.ldS)
    else:
        ctx.ldS.count += 32
        ctx.q["gpsimd"].append(hs_windows)
    emit_barrier(ctx)

    emit_quarter_pipeline(ctx, W, "d", x_dup, hs_dup,
                          ios["d"]["xn"], ios["d"]["g0"],
                          ios["d"]["x1"], ios["d"]["xn1"], ios["d"], with_z=False)
    with ctx.scope():
        dtrow = ctx.sbuf("s_dtrow", [128, TQ])
        for g_ in range(16):
            lw = ctx.dma("sync", dtrow[:, :], ios["d"]["dt"][g_ * 128:(g_ + 1) * 128, :], ctx.lsS,
                         waits=[(ctx.vS, ctx.vS.count)])
            ctx.op("vector", lambda v, g_=g_, dtrow=dtrow: v.tensor_reduce(
                dtsum[:, g_:g_ + 1], dtrow.ap().unsqueeze(1), AX.X, ALU.add),
                inc=ctx.vS, waits=[(ctx.lsS, lw)])
    emit_barrier(ctx)
    a3v = asb.ap().rearrange("p (g s) -> p g s", s=16)
    l3v = lam.ap().rearrange("p (g s) -> p g s", s=16)
    dbc = dtsum.ap().unsqueeze(2).broadcast_to([128, 16, 16])
    ctx.op("vector", lambda v: v.tensor_tensor(l3v, a3v, dbc, ALU.mult), inc=ctx.vS)
    lam_tok = ctx.vS.count
    emit_scan2(ctx, W, "ds", ios["d"], with_y=False, init_tile=None, state_out=dupstate)
    dstate_tok = ctx.vS.count

    ctx.wait("gpsimd", ctx.vS, max(lam_tok, dstate_tok))
    _base = ctx.ldS.count

    def pub_r1(g, _base=_base):
        with g.register("pr1a") as ra, g.register("pr1b") as rb, g.register("pr1c") as rc:
            g.load(ra, cfg_i[0:1, 2:3])
            g.reg_alu(ra, ra, 32768, ALU.mult)
            g.load(rb, cfg_i[0:1, 3:4])
            g.reg_alu(rb, rb, 32768, ALU.mult)
            g.load(rc, cfg_i[0:1, 10:11])
            g.dma_start(out=bass.AP(shp, ra, [[256, 128], [1, 256]]),
                        in_=dupstate[:, :]).then_inc(ctx.ldS.h, 16)
            g.dma_start(out=bass.AP(shp, rb, [[256, 128], [1, 256]]),
                        in_=lam[:, :]).then_inc(ctx.ldS.h, 16)
            g.wait_ge(ctx.ldS.h, _base + 32)
            g.dma_start(out=bass.AP(flg, rc, [[1, 1], [1, 1]]), in_=mg_sb[0:1, 1:2]).then_inc(ctx.ldS.h, 16)
    if not _nocomm:
        ctx.ldS.count += 48
        ctx.q["gpsimd"].append(pub_r1)

    emit_quarter_pipeline(ctx, W, "o", x_own, hs_own, ios["o"]["xn"], ios["o"]["g0"],
                          ios["o"]["x1"], ios["o"]["xn1"], ios["o"], with_z=True)
    emit_moe2(ctx, W, "m", ios["o"]["xn1"], g1_s)

    emit_barrier(ctx)

    psem1 = ctx.sem(dma=True)

    def wait_r1(g):
        with g.register("nd1") as nd, g.register("tgt1") as tgt, g.register("fv1") as fv, \
             g.register("ix1") as ix, g.register("mg1") as mgv:
            g.load(ix, cfg_i[0:1, 11:12])
            g.load(mgv, mg_sb[0:1, 1:2])
            g.reg_mov(nd, 1)
            g.reg_mov(tgt, 0)
            with g.While(nd):
                g.reg_add(tgt, tgt, 16)
                g.dma_start(out=flgsb[0:1, 0:1],
                            in_=bass.AP(flg, ix, [[1, 1], [1, 1]])).then_inc(psem1.h, 16)
                g.wait_ge(psem1.h, tgt)
                g.load(fv, flgsb[0:1, 0:1])
                g.reg_alu(nd, fv, mgv, ALU.not_equal)

    if not _nocomm:
        ctx.q["gpsimd"].append(wait_r1)
    ctx.op("gpsimd", lambda g: g.nop(), inc=ctx.gS)
    r1w_tok = ctx.gS.count

    def rd_r1(g):
        with g.register("rr1a") as ra, g.register("rr1b") as rb, g.register("rr1c") as rc:
            g.load(ra, cfg_i[0:1, 4:5])
            g.reg_alu(ra, ra, 32768, ALU.mult)
            g.load(rb, cfg_i[0:1, 5:6])
            g.reg_alu(rb, rb, 32768, ALU.mult)
            g.load(rc, cfg_i[0:1, 6:7])
            g.reg_alu(rc, rc, 32768, ALU.mult)
            g.dma_start(out=ctmp1[:, :], in_=bass.AP(shp, ra, [[256, 128], [1, 256]])).then_inc(ctx.ldS.h, 16)
            g.dma_start(out=ctmp2[:, :], in_=bass.AP(shp, rb, [[256, 128], [1, 256]])).then_inc(ctx.ldS.h, 16)
            g.dma_start(out=ctmp3[:, :], in_=bass.AP(shp, rc, [[256, 128], [1, 256]])).then_inc(ctx.ldS.h, 16)
    ctx.wait("gpsimd", ctx.gS, r1w_tok)
    if not _nocomm:
        ctx.ldS.count += 48
        ctx.q["gpsimd"].append(rd_r1)
    else:
        ctx.op("gpsimd", lambda g: g.memset(ctmp1[:, :], 0.0), inc=ctx.gS)
        ctx.op("gpsimd", lambda g: g.memset(ctmp2[:, :], 0.0), inc=ctx.gS)
        ctx.op("gpsimd", lambda g: g.memset(ctmp3[:, :], 0.0), inc=ctx.gS)
    r1r_tok = ctx.ldS.count
    ctx.op("scalar", lambda sx: sx.activation(out=ctmp1[:, :], in_=ctmp1[:, :], func=AF.Exp),
           inc=ctx.sS, waits=[(ctx.ldS, r1r_tok), (ctx.gS, r1w_tok)])
    ctx.op("vector", lambda v: v.tensor_mul(Hini[:, :], ctmp1[:, :], ctmp2[:, :]), inc=ctx.vS,
           waits=[(ctx.sS, ctx.sS.count)])
    ctx.op("vector", lambda v: v.tensor_add(Hini[:, :], Hini[:, :], ctmp3[:, :]), inc=ctx.vS)
    psem2 = ctx.sem(dma=True)

    def wait_r2(g):
        with g.register("nd2") as nd, g.register("tgt2") as tgt, g.register("fv2") as fv, \
             g.register("ix2") as ix, g.register("mg2") as mgv:
            g.load(ix, cfg_i[0:1, 12:13])
            g.load(mgv, mg_sb[0:1, 2:3])
            g.reg_mov(nd, 1)
            g.reg_mov(tgt, 0)
            with g.While(nd):
                g.reg_add(tgt, tgt, 16)
                g.dma_start(out=flgsb[0:1, 0:1],
                            in_=bass.AP(flg, ix, [[1, 1], [1, 1]])).then_inc(psem2.h, 16)
                g.wait_ge(psem2.h, tgt)
                g.load(fv, flgsb[0:1, 0:1])
                g.reg_alu(nd, fv, mgv, ALU.not_equal)

    if not _nocomm:
        ctx.q["gpsimd"].append(wait_r2)
    ctx.op("gpsimd", lambda g: g.nop(), inc=ctx.gS)
    r2w_tok = ctx.gS.count

    def rd_r2(g):
        with g.register("rr2") as ra:
            g.load(ra, cfg_i[0:1, 7:8])
            g.reg_alu(ra, ra, 32768, ALU.mult)
            g.dma_start(out=ctmp1[:, :], in_=bass.AP(shp, ra, [[256, 128], [1, 256]])).then_inc(ctx.ldS.h, 16)
    ctx.wait("gpsimd", ctx.gS, r2w_tok)
    ctx.wait("gpsimd", ctx.sS, ctx.sS.count)
    if not _nocomm:
        ctx.ldS.count += 16
        ctx.q["gpsimd"].append(rd_r2)
    r2r_tok = ctx.ldS.count
    hini_tok = ctx.op("vector", lambda v: v.tensor_add(Hini[:, :], Hini[:, :], ctmp1[:, :]),
                      inc=ctx.vS, waits=[(ctx.ldS, r2r_tok)])
    emit_barrier(ctx)

    emit_scan2(ctx, W, "os", ios["o"], with_y=True, init_tile=Hini, state_out=ownstate)
    ostate_tok = ctx.vS.count

    ctx.wait("gpsimd", ctx.vS, ostate_tok)
    _b2 = ctx.ldS.count
    if _nocomm:
        _b2 = None

    def pub_r2(g, _b2=_b2):
        with g.register("p2a") as ra, g.register("p2b") as rb:
            g.load(ra, cfg_i[0:1, 13:14])
            g.reg_alu(ra, ra, 32768, ALU.mult)
            g.load(rb, cfg_i[0:1, 14:15])
            g.dma_start(out=bass.AP(shp, ra, [[256, 128], [1, 256]]),
                        in_=ownstate[:, :]).then_inc(ctx.ldS.h, 16)
            g.wait_ge(ctx.ldS.h, _b2 + 16)
            g.dma_start(out=bass.AP(flg, rb, [[1, 1], [1, 1]]), in_=mg_sb[0:1, 2:3]).then_inc(ctx.ldS.h, 16)
    if not _nocomm:
        ctx.ldS.count += 32
        ctx.q["gpsimd"].append(pub_r2)

    emit_final2(ctx, W, "f", ios["o"]["y3"], ios["o"]["x1"], g1_s, out_o)
    return ctx


def host_fused(inputs, core, magic):
    b, lid = core // 4, core % 4
    p = lid % 2
    x = np.asarray(inputs["L0_attn_x"])[b]
    xf = np.ascontiguousarray(x.T).astype(np.float32)

    def xslice(q):
        s = q * TQ
        halo = np.zeros((D, 3), np.float32) if s == 0 else xf[:, s - 3:s]
        return np.ascontiguousarray(np.concatenate([halo, xf[:, s:s + TQ]], 1))

    dupq = lid % 2
    W1 = np.asarray(inputs["L1_ffn_W1"]).astype(np.float32)
    W2 = np.asarray(inputs["L1_ffn_W2"]).astype(np.float32)
    b1 = np.asarray(inputs["L1_ffn_b1"]).astype(np.float32)
    b2 = np.asarray(inputs["L1_ffn_b2"]).astype(np.float32)
    conv_w = np.asarray(inputs["L1_attn_conv_w"]).astype(np.float32)
    cfg = np.zeros(16, np.int32)
    cfg[0] = dupq * TQ
    cfg[1] = lid * TQ
    cfg[2] = p * 3 + 0
    cfg[3] = p * 3 + 1
    if lid == 2:
        cfg[4], cfg[5], cfg[6] = (1 - p) * 3 + 1, p * 3 + 0, (1 - p) * 3 + 0
    else:
        cfg[4], cfg[5], cfg[6] = 6, 6, 6
    cfg[7] = (1 - p) * 3 + 2 if lid in (1, 3) else 6
    cfg[8] = p * 8 + 3
    cfg[9] = p * 8 + 4
    cfg[10] = p * 8 + 1
    cfg[11] = (1 - p) * 8 + 1 if lid == 2 else p * 8 + 3
    cfg[12] = (1 - p) * 8 + 2 if lid in (1, 3) else p * 8 + 4
    cfg[13] = p * 3 + 2
    cfg[14] = p * 8 + 2
    return {
        "x_full": xf,
        "x_own": xslice(lid),
        "x_dup": xslice(dupq),
        "sw1": _fm(inputs["L0_attn_s_w1"]).astype(np.float32),
        "sw2": _fm(inputs["L0_attn_s_w2"]).astype(np.float32),
        "bw": _fm(inputs["L0_attn_B_w"]).astype(np.float32),
        "dw": _fm(inputs["L0_attn_D_w"]).astype(np.float32),
        "gw": _fm(inputs["L0_attn_g_w"]).astype(np.float32),
        "cw": np.ascontiguousarray(np.asarray(inputs["L0_attn_C_w"]).T).astype(np.float32),
        "sb1": _bcols(inputs["L0_attn_s_b1"], 4),
        "sb2": _bcols(inputs["L0_attn_s_b2"], 1),
        "bb": _bcols(inputs["L0_attn_B_b"], 1),
        "cdb": np.concatenate([_bcols(inputs["L0_attn_C_b"], 8), _bcols(inputs["L0_attn_D_b"], 8)], 1),
        "gb": _bcols(inputs["L0_attn_g_b"], 8),
        "A": np.ascontiguousarray(np.asarray(inputs["L0_attn_A"]).astype(np.float32)),
        "ident": np.eye(128, dtype=np.float32),
        "wgd": _fm(inputs["L0_ffn_w_gate"]).astype(np.int32),
        "wud": _fm(inputs["L0_ffn_w_up"]).astype(np.int32),
        "wdd": _fm(inputs["L0_ffn_w_down"]).astype(np.int32),
        "win": np.ascontiguousarray(np.asarray(inputs["L1_attn_W_in"]).astype(np.float32).T),
        "convc": np.ascontiguousarray(conv_w.reshape(16, 128, 4).transpose(1, 0, 2).reshape(128, 64)),
        "convb": _bcols(inputs["L1_attn_conv_b"], 16),
        "wx": np.ascontiguousarray(np.asarray(inputs["L1_attn_W_x"]).astype(np.float32).T),
        "wdt": np.ascontiguousarray(np.asarray(inputs["L1_attn_W_dt"]).astype(np.float32).T),
        "bdt": _bcols(inputs["L1_attn_b_dt"], 16),
        "alog": np.ascontiguousarray(np.asarray(inputs["L1_attn_A_log"]).astype(np.float32)),
        "dv": np.ascontiguousarray(np.asarray(inputs["L1_attn_D"]).astype(np.float32).reshape(16, 128).T),
        "wo": np.ascontiguousarray(np.asarray(inputs["L1_attn_W_out"]).astype(np.float32).T),
        "w1": np.ascontiguousarray(W1.reshape(8 * 1024, 4096)),
        "w2": np.ascontiguousarray(W2.reshape(8 * 4096, 1024)),
        "b1": np.ascontiguousarray(np.concatenate([_bcols(b1[e], 32) for e in range(8)], 1)),
        "b2": np.ascontiguousarray(np.concatenate([_bcols(b2[e], 8) for e in range(8)], 1)),
        "wr": np.ascontiguousarray(np.asarray(inputs["L1_ffn_Wr"]).astype(np.float32)),
        "br": np.ascontiguousarray(np.asarray(inputs["L1_ffn_br"]).astype(np.float32).reshape(1, 8)),
        "cfg": cfg.reshape(1, 16),
        "mg": np.asarray([magic, magic + 1, magic + 2, magic + 3], np.int32).reshape(1, 4),
    }


def kernel_fused(**inputs):
    import random
    magic = random.randint(10_000_000, 2_000_000_000)
    ctx = build_fused()
    maps = [host_fused(inputs, c, magic) for c in range(NCORES)]
    res = ctx.run(maps)
    out = np.empty((2, T, D), np.float32)
    for c in range(NCORES):
        b, q = c // 4, c % 4
        out[b, q * TQ:(q + 1) * TQ, :] = np.asarray(res[c]["out"]).T
    return out
